# version 05
# baseline (speedup 1.0000x reference)
import numpy as np
import concourse.bass as bass
import concourse.mybir as mybir
from concourse.bass import IndirectOffsetOnAxis
from concourse.expressions_rust import RuntimeValue
from concourse.bass_utils import run_bass_kernel_spmd

F32 = mybir.dt.float32
BF16 = mybir.dt.bfloat16
I32 = mybir.dt.int32
U32 = mybir.dt.uint32
AF = mybir.ActivationFunctionType
OP = mybir.AluOpType

GG = 2
T = 8192
D = 1024
NH = 16
HD = 64
E = 8
CAP = 640
FF = 4096
QP = 4
TLOC = T // QP
NTILE_LOC = TLOC // 128
NTILE = T // 128
EPS = 1e-5
BIGIDX = 60000.0


class Sem:

    def __init__(self, nc, name):
        self._cm = nc.semaphore(name)
        self.h = self._cm.__enter__()
        self.v = 0

    def tick(self, inst, n=1):
        inst.then_inc(self.h, n)
        self.v += n

    def dma(self, inst):
        self.tick(inst, 16)

    def wait(self, eng):
        if self.v > 0:
            eng.wait_ge(self.h, self.v)

    def wait_val(self, eng, val):
        if val > 0:
            eng.wait_ge(self.h, val)


class Prog:

    def __init__(self):
        self.nc = bass.Bass("TRN2", target_bir_lowering=False)
        self._ctx = []

    def sems(self):
        nc = self.nc
        for name in ("LD", "ST", "PE", "ACT", "DVE", "GP", "X1", "X2"):
            setattr(self, name, Sem(nc, name.lower()))

    def sbuf(self, name, shape, dtype):
        cm = self.nc.sbuf_tensor(name, shape, dtype)
        t = cm.__enter__()
        self._ctx.append(cm)
        return t

    def psum(self, name, shape, dtype=F32):
        cm = self.nc.psum_tensor(name, shape, dtype)
        t = cm.__enter__()
        self._ctx.append(cm)
        return t

    def block(self):
        cm = self.nc.Block()
        b = cm.__enter__()
        self._ctx.append(cm)
        return b

    def close(self):
        for cm in reversed(self._ctx):
            cm.__exit__(None, None, None)



def host_consts():
    c = {}
    c["ident"] = np.eye(128, dtype=np.float32)
    c["triu"] = np.triu(np.ones((128, 128), dtype=np.float32)).astype(np.float32)
    c["cols8"] = np.tile(np.arange(8, dtype=np.float32), (128, 1))
    c["triu_bf"] = np.triu(np.ones((128, 128), dtype=np.float32))
    c["striu_bf"] = np.triu(np.ones((128, 128), dtype=np.float32), 1)
    c["ones128x1_bf"] = np.ones((128, 1), dtype=np.float32)
    c["ones1x128"] = np.ones((1, 128), dtype=np.float32)
    c["eye64s_bf"] = np.vstack([np.eye(64, dtype=np.float32)] * 2)
    c["ident128_bf"] = np.eye(128, dtype=np.float32)
    return c


def emit_ln_tile(p, xt, stat, mv, sd, rstd, lnf, lnb):
    nc = p.nc
    nc.vector.bn_stats(stat[:, 0:6], xt[:, 0:512])
    ins = nc.vector.bn_stats(stat[:, 6:12], xt[:, 512:1024])
    p.DVE.tick(ins)
    p.DVE.wait(nc.vector)
    ins = nc.vector.bn_aggr(mv[:], stat[:])
    p.DVE.tick(ins)
    p.DVE.wait(nc.vector)
    ins = nc.vector.tensor_scalar_add(sd[:], mv[:, 1:2], EPS)
    p.DVE.tick(ins)
    p.DVE.wait(nc.scalar)
    ins = nc.scalar.activation(sd[:], sd[:], AF.Sqrt)
    p.ACT.tick(ins)
    p.ACT.wait(nc.vector)
    ins = nc.vector.reciprocal(rstd[:], sd[:])
    p.DVE.tick(ins)
    p.DVE.wait(nc.vector)
    ins = nc.vector.tensor_scalar(lnf[:], xt[:], mv[:, 0:1], rstd[:, 0:1],
                                  OP.subtract, OP.mult)
    p.DVE.tick(ins)
    if lnb is not None:
        ins = nc.vector.tensor_copy(lnb[:], lnf[:])
        p.DVE.tick(ins)


def emit_router_tile(p, lnf, identsb, wr_sb, lnT, ptr, plog, logit, scr, dec):
    nc = p.nc
    for c in range(8):
        bank = ptr[c % 2]
        p.ACT.wait(nc.tensor)
        ins = nc.tensor.transpose(bank[:], lnf[:, c * 128:(c + 1) * 128], identsb[:])
        p.PE.tick(ins)
        p.PE.wait(nc.scalar)
        ins = nc.scalar.copy(lnT[:, c * 128:(c + 1) * 128], bank[:])
        p.ACT.tick(ins)
    p.ACT.wait(nc.tensor)
    for c in range(8):
        ins = nc.tensor.matmul(plog[:], lnT[:, c * 128:(c + 1) * 128],
                               wr_sb[:, c, :], start=(c == 0), stop=(c == 7))
    p.PE.tick(ins)
    p.PE.wait(nc.scalar)
    p.DVE.wait(nc.scalar)
    ins = nc.scalar.copy(logit[:], plog[:])
    p.ACT.tick(ins)
    p.ACT.wait(nc.vector)
    mx1 = scr["mx1"]
    ins = nc.vector.reduce_max(mx1[:], logit[:], axis=mybir.AxisListType.X)
    p.DVE.tick(ins)
    p.DVE.wait(nc.vector)
    ins = nc.vector.tensor_scalar(logit[:], logit[:], mx1[:, 0:1], None, OP.subtract)
    p.DVE.tick(ins)
    p.DVE.wait(nc.scalar)
    ins = nc.scalar.activation(logit[:], logit[:], AF.Exp)
    p.ACT.tick(ins)
    p.ACT.wait(nc.vector)
    sm = scr["sm"]
    ins = nc.vector.reduce_sum(sm[:], logit[:], axis=mybir.AxisListType.X)
    p.DVE.tick(ins)
    p.DVE.wait(nc.vector)
    ins = nc.vector.reciprocal(sm[:], sm[:])
    p.DVE.tick(ins)
    p.DVE.wait(nc.vector)
    ins = nc.vector.tensor_scalar(logit[:], logit[:], sm[:, 0:1], None, OP.mult)
    p.DVE.tick(ins)
    p.DVE.wait(nc.vector)
    mx8, mi8, mif = scr["mx8"], scr["mi8"], scr["mif"]
    ins = nc.vector.max(mx8[:], logit[:])
    p.DVE.tick(ins)
    p.DVE.wait(nc.vector)
    ins = nc.vector.max_index(mi8[:], mx8[:], logit[:])
    p.DVE.tick(ins)
    p.DVE.wait(nc.vector)
    ins = nc.vector.tensor_copy(dec[:, 0:2], mi8[:, 0:2].bitcast(I32))
    p.DVE.tick(ins)
    ins = nc.vector.tensor_copy(dec[:, 2:4], mx8[:, 0:2])
    p.DVE.tick(ins)




def emit_bookkeeping(p, decw_ap, scr, consts_sb, pA, pB, pC):
    nc = p.nc
    decsb = scr["decsb"]; masks = scr["masks"]; totrow = scr["totrow"]
    totv = scr["totv"]; totv_bf = scr["totv_bf"]; offs_sb = scr["offs_sb"]
    offsrow = scr["offsrow"]; prod = scr["prod"]; posl = scr["posl"]
    vflag = scr["vflag"]; esel = scr["esel"]; slots = scr["slots"]
    bk1 = scr["bk_scr1"]; bk2 = scr["bk_scr2"]
    cols8 = consts_sb["cols8"]; triu = consts_sb["triu_bf"]
    striu = consts_sb["striu_bf"]; ones128x1 = consts_sb["ones128x1_bf"]
    ones1x128 = consts_sb["ones1x128"]

    if isinstance(decw_ap, list):
        for (off, ap_in) in decw_ap:
            ins = nc.sync.dma_start(out=decsb[:, off:off + 32, :], in_=ap_in)
            p.LD.dma(ins)
    else:
        ins = nc.sync.dma_start(out=decsb[:], in_=decw_ap)
        p.LD.dma(ins)
    p.LD.wait(nc.vector)
    for j in range(NTILE):
        for k in range(2):
            ins = nc.vector.tensor_scalar(masks[:, j, k, :], cols8[:],
                                          decsb[:, j, k:k + 1], None, OP.is_equal)
    p.DVE.tick(ins)
    p.DVE.wait(nc.tensor)
    mflat = masks.rearrange("p j k e -> p (j k e)")
    nc.tensor.matmul(pA[:, 0:512], triu[:], mflat[:, 0:512], start=True, stop=True)
    nc.tensor.matmul(pA[:, 512:1024], triu[:], mflat[:, 512:1024], start=True, stop=True)
    nc.tensor.matmul(pB[0:1, 0:512], ones128x1[:], mflat[:, 0:512], start=True, stop=True)
    ins = nc.tensor.matmul(pB[0:1, 512:1024], ones128x1[:], mflat[:, 512:1024], start=True, stop=True)
    p.PE.tick(ins)
    p.PE.wait(nc.scalar)
    ins = nc.scalar.copy(totrow[0:1, :], pB[0:1, :])
    p.ACT.tick(ins)
    p.ACT.wait(nc.sync)
    ins = nc.sync.dma_start(out=bk1.rearrange("(p n) -> p n", p=1), in_=totrow[0:1, :])
    p.ST.dma(ins)
    p.ST.wait(nc.sync)
    for kk in range(2):
        ins = nc.sync.dma_start(
            out=totv[kk * NTILE:(kk + 1) * NTILE, :],
            in_=bk1.rearrange("(j k e) -> j k e", j=NTILE, k=2)[:, kk, :])
        p.LD.dma(ins)
    p.LD.wait(nc.vector)
    ins = nc.vector.tensor_copy(totv_bf[:], totv[:])
    p.DVE.tick(ins)
    p.DVE.wait(nc.tensor)
    ins = nc.tensor.matmul(pC[:, 0:8], striu[:], totv_bf[:], start=True, stop=True)
    p.PE.tick(ins)
    p.PE.wait(nc.scalar)
    ins = nc.scalar.copy(offs_sb[:], pC[:, 0:8])
    p.ACT.tick(ins)
    p.ACT.wait(nc.sync)
    ins = nc.sync.dma_start(out=bk2.rearrange("(v e) -> v e", v=128), in_=offs_sb[:])
    p.ST.dma(ins)
    p.ST.wait(nc.sync)
    ins = nc.sync.dma_start(
        out=offsrow[0:1, :].rearrange("p (j k e) -> p j k e", j=NTILE, k=2),
        in_=bk2.rearrange("(k j e) -> j k e", k=2, j=NTILE).unsqueeze(0))
    p.LD.dma(ins)
    p.LD.wait(nc.tensor)
    for j in range(NTILE):
        for k in range(2):
            col = j * 16 + k * 8
            ins = nc.tensor.matmul(pA[:, col:col + 8], ones1x128[0:1, :],
                                   offsrow[0:1, col:col + 8], start=False, stop=True)
    p.PE.tick(ins)
    p.PE.wait(nc.vector)
    ins = nc.vector.tensor_tensor(prod[:], pA[:, :], mflat[:, :], OP.mult)
    p.DVE.tick(ins)
    p.DVE.wait(nc.vector)
    ins = nc.vector.tensor_reduce(posl[:], prod.rearrange("p (c e) -> p c e", e=E),
                                  mybir.AxisListType.X, OP.add)
    p.DVE.tick(ins)
    p.DVE.wait(nc.vector)
    nc.vector.tensor_scalar(vflag[:], posl[:], float(CAP), None, OP.is_le)
    nc.vector.tensor_copy(esel[:].rearrange("p (j k) -> p j k", k=2), decsb[:, :, 0:2])
    ins = nc.vector.scalar_tensor_tensor(slots[:], esel[:], float(CAP), posl[:],
                                         OP.mult, OP.add)
    p.DVE.tick(ins)
    p.DVE.wait(nc.vector)
    ins = nc.vector.tensor_scalar_add(slots[:], slots[:], -1.0)
    p.DVE.tick(ins)


def bookkeeping_scratch(p):
    return {
        "decsb": p.sbuf("bk_decsb", [128, NTILE, 4], F32),
        "masks": p.sbuf("bk_masks", [128, NTILE, 2, E], BF16),
        "totrow": p.sbuf("bk_totrow", [1, 1024], F32),
        "totv": p.sbuf("bk_totv", [128, E], F32),
        "totv_bf": p.sbuf("bk_totv_bf", [128, E], BF16),
        "offs_sb": p.sbuf("bk_offs", [128, E], F32),
        "offsrow": p.sbuf("bk_offsrow", [1, 1024], F32),
        "prod": p.sbuf("bk_prod", [128, 1024], F32),
        "posl": p.sbuf("bk_posl", [128, 128], F32),
        "vflag": p.sbuf("bk_vflag", [128, 128], F32),
        "esel": p.sbuf("bk_esel", [128, 128], F32),
        "slots": p.sbuf("bk_slots", [128, 128], F32),
        "bk_scr1": p.nc.dram_tensor("bk_scr1", [1024], F32, kind="Internal").ap(),
        "bk_scr2": p.nc.dram_tensor("bk_scr2", [1024], F32, kind="Internal").ap(),
    }


def load_consts(p, names):
    out = {}
    nc = p.nc
    for n in names:
        dt = BF16 if n.endswith("_bf") else F32
        shp = {"ident": [128, 128], "triu_bf": [128, 128], "striu_bf": [128, 128],
               "cols8": [128, 8], "ones128x1_bf": [128, 1], "ones1x128": [1, 128],
               "eye64s_bf": [128, 64], "ident128_bf": [128, 128]}[n]
        ap = nc.dram_tensor("c_" + n, shp, F32, kind="ExternalInput").ap()
        sb = p.sbuf("cs_" + n, shp, dt)
        if dt == F32:
            ins = nc.sync.dma_start(out=sb[:], in_=ap)
        else:
            ins = nc.gpsimd.dma_start(out=sb[:], in_=ap)
        p.LD.dma(ins)
        out[n] = sb
    return out



def build_B():
    p = Prog()
    nc = p.nc
    lnb = nc.dram_tensor("lnb", [T, D], BF16, kind="ExternalInput").ap()
    decw = nc.dram_tensor("decw", [128, NTILE, 4], F32, kind="ExternalInput").ap()
    wi2 = nc.dram_tensor("wi2", [2, D, FF], F32, kind="ExternalInput").ap()
    wo2 = nc.dram_tensor("wo2", [2, FF, D], F32, kind="ExternalInput").ap()
    eq0_i = nc.dram_tensor("eq0", [128, 2], F32, kind="ExternalInput").ap()
    etab_o = nc.dram_tensor("etab", [2 * CAP, D], BF16, kind="ExternalOutput").ap()
    table = nc.dram_tensor("dtab", [2 * CAP, D], BF16, kind="Internal").ap()

    scr = bookkeeping_scratch(p)
    eq0 = p.sbuf("eq0_sb", [128, 2], F32)
    ma = p.sbuf("ma_sb", [128, 128], F32)
    mb = p.sbuf("mb_sb", [128, 128], F32)
    sidx_f = p.sbuf("sidx_f", [128, 128], F32)
    sidx_i = p.sbuf("sidx_i", [128, 128], I32)
    lnt = [p.sbuf("lnt0", [128, D], BF16), p.sbuf("lnt1", [128, D], BF16)]
    xe = p.sbuf("xe", [128, 8, 2 * CAP], BF16)
    wib = [p.sbuf("wib0", [128, 8, 128], BF16), p.sbuf("wib1", [128, 8, 128], BF16)]
    h = p.sbuf("h", [128, 32, CAP], BF16)
    wob = p.sbuf("wob", [128, 32, D], BF16)
    eob = p.sbuf("eob", [128, D], BF16)

    pA = p.psum("pA", [128, 1024], F32)
    pB = p.psum("pB", [128, 1024], F32)
    pC = p.psum("pC", [128, 512], F32)
    pD = p.psum("pD", [128, 512], F32)

    p.block()
    p.sems()

    consts_sb = load_consts(p, ["cols8", "triu_bf", "striu_bf", "ones128x1_bf",
                                "ones1x128"])
    ins = nc.sync.dma_start(out=eq0[:], in_=eq0_i[:])
    p.LD.dma(ins)
    emit_bookkeeping(p, decw[:], scr, consts_sb, pA, pB, pC)

    vflag, slots_g, esel, posl = scr["vflag"], scr["slots"], scr["esel"], scr["posl"]
    p.DVE.wait(nc.vector)
    nc.vector.tensor_scalar(ma[:], esel[:], eq0[:, 0:1], None, OP.is_ge)
    nc.vector.tensor_scalar(mb[:], esel[:], eq0[:, 1:2], None, OP.is_le)
    ins = nc.vector.tensor_tensor(ma[:], ma[:], mb[:], OP.mult)
    p.DVE.tick(ins)
    p.DVE.wait(nc.vector)
    ins = nc.vector.tensor_tensor(ma[:], ma[:], vflag[:], OP.mult)
    p.DVE.tick(ins)
    nc.vector.tensor_scalar(sidx_f[:], esel[:], eq0[:, 0:1], None, OP.subtract)
    ins = nc.vector.scalar_tensor_tensor(sidx_f[:], sidx_f[:], float(CAP), posl[:],
                                         OP.mult, OP.add)
    p.DVE.tick(ins)
    p.DVE.wait(nc.vector)
    ins = nc.vector.tensor_scalar_add(sidx_f[:], sidx_f[:], -1.0 - BIGIDX)
    p.DVE.tick(ins)
    p.DVE.wait(nc.vector)
    ins = nc.vector.tensor_tensor(sidx_f[:], sidx_f[:], ma[:], OP.mult)
    p.DVE.tick(ins)
    p.DVE.wait(nc.vector)
    ins = nc.vector.tensor_scalar_add(sidx_f[:], sidx_f[:], BIGIDX)
    p.DVE.tick(ins)
    p.DVE.wait(nc.vector)
    ins = nc.vector.tensor_copy(sidx_i[:], sidx_f[:])
    p.DVE.tick(ins)
    p.DVE.wait(nc.gpsimd)
    p.DVE.wait(nc.sync)

    bcreg = nc.gpsimd.alloc_register("bcreg")
    nc.gpsimd.reg_mov(bcreg, 2 * CAP - 1)
    for j in range(NTILE):
        buf = lnt[j % 2]
        if j >= 2:
            p.GP.wait(nc.sync)
        ins = nc.sync.dma_start(out=buf[:], in_=lnb[j * 128:(j + 1) * 128, :])
        p.LD.dma(ins)
        p.LD.wait(nc.gpsimd)
        for k in range(2):
            col = j * 2 + k
            ins = nc.gpsimd.indirect_dma_start(
                out=table[:, :],
                out_offset=IndirectOffsetOnAxis(sidx_i[:, col:col + 1], axis=0),
                in_=buf[:, :], in_offset=None,
                bounds_check=bcreg, oob_is_err=False)
            p.GP.dma(ins)

    p.GP.wait(nc.sync)
    for c in range(8):
        ins = nc.sync.dma_start_transpose(xe[:, c, :],
                                          table[0:2 * CAP, c * 128:(c + 1) * 128])
        p.LD.dma(ins)
    p.LD.wait(nc.tensor)
    p.LD.wait(nc.gpsimd)

    ins = nc.gpsimd.dma_start(out=wob[:],
                              in_=wo2[0].rearrange("(kc p) n -> p kc n", p=128))
    p.GP.dma(ins)

    for ei in range(2):
        for fc in range(32):
            wbuf = wib[fc % 2]
            p.PE.wait(nc.gpsimd)
            ins = nc.gpsimd.dma_start(
                out=wbuf[:], in_=wi2[ei, :, fc * 128:(fc + 1) * 128]
                .rearrange("(kc p) m -> p kc m", p=128))
            p.GP.dma(ins)
            p.GP.wait(nc.tensor)
            ps = [pA, pB][fc % 2]
            p.ACT.wait(nc.tensor)
            for (n0, nw) in ((0, 512), (512, 128)):
                for kc in range(8):
                    ins = nc.tensor.matmul(
                        ps[:, n0:n0 + nw], wbuf[:, kc, :],
                        xe[:, kc, ei * CAP + n0: ei * CAP + n0 + nw],
                        start=(kc == 0), stop=(kc == 7))
            p.PE.tick(ins)
            p.PE.wait(nc.scalar)
            ins = nc.scalar.activation(h[:, fc, :], ps[:, 0:CAP], AF.Relu)
            p.ACT.tick(ins)
        if ei == 1:
            p.PE.wait(nc.gpsimd)
            ins = nc.gpsimd.dma_start(out=wob[:],
                                      in_=wo2[1].rearrange("(kc p) n -> p kc n", p=128))
            p.GP.dma(ins)
        p.GP.wait(nc.tensor)
        p.ACT.wait(nc.tensor)
        for s5 in range(5):
            for n2 in range(2):
                ps2 = [pC, pD][(s5 * 2 + n2) % 2]
                p.ACT.wait(nc.tensor)
                for kc2 in range(32):
                    ins = nc.tensor.matmul(
                        ps2[:, 0:512],
                        h[:, kc2, s5 * 128:(s5 + 1) * 128],
                        wob[:, kc2, n2 * 512:(n2 + 1) * 512],
                        start=(kc2 == 0), stop=(kc2 == 31))
                p.PE.tick(ins)
                p.PE.wait(nc.scalar)
                p.ST.wait(nc.scalar)
                ins = nc.scalar.copy(eob[:, n2 * 512:(n2 + 1) * 512], ps2[:, 0:512])
                p.ACT.tick(ins)
            p.ACT.wait(nc.sync)
            ins = nc.sync.dma_start(
                out=etab_o[ei * CAP + s5 * 128: ei * CAP + (s5 + 1) * 128, :],
                in_=eob[:])
            p.ST.dma(ins)
    p.ST.wait(nc.gpsimd)
    p.close()
    return p.nc




def build_C(final=False):
    p = Prog()
    nc = p.nc
    x0s = nc.dram_tensor("xs", [TLOC, D], F32, kind="ExternalInput").ap()
    decw = nc.dram_tensor("decw", [128, NTILE, 4], F32, kind="ExternalInput").ap()
    etab = nc.dram_tensor("etab", [E * CAP, D], BF16, kind="ExternalInput").ap()
    myq = nc.dram_tensor("myq", [1, 1], I32, kind="ExternalInput").ap()
    if final:
        x1_o = nc.dram_tensor("x3s", [TLOC, D], F32, kind="ExternalOutput").ap()
    else:
        x1_o = nc.dram_tensor("x1s", [TLOC, D], F32, kind="ExternalOutput").ap()
        ln2_o = nc.dram_tensor("ln2b", [TLOC, D], BF16, kind="ExternalOutput").ap()
    gscr = nc.dram_tensor("gscr", [128, 256], F32, kind="Internal").ap()

    scr = bookkeeping_scratch(p)
    gidx_f = p.sbuf("gidx_f", [128, 128], F32)
    gate2 = p.sbuf("gate2", [128, 128], F32)
    myq_sb = p.sbuf("myq_sb", [1, 2], I32)
    gidx_my_f = p.sbuf("gidx_my_f", [128, 32], F32)
    gidx_my = p.sbuf("gidx_my", [128, 32], I32)
    gate_my = p.sbuf("gate_my", [128, 32], F32)
    yk = [p.sbuf("yk0", [128, D], BF16), p.sbuf("yk1", [128, D], BF16)]
    ytmp = p.sbuf("ytmp", [128, D], F32)
    xt = p.sbuf("xt", [128, D], F32)
    stat = p.sbuf("stat", [128, 12], F32)
    mv = p.sbuf("mv", [128, 2], F32)
    sd = p.sbuf("sd", [128, 1], F32)
    rstd = p.sbuf("rstd", [128, 1], F32)
    lnf = p.sbuf("lnf", [128, D], F32)
    lnb = p.sbuf("lnbs", [128, D], BF16)

    pA = p.psum("pA", [128, 1024], F32)
    pB = p.psum("pB", [128, 1024], F32)
    pC = p.psum("pC", [128, 512], F32)

    p.block()
    p.sems()

    consts_sb = load_consts(p, ["cols8", "triu_bf", "striu_bf", "ones128x1_bf",
                                "ones1x128"])
    ins = nc.sync.dma_start(out=myq_sb[:, 0:1], in_=myq[:])
    p.LD.dma(ins)
    emit_bookkeeping(p, decw[:], scr, consts_sb, pA, pB, pC)

    vflag, slots, decsb = scr["vflag"], scr["slots"], scr["decsb"]
    p.DVE.wait(nc.vector)
    nc.vector.tensor_scalar_add(gidx_f[:], slots[:], -BIGIDX)
    ins = nc.vector.tensor_tensor(gidx_f[:], gidx_f[:], vflag[:], OP.mult)
    p.DVE.tick(ins)
    p.DVE.wait(nc.vector)
    ins = nc.vector.tensor_scalar_add(gidx_f[:], gidx_f[:], BIGIDX)
    p.DVE.tick(ins)
    nc.vector.tensor_copy(gate2[:].rearrange("p (j k) -> p j k", k=2),
                          decsb[:, :, 2:4])
    ins = nc.vector.tensor_tensor(gate2[:], gate2[:], vflag[:], OP.mult)
    p.DVE.tick(ins)
    p.DVE.wait(nc.sync)
    ins = nc.sync.dma_start(out=gscr[:, 0:128], in_=gidx_f[:])
    p.ST.dma(ins)
    ins = nc.sync.dma_start(out=gscr[:, 128:256], in_=gate2[:])
    p.ST.dma(ins)
    p.ST.wait(nc.sync)
    p.LD.wait(nc.sync)
    offreg = nc.sync.alloc_register("offr")
    nc.sync.load(offreg, myq_sb[0:1, 0:1])
    off = RuntimeValue(offreg)
    gidx_ap = bass.AP(gscr.tensor, off, [[256, 128], [1, 32]])
    gate_ap = bass.AP(gscr.tensor, off + 128, [[256, 128], [1, 32]])
    ins = nc.sync.dma_start(out=gidx_my_f[:], in_=gidx_ap)
    p.LD.dma(ins)
    ins = nc.sync.dma_start(out=gate_my[:], in_=gate_ap)
    p.LD.dma(ins)
    p.LD.wait(nc.vector)
    ins = nc.vector.tensor_copy(gidx_my[:], gidx_my_f[:])
    p.DVE.tick(ins)
    p.DVE.wait(nc.gpsimd)

    bcreg = nc.gpsimd.alloc_register("bcreg")
    nc.gpsimd.reg_mov(bcreg, E * CAP - 1)
    for j in range(NTILE_LOC):
        if j > 0:
            p.DVE.wait(nc.sync)
            p.ST.wait(nc.vector)
        ins = nc.sync.dma_start(out=xt[:], in_=x0s[j * 128:(j + 1) * 128, :])
        p.LD.dma(ins)
        for k in range(2):
            ins = nc.vector.memset(yk[k][:], 0.0)
            p.DVE.tick(ins)
        p.DVE.wait(nc.gpsimd)
        for k in range(2):
            ins = nc.gpsimd.indirect_dma_start(
                out=yk[k][:, :], out_offset=None, in_=etab[:, :],
                in_offset=IndirectOffsetOnAxis(gidx_my[:, 2 * j + k:2 * j + k + 1],
                                               axis=0),
                bounds_check=bcreg, oob_is_err=False)
            p.GP.dma(ins)
        p.GP.wait(nc.vector)
        p.LD.wait(nc.vector)
        ins = nc.vector.tensor_scalar(ytmp[:], yk[0][:], gate_my[:, 2 * j:2 * j + 1],
                                      None, OP.mult)
        p.DVE.tick(ins)
        p.DVE.wait(nc.vector)
        ins = nc.vector.scalar_tensor_tensor(ytmp[:], yk[1][:],
                                             gate_my[:, 2 * j + 1:2 * j + 2],
                                             ytmp[:], OP.mult, OP.add)
        p.DVE.tick(ins)
        p.DVE.wait(nc.vector)
        ins = nc.vector.scalar_tensor_tensor(xt[:], ytmp[:], 0.5, xt[:],
                                             OP.mult, OP.add)
        p.DVE.tick(ins)
        p.DVE.wait(nc.sync)
        ins = nc.sync.dma_start(out=x1_o[j * 128:(j + 1) * 128, :], in_=xt[:])
        p.ST.dma(ins)
        if not final:
            emit_ln_tile(p, xt, stat, mv, sd, rstd, lnf, lnb)
            p.DVE.wait(nc.sync)
            ins = nc.sync.dma_start(out=ln2_o[j * 128:(j + 1) * 128, :], in_=lnb[:])
            p.ST.dma(ins)
    p.ST.wait(nc.gpsimd)
    p.close()
    return p.nc




NQC = T // 512
NKT = T // 128


def build_D(dbg=0, ntc=NQC, natt=None, epi=99):
    p = Prog()
    nc = p.nc
    ln2b = nc.dram_tensor("ln2b", [T, D], BF16, kind="ExternalInput").ap()
    wq4 = nc.dram_tensor("wq4", [D, 1024], F32, kind="ExternalInput").ap()
    gam = nc.dram_tensor("gam", [128, 512], F32, kind="ExternalInput").ap()
    bet = nc.dram_tensor("bet", [128, 512], F32, kind="ExternalInput").ap()
    gatedb = nc.dram_tensor("gatedb", [T, 256], BF16, kind="ExternalOutput").ap()
    if dbg == 6:
        dbg_done = nc.dram_tensor("dbg_done", [1, 16], F32, kind="ExternalOutput").ap()
        dbg_q = nc.dram_tensor("dbg_q", [128, 2, T], BF16, kind="ExternalOutput").ap()
        dbg_g = nc.dram_tensor("dbg_g", [128, NKT, 256], BF16, kind="ExternalOutput").ap()
        dbg_af = nc.dram_tensor("dbg_af", [128, 512], BF16, kind="ExternalOutput").ap()
        dbg_at = nc.dram_tensor("dbg_at", [128, 512], F32, kind="ExternalOutput").ap()
        dbg_nr = nc.dram_tensor("dbg_nr", [128, 512], F32, kind="ExternalOutput").ap()
        dbg_gg = nc.dram_tensor("dbg_gg", [128, 4, 128], BF16, kind="ExternalOutput").ap()
    if dbg and dbg != 6:
        dbg_q = nc.dram_tensor("dbg_q", [128, 2, T], BF16, kind="ExternalOutput").ap()
        dbg_k = nc.dram_tensor("dbg_k", [128, 2, T], BF16, kind="ExternalOutput").ap()
        dbg_v = nc.dram_tensor("dbg_v", [128, NKT, 256], BF16, kind="ExternalOutput").ap()
        dbg_g = nc.dram_tensor("dbg_g", [128, NKT, 256], BF16, kind="ExternalOutput").ap()
        dbg_done = nc.dram_tensor("dbg_done", [1, 16], F32, kind="ExternalOutput").ap()

    wq_sb = p.sbuf("wq_sb", [128, 8, 1024], BF16)
    lnF = [p.sbuf("lnF0", [128, 8, 512], BF16), p.sbuf("lnF1", [128, 8, 512], BF16)]
    q_sb = p.sbuf("q_sb", [128, 2, T], BF16)
    k_sb = p.sbuf("k_sb", [128, 2, T], BF16)
    v_sb = p.sbuf("v_sb", [128, NKT, 256], BF16)
    gate_sb = p.sbuf("gate_sb", [128, NKT, 256], BF16)
    w_sb = [p.sbuf("w_sb0", [128, 1024], BF16), p.sbuf("w_sb1", [128, 1024], BF16)]
    att_fm = p.sbuf("att_fm", [128, 512], BF16)
    att_tok = p.sbuf("att_tok", [128, 512], F32)
    sq_tok = p.sbuf("sq_tok", [128, 512], F32)
    ssum = p.sbuf("ssum", [128, 8], F32)
    ssq = p.sbuf("ssq", [128, 8], F32)
    mu = p.sbuf("mu", [128, 8], F32)
    var = p.sbuf("var", [128, 8], F32)
    rstd = p.sbuf("rstd", [128, 8], F32)
    mu2 = p.sbuf("mu2", [128, 8], F32)
    gam_sb = p.sbuf("gam_sb", [128, 512], F32)
    bet_sb = p.sbuf("bet_sb", [128, 512], F32)
    nrm = p.sbuf("nrm", [128, 512], F32)
    gated = p.sbuf("gated", [128, 4, 128], BF16)

    psS = [p.psum("psS0", [128, 1024], F32), p.psum("psS1", [128, 1024], F32)]
    psA = p.psum("psA", [128, 512], F32)
    psAo = p.psum("psAo", [128, 512], F32)
    psT = p.psum("psT", [128, 1024], BF16)
    psT2 = p.psum("psT2", [128, 1024], BF16)
    psV = psA

    p.block()
    p.sems()

    consts_sb = load_consts(p, ["eye64s_bf"])
    ins = nc.gpsimd.dma_start(out=wq_sb[:],
                              in_=wq4.rearrange("(c p) m -> p c m", p=128))
    p.GP.dma(ins)
    ins = nc.sync.dma_start(out=gam_sb[:], in_=gam[:])
    p.LD.dma(ins)
    ins = nc.sync.dma_start(out=bet_sb[:], in_=bet[:])
    p.LD.dma(ins)
    p.GP.wait(nc.tensor)

    qk_dst = {0: (q_sb, 0), 1: (q_sb, 1), 2: (k_sb, 0), 3: (k_sb, 1)}
    p.GP.wait(nc.sync)
    p.LD.wait(nc.sync)
    for tc in range(ntc):
        buf = lnF[tc % 2]
        p.PE.wait(nc.sync)
        for c in range(8):
            ins = nc.sync.dma_start_transpose(
                buf[:, c, :], ln2b[tc * 512:(tc + 1) * 512, c * 128:(c + 1) * 128])
            p.LD.dma(ins)
        p.LD.wait(nc.tensor)
        p.ACT.wait(nc.tensor)
        for mi in range(4):
            ps = psS[mi // 2]
            off = (mi % 2) * 512
            for c in range(8):
                ins = nc.tensor.matmul(ps[:, off:off + 512],
                                       wq_sb[:, c, mi * 128:(mi + 1) * 128],
                                       buf[:, c, :], start=(c == 0), stop=(c == 7))
        p.PE.tick(ins)
        p.PE.wait(nc.scalar)
        for mi in range(4):
            dst, pr = qk_dst[mi]
            ps = psS[mi // 2]
            off = (mi % 2) * 512
            ins = nc.scalar.activation(dst[:, pr, tc * 512:(tc + 1) * 512],
                                       ps[:, off:off + 512], AF.Silu)
        p.ACT.tick(ins)
        for sub in range(4):
            tkt = tc * 4 + sub
            if sub > 0:
                p.ACT.wait(nc.tensor)
            for c in range(8):
                nc.tensor.matmul(psV[:, 0:256], buf[:, c, sub * 128:(sub + 1) * 128],
                                 wq_sb[:, c, 512:768], start=(c == 0), stop=(c == 7))
            for c in range(8):
                ins = nc.tensor.matmul(psV[:, 256:512],
                                       buf[:, c, sub * 128:(sub + 1) * 128],
                                       wq_sb[:, c, 768:1024], start=(c == 0), stop=(c == 7))
            p.PE.tick(ins)
            p.PE.wait(nc.scalar)
            nc.scalar.activation(gate_sb[:, tkt, :], psV[:, 0:256], AF.Silu)
            ins = nc.scalar.activation(v_sb[:, tkt, :], psV[:, 256:512], AF.Silu)
            p.ACT.tick(ins)

    if dbg and dbg != 6:
        p.ACT.wait(nc.sync)
        for (dst, srcb) in ((dbg_q, q_sb), (dbg_k, k_sb), (dbg_v, v_sb), (dbg_g, gate_sb)):
            ins = nc.sync.dma_start(out=dst, in_=srcb[:])
            p.ST.dma(ins)

    SCALE = 1.0 / 8.0
    attiters = []
    for pair in range(2):
        for qc in range(NQC):
            attiters.append((pair, qc))
    if dbg == 2:
        attiters = []
    if natt is not None:
        attiters = attiters[:natt]
    for (pair, qc) in attiters:
        if True:
            silu_acts = []
            av_pes = []
            for kt in range(NKT):
                par = kt % 2
                ps = psS[par]
                if kt >= 2:
                    p.ACT.wait_val(nc.tensor, silu_acts[kt - 2])
                else:
                    p.ACT.wait(nc.tensor)
                nc.tensor.matmul(ps[:, 0:512], k_sb[0:64, pair, kt * 128:(kt + 1) * 128],
                                 q_sb[0:64, pair, qc * 512:(qc + 1) * 512],
                                 start=True, stop=True, tile_position=(0, 0))
                ins = nc.tensor.matmul(ps[:, 512:1024],
                                       k_sb[64:128, pair, kt * 128:(kt + 1) * 128],
                                       q_sb[64:128, pair, qc * 512:(qc + 1) * 512],
                                       start=True, stop=True, tile_position=(64, 0))
                p.PE.tick(ins)
                p.PE.wait(nc.scalar)
                if kt >= 2:
                    pass
                ins = nc.scalar.activation(w_sb[par][:], ps[:, :], AF.Silu, scale=SCALE)
                p.ACT.tick(ins)
                silu_acts.append(p.ACT.v)
                p.ACT.wait_val(nc.tensor, silu_acts[kt])
                nc.tensor.matmul(psA[0:64, :], v_sb[:, kt, pair * 128:pair * 128 + 64],
                                 w_sb[par][:, 0:512], start=(kt == 0), stop=(kt == NKT - 1),
                                 tile_position=(0, 0))
                ins = nc.tensor.matmul(psAo[64:128, :],
                                       v_sb[:, kt, pair * 128 + 64:pair * 128 + 128],
                                       w_sb[par][:, 512:1024], start=(kt == 0),
                                       stop=(kt == NKT - 1), tile_position=(0, 64))
                p.PE.tick(ins)
                av_pes.append(p.PE.v)

            p.PE.wait(nc.scalar)
            nc.scalar.copy(att_fm[0:64, :], psA[0:64, :])
            ins = nc.scalar.copy(att_fm[64:128, :], psAo[64:128, :])
            p.ACT.tick(ins)
            if epi == 1:
                continue
            p.ACT.wait(nc.tensor)
            for sub in range(4):
                for hh in range(2):
                    dstp = psT if hh == 0 else psT2
                    ins = nc.tensor.transpose(
                        dstp[:, sub * 64:(sub + 1) * 64],
                        att_fm[hh * 64:(hh + 1) * 64, sub * 128:(sub + 1) * 128],
                        consts_sb["eye64s_bf"][hh * 64:(hh + 1) * 64, :])
            p.PE.tick(ins)
            if epi == 2:
                continue
            p.PE.wait(nc.scalar)
            nc.scalar.copy(
                att_tok[:].rearrange("p (s h d) -> p s h d", s=4, h=2)[:, :, 0, :],
                psT[:, 0:256].rearrange("p (s d) -> p s d", s=4))
            ins = nc.scalar.copy(
                att_tok[:].rearrange("p (s h d) -> p s h d", s=4, h=2)[:, :, 1, :],
                psT2[:, 0:256].rearrange("p (s d) -> p s d", s=4))
            p.ACT.tick(ins)
            if epi == 3:
                continue
            p.ACT.wait(nc.scalar)
            ins = nc.scalar.activation(sq_tok[:], att_tok[:], AF.Square)
            p.ACT.tick(ins)
            p.ACT.wait(nc.vector)
            nc.vector.reduce_sum(ssum[:], att_tok[:].rearrange("p (s e) -> p s e", e=64),
                                 axis=mybir.AxisListType.X)
            ins = nc.vector.reduce_sum(ssq[:], sq_tok[:].rearrange("p (s e) -> p s e", e=64),
                                       axis=mybir.AxisListType.X)
            p.DVE.tick(ins)
            if epi == 4:
                continue
            p.DVE.wait(nc.vector)
            nc.vector.tensor_scalar(mu[:], ssum[:], 1.0 / HD, None, OP.mult)
            ins = nc.vector.tensor_scalar(var[:], ssq[:], 1.0 / HD, None, OP.mult)
            p.DVE.tick(ins)
            p.DVE.wait(nc.vector)
            ins = nc.vector.tensor_tensor(mu2[:], mu[:], mu[:], OP.mult)
            p.DVE.tick(ins)
            p.DVE.wait(nc.vector)
            ins = nc.vector.tensor_tensor(var[:], var[:], mu2[:], OP.subtract)
            p.DVE.tick(ins)
            p.DVE.wait(nc.vector)
            ins = nc.vector.tensor_scalar_add(var[:], var[:], EPS)
            p.DVE.tick(ins)
            p.DVE.wait(nc.scalar)
            ins = nc.scalar.activation(var[:], var[:], AF.Sqrt)
            p.ACT.tick(ins)
            p.ACT.wait(nc.vector)
            ins = nc.vector.reciprocal(rstd[:], var[:])
            p.DVE.tick(ins)
            p.DVE.wait(nc.vector)
            for c8 in range(8):
                ins = nc.vector.tensor_scalar(nrm[:, c8 * 64:(c8 + 1) * 64],
                                              att_tok[:, c8 * 64:(c8 + 1) * 64],
                                              mu[:, c8:c8 + 1], rstd[:, c8:c8 + 1],
                                              OP.subtract, OP.mult)
            p.DVE.tick(ins)
            p.DVE.wait(nc.vector)
            ins = nc.vector.tensor_tensor(nrm[:], nrm[:], gam_sb[:], OP.mult)
            p.DVE.tick(ins)
            p.DVE.wait(nc.vector)
            ins = nc.vector.tensor_tensor(nrm[:], nrm[:], bet_sb[:], OP.add)
            p.DVE.tick(ins)
            p.DVE.wait(nc.vector)
            p.ST.wait(nc.vector)
            ins = nc.vector.tensor_tensor(
                gated[:], nrm[:].rearrange("p (s c) -> p s c", c=128),
                gate_sb[:, qc * 4:qc * 4 + 4, pair * 128:(pair + 1) * 128], OP.mult)
            p.DVE.tick(ins)
            p.DVE.wait(nc.sync)
            ins = nc.sync.dma_start(
                out=gatedb[qc * 512:(qc + 1) * 512, pair * 128:(pair + 1) * 128]
                .rearrange("(s p) c -> p s c", p=128),
                in_=gated[:])
            p.ST.dma(ins)
    if dbg == 6:
        p.DVE.wait(nc.sync)
        p.ACT.wait(nc.sync)
        for (d2, s2) in ((dbg_q, q_sb[:]), (dbg_g, gate_sb[:]), (dbg_af, att_fm[:]),
                         (dbg_at, att_tok[:]), (dbg_nr, nrm[:]),
                         (dbg_gg, gated[:])):
            ins = nc.sync.dma_start(out=d2, in_=s2)
            p.ST.dma(ins)
    if dbg:
        one = p.sbuf("one_sb", [1, 16], F32)
        ins = nc.vector.memset(one[:], 7.0)
        p.DVE.tick(ins)
        p.DVE.wait(nc.sync)
        ins = nc.sync.dma_start(out=dbg_done, in_=one[:])
        p.ST.dma(ins)
    p.ST.wait(nc.gpsimd)
    p.close()
    return p.nc




def build_E():
    p = Prog()
    nc = p.nc
    x1s = nc.dram_tensor("x1s", [TLOC, D], F32, kind="ExternalInput").ap()
    gfull = nc.dram_tensor("gfull", [TLOC, D], BF16, kind="ExternalInput").ap()
    wo = nc.dram_tensor("wo", [D, D], F32, kind="ExternalInput").ap()
    wr = nc.dram_tensor("wr", [D, E], F32, kind="ExternalInput").ap()
    ident = nc.dram_tensor("ident", [128, 128], F32, kind="ExternalInput").ap()
    x2_o = nc.dram_tensor("x2s", [TLOC, D], F32, kind="ExternalOutput").ap()
    ln3_o = nc.dram_tensor("ln3b", [TLOC, D], BF16, kind="ExternalOutput").ap()
    dec_o = nc.dram_tensor("dec", [TLOC, 4], F32, kind="ExternalOutput").ap()

    wo_sb = p.sbuf("wo_sb", [128, 8, D], BF16)
    gfT = [p.sbuf("gfT0", [128, 8, 128], BF16), p.sbuf("gfT1", [128, 8, 128], BF16)]
    xt = p.sbuf("xt", [128, D], F32)
    stat = p.sbuf("stat", [128, 12], F32)
    mv = p.sbuf("mv", [128, 2], F32)
    sd = p.sbuf("sd", [128, 1], F32)
    rstd = p.sbuf("rstd", [128, 1], F32)
    lnf = p.sbuf("lnf", [128, D], F32)
    lnb = p.sbuf("lnbs", [128, D], BF16)
    lnT = p.sbuf("lnT", [128, D], F32)
    identsb = p.sbuf("identsb", [128, 128], F32)
    wr_sb = p.sbuf("wr_sb", [128, 8, E], F32)
    logit = p.sbuf("logit", [128, E], F32)
    scr = {
        "mx1": p.sbuf("mx1", [128, 1], F32),
        "sm": p.sbuf("sm", [128, 1], F32),
        "mx8": p.sbuf("mx8", [128, 8], F32),
        "mi8": p.sbuf("mi8", [128, 8], U32),
        "mif": p.sbuf("mif", [128, 8], F32),
    }
    dec = p.sbuf("dec_sb", [128, 4], F32)
    psO = [p.psum("psO0", [128, 512], F32), p.psum("psO1", [128, 512], F32)]
    ptr = [p.psum("ptr0", [128, 512], F32), p.psum("ptr1", [128, 512], F32)]
    plog = p.psum("plog", [128, 512], F32)

    p.block()
    p.sems()

    ins = nc.sync.dma_start(out=identsb[:], in_=ident[:])
    p.LD.dma(ins)
    ins = nc.sync.dma_start(out=wr_sb[:], in_=wr.rearrange("(c p) e -> p c e", p=128))
    p.LD.dma(ins)
    ins = nc.gpsimd.dma_start(out=wo_sb[:], in_=wo.rearrange("(kc p) n -> p kc n", p=128))
    p.GP.dma(ins)
    p.GP.wait(nc.tensor)
    p.GP.wait(nc.sync)

    for j in range(NTILE_LOC):
        buf = gfT[j % 2]
        p.PE.wait(nc.sync)
        for c in range(8):
            ins = nc.sync.dma_start_transpose(
                buf[:, c, :], gfull[j * 128:(j + 1) * 128, c * 128:(c + 1) * 128])
            p.LD.dma(ins)
        if j > 0:
            p.DVE.wait(nc.sync)
            p.ST.wait(nc.vector)
        ins = nc.sync.dma_start(out=xt[:], in_=x1s[j * 128:(j + 1) * 128, :])
        p.LD.dma(ins)
        p.LD.wait(nc.tensor)
        p.DVE.wait(nc.tensor)
        for n2 in range(2):
            for kc in range(8):
                ins = nc.tensor.matmul(psO[n2][:, :], buf[:, kc, :],
                                       wo_sb[:, kc, n2 * 512:(n2 + 1) * 512],
                                       start=(kc == 0), stop=(kc == 7))
        p.PE.tick(ins)
        p.PE.wait(nc.vector)
        p.LD.wait(nc.vector)
        for n2 in range(2):
            ins = nc.vector.tensor_add(xt[:, n2 * 512:(n2 + 1) * 512],
                                       xt[:, n2 * 512:(n2 + 1) * 512], psO[n2][:, :])
        p.DVE.tick(ins)
        p.DVE.wait(nc.sync)
        ins = nc.sync.dma_start(out=x2_o[j * 128:(j + 1) * 128, :], in_=xt[:])
        p.ST.dma(ins)
        emit_ln_tile(p, xt, stat, mv, sd, rstd, lnf, lnb)
        p.DVE.wait(nc.sync)
        ins = nc.sync.dma_start(out=ln3_o[j * 128:(j + 1) * 128, :], in_=lnb[:])
        p.ST.dma(ins)
        p.DVE.wait(nc.tensor)
        emit_router_tile(p, lnf, identsb, wr_sb, lnT,
                         [ptr[0][:, 0:128], ptr[1][:, 0:128]], plog[:, 0:8],
                         logit, scr, dec)
        p.DVE.wait(nc.sync)
        ins = nc.sync.dma_start(out=dec_o[j * 128:(j + 1) * 128, :], in_=dec[:])
        p.ST.dma(ins)
        p.ST.wait(nc.sync)
    p.ST.wait(nc.gpsimd)
    p.close()
    return p.nc




MAGIC = 0x5A5A17

_EPOCH_STATE = [0]


def _launch_epoch():
    import time
    e = (time.time_ns() >> 8) & 0x3FFFFFFF
    if e <= _EPOCH_STATE[0]:
        e = _EPOCH_STATE[0] + 1
    _EPOCH_STATE[0] = e
    return e | 0x40000000


def build_FUSED(dbg=0):
    p = Prog()
    nc = p.nc
    HT = T // 2
    NTH = HT // 128

    x0h = nc.dram_tensor("x0h", [HT, D], F32, kind="ExternalInput").ap()
    wq8 = nc.dram_tensor("wq8", [D, 2048], F32, kind="ExternalInput").ap()
    wob_i = nc.dram_tensor("wob_i", [D, D], F32, kind="ExternalInput").ap()
    wr = nc.dram_tensor("wr", [D, E], F32, kind="ExternalInput").ap()
    ident = nc.dram_tensor("ident", [128, 128], F32, kind="ExternalInput").ap()
    gam = nc.dram_tensor("gam", [128, 512], F32, kind="ExternalInput").ap()
    bet = nc.dram_tensor("bet", [128, 512], F32, kind="ExternalInput").ap()
    wi4 = nc.dram_tensor("wi4", [4, D, FF], F32, kind="ExternalInput").ap()
    wo4 = nc.dram_tensor("wo4", [4, FF, D], F32, kind="ExternalInput").ap()
    meta = nc.dram_tensor("meta", [1, 8], I32, kind="ExternalInput").ap()
    ridx = nc.dram_tensor("ridx", [128, 180], I32, kind="ExternalInput").ap()
    x3h = nc.dram_tensor("x3h", [HT, D], F32, kind="ExternalOutput").ap()
    if dbg:
        dbg_dec = nc.dram_tensor("dbg_dec", [128, T // 256, 4], F32, kind="ExternalOutput").ap()
        dbg_x1 = nc.dram_tensor("dbg_x1", [HT, D], F32, kind="ExternalOutput").ap()
        dbg_ln = nc.dram_tensor("dbg_ln", [T, D], BF16, kind="ExternalOutput").ap()
        dbg_gt = nc.dram_tensor("dbg_gt", [4, 2 * T, 128], BF16, kind="ExternalOutput").ap()
        dbg_x2 = nc.dram_tensor("dbg_x2", [HT, D], F32, kind="ExternalOutput").ap()
        dbg_et = nc.dram_tensor("dbg_et", [E * CAP, D], BF16, kind="ExternalOutput").ap()
        dbg_dt = nc.dram_tensor("dbg_dt", [E * CAP, D], BF16, kind="ExternalOutput").ap()
        dbg_gi = nc.dram_tensor("dbg_gi", [128, 64], I32, kind="ExternalOutput").ap()
        dbg_gg = nc.dram_tensor("dbg_gg", [128, 64], F32, kind="ExternalOutput").ap()
        dbg_xe = nc.dram_tensor("dbg_xe", [128, 5120], BF16, kind="ExternalOutput").ap()
        dbg_h = nc.dram_tensor("dbg_h", [128, 20480], BF16, kind="ExternalOutput").ap()
        dbg_wib = nc.dram_tensor("dbg_wib", [128, 1024], BF16, kind="ExternalOutput").ap()

    sh_flags = nc.dram_tensor("sh_flags", [8, 64], I32, kind="Internal",
                              addr_space="Shared").ap()
    sh_dec = nc.dram_tensor("sh_dec", [2, 128, 32, 4], F32, kind="Internal",
                            addr_space="Shared").ap()
    sh_dtab = nc.dram_tensor("sh_dtab", [E * CAP, D], BF16, kind="Internal",
                             addr_space="Shared").ap()
    sh_etab = nc.dram_tensor("sh_etab", [E * CAP, D], BF16, kind="Internal",
                             addr_space="Shared").ap()
    sh_ln = nc.dram_tensor("sh_ln", [T, D], BF16, kind="Internal",
                           addr_space="Shared").ap()
    sh_gat = [nc.dram_tensor(f"sh_gat{w}", [2 * T, 128], BF16, kind="Internal",
                             addr_space="Shared").ap() for w in range(4)]

    lnh = nc.dram_tensor("lnh", [HT, D], BF16, kind="Internal").ap()
    x1h = nc.dram_tensor("x1h", [HT, D], F32, kind="Internal").ap()
    gscr = nc.dram_tensor("gscr", [128, 256], F32, kind="Internal").ap()

    scr = bookkeeping_scratch(p)
    xt = p.sbuf("xt", [128, D], F32)
    stat = p.sbuf("stat", [128, 12], F32)
    mv = p.sbuf("mv", [128, 2], F32)
    sd = p.sbuf("sd", [128, 1], F32)
    rstd1 = p.sbuf("rstd1", [128, 1], F32)
    lnf = p.sbuf("lnf", [128, D], F32)
    lnb = p.sbuf("lnbs", [128, D], BF16)
    lnT = p.sbuf("lnT", [128, D], F32)
    identsb = p.sbuf("identsb", [128, 128], F32)
    wr_sb = p.sbuf("wr_sb", [128, 8, E], F32)
    logit = p.sbuf("logit", [128, E], F32)
    rscr = {
        "mx1": p.sbuf("mx1", [128, 1], F32),
        "sm": p.sbuf("sm", [128, 1], F32),
        "mx8": p.sbuf("mx8", [128, 8], F32),
        "mi8": p.sbuf("mi8", [128, 8], U32),
        "mif": p.sbuf("mif", [128, 8], F32),
    }
    dec_acc = p.sbuf("dec_acc", [128, NTH, 4], F32)
    sidx_f = p.sbuf("sidx_f", [128, 128], F32)
    sidx_i = p.sbuf("sidx_i", [128, 128], I32)
    gidx_f = p.sbuf("gidx_f", [128, 128], F32)
    gate2 = p.sbuf("gate2", [128, 128], F32)
    gidx_my = p.sbuf("gidx_my", [128, 64], I32)
    gate_my = p.sbuf("gate_my", [128, 64], F32)
    yk = [p.sbuf("yk0", [128, D], BF16), p.sbuf("yk1", [128, D], BF16)]
    ytmp = p.sbuf("ytmp", [128, D], F32)
    gam_sb = p.sbuf("gam_sb", [128, 512], F32)
    bet_sb = p.sbuf("bet_sb", [128, 512], F32)
    ARENA_N = 60416
    arena = p.sbuf("arena", [128, ARENA_N], BF16)
    xe = arena[:, 0:5120].rearrange("p (c n) -> p c n", c=8)
    h_sb = arena[:, 5120:25600].rearrange("p (c n) -> p c n", c=32)
    wo_sb = arena[:, 25600:58368].rearrange("p (c n) -> p c n", c=32)
    wib = [arena[:, 58368:59392].rearrange("p (c n) -> p c n", c=8),
           arena[:, 59392:60416].rearrange("p (c n) -> p c n", c=8)]
    q_sb = arena[:, 0:8192]
    k_sb = arena[:, 8192:16384]
    v_sb = arena[:, 16384:24576].rearrange("p (c n) -> p c n", c=NKT)
    gate_sb = arena[:, 24576:32768].rearrange("p (c n) -> p c n", c=NKT)
    gfT = arena[:, 33024:34048].rearrange("p (c n) -> p c n", c=8)
    wo8_sb = arena[:, 0:8192].rearrange("p (c n) -> p c n", c=8)
    gb = [arena[:, 34048:34176], arena[:, 34176:34304]]
    wqw = p.sbuf("wqw", [128, 8, 512], BF16)
    lnF = [p.sbuf("lnF0", [128, 8, 512], BF16), p.sbuf("lnF1", [128, 8, 512], BF16)]
    w_sb = [p.sbuf("w_sb0", [128, 1024], BF16), p.sbuf("w_sb1", [128, 1024], BF16)]
    att_fm = p.sbuf("att_fm", [128, 512], BF16)
    att_tok = p.sbuf("att_tok", [128, 512], F32)
    sq_tok = p.sbuf("sq_tok", [128, 512], F32)
    ssum = p.sbuf("ssum", [128, 8], F32)
    ssq = p.sbuf("ssq", [128, 8], F32)
    mu = p.sbuf("mu", [128, 8], F32)
    var = p.sbuf("var", [128, 8], F32)
    rstd8 = p.sbuf("rstd8", [128, 8], F32)
    mu2 = p.sbuf("mu2", [128, 8], F32)
    nrm = p.sbuf("nrm", [128, 512], F32)
    gated = p.sbuf("gated", [128, 4, 128], BF16)
    eob = p.sbuf("eob", [128, D], BF16)
    meta_sb = p.sbuf("meta_sb", [1, 8], I32)
    ridx_sb = p.sbuf("ridx_sb", [128, 180], I32)

    PS0 = p.psum("PS0", [128, 1024], F32)
    PS1 = p.psum("PS1", [128, 1024], F32)
    PS2 = p.psum("PS2", [128, 512], F32)
    PS3 = p.psum("PS3", [128, 512], F32)
    PT0 = p.psum("PT0", [128, 1024], BF16)
    PT1 = p.psum("PT1", [128, 1024], BF16)

    p.block()
    p.sems()
    BAR = Sem(nc, "bar")
    nbar = [0]

    consts_sb = load_consts(p, ["cols8", "triu_bf", "striu_bf", "ones128x1_bf",
                                "ones1x128", "eye64s_bf", "ident128_bf"])
    for (dst, srcx) in ((identsb, ident), (gam_sb, gam), (bet_sb, bet),
                        (meta_sb, meta), (ridx_sb, ridx)):
        ins = nc.sync.dma_start(out=dst[:], in_=srcx[:])
        p.LD.dma(ins)
    ins = nc.sync.dma_start(out=wr_sb[:], in_=wr.rearrange("(c p) e -> p c e", p=128))
    p.LD.dma(ins)
    p.LD.wait(nc.sync)
    p.LD.wait(nc.gpsimd)

    r_tok_s = nc.sync.alloc_register("r_tok_s")
    nc.sync.load(r_tok_s, meta_sb[0:1, 0:1])
    rv_tok = RuntimeValue(r_tok_s)
    r_gcol_s = nc.sync.alloc_register("r_gcol_s")
    nc.sync.load(r_gcol_s, meta_sb[0:1, 1:2])
    rv_gcol = RuntimeValue(r_gcol_s)
    r_erow_s = nc.sync.alloc_register("r_erow_s")
    nc.sync.load(r_erow_s, meta_sb[0:1, 2:3])
    rv_erow = RuntimeValue(r_erow_s)
    r_bk_s = nc.sync.alloc_register("r_bk_s")
    nc.sync.load(r_bk_s, meta_sb[0:1, 3:4])
    rv_bk = RuntimeValue(r_bk_s)
    r_h_s = nc.sync.alloc_register("r_h_s")
    nc.sync.load(r_h_s, meta_sb[0:1, 6:7])
    rv_h = RuntimeValue(r_h_s)
    r_my_g = nc.gpsimd.alloc_register("r_my_g")
    nc.gpsimd.load(r_my_g, meta_sb[0:1, 4:5])
    rv_myf = RuntimeValue(r_my_g)
    r_pa_g = nc.gpsimd.alloc_register("r_pa_g")
    nc.gpsimd.load(r_pa_g, meta_sb[0:1, 5:6])
    rv_paf = RuntimeValue(r_pa_g)
    bc_dtab = nc.gpsimd.alloc_register("bc_dtab")
    nc.gpsimd.reg_mov(bc_dtab, E * CAP - 1)
    bc_ln = nc.gpsimd.alloc_register("bc_ln")
    nc.gpsimd.reg_mov(bc_ln, T - 1)
    bc_gat = nc.gpsimd.alloc_register("bc_gat")
    nc.gpsimd.reg_mov(bc_gat, 2 * T - 1)

    spin_order = [nc.scalar, nc.tensor, nc.sync, nc.scalar, nc.tensor,
                  nc.vector, nc.scalar, nc.tensor, nc.sync, nc.vector]
    spin_state = {}

    def get_spin(eng):
        key = str(eng.engine)
        if key not in spin_state:
            p.LD.wait(eng)
            r_myc = eng.alloc_register("bm" + key[-3:])
            eng.load(r_myc, meta_sb[0:1, 4:5])
            r_pac = eng.alloc_register("bp" + key[-3:])
            eng.load(r_pac, meta_sb[0:1, 5:6])
            f = eng.alloc_register("bf" + key[-3:])
            ap_myc = bass.AP(sh_flags.tensor, RuntimeValue(r_myc), [[64, 1], [1, 1]])
            ap_pac = bass.AP(sh_flags.tensor, RuntimeValue(r_pac), [[64, 1], [1, 1]])
            spin_state[key] = (ap_myc, ap_pac, f)
        return spin_state[key]

    ap_myc, ap_pac, f = get_spin(nc.vector)
    nc.vector.store(ap_myc, 0)
    r_ep = nc.vector.alloc_register("r_ep")
    nc.vector.load(r_ep, meta_sb[0:1, 7:8])
    r_myi = nc.vector.alloc_register("r_myi")
    nc.vector.load(r_myi, meta_sb[0:1, 4:5])
    nc.vector.reg_add(r_myi, r_myi, 32)
    r_pai = nc.vector.alloc_register("r_pai")
    nc.vector.load(r_pai, meta_sb[0:1, 5:6])
    nc.vector.reg_add(r_pai, r_pai, 32)
    ap_myi = bass.AP(sh_flags.tensor, RuntimeValue(r_myi), [[64, 1], [1, 1]])
    ap_pai = bass.AP(sh_flags.tensor, RuntimeValue(r_pai), [[64, 1], [1, 1]])
    nc.vector.store(ap_myi, RuntimeValue(r_ep))
    r_di = nc.vector.alloc_register("r_di")
    nc.vector.load(f, ap_pai)
    nc.vector.reg_alu(r_di, RuntimeValue(f), RuntimeValue(r_ep),
                      mybir.AluOpType.subtract)
    with nc.vector.While(RuntimeValue(r_di) != 0):
        nc.vector.load(f, ap_pai)
        nc.vector.reg_alu(r_di, RuntimeValue(f), RuntimeValue(r_ep),
                          mybir.AluOpType.subtract)
    nc.vector.sem_inc(BAR.h, 1)
    BAR.v += 1

    def barrier():
        nbar[0] += 1
        ph = nbar[0]
        eng = spin_order[ph - 1]
        ap_myc, ap_pac, f = get_spin(eng)
        BAR.wait(eng)
        p.ST.wait(eng)
        p.GP.wait(eng)
        eng.store(ap_myc, ph)
        eng.load(f, ap_pac)
        with eng.While(RuntimeValue(f) < ph):
            eng.load(f, ap_pac)
        eng.sem_inc(BAR.h, 1)
        BAR.v += 1

    def local_join():
        for eng in (nc.sync, nc.tensor, nc.scalar, nc.vector, nc.gpsimd):
            for s in (p.LD, p.ST, p.PE, p.ACT, p.DVE, p.GP):
                s.wait(eng)
            BAR.wait(eng)

    def ln_router_phase(xsrc_ap):
        for j in range(NTH):
            if j > 0:
                p.DVE.wait(nc.sync)
                p.PE.wait(nc.sync)
            ins = nc.sync.dma_start(out=xt[:], in_=xsrc_ap[j * 128:(j + 1) * 128, :])
            p.LD.dma(ins)
            p.LD.wait(nc.vector)
            if j > 0:
                p.PE.wait(nc.vector)
            emit_ln_tile(p, xt, stat, mv, sd, rstd1, lnf, lnb)
            p.DVE.wait(nc.sync)
            p.ST.wait(nc.sync)
            ins = nc.sync.dma_start(out=lnh[j * 128:(j + 1) * 128, :], in_=lnb[:])
            p.ST.dma(ins)
            p.DVE.wait(nc.tensor)
            emit_router_tile(p, lnf, identsb, wr_sb, lnT,
                             [PS2[:, 0:128], PS3[:, 0:128]], PS1[0:128, 0:8],
                             logit, rscr, dec_acc[:, j, :])
        p.DVE.wait(nc.sync)
        dec_dst = bass.AP(sh_dec.tensor, rv_h * (128 * 32 * 4),
                          [[128, 128], [4, 32], [1, 4]])
        ins = nc.sync.dma_start(out=dec_dst, in_=dec_acc[:])
        p.ST.dma(ins)
        p.ST.wait(nc.gpsimd)
        barrier()

    ln_router_phase(x0h)
    if dbg:
        p.DVE.wait(nc.sync)
        ins = nc.sync.dma_start(out=dbg_dec[:], in_=dec_acc[:])
        p.ST.dma(ins)

    def bookkeep_and_scatter():
        local_join()
        dec_loads = [(hh * 32, sh_dec[hh]) for hh in range(2)]
        emit_bookkeeping(p, dec_loads, scr, consts_sb, PS0, PS1, PS2)
        vflag, slots = scr["vflag"], scr["slots"]
        p.DVE.wait(nc.vector)
        nc.vector.tensor_scalar_add(sidx_f[:], slots[:], -BIGIDX)
        ins = nc.vector.tensor_tensor(sidx_f[:], sidx_f[:], vflag[:], OP.mult)
        p.DVE.tick(ins)
        p.DVE.wait(nc.vector)
        ins = nc.vector.tensor_scalar_add(sidx_f[:], sidx_f[:], BIGIDX)
        p.DVE.tick(ins)
        p.DVE.wait(nc.vector)
        nc.vector.tensor_scalar_add(gidx_f[:], slots[:], -BIGIDX)
        ins = nc.vector.tensor_tensor(gidx_f[:], gidx_f[:], vflag[:], OP.mult)
        p.DVE.tick(ins)
        p.DVE.wait(nc.vector)
        ins = nc.vector.tensor_scalar_add(gidx_f[:], gidx_f[:], BIGIDX)
        p.DVE.tick(ins)
        nc.vector.tensor_copy(gate2[:].rearrange("p (j k) -> p j k", k=2),
                              scr["decsb"][:, :, 2:4])
        ins = nc.vector.tensor_tensor(gate2[:], gate2[:], vflag[:], OP.mult)
        p.DVE.tick(ins)
        p.DVE.wait(nc.sync)
        ins = nc.sync.dma_start(out=gscr[:, 0:128], in_=gidx_f[:])
        p.ST.dma(ins)
        ins = nc.sync.dma_start(out=gscr[:, 128:256], in_=gate2[:])
        p.ST.dma(ins)
        p.ST.wait(nc.sync)
        gidx_ap = bass.AP(gscr.tensor, rv_bk, [[256, 128], [1, 64]])
        gate_ap = bass.AP(gscr.tensor, rv_bk + 128, [[256, 128], [1, 64]])
        ins = nc.sync.dma_start(out=gidx_f[:, 0:64], in_=gidx_ap)
        p.LD.dma(ins)
        ins = nc.sync.dma_start(out=gate_my[:], in_=gate_ap)
        p.LD.dma(ins)
        p.LD.wait(nc.vector)
        ins = nc.vector.tensor_copy(gidx_my[:], gidx_f[:, 0:64])
        p.DVE.tick(ins)
        p.DVE.wait(nc.sync)
        ins = nc.sync.dma_start(out=gscr[:, 0:128], in_=sidx_f[:])
        p.ST.dma(ins)
        p.ST.wait(nc.sync)
        sidx_ap = bass.AP(gscr.tensor, rv_bk, [[256, 128], [1, 64]])
        ins = nc.sync.dma_start(out=sidx_f[:, 0:64], in_=sidx_ap)
        p.LD.dma(ins)
        p.LD.wait(nc.vector)
        ins = nc.vector.tensor_copy(sidx_i[:, 0:64], sidx_f[:, 0:64])
        p.DVE.tick(ins)
        p.DVE.wait(nc.gpsimd)
        p.DVE.wait(nc.sync)
        for jj in range(NTH):
            buf = yk[jj % 2]
            if jj >= 2:
                p.GP.wait(nc.sync)
            ins = nc.sync.dma_start(out=buf[:], in_=lnh[jj * 128:(jj + 1) * 128, :])
            p.LD.dma(ins)
            p.LD.wait(nc.gpsimd)
            for k in range(2):
                col = jj * 2 + k
                ins = nc.gpsimd.indirect_dma_start(
                    out=sh_dtab[:, :],
                    out_offset=IndirectOffsetOnAxis(sidx_i[:, col:col + 1], axis=0),
                    in_=buf[:, :], in_offset=None,
                    bounds_check=bc_dtab, oob_is_err=False)
                p.GP.dma(ins)
        p.GP.wait(nc.gpsimd)
        barrier()

    bookkeep_and_scatter()
    if dbg:
        ins = nc.gpsimd.dma_start(out=dbg_dt[:], in_=sh_dtab[:])
        p.GP.dma(ins)
        p.DVE.wait(nc.gpsimd)
        ins = nc.gpsimd.dma_start(out=dbg_gi[:], in_=gidx_my[:])
        p.GP.dma(ins)
        ins = nc.gpsimd.dma_start(out=dbg_gg[:], in_=gate_my[:])
        p.GP.dma(ins)
        p.GP.wait(nc.gpsimd)

    def expert_phase():
        local_join()
        for ei in range(4):
            for t5 in range(5):
                p.PE.wait(nc.gpsimd)
                ins = nc.gpsimd.indirect_dma_start(
                    out=yk[0][:, :], out_offset=None, in_=sh_dtab[:, :],
                    in_offset=IndirectOffsetOnAxis(
                        ridx_sb[:, ei * 5 + t5:ei * 5 + t5 + 1], axis=0),
                    bounds_check=bc_dtab, oob_is_err=False)
                p.GP.dma(ins)
                p.GP.wait(nc.tensor)
                p.ACT.wait(nc.tensor)
                for c8 in range(8):
                    ins = nc.tensor.transpose(
                        PT0[:, 0:128], yk[0][:, c8 * 128:(c8 + 1) * 128],
                        consts_sb["ident128_bf"][:])
                    p.PE.tick(ins)
                    p.PE.wait(nc.scalar)
                    ins = nc.scalar.copy(xe[:, c8, t5 * 128:(t5 + 1) * 128],
                                         PT0[:, 0:128])
                    p.ACT.tick(ins)
                    p.ACT.wait(nc.tensor)
            p.ACT.wait(nc.sync)
            for fc in range(32):
                wbuf = wib[fc % 2]
                p.PE.wait(nc.gpsimd)
                ins = nc.gpsimd.dma_start(
                    out=wbuf[:], in_=wi4[ei, :, fc * 128:(fc + 1) * 128]
                    .rearrange("(kc p) m -> p kc m", p=128))
                p.GP.dma(ins)
                p.GP.wait(nc.tensor)
                ps = [PS0, PS1][fc % 2]
                p.ACT.wait(nc.tensor)
                for (n0, nw) in ((0, 512), (512, 128)):
                    for kc in range(8):
                        ins = nc.tensor.matmul(
                            ps[:, n0:n0 + nw], wbuf[:, kc, :],
                            xe[:, kc, n0:n0 + nw],
                            start=(kc == 0), stop=(kc == 7))
                p.PE.tick(ins)
                p.PE.wait(nc.scalar)
                ins = nc.scalar.activation(h_sb[:, fc, :], ps[:, 0:CAP], AF.Relu)
                p.ACT.tick(ins)
            p.PE.wait(nc.gpsimd)
            ins = nc.gpsimd.dma_start(
                out=wo_sb, in_=wo4[ei].rearrange("(kc p) n -> p kc n", p=128))
            p.GP.dma(ins)
            p.GP.wait(nc.tensor)
            p.ACT.wait(nc.tensor)
            for s5 in range(5):
                for n2 in range(2):
                    ps2 = [PS2, PS3][(s5 * 2 + n2) % 2]
                    p.ACT.wait(nc.tensor)
                    for kc2 in range(32):
                        ins = nc.tensor.matmul(
                            ps2[:, 0:512],
                            h_sb[:, kc2, s5 * 128:(s5 + 1) * 128],
                            wo_sb[:, kc2, n2 * 512:(n2 + 1) * 512],
                            start=(kc2 == 0), stop=(kc2 == 31))
                    p.PE.tick(ins)
                    p.PE.wait(nc.scalar)
                    p.GP.wait(nc.scalar)
                    ins = nc.scalar.copy(eob[:, n2 * 512:(n2 + 1) * 512], ps2[:, 0:512])
                    p.ACT.tick(ins)
                p.ACT.wait(nc.gpsimd)
                ins = nc.gpsimd.indirect_dma_start(
                    out=sh_etab[:, :],
                    out_offset=IndirectOffsetOnAxis(
                        ridx_sb[:, ei * 5 + s5:ei * 5 + s5 + 1], axis=0),
                    in_=eob[:, :], in_offset=None,
                    bounds_check=bc_dtab, oob_is_err=False)
                p.GP.dma(ins)
        barrier()

    expert_phase()
    if dbg:
        ins = nc.gpsimd.dma_start(out=dbg_et[:], in_=sh_etab[:])
        p.GP.dma(ins)
        ins = nc.gpsimd.dma_start(out=dbg_xe[:], in_=arena[:, 0:5120])
        p.GP.dma(ins)
        ins = nc.gpsimd.dma_start(out=dbg_h[:], in_=arena[:, 5120:25600])
        p.GP.dma(ins)
        ins = nc.gpsimd.dma_start(out=dbg_wib[:], in_=arena[:, 59392:60416])
        p.GP.dma(ins)
        p.GP.wait(nc.gpsimd)

    def combine_phase(xsrc_ap, final):
        local_join()
        bcz = bc_dtab
        for jj in range(NTH):
            if jj > 0:
                p.DVE.wait(nc.sync)
                p.ST.wait(nc.vector)
                p.GP.wait(nc.vector)
            ins = nc.sync.dma_start(out=xt[:], in_=xsrc_ap[jj * 128:(jj + 1) * 128, :])
            p.LD.dma(ins)
            for k in range(2):
                ins = nc.vector.memset(yk[k][:], 0.0)
                p.DVE.tick(ins)
            p.DVE.wait(nc.gpsimd)
            for k in range(2):
                ins = nc.gpsimd.indirect_dma_start(
                    out=yk[k][:, :], out_offset=None, in_=sh_etab[:, :],
                    in_offset=IndirectOffsetOnAxis(
                        gidx_my[:, 2 * jj + k:2 * jj + k + 1], axis=0),
                    bounds_check=bcz, oob_is_err=False)
                p.GP.dma(ins)
            p.GP.wait(nc.vector)
            p.LD.wait(nc.vector)
            ins = nc.vector.tensor_scalar(ytmp[:], yk[0][:],
                                          gate_my[:, 2 * jj:2 * jj + 1],
                                          None, OP.mult)
            p.DVE.tick(ins)
            p.DVE.wait(nc.vector)
            ins = nc.vector.scalar_tensor_tensor(ytmp[:], yk[1][:],
                                                 gate_my[:, 2 * jj + 1:2 * jj + 2],
                                                 ytmp[:], OP.mult, OP.add)
            p.DVE.tick(ins)
            p.DVE.wait(nc.vector)
            ins = nc.vector.scalar_tensor_tensor(xt[:], ytmp[:], 0.5, xt[:],
                                                 OP.mult, OP.add)
            p.DVE.tick(ins)
            p.DVE.wait(nc.sync)
            if final:
                ins = nc.sync.dma_start(out=x3h[jj * 128:(jj + 1) * 128, :], in_=xt[:])
                p.ST.dma(ins)
            else:
                ins = nc.sync.dma_start(out=x1h[jj * 128:(jj + 1) * 128, :], in_=xt[:])
                p.ST.dma(ins)
                if jj > 0:
                    p.PE.wait(nc.vector)
                emit_ln_tile(p, xt, stat, mv, sd, rstd1, lnf, lnb)
                p.DVE.wait(nc.gpsimd)
                ins = nc.gpsimd.indirect_dma_start(
                    out=sh_ln[:, :],
                    out_offset=IndirectOffsetOnAxis(
                        ridx_sb[:, 20 + jj:21 + jj], axis=0),
                    in_=lnb[:, :], in_offset=None,
                    bounds_check=bc_ln, oob_is_err=False)
                p.GP.dma(ins)
        p.ST.wait(nc.gpsimd)
        if not final:
            barrier()

    combine_phase(x0h, False)
    if dbg:
        ins = nc.gpsimd.dma_start(out=dbg_x1[:], in_=x1h[:])
        p.GP.dma(ins)
        ins = nc.gpsimd.dma_start(out=dbg_ln[:], in_=sh_ln[:])
        p.GP.dma(ins)
        p.GP.wait(nc.gpsimd)

    local_join()
    SCALE = 1.0 / 8.0
    for wave in range(4):
        local_join()
        ins = nc.gpsimd.dma_start(
            out=wqw[:], in_=wq8[:, wave * 512:(wave + 1) * 512]
            .rearrange("(c p) m -> p c m", p=128))
        p.GP.dma(ins)
        p.GP.wait(nc.sync)
        p.GP.wait(nc.tensor)
        for tc in range(NQC):
            buf = lnF[tc % 2]
            p.PE.wait(nc.sync)
            for c in range(8):
                ins = nc.sync.dma_start_transpose(
                    buf[:, c, :],
                    sh_ln[tc * 512:(tc + 1) * 512, c * 128:(c + 1) * 128])
                p.LD.dma(ins)
            p.LD.wait(nc.tensor)
            p.ACT.wait(nc.tensor)
            for mi in range(2):
                ps = [PS0, PS1][mi]
                for c in range(8):
                    ins = nc.tensor.matmul(ps[:, 0:512],
                                           wqw[:, c, mi * 128:(mi + 1) * 128],
                                           buf[:, c, :], start=(c == 0), stop=(c == 7))
            p.PE.tick(ins)
            p.PE.wait(nc.scalar)
            nc.scalar.activation(q_sb[:, tc * 512:(tc + 1) * 512], PS0[:, 0:512], AF.Silu)
            ins = nc.scalar.activation(k_sb[:, tc * 512:(tc + 1) * 512], PS1[:, 0:512], AF.Silu)
            p.ACT.tick(ins)
            for sub in range(4):
                tkt = tc * 4 + sub
                p.ACT.wait(nc.tensor)
                for c in range(8):
                    nc.tensor.matmul(PS2[:, 0:128],
                                     buf[:, c, sub * 128:(sub + 1) * 128],
                                     wqw[:, c, 256:384], start=(c == 0), stop=(c == 7))
                for c in range(8):
                    ins = nc.tensor.matmul(PS3[:, 0:128],
                                           buf[:, c, sub * 128:(sub + 1) * 128],
                                           wqw[:, c, 384:512],
                                           start=(c == 0), stop=(c == 7))
                p.PE.tick(ins)
                p.PE.wait(nc.scalar)
                nc.scalar.activation(gate_sb[:, tkt, :], PS2[:, 0:128], AF.Silu)
                ins = nc.scalar.activation(v_sb[:, tkt, :], PS3[:, 0:128], AF.Silu)
                p.ACT.tick(ins)
        for qc in range(NQC):
            silu_acts = []
            for kt in range(NKT):
                par = kt % 2
                ps = [PS0, PS1][par]
                if kt >= 2:
                    p.ACT.wait_val(nc.tensor, silu_acts[kt - 2])
                else:
                    p.ACT.wait(nc.tensor)
                nc.tensor.matmul(ps[:, 0:512],
                                 k_sb[0:64, kt * 128:(kt + 1) * 128],
                                 q_sb[0:64, qc * 512:(qc + 1) * 512],
                                 start=True, stop=True, tile_position=(0, 0))
                ins = nc.tensor.matmul(ps[:, 512:1024],
                                       k_sb[64:128, kt * 128:(kt + 1) * 128],
                                       q_sb[64:128, qc * 512:(qc + 1) * 512],
                                       start=True, stop=True, tile_position=(64, 0))
                p.PE.tick(ins)
                p.PE.wait(nc.scalar)
                ins = nc.scalar.activation(w_sb[par][:], ps[:, :], AF.Silu,
                                           scale=SCALE)
                p.ACT.tick(ins)
                silu_acts.append(p.ACT.v)
                p.ACT.wait_val(nc.tensor, silu_acts[kt])
                nc.tensor.matmul(PS2[0:64, :],
                                 v_sb[:, kt, 0:64],
                                 w_sb[par][:, 0:512], start=(kt == 0),
                                 stop=(kt == NKT - 1), tile_position=(0, 0))
                ins = nc.tensor.matmul(PS3[64:128, :],
                                       v_sb[:, kt, 64:128],
                                       w_sb[par][:, 512:1024], start=(kt == 0),
                                       stop=(kt == NKT - 1), tile_position=(0, 64))
                p.PE.tick(ins)
            p.PE.wait(nc.scalar)
            nc.scalar.copy(att_fm[0:64, :], PS2[0:64, :])
            ins = nc.scalar.copy(att_fm[64:128, :], PS3[64:128, :])
            p.ACT.tick(ins)
            p.ACT.wait(nc.tensor)
            for sub in range(4):
                for hh in range(2):
                    dstp = PT0 if hh == 0 else PT1
                    ins = nc.tensor.transpose(
                        dstp[:, sub * 64:(sub + 1) * 64],
                        att_fm[hh * 64:(hh + 1) * 64, sub * 128:(sub + 1) * 128],
                        consts_sb["eye64s_bf"][hh * 64:(hh + 1) * 64, :])
            p.PE.tick(ins)
            p.PE.wait(nc.scalar)
            nc.scalar.copy(
                att_tok[:].rearrange("p (s h d) -> p s h d", s=4, h=2)[:, :, 0, :],
                PT0[:, 0:256].rearrange("p (s d) -> p s d", s=4))
            ins = nc.scalar.copy(
                att_tok[:].rearrange("p (s h d) -> p s h d", s=4, h=2)[:, :, 1, :],
                PT1[:, 0:256].rearrange("p (s d) -> p s d", s=4))
            p.ACT.tick(ins)
            p.ACT.wait(nc.scalar)
            ins = nc.scalar.activation(sq_tok[:], att_tok[:], AF.Square)
            p.ACT.tick(ins)
            p.ACT.wait(nc.vector)
            nc.vector.reduce_sum(ssum[:],
                                 att_tok[:].rearrange("p (s e) -> p s e", e=64),
                                 axis=mybir.AxisListType.X)
            ins = nc.vector.reduce_sum(ssq[:],
                                       sq_tok[:].rearrange("p (s e) -> p s e", e=64),
                                       axis=mybir.AxisListType.X)
            p.DVE.tick(ins)
            p.DVE.wait(nc.vector)
            nc.vector.tensor_scalar(mu[:], ssum[:], 1.0 / HD, None, OP.mult)
            ins = nc.vector.tensor_scalar(var[:], ssq[:], 1.0 / HD, None, OP.mult)
            p.DVE.tick(ins)
            p.DVE.wait(nc.vector)
            ins = nc.vector.tensor_tensor(mu2[:], mu[:], mu[:], OP.mult)
            p.DVE.tick(ins)
            p.DVE.wait(nc.vector)
            ins = nc.vector.tensor_tensor(var[:], var[:], mu2[:], OP.subtract)
            p.DVE.tick(ins)
            p.DVE.wait(nc.vector)
            ins = nc.vector.tensor_scalar_add(var[:], var[:], EPS)
            p.DVE.tick(ins)
            p.DVE.wait(nc.scalar)
            ins = nc.scalar.activation(var[:], var[:], AF.Sqrt)
            p.ACT.tick(ins)
            p.ACT.wait(nc.vector)
            ins = nc.vector.reciprocal(rstd8[:], var[:])
            p.DVE.tick(ins)
            p.DVE.wait(nc.vector)
            for c8 in range(8):
                ins = nc.vector.tensor_scalar(nrm[:, c8 * 64:(c8 + 1) * 64],
                                              att_tok[:, c8 * 64:(c8 + 1) * 64],
                                              mu[:, c8:c8 + 1], rstd8[:, c8:c8 + 1],
                                              OP.subtract, OP.mult)
            p.DVE.tick(ins)
            p.DVE.wait(nc.vector)
            ins = nc.vector.tensor_tensor(nrm[:], nrm[:], gam_sb[:], OP.mult)
            p.DVE.tick(ins)
            p.DVE.wait(nc.vector)
            ins = nc.vector.tensor_tensor(nrm[:], nrm[:], bet_sb[:], OP.add)
            p.DVE.tick(ins)
            p.DVE.wait(nc.vector)
            p.GP.wait(nc.vector)
            ins = nc.vector.tensor_tensor(
                gated[:], nrm[:].rearrange("p (s c) -> p s c", c=128),
                gate_sb[:, qc * 4:qc * 4 + 4, :], OP.mult)
            p.DVE.tick(ins)
            p.DVE.wait(nc.gpsimd)
            for sub in range(4):
                ins = nc.gpsimd.indirect_dma_start(
                    out=sh_gat[wave][:, :],
                    out_offset=IndirectOffsetOnAxis(
                        ridx_sb[:, 52 + qc * 4 + sub:53 + qc * 4 + sub], axis=0),
                    in_=gated[:, sub, :], in_offset=None,
                    bounds_check=bc_gat, oob_is_err=False)
                p.GP.dma(ins)
    barrier()
    if dbg:
        for w in range(4):
            ins = nc.gpsimd.dma_start(out=dbg_gt[w], in_=sh_gat[w][:])
            p.GP.dma(ins)
        p.GP.wait(nc.gpsimd)

    local_join()
    ins = nc.gpsimd.dma_start(out=wo8_sb,
                              in_=wob_i.rearrange("(kc p) n -> p kc n", p=128))
    p.GP.dma(ins)
    p.GP.wait(nc.tensor)
    p.GP.wait(nc.sync)
    for jj in range(NTH):
        for c in range(8):
            hp, cc = c // 4, c % 4
            p.PE.wait(nc.gpsimd)
            ins = nc.gpsimd.indirect_dma_start(
                out=gb[c % 2][:, :], out_offset=None, in_=sh_gat[cc][:, :],
                in_offset=IndirectOffsetOnAxis(
                    ridx_sb[:, 116 + hp * 32 + jj:117 + hp * 32 + jj], axis=0),
                bounds_check=bc_gat, oob_is_err=False)
            p.GP.dma(ins)
            p.GP.wait(nc.tensor)
            p.ACT.wait(nc.tensor)
            ins = nc.tensor.transpose(
                [PT0, PT1][c % 2][:, 0:128], gb[c % 2][:, :],
                consts_sb["ident128_bf"][:])
            p.PE.tick(ins)
            p.PE.wait(nc.scalar)
            ins = nc.scalar.copy(gfT[:, c, :], [PT0, PT1][c % 2][:, 0:128])
            p.ACT.tick(ins)
        if jj > 0:
            p.DVE.wait(nc.sync)
            p.ST.wait(nc.vector)
        ins = nc.sync.dma_start(out=xt[:], in_=x1h[jj * 128:(jj + 1) * 128, :])
        p.LD.dma(ins)
        p.LD.wait(nc.tensor)
        p.DVE.wait(nc.tensor)
        p.ACT.wait(nc.tensor)
        for n2 in range(2):
            for kc in range(8):
                ins = nc.tensor.matmul([PS2, PS3][n2][:, :], gfT[:, kc, :],
                                       wo8_sb[:, kc, n2 * 512:(n2 + 1) * 512],
                                       start=(kc == 0), stop=(kc == 7))
        p.PE.tick(ins)
        p.PE.wait(nc.vector)
        p.LD.wait(nc.vector)
        for n2 in range(2):
            ins = nc.vector.tensor_add(xt[:, n2 * 512:(n2 + 1) * 512],
                                       xt[:, n2 * 512:(n2 + 1) * 512],
                                       [PS2, PS3][n2][:, :])
        p.DVE.tick(ins)
        p.DVE.wait(nc.sync)
        ins = nc.sync.dma_start(out=x1h[jj * 128:(jj + 1) * 128, :], in_=xt[:])
        p.ST.dma(ins)
        if jj > 0:
            p.PE.wait(nc.vector)
        emit_ln_tile(p, xt, stat, mv, sd, rstd1, lnf, lnb)
        p.DVE.wait(nc.sync)
        p.ST.wait(nc.sync)
        ins = nc.sync.dma_start(out=lnh[jj * 128:(jj + 1) * 128, :], in_=lnb[:])
        p.ST.dma(ins)
        p.DVE.wait(nc.tensor)
        emit_router_tile(p, lnf, identsb, wr_sb, lnT,
                         [PS2[:, 0:128], PS3[:, 0:128]], PS1[0:128, 0:8],
                         logit, rscr, dec_acc[:, jj, :])
    p.DVE.wait(nc.sync)
    dec_dst = bass.AP(sh_dec.tensor, rv_h * (128 * 32 * 4),
                      [[128, 128], [4, 32], [1, 4]])
    ins = nc.sync.dma_start(out=dec_dst, in_=dec_acc[:])
    p.ST.dma(ins)
    p.ST.wait(nc.gpsimd)
    barrier()

    if dbg:
        ins = nc.gpsimd.dma_start(out=dbg_x2[:], in_=x1h[:])
        p.GP.dma(ins)
        p.GP.wait(nc.gpsimd)
    bookkeep_and_scatter()
    expert_phase()
    combine_phase(x1h, True)

    p.ST.wait(nc.gpsimd)
    p.close()
    return p.nc


def kernel_fused(L0_attn_hidden_states, L0_attn_w_qkuv, L0_attn_w_out,
                 L0_attn_ln_gamma, L0_attn_ln_beta, L0_ffn_w_router,
                 L0_ffn_wi, L0_ffn_wo):
    x0 = np.ascontiguousarray(np.asarray(L0_attn_hidden_states, dtype=np.float32)
                              ).reshape(GG, T, D)
    wq = np.ascontiguousarray(np.asarray(L0_attn_w_qkuv, dtype=np.float32))
    w_out = np.ascontiguousarray(np.asarray(L0_attn_w_out, dtype=np.float32))
    gamma = np.asarray(L0_attn_ln_gamma, dtype=np.float32)
    beta = np.asarray(L0_attn_ln_beta, dtype=np.float32)
    wr = np.ascontiguousarray(np.asarray(L0_ffn_w_router, dtype=np.float32))
    wi = np.ascontiguousarray(np.asarray(L0_ffn_wi, dtype=np.float32))
    wo = np.ascontiguousarray(np.asarray(L0_ffn_wo, dtype=np.float32))

    consts = host_consts()
    cns = {"c_" + n: consts[n] for n in
           ("cols8", "triu_bf", "striu_bf", "ones128x1_bf", "ones1x128",
            "eye64s_bf", "ident128_bf")}
    epoch = _launch_epoch()
    in_maps = []
    for core in range(8):
        g = (core // 2) % 2
        h = core % 2
        cols = []
        for wave in range(4):
            h0 = 8 * h + 2 * wave
            cols += list(range(h0 * 64, (h0 + 2) * 64))
            cols += list(range(1024 + h0 * 64, 1024 + (h0 + 2) * 64))
            cols += list(range(3072 + h0 * 64, 3072 + (h0 + 2) * 64))
            cols += list(range(2048 + h0 * 64, 2048 + (h0 + 2) * 64))
        meta = np.array([[4096 * h, 512 * h, 2560 * h, 64 * h,
                          (2 + h) * 64, (3 - h) * 64, h, epoch]],
                        dtype=np.int32)
        pp = np.arange(128, dtype=np.int32)
        rix = np.zeros((128, 180), np.int32)
        for t in range(20):
            rix[:, t] = 2560 * h + 128 * t + pp
        for jj in range(32):
            rix[:, 20 + jj] = 4096 * h + 128 * jj + pp
        for qc in range(16):
            for sub in range(4):
                rix[:, 52 + qc * 4 + sub] = 8192 * h + qc * 512 + sub * 128 + pp
        for hp in range(2):
            for jj in range(32):
                rix[:, 116 + hp * 32 + jj] = 8192 * hp + 4096 * h + 128 * jj + pp
        in_maps.append({
            **cns,
            "x0h": np.ascontiguousarray(x0[g, h * 4096:(h + 1) * 4096]),
            "wq8": np.ascontiguousarray(wq[:, cols]),
            "wob_i": w_out, "wr": wr, "ident": consts["ident"],
            "gam": np.ascontiguousarray(np.tile(gamma, (128, 8))),
            "bet": np.ascontiguousarray(np.tile(beta, (128, 8))),
            "wi4": wi[4 * h:4 * h + 4], "wo4": wo[4 * h:4 * h + 4],
            "meta": meta, "ridx": rix,
        })
    nc = _get("FUSED", build_FUSED)
    res = run_bass_kernel_spmd(nc, in_maps, core_ids=list(range(8)))
    out = np.empty((GG, T, D), dtype=np.float32)
    for core in range(4):
        g = core // 2
        h = core % 2
        out[g, h * 4096:(h + 1) * 4096] = np.asarray(res.results[core]["x3h"])
    return out



def build_A():
    p = Prog()
    nc = p.nc
    x0s = nc.dram_tensor("x0s", [TLOC, D], F32, kind="ExternalInput").ap()
    wr = nc.dram_tensor("wr", [D, E], F32, kind="ExternalInput").ap()
    ident = nc.dram_tensor("ident", [128, 128], F32, kind="ExternalInput").ap()
    lnb_o = nc.dram_tensor("lnb", [TLOC, D], BF16, kind="ExternalOutput").ap()
    dec_o = nc.dram_tensor("dec", [TLOC, 4], F32, kind="ExternalOutput").ap()

    xt = p.sbuf("xt", [128, D], F32)
    stat = p.sbuf("stat", [128, 12], F32)
    mv = p.sbuf("mv", [128, 2], F32)
    sd = p.sbuf("sd", [128, 1], F32)
    rstd = p.sbuf("rstd", [128, 1], F32)
    lnf = p.sbuf("lnf", [128, D], F32)
    lnb = p.sbuf("lnbs", [128, D], BF16)
    lnT = p.sbuf("lnT", [128, D], F32)
    identsb = p.sbuf("identsb", [128, 128], F32)
    wr_sb = p.sbuf("wr_sb", [128, 8, E], F32)
    logit = p.sbuf("logit", [128, E], F32)
    scr = {
        "mx1": p.sbuf("mx1", [128, 1], F32),
        "sm": p.sbuf("sm", [128, 1], F32),
        "mx8": p.sbuf("mx8", [128, 8], F32),
        "mi8": p.sbuf("mi8", [128, 8], U32),
        "mif": p.sbuf("mif", [128, 8], F32),
    }
    dec = p.sbuf("dec_sb", [128, 4], F32)
    ptr = [p.psum("ptr0", [128, 128], F32), p.psum("ptr1", [128, 128], F32)]
    plog = p.psum("plog", [128, E], F32)

    p.block()
    p.sems()

    ins = nc.sync.dma_start(out=identsb[:], in_=ident[:])
    p.LD.dma(ins)
    ins = nc.sync.dma_start(out=wr_sb[:], in_=wr.rearrange("(c p) e -> p c e", p=128))
    p.LD.dma(ins)

    for i in range(NTILE_LOC):
        if i > 0:
            p.DVE.wait(nc.sync)
            p.ST.wait(nc.vector)
        ins = nc.sync.dma_start(out=xt[:], in_=x0s[i * 128:(i + 1) * 128, :])
        p.LD.dma(ins)
        p.LD.wait(nc.vector)
        if i > 0:
            p.PE.wait(nc.vector)
        emit_ln_tile(p, xt, stat, mv, sd, rstd, lnf, lnb)
        p.DVE.wait(nc.sync)
        ins = nc.sync.dma_start(out=lnb_o[i * 128:(i + 1) * 128, :], in_=lnb[:])
        p.ST.dma(ins)
        p.DVE.wait(nc.tensor)
        emit_router_tile(p, lnf, identsb, wr_sb, lnT, ptr, plog, logit, scr, dec)
        p.DVE.wait(nc.sync)
        ins = nc.sync.dma_start(out=dec_o[i * 128:(i + 1) * 128, :], in_=dec[:])
        p.ST.dma(ins)
        p.ST.wait(nc.sync)
    p.ST.wait(nc.gpsimd)
    p.close()
    return p.nc



_CACHE = {}


def _get(name, builder):
    if name not in _CACHE:
        _CACHE[name] = builder()
    return _CACHE[name]


def kernel(**inputs):
    return kernel_fused(**inputs)


def kernel_multilaunch(L0_attn_hidden_states, L0_attn_w_qkuv, L0_attn_w_out,
           L0_attn_ln_gamma, L0_attn_ln_beta, L0_ffn_w_router,
           L0_ffn_wi, L0_ffn_wo):
    x0 = np.ascontiguousarray(np.asarray(L0_attn_hidden_states, dtype=np.float32)
                              ).reshape(GG, T, D)
    wq = np.ascontiguousarray(np.asarray(L0_attn_w_qkuv, dtype=np.float32))
    w_out = np.ascontiguousarray(np.asarray(L0_attn_w_out, dtype=np.float32))
    gamma = np.asarray(L0_attn_ln_gamma, dtype=np.float32)
    beta = np.asarray(L0_attn_ln_beta, dtype=np.float32)
    wr = np.ascontiguousarray(np.asarray(L0_ffn_w_router, dtype=np.float32))
    wi = np.ascontiguousarray(np.asarray(L0_ffn_wi, dtype=np.float32))
    wo = np.ascontiguousarray(np.asarray(L0_ffn_wo, dtype=np.float32))

    consts = host_consts()
    cns = {"c_" + n: consts[n] for n in
           ("cols8", "triu_bf", "striu_bf", "ones128x1_bf", "ones1x128")}
    cores = list(range(8))
    gq = [divmod(c, QP) for c in cores]

    def decw_of(dec):
        return [np.ascontiguousarray(
            dec[g].reshape(NTILE, 128, 4).transpose(1, 0, 2)) for g in range(GG)]

    def gather_full(res, key, axis=0):
        return [np.concatenate([np.asarray(res.results[g * QP + q][key])
                                for q in range(QP)], axis=axis) for g in range(GG)]

    def per_core(res, key):
        return [np.asarray(r[key]) for r in res.results]

    resA = run_bass_kernel_spmd(_get("A", build_A), [
        {"x0s": np.ascontiguousarray(x0[g, q * TLOC:(q + 1) * TLOC]),
         "wr": wr, "ident": consts["ident"]}
        for (g, q) in gq], core_ids=cores)
    lnb1 = gather_full(resA, "lnb")
    dec1 = gather_full(resA, "dec")
    decw1 = decw_of(dec1)

    def expert_maps(lnb, decw):
        return [
            {**cns, "lnb": np.ascontiguousarray(lnb[g]),
             "decw": decw[g],
             "wi2": wi[2 * q:2 * q + 2], "wo2": wo[2 * q:2 * q + 2],
             "eq0": np.tile(np.array([2.0 * q, 2.0 * q + 1], dtype=np.float32),
                            (128, 1))}
            for (g, q) in gq]
    resB = run_bass_kernel_spmd(_get("B", build_B), expert_maps(lnb1, decw1),
                                core_ids=cores)
    etab1 = gather_full(resB, "etab")

    resC = run_bass_kernel_spmd(_get("C", build_C), [
        {**cns, "xs": np.ascontiguousarray(x0[g, q * TLOC:(q + 1) * TLOC]),
         "decw": decw1[g], "etab": etab1[g],
         "myq": np.array([[32 * q]], dtype=np.int32)}
        for (g, q) in gq], core_ids=cores)
    x1s = per_core(resC, "x1s")
    ln2 = gather_full(resC, "ln2b")

    din = []
    for (g, q) in gq:
        cols = []
        for pair in range(2):
            h0 = 4 * q + 2 * pair
            cols += list(range(h0 * 64, (h0 + 2) * 64))
        for pair in range(2):
            h0 = 4 * q + 2 * pair
            cols += list(range(1024 + h0 * 64, 1024 + (h0 + 2) * 64))
        h0 = 4 * q
        cols += list(range(3072 + h0 * 64, 3072 + (h0 + 4) * 64))
        cols += list(range(2048 + h0 * 64, 2048 + (h0 + 4) * 64))
        din.append({
            "c_eye64s_bf": consts["eye64s_bf"],
            "ln2b": np.ascontiguousarray(ln2[g]),
            "wq4": np.ascontiguousarray(wq[:, cols]),
            "gam": np.ascontiguousarray(np.tile(gamma, (128, 8))),
            "bet": np.ascontiguousarray(np.tile(beta, (128, 8)))})
    resD = run_bass_kernel_spmd(_get("D", build_D), din, core_ids=cores)
    gated = gather_full(resD, "gatedb", axis=1)

    resE = run_bass_kernel_spmd(_get("E", build_E), [
        {"x1s": x1s[c], "gfull": np.ascontiguousarray(
            gated[g][q * TLOC:(q + 1) * TLOC]),
         "wo": w_out, "wr": wr, "ident": consts["ident"]}
        for c, (g, q) in enumerate(gq)], core_ids=cores)
    x2s = per_core(resE, "x2s")
    ln3 = gather_full(resE, "ln3b")
    dec3 = gather_full(resE, "dec")
    decw3 = decw_of(dec3)

    resF = run_bass_kernel_spmd(_get("B", build_B), expert_maps(ln3, decw3),
                                core_ids=cores)
    etab3 = gather_full(resF, "etab")

    resG = run_bass_kernel_spmd(_get("G", lambda: build_C(final=True)), [
        {**cns, "xs": x2s[c], "decw": decw3[g], "etab": etab3[g],
         "myq": np.array([[32 * q]], dtype=np.int32)}
        for c, (g, q) in enumerate(gq)], core_ids=cores)
    x3 = gather_full(resG, "x3s")
    out = np.stack(x3).reshape(GG, T, D).astype(np.float32)
    return out


# version 06
# speedup vs baseline: 1.3111x; 1.3111x over previous
import numpy as np
import concourse.bass as bass
import concourse.mybir as mybir
from concourse.bass import IndirectOffsetOnAxis
from concourse.expressions_rust import RuntimeValue
from concourse.bass_utils import run_bass_kernel_spmd

F32 = mybir.dt.float32
BF16 = mybir.dt.bfloat16
I32 = mybir.dt.int32
U32 = mybir.dt.uint32
AF = mybir.ActivationFunctionType
OP = mybir.AluOpType

GG = 2
T = 8192
D = 1024
NH = 16
HD = 64
E = 8
CAP = 640
FF = 4096
QP = 4
TLOC = T // QP
NTILE_LOC = TLOC // 128
NTILE = T // 128
EPS = 1e-5
BIGIDX = 60000.0


class Sem:

    def __init__(self, nc, name):
        self._cm = nc.semaphore(name)
        self.h = self._cm.__enter__()
        self.v = 0

    def tick(self, inst, n=1):
        inst.then_inc(self.h, n)
        self.v += n

    def dma(self, inst):
        self.tick(inst, 16)

    def wait(self, eng):
        if self.v > 0:
            eng.wait_ge(self.h, self.v)

    def wait_val(self, eng, val):
        if val > 0:
            eng.wait_ge(self.h, val)


class Prog:

    def __init__(self):
        self.nc = bass.Bass("TRN2", target_bir_lowering=False)
        self._ctx = []

    def sems(self):
        nc = self.nc
        for name in ("LD", "ST", "PE", "ACT", "DVE", "GP", "X1", "X2"):
            setattr(self, name, Sem(nc, name.lower()))

    def sbuf(self, name, shape, dtype):
        cm = self.nc.sbuf_tensor(name, shape, dtype)
        t = cm.__enter__()
        self._ctx.append(cm)
        return t

    def psum(self, name, shape, dtype=F32):
        cm = self.nc.psum_tensor(name, shape, dtype)
        t = cm.__enter__()
        self._ctx.append(cm)
        return t

    def block(self):
        cm = self.nc.Block()
        b = cm.__enter__()
        self._ctx.append(cm)
        return b

    def close(self):
        for cm in reversed(self._ctx):
            cm.__exit__(None, None, None)



def host_consts():
    c = {}
    c["ident"] = np.eye(128, dtype=np.float32)
    c["triu"] = np.triu(np.ones((128, 128), dtype=np.float32)).astype(np.float32)
    c["cols8"] = np.tile(np.arange(8, dtype=np.float32), (128, 1))
    c["triu_bf"] = np.triu(np.ones((128, 128), dtype=np.float32))
    c["striu_bf"] = np.triu(np.ones((128, 128), dtype=np.float32), 1)
    c["ones128x1_bf"] = np.ones((128, 1), dtype=np.float32)
    c["ones1x128"] = np.ones((1, 128), dtype=np.float32)
    c["eye64s_bf"] = np.vstack([np.eye(64, dtype=np.float32)] * 2)
    c["ident128_bf"] = np.eye(128, dtype=np.float32)
    return c


def emit_ln_tile(p, xt, stat, mv, sd, rstd, lnf, lnb):
    nc = p.nc
    nc.vector.bn_stats(stat[:, 0:6], xt[:, 0:512])
    ins = nc.vector.bn_stats(stat[:, 6:12], xt[:, 512:1024])
    p.DVE.tick(ins)
    p.DVE.wait(nc.vector)
    ins = nc.vector.bn_aggr(mv[:], stat[:])
    p.DVE.tick(ins)
    p.DVE.wait(nc.vector)
    ins = nc.vector.tensor_scalar_add(sd[:], mv[:, 1:2], EPS)
    p.DVE.tick(ins)
    p.DVE.wait(nc.scalar)
    ins = nc.scalar.activation(sd[:], sd[:], AF.Sqrt)
    p.ACT.tick(ins)
    p.ACT.wait(nc.vector)
    ins = nc.vector.reciprocal(rstd[:], sd[:])
    p.DVE.tick(ins)
    p.DVE.wait(nc.vector)
    ins = nc.vector.tensor_scalar(lnf[:], xt[:], mv[:, 0:1], rstd[:, 0:1],
                                  OP.subtract, OP.mult)
    p.DVE.tick(ins)
    if lnb is not None:
        ins = nc.vector.tensor_copy(lnb[:], lnf[:])
        p.DVE.tick(ins)


def emit_router_tile(p, lnf, identsb, wr_sb, lnT, ptr, plog, logit, scr, dec):
    nc = p.nc
    for c in range(8):
        bank = ptr[c % 2]
        p.ACT.wait(nc.tensor)
        ins = nc.tensor.transpose(bank[:], lnf[:, c * 128:(c + 1) * 128], identsb[:])
        p.PE.tick(ins)
        p.PE.wait(nc.scalar)
        ins = nc.scalar.copy(lnT[:, c * 128:(c + 1) * 128], bank[:])
        p.ACT.tick(ins)
    p.ACT.wait(nc.tensor)
    for c in range(8):
        ins = nc.tensor.matmul(plog[:], lnT[:, c * 128:(c + 1) * 128],
                               wr_sb[:, c, :], start=(c == 0), stop=(c == 7))
    p.PE.tick(ins)
    p.PE.wait(nc.scalar)
    p.DVE.wait(nc.scalar)
    ins = nc.scalar.copy(logit[:], plog[:])
    p.ACT.tick(ins)
    p.ACT.wait(nc.vector)
    mx1 = scr["mx1"]
    ins = nc.vector.reduce_max(mx1[:], logit[:], axis=mybir.AxisListType.X)
    p.DVE.tick(ins)
    p.DVE.wait(nc.vector)
    ins = nc.vector.tensor_scalar(logit[:], logit[:], mx1[:, 0:1], None, OP.subtract)
    p.DVE.tick(ins)
    p.DVE.wait(nc.scalar)
    ins = nc.scalar.activation(logit[:], logit[:], AF.Exp)
    p.ACT.tick(ins)
    p.ACT.wait(nc.vector)
    sm = scr["sm"]
    ins = nc.vector.reduce_sum(sm[:], logit[:], axis=mybir.AxisListType.X)
    p.DVE.tick(ins)
    p.DVE.wait(nc.vector)
    ins = nc.vector.reciprocal(sm[:], sm[:])
    p.DVE.tick(ins)
    p.DVE.wait(nc.vector)
    ins = nc.vector.tensor_scalar(logit[:], logit[:], sm[:, 0:1], None, OP.mult)
    p.DVE.tick(ins)
    p.DVE.wait(nc.vector)
    mx8, mi8, mif = scr["mx8"], scr["mi8"], scr["mif"]
    ins = nc.vector.max(mx8[:], logit[:])
    p.DVE.tick(ins)
    p.DVE.wait(nc.vector)
    ins = nc.vector.max_index(mi8[:], mx8[:], logit[:])
    p.DVE.tick(ins)
    p.DVE.wait(nc.vector)
    ins = nc.vector.tensor_copy(dec[:, 0:2], mi8[:, 0:2].bitcast(I32))
    p.DVE.tick(ins)
    ins = nc.vector.tensor_copy(dec[:, 2:4], mx8[:, 0:2])
    p.DVE.tick(ins)




def emit_bookkeeping(p, decw_ap, scr, consts_sb, pA, pB, pC):
    nc = p.nc
    decsb = scr["decsb"]; masks = scr["masks"]; totrow = scr["totrow"]
    totv = scr["totv"]; totv_bf = scr["totv_bf"]; offs_sb = scr["offs_sb"]
    offsrow = scr["offsrow"]; prod = scr["prod"]; posl = scr["posl"]
    vflag = scr["vflag"]; esel = scr["esel"]; slots = scr["slots"]
    bk1 = scr["bk_scr1"]; bk2 = scr["bk_scr2"]
    cols8 = consts_sb["cols8"]; triu = consts_sb["triu_bf"]
    striu = consts_sb["striu_bf"]; ones128x1 = consts_sb["ones128x1_bf"]
    ones1x128 = consts_sb["ones1x128"]

    if isinstance(decw_ap, list):
        for (off, ap_in) in decw_ap:
            ins = nc.sync.dma_start(out=decsb[:, off:off + 32, :], in_=ap_in)
            p.LD.dma(ins)
    else:
        ins = nc.sync.dma_start(out=decsb[:], in_=decw_ap)
        p.LD.dma(ins)
    p.LD.wait(nc.vector)
    for j in range(NTILE):
        for k in range(2):
            ins = nc.vector.tensor_scalar(masks[:, j, k, :], cols8[:],
                                          decsb[:, j, k:k + 1], None, OP.is_equal)
    p.DVE.tick(ins)
    p.DVE.wait(nc.tensor)
    mflat = masks.rearrange("p j k e -> p (j k e)")
    nc.tensor.matmul(pA[:, 0:512], triu[:], mflat[:, 0:512], start=True, stop=True)
    nc.tensor.matmul(pA[:, 512:1024], triu[:], mflat[:, 512:1024], start=True, stop=True)
    nc.tensor.matmul(pB[0:1, 0:512], ones128x1[:], mflat[:, 0:512], start=True, stop=True)
    ins = nc.tensor.matmul(pB[0:1, 512:1024], ones128x1[:], mflat[:, 512:1024], start=True, stop=True)
    p.PE.tick(ins)
    p.PE.wait(nc.scalar)
    ins = nc.scalar.copy(totrow[0:1, :], pB[0:1, :])
    p.ACT.tick(ins)
    p.ACT.wait(nc.sync)
    ins = nc.sync.dma_start(out=bk1.rearrange("(p n) -> p n", p=1), in_=totrow[0:1, :])
    p.ST.dma(ins)
    p.ST.wait(nc.sync)
    for kk in range(2):
        ins = nc.sync.dma_start(
            out=totv[kk * NTILE:(kk + 1) * NTILE, :],
            in_=bk1.rearrange("(j k e) -> j k e", j=NTILE, k=2)[:, kk, :])
        p.LD.dma(ins)
    p.LD.wait(nc.vector)
    ins = nc.vector.tensor_copy(totv_bf[:], totv[:])
    p.DVE.tick(ins)
    p.DVE.wait(nc.tensor)
    ins = nc.tensor.matmul(pC[:, 0:8], striu[:], totv_bf[:], start=True, stop=True)
    p.PE.tick(ins)
    p.PE.wait(nc.scalar)
    ins = nc.scalar.copy(offs_sb[:], pC[:, 0:8])
    p.ACT.tick(ins)
    p.ACT.wait(nc.sync)
    ins = nc.sync.dma_start(out=bk2.rearrange("(v e) -> v e", v=128), in_=offs_sb[:])
    p.ST.dma(ins)
    p.ST.wait(nc.sync)
    ins = nc.sync.dma_start(
        out=offsrow[0:1, :].rearrange("p (j k e) -> p j k e", j=NTILE, k=2),
        in_=bk2.rearrange("(k j e) -> j k e", k=2, j=NTILE).unsqueeze(0))
    p.LD.dma(ins)
    p.LD.wait(nc.tensor)
    for j in range(NTILE):
        for k in range(2):
            col = j * 16 + k * 8
            ins = nc.tensor.matmul(pA[:, col:col + 8], ones1x128[0:1, :],
                                   offsrow[0:1, col:col + 8], start=False, stop=True)
    p.PE.tick(ins)
    p.PE.wait(nc.vector)
    ins = nc.vector.tensor_tensor(prod[:], pA[:, :], mflat[:, :], OP.mult)
    p.DVE.tick(ins)
    p.DVE.wait(nc.vector)
    ins = nc.vector.tensor_reduce(posl[:], prod.rearrange("p (c e) -> p c e", e=E),
                                  mybir.AxisListType.X, OP.add)
    p.DVE.tick(ins)
    p.DVE.wait(nc.vector)
    nc.vector.tensor_scalar(vflag[:], posl[:], float(CAP), None, OP.is_le)
    nc.vector.tensor_copy(esel[:].rearrange("p (j k) -> p j k", k=2), decsb[:, :, 0:2])
    ins = nc.vector.scalar_tensor_tensor(slots[:], esel[:], float(CAP), posl[:],
                                         OP.mult, OP.add)
    p.DVE.tick(ins)
    p.DVE.wait(nc.vector)
    ins = nc.vector.tensor_scalar_add(slots[:], slots[:], -1.0)
    p.DVE.tick(ins)


def bookkeeping_scratch(p):
    return {
        "decsb": p.sbuf("bk_decsb", [128, NTILE, 4], F32),
        "masks": p.sbuf("bk_masks", [128, NTILE, 2, E], BF16),
        "totrow": p.sbuf("bk_totrow", [1, 1024], F32),
        "totv": p.sbuf("bk_totv", [128, E], F32),
        "totv_bf": p.sbuf("bk_totv_bf", [128, E], BF16),
        "offs_sb": p.sbuf("bk_offs", [128, E], F32),
        "offsrow": p.sbuf("bk_offsrow", [1, 1024], F32),
        "prod": p.sbuf("bk_prod", [128, 1024], F32),
        "posl": p.sbuf("bk_posl", [128, 128], F32),
        "vflag": p.sbuf("bk_vflag", [128, 128], F32),
        "esel": p.sbuf("bk_esel", [128, 128], F32),
        "slots": p.sbuf("bk_slots", [128, 128], F32),
        "bk_scr1": p.nc.dram_tensor("bk_scr1", [1024], F32, kind="Internal").ap(),
        "bk_scr2": p.nc.dram_tensor("bk_scr2", [1024], F32, kind="Internal").ap(),
    }


def load_consts(p, names):
    out = {}
    nc = p.nc
    for n in names:
        dt = BF16 if n.endswith("_bf") else F32
        shp = {"ident": [128, 128], "triu_bf": [128, 128], "striu_bf": [128, 128],
               "cols8": [128, 8], "ones128x1_bf": [128, 1], "ones1x128": [1, 128],
               "eye64s_bf": [128, 64], "ident128_bf": [128, 128]}[n]
        ap = nc.dram_tensor("c_" + n, shp, F32, kind="ExternalInput").ap()
        sb = p.sbuf("cs_" + n, shp, dt)
        if dt == F32:
            ins = nc.sync.dma_start(out=sb[:], in_=ap)
        else:
            ins = nc.gpsimd.dma_start(out=sb[:], in_=ap)
        p.LD.dma(ins)
        out[n] = sb
    return out



def build_B():
    p = Prog()
    nc = p.nc
    lnb = nc.dram_tensor("lnb", [T, D], BF16, kind="ExternalInput").ap()
    decw = nc.dram_tensor("decw", [128, NTILE, 4], F32, kind="ExternalInput").ap()
    wi2 = nc.dram_tensor("wi2", [2, D, FF], F32, kind="ExternalInput").ap()
    wo2 = nc.dram_tensor("wo2", [2, FF, D], F32, kind="ExternalInput").ap()
    eq0_i = nc.dram_tensor("eq0", [128, 2], F32, kind="ExternalInput").ap()
    etab_o = nc.dram_tensor("etab", [2 * CAP, D], BF16, kind="ExternalOutput").ap()
    table = nc.dram_tensor("dtab", [2 * CAP, D], BF16, kind="Internal").ap()

    scr = bookkeeping_scratch(p)
    eq0 = p.sbuf("eq0_sb", [128, 2], F32)
    ma = p.sbuf("ma_sb", [128, 128], F32)
    mb = p.sbuf("mb_sb", [128, 128], F32)
    sidx_f = p.sbuf("sidx_f", [128, 128], F32)
    sidx_i = p.sbuf("sidx_i", [128, 128], I32)
    lnt = [p.sbuf("lnt0", [128, D], BF16), p.sbuf("lnt1", [128, D], BF16)]
    xe = p.sbuf("xe", [128, 8, 2 * CAP], BF16)
    wib = [p.sbuf("wib0", [128, 8, 128], BF16), p.sbuf("wib1", [128, 8, 128], BF16)]
    h = p.sbuf("h", [128, 32, CAP], BF16)
    wob = p.sbuf("wob", [128, 32, D], BF16)
    eob = p.sbuf("eob", [128, D], BF16)

    pA = p.psum("pA", [128, 1024], F32)
    pB = p.psum("pB", [128, 1024], F32)
    pC = p.psum("pC", [128, 512], F32)
    pD = p.psum("pD", [128, 512], F32)

    p.block()
    p.sems()

    consts_sb = load_consts(p, ["cols8", "triu_bf", "striu_bf", "ones128x1_bf",
                                "ones1x128"])
    ins = nc.sync.dma_start(out=eq0[:], in_=eq0_i[:])
    p.LD.dma(ins)
    emit_bookkeeping(p, decw[:], scr, consts_sb, pA, pB, pC)

    vflag, slots_g, esel, posl = scr["vflag"], scr["slots"], scr["esel"], scr["posl"]
    p.DVE.wait(nc.vector)
    nc.vector.tensor_scalar(ma[:], esel[:], eq0[:, 0:1], None, OP.is_ge)
    nc.vector.tensor_scalar(mb[:], esel[:], eq0[:, 1:2], None, OP.is_le)
    ins = nc.vector.tensor_tensor(ma[:], ma[:], mb[:], OP.mult)
    p.DVE.tick(ins)
    p.DVE.wait(nc.vector)
    ins = nc.vector.tensor_tensor(ma[:], ma[:], vflag[:], OP.mult)
    p.DVE.tick(ins)
    nc.vector.tensor_scalar(sidx_f[:], esel[:], eq0[:, 0:1], None, OP.subtract)
    ins = nc.vector.scalar_tensor_tensor(sidx_f[:], sidx_f[:], float(CAP), posl[:],
                                         OP.mult, OP.add)
    p.DVE.tick(ins)
    p.DVE.wait(nc.vector)
    ins = nc.vector.tensor_scalar_add(sidx_f[:], sidx_f[:], -1.0 - BIGIDX)
    p.DVE.tick(ins)
    p.DVE.wait(nc.vector)
    ins = nc.vector.tensor_tensor(sidx_f[:], sidx_f[:], ma[:], OP.mult)
    p.DVE.tick(ins)
    p.DVE.wait(nc.vector)
    ins = nc.vector.tensor_scalar_add(sidx_f[:], sidx_f[:], BIGIDX)
    p.DVE.tick(ins)
    p.DVE.wait(nc.vector)
    ins = nc.vector.tensor_copy(sidx_i[:], sidx_f[:])
    p.DVE.tick(ins)
    p.DVE.wait(nc.gpsimd)
    p.DVE.wait(nc.sync)

    bcreg = nc.gpsimd.alloc_register("bcreg")
    nc.gpsimd.reg_mov(bcreg, 2 * CAP - 1)
    for j in range(NTILE):
        buf = lnt[j % 2]
        if j >= 2:
            p.GP.wait(nc.sync)
        ins = nc.sync.dma_start(out=buf[:], in_=lnb[j * 128:(j + 1) * 128, :])
        p.LD.dma(ins)
        p.LD.wait(nc.gpsimd)
        for k in range(2):
            col = j * 2 + k
            ins = nc.gpsimd.indirect_dma_start(
                out=table[:, :],
                out_offset=IndirectOffsetOnAxis(sidx_i[:, col:col + 1], axis=0),
                in_=buf[:, :], in_offset=None,
                bounds_check=bcreg, oob_is_err=False)
            p.GP.dma(ins)

    p.GP.wait(nc.sync)
    for c in range(8):
        ins = nc.sync.dma_start_transpose(xe[:, c, :],
                                          table[0:2 * CAP, c * 128:(c + 1) * 128])
        p.LD.dma(ins)
    p.LD.wait(nc.tensor)
    p.LD.wait(nc.gpsimd)

    ins = nc.gpsimd.dma_start(out=wob[:],
                              in_=wo2[0].rearrange("(kc p) n -> p kc n", p=128))
    p.GP.dma(ins)

    for ei in range(2):
        for fc in range(32):
            wbuf = wib[fc % 2]
            p.PE.wait(nc.gpsimd)
            ins = nc.gpsimd.dma_start(
                out=wbuf[:], in_=wi2[ei, :, fc * 128:(fc + 1) * 128]
                .rearrange("(kc p) m -> p kc m", p=128))
            p.GP.dma(ins)
            p.GP.wait(nc.tensor)
            ps = [pA, pB][fc % 2]
            p.ACT.wait(nc.tensor)
            for (n0, nw) in ((0, 512), (512, 128)):
                for kc in range(8):
                    ins = nc.tensor.matmul(
                        ps[:, n0:n0 + nw], wbuf[:, kc, :],
                        xe[:, kc, ei * CAP + n0: ei * CAP + n0 + nw],
                        start=(kc == 0), stop=(kc == 7))
            p.PE.tick(ins)
            p.PE.wait(nc.scalar)
            ins = nc.scalar.activation(h[:, fc, :], ps[:, 0:CAP], AF.Relu)
            p.ACT.tick(ins)
        if ei == 1:
            p.PE.wait(nc.gpsimd)
            ins = nc.gpsimd.dma_start(out=wob[:],
                                      in_=wo2[1].rearrange("(kc p) n -> p kc n", p=128))
            p.GP.dma(ins)
        p.GP.wait(nc.tensor)
        p.ACT.wait(nc.tensor)
        for s5 in range(5):
            for n2 in range(2):
                ps2 = [pC, pD][(s5 * 2 + n2) % 2]
                p.ACT.wait(nc.tensor)
                for kc2 in range(32):
                    ins = nc.tensor.matmul(
                        ps2[:, 0:512],
                        h[:, kc2, s5 * 128:(s5 + 1) * 128],
                        wob[:, kc2, n2 * 512:(n2 + 1) * 512],
                        start=(kc2 == 0), stop=(kc2 == 31))
                p.PE.tick(ins)
                p.PE.wait(nc.scalar)
                p.ST.wait(nc.scalar)
                ins = nc.scalar.copy(eob[:, n2 * 512:(n2 + 1) * 512], ps2[:, 0:512])
                p.ACT.tick(ins)
            p.ACT.wait(nc.sync)
            ins = nc.sync.dma_start(
                out=etab_o[ei * CAP + s5 * 128: ei * CAP + (s5 + 1) * 128, :],
                in_=eob[:])
            p.ST.dma(ins)
    p.ST.wait(nc.gpsimd)
    p.close()
    return p.nc




def build_C(final=False):
    p = Prog()
    nc = p.nc
    x0s = nc.dram_tensor("xs", [TLOC, D], F32, kind="ExternalInput").ap()
    decw = nc.dram_tensor("decw", [128, NTILE, 4], F32, kind="ExternalInput").ap()
    etab = nc.dram_tensor("etab", [E * CAP, D], BF16, kind="ExternalInput").ap()
    myq = nc.dram_tensor("myq", [1, 1], I32, kind="ExternalInput").ap()
    if final:
        x1_o = nc.dram_tensor("x3s", [TLOC, D], F32, kind="ExternalOutput").ap()
    else:
        x1_o = nc.dram_tensor("x1s", [TLOC, D], F32, kind="ExternalOutput").ap()
        ln2_o = nc.dram_tensor("ln2b", [TLOC, D], BF16, kind="ExternalOutput").ap()
    gscr = nc.dram_tensor("gscr", [128, 256], F32, kind="Internal").ap()

    scr = bookkeeping_scratch(p)
    gidx_f = p.sbuf("gidx_f", [128, 128], F32)
    gate2 = p.sbuf("gate2", [128, 128], F32)
    myq_sb = p.sbuf("myq_sb", [1, 2], I32)
    gidx_my_f = p.sbuf("gidx_my_f", [128, 32], F32)
    gidx_my = p.sbuf("gidx_my", [128, 32], I32)
    gate_my = p.sbuf("gate_my", [128, 32], F32)
    yk = [p.sbuf("yk0", [128, D], BF16), p.sbuf("yk1", [128, D], BF16)]
    ytmp = p.sbuf("ytmp", [128, D], F32)
    xt = p.sbuf("xt", [128, D], F32)
    stat = p.sbuf("stat", [128, 12], F32)
    mv = p.sbuf("mv", [128, 2], F32)
    sd = p.sbuf("sd", [128, 1], F32)
    rstd = p.sbuf("rstd", [128, 1], F32)
    lnf = p.sbuf("lnf", [128, D], F32)
    lnb = p.sbuf("lnbs", [128, D], BF16)

    pA = p.psum("pA", [128, 1024], F32)
    pB = p.psum("pB", [128, 1024], F32)
    pC = p.psum("pC", [128, 512], F32)

    p.block()
    p.sems()

    consts_sb = load_consts(p, ["cols8", "triu_bf", "striu_bf", "ones128x1_bf",
                                "ones1x128"])
    ins = nc.sync.dma_start(out=myq_sb[:, 0:1], in_=myq[:])
    p.LD.dma(ins)
    emit_bookkeeping(p, decw[:], scr, consts_sb, pA, pB, pC)

    vflag, slots, decsb = scr["vflag"], scr["slots"], scr["decsb"]
    p.DVE.wait(nc.vector)
    nc.vector.tensor_scalar_add(gidx_f[:], slots[:], -BIGIDX)
    ins = nc.vector.tensor_tensor(gidx_f[:], gidx_f[:], vflag[:], OP.mult)
    p.DVE.tick(ins)
    p.DVE.wait(nc.vector)
    ins = nc.vector.tensor_scalar_add(gidx_f[:], gidx_f[:], BIGIDX)
    p.DVE.tick(ins)
    nc.vector.tensor_copy(gate2[:].rearrange("p (j k) -> p j k", k=2),
                          decsb[:, :, 2:4])
    ins = nc.vector.tensor_tensor(gate2[:], gate2[:], vflag[:], OP.mult)
    p.DVE.tick(ins)
    p.DVE.wait(nc.sync)
    ins = nc.sync.dma_start(out=gscr[:, 0:128], in_=gidx_f[:])
    p.ST.dma(ins)
    ins = nc.sync.dma_start(out=gscr[:, 128:256], in_=gate2[:])
    p.ST.dma(ins)
    p.ST.wait(nc.sync)
    p.LD.wait(nc.sync)
    offreg = nc.sync.alloc_register("offr")
    nc.sync.load(offreg, myq_sb[0:1, 0:1])
    off = RuntimeValue(offreg)
    gidx_ap = bass.AP(gscr.tensor, off, [[256, 128], [1, 32]])
    gate_ap = bass.AP(gscr.tensor, off + 128, [[256, 128], [1, 32]])
    ins = nc.sync.dma_start(out=gidx_my_f[:], in_=gidx_ap)
    p.LD.dma(ins)
    ins = nc.sync.dma_start(out=gate_my[:], in_=gate_ap)
    p.LD.dma(ins)
    p.LD.wait(nc.vector)
    ins = nc.vector.tensor_copy(gidx_my[:], gidx_my_f[:])
    p.DVE.tick(ins)
    p.DVE.wait(nc.gpsimd)

    bcreg = nc.gpsimd.alloc_register("bcreg")
    nc.gpsimd.reg_mov(bcreg, E * CAP - 1)
    for j in range(NTILE_LOC):
        if j > 0:
            p.DVE.wait(nc.sync)
            p.ST.wait(nc.vector)
        ins = nc.sync.dma_start(out=xt[:], in_=x0s[j * 128:(j + 1) * 128, :])
        p.LD.dma(ins)
        for k in range(2):
            ins = nc.vector.memset(yk[k][:], 0.0)
            p.DVE.tick(ins)
        p.DVE.wait(nc.gpsimd)
        for k in range(2):
            ins = nc.gpsimd.indirect_dma_start(
                out=yk[k][:, :], out_offset=None, in_=etab[:, :],
                in_offset=IndirectOffsetOnAxis(gidx_my[:, 2 * j + k:2 * j + k + 1],
                                               axis=0),
                bounds_check=bcreg, oob_is_err=False)
            p.GP.dma(ins)
        p.GP.wait(nc.vector)
        p.LD.wait(nc.vector)
        ins = nc.vector.tensor_scalar(ytmp[:], yk[0][:], gate_my[:, 2 * j:2 * j + 1],
                                      None, OP.mult)
        p.DVE.tick(ins)
        p.DVE.wait(nc.vector)
        ins = nc.vector.scalar_tensor_tensor(ytmp[:], yk[1][:],
                                             gate_my[:, 2 * j + 1:2 * j + 2],
                                             ytmp[:], OP.mult, OP.add)
        p.DVE.tick(ins)
        p.DVE.wait(nc.vector)
        ins = nc.vector.scalar_tensor_tensor(xt[:], ytmp[:], 0.5, xt[:],
                                             OP.mult, OP.add)
        p.DVE.tick(ins)
        p.DVE.wait(nc.sync)
        ins = nc.sync.dma_start(out=x1_o[j * 128:(j + 1) * 128, :], in_=xt[:])
        p.ST.dma(ins)
        if not final:
            emit_ln_tile(p, xt, stat, mv, sd, rstd, lnf, lnb)
            p.DVE.wait(nc.sync)
            ins = nc.sync.dma_start(out=ln2_o[j * 128:(j + 1) * 128, :], in_=lnb[:])
            p.ST.dma(ins)
    p.ST.wait(nc.gpsimd)
    p.close()
    return p.nc




NQC = T // 512
NKT = T // 128


def build_D(dbg=0, ntc=NQC, natt=None, epi=99):
    p = Prog()
    nc = p.nc
    ln2b = nc.dram_tensor("ln2b", [T, D], BF16, kind="ExternalInput").ap()
    wq4 = nc.dram_tensor("wq4", [D, 1024], F32, kind="ExternalInput").ap()
    gam = nc.dram_tensor("gam", [128, 512], F32, kind="ExternalInput").ap()
    bet = nc.dram_tensor("bet", [128, 512], F32, kind="ExternalInput").ap()
    gatedb = nc.dram_tensor("gatedb", [T, 256], BF16, kind="ExternalOutput").ap()
    if dbg == 6:
        dbg_done = nc.dram_tensor("dbg_done", [1, 16], F32, kind="ExternalOutput").ap()
        dbg_q = nc.dram_tensor("dbg_q", [128, 2, T], BF16, kind="ExternalOutput").ap()
        dbg_g = nc.dram_tensor("dbg_g", [128, NKT, 256], BF16, kind="ExternalOutput").ap()
        dbg_af = nc.dram_tensor("dbg_af", [128, 512], BF16, kind="ExternalOutput").ap()
        dbg_at = nc.dram_tensor("dbg_at", [128, 512], F32, kind="ExternalOutput").ap()
        dbg_nr = nc.dram_tensor("dbg_nr", [128, 512], F32, kind="ExternalOutput").ap()
        dbg_gg = nc.dram_tensor("dbg_gg", [128, 4, 128], BF16, kind="ExternalOutput").ap()
    if dbg and dbg != 6:
        dbg_q = nc.dram_tensor("dbg_q", [128, 2, T], BF16, kind="ExternalOutput").ap()
        dbg_k = nc.dram_tensor("dbg_k", [128, 2, T], BF16, kind="ExternalOutput").ap()
        dbg_v = nc.dram_tensor("dbg_v", [128, NKT, 256], BF16, kind="ExternalOutput").ap()
        dbg_g = nc.dram_tensor("dbg_g", [128, NKT, 256], BF16, kind="ExternalOutput").ap()
        dbg_done = nc.dram_tensor("dbg_done", [1, 16], F32, kind="ExternalOutput").ap()

    wq_sb = p.sbuf("wq_sb", [128, 8, 1024], BF16)
    lnF = [p.sbuf("lnF0", [128, 8, 512], BF16), p.sbuf("lnF1", [128, 8, 512], BF16)]
    q_sb = p.sbuf("q_sb", [128, 2, T], BF16)
    k_sb = p.sbuf("k_sb", [128, 2, T], BF16)
    v_sb = p.sbuf("v_sb", [128, NKT, 256], BF16)
    gate_sb = p.sbuf("gate_sb", [128, NKT, 256], BF16)
    w_sb = [p.sbuf("w_sb0", [128, 1024], BF16), p.sbuf("w_sb1", [128, 1024], BF16)]
    att_fm = p.sbuf("att_fm", [128, 512], BF16)
    att_tok = p.sbuf("att_tok", [128, 512], F32)
    sq_tok = p.sbuf("sq_tok", [128, 512], F32)
    ssum = p.sbuf("ssum", [128, 8], F32)
    ssq = p.sbuf("ssq", [128, 8], F32)
    mu = p.sbuf("mu", [128, 8], F32)
    var = p.sbuf("var", [128, 8], F32)
    rstd = p.sbuf("rstd", [128, 8], F32)
    mu2 = p.sbuf("mu2", [128, 8], F32)
    gam_sb = p.sbuf("gam_sb", [128, 512], F32)
    bet_sb = p.sbuf("bet_sb", [128, 512], F32)
    nrm = p.sbuf("nrm", [128, 512], F32)
    gated = p.sbuf("gated", [128, 4, 128], BF16)

    psS = [p.psum("psS0", [128, 1024], F32), p.psum("psS1", [128, 1024], F32)]
    psA = p.psum("psA", [128, 512], F32)
    psAo = p.psum("psAo", [128, 512], F32)
    psT = p.psum("psT", [128, 1024], BF16)
    psT2 = p.psum("psT2", [128, 1024], BF16)
    psV = psA

    p.block()
    p.sems()

    consts_sb = load_consts(p, ["eye64s_bf"])
    ins = nc.gpsimd.dma_start(out=wq_sb[:],
                              in_=wq4.rearrange("(c p) m -> p c m", p=128))
    p.GP.dma(ins)
    ins = nc.sync.dma_start(out=gam_sb[:], in_=gam[:])
    p.LD.dma(ins)
    ins = nc.sync.dma_start(out=bet_sb[:], in_=bet[:])
    p.LD.dma(ins)
    p.GP.wait(nc.tensor)

    qk_dst = {0: (q_sb, 0), 1: (q_sb, 1), 2: (k_sb, 0), 3: (k_sb, 1)}
    p.GP.wait(nc.sync)
    p.LD.wait(nc.sync)
    for tc in range(ntc):
        buf = lnF[tc % 2]
        p.PE.wait(nc.sync)
        for c in range(8):
            ins = nc.sync.dma_start_transpose(
                buf[:, c, :], ln2b[tc * 512:(tc + 1) * 512, c * 128:(c + 1) * 128])
            p.LD.dma(ins)
        p.LD.wait(nc.tensor)
        p.ACT.wait(nc.tensor)
        for mi in range(4):
            ps = psS[mi // 2]
            off = (mi % 2) * 512
            for c in range(8):
                ins = nc.tensor.matmul(ps[:, off:off + 512],
                                       wq_sb[:, c, mi * 128:(mi + 1) * 128],
                                       buf[:, c, :], start=(c == 0), stop=(c == 7))
        p.PE.tick(ins)
        p.PE.wait(nc.scalar)
        for mi in range(4):
            dst, pr = qk_dst[mi]
            ps = psS[mi // 2]
            off = (mi % 2) * 512
            ins = nc.scalar.activation(dst[:, pr, tc * 512:(tc + 1) * 512],
                                       ps[:, off:off + 512], AF.Silu)
        p.ACT.tick(ins)
        for sub in range(4):
            tkt = tc * 4 + sub
            if sub > 0:
                p.ACT.wait(nc.tensor)
            for c in range(8):
                nc.tensor.matmul(psV[:, 0:256], buf[:, c, sub * 128:(sub + 1) * 128],
                                 wq_sb[:, c, 512:768], start=(c == 0), stop=(c == 7))
            for c in range(8):
                ins = nc.tensor.matmul(psV[:, 256:512],
                                       buf[:, c, sub * 128:(sub + 1) * 128],
                                       wq_sb[:, c, 768:1024], start=(c == 0), stop=(c == 7))
            p.PE.tick(ins)
            p.PE.wait(nc.scalar)
            nc.scalar.activation(gate_sb[:, tkt, :], psV[:, 0:256], AF.Silu)
            ins = nc.scalar.activation(v_sb[:, tkt, :], psV[:, 256:512], AF.Silu)
            p.ACT.tick(ins)

    if dbg and dbg != 6:
        p.ACT.wait(nc.sync)
        for (dst, srcb) in ((dbg_q, q_sb), (dbg_k, k_sb), (dbg_v, v_sb), (dbg_g, gate_sb)):
            ins = nc.sync.dma_start(out=dst, in_=srcb[:])
            p.ST.dma(ins)

    SCALE = 1.0 / 8.0
    attiters = []
    for pair in range(2):
        for qc in range(NQC):
            attiters.append((pair, qc))
    if dbg == 2:
        attiters = []
    if natt is not None:
        attiters = attiters[:natt]
    for (pair, qc) in attiters:
        if True:
            silu_acts = []
            av_pes = []
            for kt in range(NKT):
                par = kt % 2
                ps = psS[par]
                if kt >= 2:
                    p.ACT.wait_val(nc.tensor, silu_acts[kt - 2])
                else:
                    p.ACT.wait(nc.tensor)
                nc.tensor.matmul(ps[:, 0:512], k_sb[0:64, pair, kt * 128:(kt + 1) * 128],
                                 q_sb[0:64, pair, qc * 512:(qc + 1) * 512],
                                 start=True, stop=True, tile_position=(0, 0))
                ins = nc.tensor.matmul(ps[:, 512:1024],
                                       k_sb[64:128, pair, kt * 128:(kt + 1) * 128],
                                       q_sb[64:128, pair, qc * 512:(qc + 1) * 512],
                                       start=True, stop=True, tile_position=(64, 0))
                p.PE.tick(ins)
                p.PE.wait(nc.scalar)
                if kt >= 2:
                    pass
                ins = nc.scalar.activation(w_sb[par][:], ps[:, :], AF.Silu, scale=SCALE)
                p.ACT.tick(ins)
                silu_acts.append(p.ACT.v)
                p.ACT.wait_val(nc.tensor, silu_acts[kt])
                nc.tensor.matmul(psA[0:64, :], v_sb[:, kt, pair * 128:pair * 128 + 64],
                                 w_sb[par][:, 0:512], start=(kt == 0), stop=(kt == NKT - 1),
                                 tile_position=(0, 0))
                ins = nc.tensor.matmul(psAo[64:128, :],
                                       v_sb[:, kt, pair * 128 + 64:pair * 128 + 128],
                                       w_sb[par][:, 512:1024], start=(kt == 0),
                                       stop=(kt == NKT - 1), tile_position=(0, 64))
                p.PE.tick(ins)
                av_pes.append(p.PE.v)

            p.PE.wait(nc.scalar)
            nc.scalar.copy(att_fm[0:64, :], psA[0:64, :])
            ins = nc.scalar.copy(att_fm[64:128, :], psAo[64:128, :])
            p.ACT.tick(ins)
            if epi == 1:
                continue
            p.ACT.wait(nc.tensor)
            for sub in range(4):
                for hh in range(2):
                    dstp = psT if hh == 0 else psT2
                    ins = nc.tensor.transpose(
                        dstp[:, sub * 64:(sub + 1) * 64],
                        att_fm[hh * 64:(hh + 1) * 64, sub * 128:(sub + 1) * 128],
                        consts_sb["eye64s_bf"][hh * 64:(hh + 1) * 64, :])
            p.PE.tick(ins)
            if epi == 2:
                continue
            p.PE.wait(nc.scalar)
            nc.scalar.copy(
                att_tok[:].rearrange("p (s h d) -> p s h d", s=4, h=2)[:, :, 0, :],
                psT[:, 0:256].rearrange("p (s d) -> p s d", s=4))
            ins = nc.scalar.copy(
                att_tok[:].rearrange("p (s h d) -> p s h d", s=4, h=2)[:, :, 1, :],
                psT2[:, 0:256].rearrange("p (s d) -> p s d", s=4))
            p.ACT.tick(ins)
            if epi == 3:
                continue
            p.ACT.wait(nc.scalar)
            ins = nc.scalar.activation(sq_tok[:], att_tok[:], AF.Square)
            p.ACT.tick(ins)
            p.ACT.wait(nc.vector)
            nc.vector.reduce_sum(ssum[:], att_tok[:].rearrange("p (s e) -> p s e", e=64),
                                 axis=mybir.AxisListType.X)
            ins = nc.vector.reduce_sum(ssq[:], sq_tok[:].rearrange("p (s e) -> p s e", e=64),
                                       axis=mybir.AxisListType.X)
            p.DVE.tick(ins)
            if epi == 4:
                continue
            p.DVE.wait(nc.vector)
            nc.vector.tensor_scalar(mu[:], ssum[:], 1.0 / HD, None, OP.mult)
            ins = nc.vector.tensor_scalar(var[:], ssq[:], 1.0 / HD, None, OP.mult)
            p.DVE.tick(ins)
            p.DVE.wait(nc.vector)
            ins = nc.vector.tensor_tensor(mu2[:], mu[:], mu[:], OP.mult)
            p.DVE.tick(ins)
            p.DVE.wait(nc.vector)
            ins = nc.vector.tensor_tensor(var[:], var[:], mu2[:], OP.subtract)
            p.DVE.tick(ins)
            p.DVE.wait(nc.vector)
            ins = nc.vector.tensor_scalar_add(var[:], var[:], EPS)
            p.DVE.tick(ins)
            p.DVE.wait(nc.scalar)
            ins = nc.scalar.activation(var[:], var[:], AF.Sqrt)
            p.ACT.tick(ins)
            p.ACT.wait(nc.vector)
            ins = nc.vector.reciprocal(rstd[:], var[:])
            p.DVE.tick(ins)
            p.DVE.wait(nc.vector)
            for c8 in range(8):
                ins = nc.vector.tensor_scalar(nrm[:, c8 * 64:(c8 + 1) * 64],
                                              att_tok[:, c8 * 64:(c8 + 1) * 64],
                                              mu[:, c8:c8 + 1], rstd[:, c8:c8 + 1],
                                              OP.subtract, OP.mult)
            p.DVE.tick(ins)
            p.DVE.wait(nc.vector)
            ins = nc.vector.tensor_tensor(nrm[:], nrm[:], gam_sb[:], OP.mult)
            p.DVE.tick(ins)
            p.DVE.wait(nc.vector)
            ins = nc.vector.tensor_tensor(nrm[:], nrm[:], bet_sb[:], OP.add)
            p.DVE.tick(ins)
            p.DVE.wait(nc.vector)
            p.ST.wait(nc.vector)
            ins = nc.vector.tensor_tensor(
                gated[:], nrm[:].rearrange("p (s c) -> p s c", c=128),
                gate_sb[:, qc * 4:qc * 4 + 4, pair * 128:(pair + 1) * 128], OP.mult)
            p.DVE.tick(ins)
            p.DVE.wait(nc.sync)
            ins = nc.sync.dma_start(
                out=gatedb[qc * 512:(qc + 1) * 512, pair * 128:(pair + 1) * 128]
                .rearrange("(s p) c -> p s c", p=128),
                in_=gated[:])
            p.ST.dma(ins)
    if dbg == 6:
        p.DVE.wait(nc.sync)
        p.ACT.wait(nc.sync)
        for (d2, s2) in ((dbg_q, q_sb[:]), (dbg_g, gate_sb[:]), (dbg_af, att_fm[:]),
                         (dbg_at, att_tok[:]), (dbg_nr, nrm[:]),
                         (dbg_gg, gated[:])):
            ins = nc.sync.dma_start(out=d2, in_=s2)
            p.ST.dma(ins)
    if dbg:
        one = p.sbuf("one_sb", [1, 16], F32)
        ins = nc.vector.memset(one[:], 7.0)
        p.DVE.tick(ins)
        p.DVE.wait(nc.sync)
        ins = nc.sync.dma_start(out=dbg_done, in_=one[:])
        p.ST.dma(ins)
    p.ST.wait(nc.gpsimd)
    p.close()
    return p.nc




def build_E():
    p = Prog()
    nc = p.nc
    x1s = nc.dram_tensor("x1s", [TLOC, D], F32, kind="ExternalInput").ap()
    gfull = nc.dram_tensor("gfull", [TLOC, D], BF16, kind="ExternalInput").ap()
    wo = nc.dram_tensor("wo", [D, D], F32, kind="ExternalInput").ap()
    wr = nc.dram_tensor("wr", [D, E], F32, kind="ExternalInput").ap()
    ident = nc.dram_tensor("ident", [128, 128], F32, kind="ExternalInput").ap()
    x2_o = nc.dram_tensor("x2s", [TLOC, D], F32, kind="ExternalOutput").ap()
    ln3_o = nc.dram_tensor("ln3b", [TLOC, D], BF16, kind="ExternalOutput").ap()
    dec_o = nc.dram_tensor("dec", [TLOC, 4], F32, kind="ExternalOutput").ap()

    wo_sb = p.sbuf("wo_sb", [128, 8, D], BF16)
    gfT = [p.sbuf("gfT0", [128, 8, 128], BF16), p.sbuf("gfT1", [128, 8, 128], BF16)]
    xt = p.sbuf("xt", [128, D], F32)
    stat = p.sbuf("stat", [128, 12], F32)
    mv = p.sbuf("mv", [128, 2], F32)
    sd = p.sbuf("sd", [128, 1], F32)
    rstd = p.sbuf("rstd", [128, 1], F32)
    lnf = p.sbuf("lnf", [128, D], F32)
    lnb = p.sbuf("lnbs", [128, D], BF16)
    lnT = p.sbuf("lnT", [128, D], F32)
    identsb = p.sbuf("identsb", [128, 128], F32)
    wr_sb = p.sbuf("wr_sb", [128, 8, E], F32)
    logit = p.sbuf("logit", [128, E], F32)
    scr = {
        "mx1": p.sbuf("mx1", [128, 1], F32),
        "sm": p.sbuf("sm", [128, 1], F32),
        "mx8": p.sbuf("mx8", [128, 8], F32),
        "mi8": p.sbuf("mi8", [128, 8], U32),
        "mif": p.sbuf("mif", [128, 8], F32),
    }
    dec = p.sbuf("dec_sb", [128, 4], F32)
    psO = [p.psum("psO0", [128, 512], F32), p.psum("psO1", [128, 512], F32)]
    ptr = [p.psum("ptr0", [128, 512], F32), p.psum("ptr1", [128, 512], F32)]
    plog = p.psum("plog", [128, 512], F32)

    p.block()
    p.sems()

    ins = nc.sync.dma_start(out=identsb[:], in_=ident[:])
    p.LD.dma(ins)
    ins = nc.sync.dma_start(out=wr_sb[:], in_=wr.rearrange("(c p) e -> p c e", p=128))
    p.LD.dma(ins)
    ins = nc.gpsimd.dma_start(out=wo_sb[:], in_=wo.rearrange("(kc p) n -> p kc n", p=128))
    p.GP.dma(ins)
    p.GP.wait(nc.tensor)
    p.GP.wait(nc.sync)

    for j in range(NTILE_LOC):
        buf = gfT[j % 2]
        p.PE.wait(nc.sync)
        for c in range(8):
            ins = nc.sync.dma_start_transpose(
                buf[:, c, :], gfull[j * 128:(j + 1) * 128, c * 128:(c + 1) * 128])
            p.LD.dma(ins)
        if j > 0:
            p.DVE.wait(nc.sync)
            p.ST.wait(nc.vector)
        ins = nc.sync.dma_start(out=xt[:], in_=x1s[j * 128:(j + 1) * 128, :])
        p.LD.dma(ins)
        p.LD.wait(nc.tensor)
        p.DVE.wait(nc.tensor)
        for n2 in range(2):
            for kc in range(8):
                ins = nc.tensor.matmul(psO[n2][:, :], buf[:, kc, :],
                                       wo_sb[:, kc, n2 * 512:(n2 + 1) * 512],
                                       start=(kc == 0), stop=(kc == 7))
        p.PE.tick(ins)
        p.PE.wait(nc.vector)
        p.LD.wait(nc.vector)
        for n2 in range(2):
            ins = nc.vector.tensor_add(xt[:, n2 * 512:(n2 + 1) * 512],
                                       xt[:, n2 * 512:(n2 + 1) * 512], psO[n2][:, :])
        p.DVE.tick(ins)
        p.DVE.wait(nc.sync)
        ins = nc.sync.dma_start(out=x2_o[j * 128:(j + 1) * 128, :], in_=xt[:])
        p.ST.dma(ins)
        emit_ln_tile(p, xt, stat, mv, sd, rstd, lnf, lnb)
        p.DVE.wait(nc.sync)
        ins = nc.sync.dma_start(out=ln3_o[j * 128:(j + 1) * 128, :], in_=lnb[:])
        p.ST.dma(ins)
        p.DVE.wait(nc.tensor)
        emit_router_tile(p, lnf, identsb, wr_sb, lnT,
                         [ptr[0][:, 0:128], ptr[1][:, 0:128]], plog[:, 0:8],
                         logit, scr, dec)
        p.DVE.wait(nc.sync)
        ins = nc.sync.dma_start(out=dec_o[j * 128:(j + 1) * 128, :], in_=dec[:])
        p.ST.dma(ins)
        p.ST.wait(nc.sync)
    p.ST.wait(nc.gpsimd)
    p.close()
    return p.nc




MAGIC = 0x5A5A17

_EPOCH_STATE = [0]


def _launch_epoch():
    import time
    e = (time.time_ns() >> 8) & 0x3FFFFFFF
    if e <= _EPOCH_STATE[0]:
        e = _EPOCH_STATE[0] + 1
    _EPOCH_STATE[0] = e
    return e | 0x40000000


def build_FUSED(dbg=0):
    p = Prog()
    nc = p.nc
    HT = T // 2
    NTH = HT // 128

    x0h = nc.dram_tensor("x0h", [HT, D], F32, kind="ExternalInput").ap()
    wq8 = nc.dram_tensor("wq8", [D, 2048], F32, kind="ExternalInput").ap()
    wob_i = nc.dram_tensor("wob_i", [D, D], F32, kind="ExternalInput").ap()
    wr = nc.dram_tensor("wr", [D, E], F32, kind="ExternalInput").ap()
    ident = nc.dram_tensor("ident", [128, 128], F32, kind="ExternalInput").ap()
    gam = nc.dram_tensor("gam", [128, 512], F32, kind="ExternalInput").ap()
    bet = nc.dram_tensor("bet", [128, 512], F32, kind="ExternalInput").ap()
    wi4 = nc.dram_tensor("wi4", [4, D, FF], F32, kind="ExternalInput").ap()
    wo4 = nc.dram_tensor("wo4", [4, FF, D], F32, kind="ExternalInput").ap()
    meta = nc.dram_tensor("meta", [1, 8], I32, kind="ExternalInput").ap()
    ridx = nc.dram_tensor("ridx", [128, 180], I32, kind="ExternalInput").ap()
    x3h = nc.dram_tensor("x3h", [HT, D], F32, kind="ExternalOutput").ap()
    if dbg:
        dbg_dec = nc.dram_tensor("dbg_dec", [128, T // 256, 4], F32, kind="ExternalOutput").ap()
        dbg_x1 = nc.dram_tensor("dbg_x1", [HT, D], F32, kind="ExternalOutput").ap()
        dbg_ln = nc.dram_tensor("dbg_ln", [T, D], BF16, kind="ExternalOutput").ap()
        dbg_gt = nc.dram_tensor("dbg_gt", [4, 2 * T, 128], BF16, kind="ExternalOutput").ap()
        dbg_x2 = nc.dram_tensor("dbg_x2", [HT, D], F32, kind="ExternalOutput").ap()
        dbg_et = nc.dram_tensor("dbg_et", [E * CAP, D], BF16, kind="ExternalOutput").ap()
        dbg_dt = nc.dram_tensor("dbg_dt", [E * CAP, D], BF16, kind="ExternalOutput").ap()
        dbg_gi = nc.dram_tensor("dbg_gi", [128, 64], I32, kind="ExternalOutput").ap()
        dbg_gg = nc.dram_tensor("dbg_gg", [128, 64], F32, kind="ExternalOutput").ap()
        dbg_xe = nc.dram_tensor("dbg_xe", [128, 5120], BF16, kind="ExternalOutput").ap()
        dbg_h = nc.dram_tensor("dbg_h", [128, 20480], BF16, kind="ExternalOutput").ap()
        dbg_wib = nc.dram_tensor("dbg_wib", [128, 1024], BF16, kind="ExternalOutput").ap()

    sh_flags = nc.dram_tensor("sh_flags", [8, 64], I32, kind="Internal",
                              addr_space="Shared").ap()
    sh_dec = nc.dram_tensor("sh_dec", [2, 128, 32, 4], F32, kind="Internal",
                            addr_space="Shared").ap()
    sh_dtab = nc.dram_tensor("sh_dtab", [E * CAP, D], BF16, kind="Internal",
                             addr_space="Shared").ap()
    sh_etab = nc.dram_tensor("sh_etab", [E * CAP, D], BF16, kind="Internal",
                             addr_space="Shared").ap()
    sh_ln = nc.dram_tensor("sh_ln", [T, D], BF16, kind="Internal",
                           addr_space="Shared").ap()
    sh_gat = [nc.dram_tensor(f"sh_gat{w}", [2 * T, 128], BF16, kind="Internal",
                             addr_space="Shared").ap() for w in range(4)]

    lnh = nc.dram_tensor("lnh", [HT, D], BF16, kind="Internal").ap()
    x1h = nc.dram_tensor("x1h", [HT, D], F32, kind="Internal").ap()
    gscr = nc.dram_tensor("gscr", [128, 256], F32, kind="Internal").ap()

    scr = bookkeeping_scratch(p)
    xt = p.sbuf("xt", [128, D], F32)
    stat = p.sbuf("stat", [128, 12], F32)
    mv = p.sbuf("mv", [128, 2], F32)
    sd = p.sbuf("sd", [128, 1], F32)
    rstd1 = p.sbuf("rstd1", [128, 1], F32)
    lnf = p.sbuf("lnf", [128, D], F32)
    lnb = p.sbuf("lnbs", [128, D], BF16)
    lnT = p.sbuf("lnT", [128, D], F32)
    identsb = p.sbuf("identsb", [128, 128], F32)
    wr_sb = p.sbuf("wr_sb", [128, 8, E], F32)
    logit = p.sbuf("logit", [128, E], F32)
    rscr = {
        "mx1": p.sbuf("mx1", [128, 1], F32),
        "sm": p.sbuf("sm", [128, 1], F32),
        "mx8": p.sbuf("mx8", [128, 8], F32),
        "mi8": p.sbuf("mi8", [128, 8], U32),
        "mif": p.sbuf("mif", [128, 8], F32),
    }
    dec_acc = p.sbuf("dec_acc", [128, NTH, 4], F32)
    sidx_f = p.sbuf("sidx_f", [128, 128], F32)
    sidx_i = p.sbuf("sidx_i", [128, 128], I32)
    gidx_f = p.sbuf("gidx_f", [128, 128], F32)
    gate2 = p.sbuf("gate2", [128, 128], F32)
    gidx_my = p.sbuf("gidx_my", [128, 64], I32)
    gate_my = p.sbuf("gate_my", [128, 64], F32)
    yk = [p.sbuf("yk0", [128, D], BF16), p.sbuf("yk1", [128, D], BF16)]
    ytmp = p.sbuf("ytmp", [128, D], F32)
    gam_sb = p.sbuf("gam_sb", [128, 512], F32)
    bet_sb = p.sbuf("bet_sb", [128, 512], F32)
    ARENA_N = 60416
    arena = p.sbuf("arena", [128, ARENA_N], BF16)
    xe = arena[:, 0:5120].rearrange("p (c n) -> p c n", c=8)
    h_sb = arena[:, 5120:25600].rearrange("p (c n) -> p c n", c=32)
    wo_sb = arena[:, 25600:58368].rearrange("p (c n) -> p c n", c=32)
    wib = [arena[:, 58368:59392].rearrange("p (c n) -> p c n", c=8),
           arena[:, 59392:60416].rearrange("p (c n) -> p c n", c=8)]
    q_sb = arena[:, 0:8192]
    k_sb = arena[:, 8192:16384]
    v_sb = arena[:, 16384:24576].rearrange("p (c n) -> p c n", c=NKT)
    gate_sb = arena[:, 24576:32768].rearrange("p (c n) -> p c n", c=NKT)
    gfT = arena[:, 33024:34048].rearrange("p (c n) -> p c n", c=8)
    wo8_sb = arena[:, 0:8192].rearrange("p (c n) -> p c n", c=8)
    gb = [arena[:, 34048:34176], arena[:, 34176:34304]]
    wqw = p.sbuf("wqw", [128, 8, 512], BF16)
    lnF = [p.sbuf("lnF0", [128, 8, 512], BF16), p.sbuf("lnF1", [128, 8, 512], BF16)]
    w_sb = [p.sbuf("w_sb0", [128, 1024], BF16), p.sbuf("w_sb1", [128, 1024], BF16)]
    att_fm = p.sbuf("att_fm", [128, 512], BF16)
    att_tok = p.sbuf("att_tok", [128, 512], F32)
    sq_tok = p.sbuf("sq_tok", [128, 512], F32)
    ssum = p.sbuf("ssum", [128, 8], F32)
    ssq = p.sbuf("ssq", [128, 8], F32)
    mu = p.sbuf("mu", [128, 8], F32)
    var = p.sbuf("var", [128, 8], F32)
    rstd8 = p.sbuf("rstd8", [128, 8], F32)
    mu2 = p.sbuf("mu2", [128, 8], F32)
    nrm = p.sbuf("nrm", [128, 512], F32)
    gated = p.sbuf("gated", [128, 4, 128], BF16)
    eob = p.sbuf("eob", [128, D], BF16)
    meta_sb = p.sbuf("meta_sb", [1, 8], I32)
    ridx_sb = p.sbuf("ridx_sb", [128, 180], I32)

    PS0 = p.psum("PS0", [128, 1024], F32)
    PS1 = p.psum("PS1", [128, 1024], F32)
    PS2 = p.psum("PS2", [128, 512], F32)
    PS3 = p.psum("PS3", [128, 512], F32)
    PT0 = p.psum("PT0", [128, 1024], BF16)
    PT1 = p.psum("PT1", [128, 1024], BF16)

    p.block()
    p.sems()
    BAR = Sem(nc, "bar")
    nbar = [0]

    consts_sb = load_consts(p, ["cols8", "triu_bf", "striu_bf", "ones128x1_bf",
                                "ones1x128", "eye64s_bf", "ident128_bf"])
    for (dst, srcx) in ((identsb, ident), (gam_sb, gam), (bet_sb, bet),
                        (meta_sb, meta), (ridx_sb, ridx)):
        ins = nc.sync.dma_start(out=dst[:], in_=srcx[:])
        p.LD.dma(ins)
    ins = nc.sync.dma_start(out=wr_sb[:], in_=wr.rearrange("(c p) e -> p c e", p=128))
    p.LD.dma(ins)
    p.LD.wait(nc.sync)
    p.LD.wait(nc.gpsimd)

    r_tok_s = nc.sync.alloc_register("r_tok_s")
    nc.sync.load(r_tok_s, meta_sb[0:1, 0:1])
    rv_tok = RuntimeValue(r_tok_s)
    r_gcol_s = nc.sync.alloc_register("r_gcol_s")
    nc.sync.load(r_gcol_s, meta_sb[0:1, 1:2])
    rv_gcol = RuntimeValue(r_gcol_s)
    r_erow_s = nc.sync.alloc_register("r_erow_s")
    nc.sync.load(r_erow_s, meta_sb[0:1, 2:3])
    rv_erow = RuntimeValue(r_erow_s)
    r_bk_s = nc.sync.alloc_register("r_bk_s")
    nc.sync.load(r_bk_s, meta_sb[0:1, 3:4])
    rv_bk = RuntimeValue(r_bk_s)
    r_h_s = nc.sync.alloc_register("r_h_s")
    nc.sync.load(r_h_s, meta_sb[0:1, 6:7])
    rv_h = RuntimeValue(r_h_s)
    r_my_g = nc.gpsimd.alloc_register("r_my_g")
    nc.gpsimd.load(r_my_g, meta_sb[0:1, 4:5])
    rv_myf = RuntimeValue(r_my_g)
    r_pa_g = nc.gpsimd.alloc_register("r_pa_g")
    nc.gpsimd.load(r_pa_g, meta_sb[0:1, 5:6])
    rv_paf = RuntimeValue(r_pa_g)
    bc_dtab = nc.gpsimd.alloc_register("bc_dtab")
    nc.gpsimd.reg_mov(bc_dtab, E * CAP - 1)
    bc_ln = nc.gpsimd.alloc_register("bc_ln")
    nc.gpsimd.reg_mov(bc_ln, T - 1)
    bc_gat = nc.gpsimd.alloc_register("bc_gat")
    nc.gpsimd.reg_mov(bc_gat, 2 * T - 1)

    spin_order = [nc.scalar, nc.tensor, nc.sync, nc.scalar, nc.tensor,
                  nc.vector, nc.scalar, nc.tensor, nc.sync, nc.vector]
    spin_state = {}

    def get_spin(eng):
        key = str(eng.engine)
        if key not in spin_state:
            p.LD.wait(eng)
            r_myc = eng.alloc_register("bm" + key[-3:])
            eng.load(r_myc, meta_sb[0:1, 4:5])
            r_pac = eng.alloc_register("bp" + key[-3:])
            eng.load(r_pac, meta_sb[0:1, 5:6])
            f = eng.alloc_register("bf" + key[-3:])
            ap_myc = bass.AP(sh_flags.tensor, RuntimeValue(r_myc), [[64, 1], [1, 1]])
            ap_pac = bass.AP(sh_flags.tensor, RuntimeValue(r_pac), [[64, 1], [1, 1]])
            spin_state[key] = (ap_myc, ap_pac, f)
        return spin_state[key]

    ap_myc, ap_pac, f = get_spin(nc.vector)
    nc.vector.store(ap_myc, 0)
    r_ep = nc.vector.alloc_register("r_ep")
    nc.vector.load(r_ep, meta_sb[0:1, 7:8])
    r_myi = nc.vector.alloc_register("r_myi")
    nc.vector.load(r_myi, meta_sb[0:1, 4:5])
    nc.vector.reg_add(r_myi, r_myi, 32)
    r_pai = nc.vector.alloc_register("r_pai")
    nc.vector.load(r_pai, meta_sb[0:1, 5:6])
    nc.vector.reg_add(r_pai, r_pai, 32)
    ap_myi = bass.AP(sh_flags.tensor, RuntimeValue(r_myi), [[64, 1], [1, 1]])
    ap_pai = bass.AP(sh_flags.tensor, RuntimeValue(r_pai), [[64, 1], [1, 1]])
    nc.vector.store(ap_myi, RuntimeValue(r_ep))
    r_di = nc.vector.alloc_register("r_di")
    nc.vector.load(f, ap_pai)
    nc.vector.reg_alu(r_di, RuntimeValue(f), RuntimeValue(r_ep),
                      mybir.AluOpType.subtract)
    with nc.vector.While(RuntimeValue(r_di) != 0):
        nc.vector.load(f, ap_pai)
        nc.vector.reg_alu(r_di, RuntimeValue(f), RuntimeValue(r_ep),
                          mybir.AluOpType.subtract)
    nc.vector.sem_inc(BAR.h, 1)
    BAR.v += 1

    def barrier():
        nbar[0] += 1
        ph = nbar[0]
        eng = spin_order[ph - 1]
        ap_myc, ap_pac, f = get_spin(eng)
        BAR.wait(eng)
        p.ST.wait(eng)
        p.GP.wait(eng)
        eng.store(ap_myc, ph)
        eng.load(f, ap_pac)
        with eng.While(RuntimeValue(f) < ph):
            eng.load(f, ap_pac)
        eng.sem_inc(BAR.h, 1)
        BAR.v += 1

    def local_join():
        for eng in (nc.sync, nc.tensor, nc.scalar, nc.vector, nc.gpsimd):
            for s in (p.LD, p.ST, p.PE, p.ACT, p.DVE, p.GP):
                s.wait(eng)
            BAR.wait(eng)

    def ln_router_phase(xsrc_ap):
        for j in range(NTH):
            if j > 0:
                p.DVE.wait(nc.sync)
                p.PE.wait(nc.sync)
            ins = nc.sync.dma_start(out=xt[:], in_=xsrc_ap[j * 128:(j + 1) * 128, :])
            p.LD.dma(ins)
            p.LD.wait(nc.vector)
            if j > 0:
                p.PE.wait(nc.vector)
            emit_ln_tile(p, xt, stat, mv, sd, rstd1, lnf, lnb)
            p.DVE.wait(nc.sync)
            p.ST.wait(nc.sync)
            ins = nc.sync.dma_start(out=lnh[j * 128:(j + 1) * 128, :], in_=lnb[:])
            p.ST.dma(ins)
            p.DVE.wait(nc.tensor)
            emit_router_tile(p, lnf, identsb, wr_sb, lnT,
                             [PS2[:, 0:128], PS3[:, 0:128]], PS1[0:128, 0:8],
                             logit, rscr, dec_acc[:, j, :])
        p.DVE.wait(nc.sync)
        dec_dst = bass.AP(sh_dec.tensor, rv_h * (128 * 32 * 4),
                          [[128, 128], [4, 32], [1, 4]])
        ins = nc.sync.dma_start(out=dec_dst, in_=dec_acc[:])
        p.ST.dma(ins)
        p.ST.wait(nc.gpsimd)
        barrier()

    ln_router_phase(x0h)
    if dbg:
        p.DVE.wait(nc.sync)
        ins = nc.sync.dma_start(out=dbg_dec[:], in_=dec_acc[:])
        p.ST.dma(ins)

    def bookkeep_and_scatter():
        local_join()
        dec_loads = [(hh * 32, sh_dec[hh]) for hh in range(2)]
        emit_bookkeeping(p, dec_loads, scr, consts_sb, PS0, PS1, PS2)
        vflag, slots = scr["vflag"], scr["slots"]
        p.DVE.wait(nc.vector)
        nc.vector.tensor_scalar_add(sidx_f[:], slots[:], -BIGIDX)
        ins = nc.vector.tensor_tensor(sidx_f[:], sidx_f[:], vflag[:], OP.mult)
        p.DVE.tick(ins)
        p.DVE.wait(nc.vector)
        ins = nc.vector.tensor_scalar_add(sidx_f[:], sidx_f[:], BIGIDX)
        p.DVE.tick(ins)
        p.DVE.wait(nc.vector)
        nc.vector.tensor_scalar_add(gidx_f[:], slots[:], -BIGIDX)
        ins = nc.vector.tensor_tensor(gidx_f[:], gidx_f[:], vflag[:], OP.mult)
        p.DVE.tick(ins)
        p.DVE.wait(nc.vector)
        ins = nc.vector.tensor_scalar_add(gidx_f[:], gidx_f[:], BIGIDX)
        p.DVE.tick(ins)
        nc.vector.tensor_copy(gate2[:].rearrange("p (j k) -> p j k", k=2),
                              scr["decsb"][:, :, 2:4])
        ins = nc.vector.tensor_tensor(gate2[:], gate2[:], vflag[:], OP.mult)
        p.DVE.tick(ins)
        p.DVE.wait(nc.sync)
        ins = nc.sync.dma_start(out=gscr[:, 0:128], in_=gidx_f[:])
        p.ST.dma(ins)
        ins = nc.sync.dma_start(out=gscr[:, 128:256], in_=gate2[:])
        p.ST.dma(ins)
        p.ST.wait(nc.sync)
        gidx_ap = bass.AP(gscr.tensor, rv_bk, [[256, 128], [1, 64]])
        gate_ap = bass.AP(gscr.tensor, rv_bk + 128, [[256, 128], [1, 64]])
        ins = nc.sync.dma_start(out=gidx_f[:, 0:64], in_=gidx_ap)
        p.LD.dma(ins)
        ins = nc.sync.dma_start(out=gate_my[:], in_=gate_ap)
        p.LD.dma(ins)
        p.LD.wait(nc.vector)
        ins = nc.vector.tensor_copy(gidx_my[:], gidx_f[:, 0:64])
        p.DVE.tick(ins)
        p.DVE.wait(nc.sync)
        ins = nc.sync.dma_start(out=gscr[:, 0:128], in_=sidx_f[:])
        p.ST.dma(ins)
        p.ST.wait(nc.sync)
        sidx_ap = bass.AP(gscr.tensor, rv_bk, [[256, 128], [1, 64]])
        ins = nc.sync.dma_start(out=sidx_f[:, 0:64], in_=sidx_ap)
        p.LD.dma(ins)
        p.LD.wait(nc.vector)
        ins = nc.vector.tensor_copy(sidx_i[:, 0:64], sidx_f[:, 0:64])
        p.DVE.tick(ins)
        p.DVE.wait(nc.gpsimd)
        p.DVE.wait(nc.sync)
        for jj in range(NTH):
            buf = yk[jj % 2]
            if jj >= 2:
                p.GP.wait(nc.sync)
            ins = nc.sync.dma_start(out=buf[:], in_=lnh[jj * 128:(jj + 1) * 128, :])
            p.LD.dma(ins)
            p.LD.wait(nc.gpsimd)
            for k in range(2):
                col = jj * 2 + k
                ins = nc.gpsimd.indirect_dma_start(
                    out=sh_dtab[:, :],
                    out_offset=IndirectOffsetOnAxis(sidx_i[:, col:col + 1], axis=0),
                    in_=buf[:, :], in_offset=None,
                    bounds_check=bc_dtab, oob_is_err=False)
                p.GP.dma(ins)
        p.GP.wait(nc.gpsimd)
        barrier()

    bookkeep_and_scatter()
    if dbg:
        ins = nc.gpsimd.dma_start(out=dbg_dt[:], in_=sh_dtab[:])
        p.GP.dma(ins)
        p.DVE.wait(nc.gpsimd)
        ins = nc.gpsimd.dma_start(out=dbg_gi[:], in_=gidx_my[:])
        p.GP.dma(ins)
        ins = nc.gpsimd.dma_start(out=dbg_gg[:], in_=gate_my[:])
        p.GP.dma(ins)
        p.GP.wait(nc.gpsimd)

    def expert_phase():
        local_join()
        for ei in range(4):
            for t5 in range(5):
                p.PE.wait(nc.gpsimd)
                ins = nc.gpsimd.indirect_dma_start(
                    out=yk[0][:, :], out_offset=None, in_=sh_dtab[:, :],
                    in_offset=IndirectOffsetOnAxis(
                        ridx_sb[:, ei * 5 + t5:ei * 5 + t5 + 1], axis=0),
                    bounds_check=bc_dtab, oob_is_err=False)
                p.GP.dma(ins)
                p.GP.wait(nc.tensor)
                p.ACT.wait(nc.tensor)
                for c8 in range(8):
                    ins = nc.tensor.transpose(
                        PT0[:, 0:128], yk[0][:, c8 * 128:(c8 + 1) * 128],
                        consts_sb["ident128_bf"][:])
                    p.PE.tick(ins)
                    p.PE.wait(nc.scalar)
                    ins = nc.scalar.copy(xe[:, c8, t5 * 128:(t5 + 1) * 128],
                                         PT0[:, 0:128])
                    p.ACT.tick(ins)
                    p.ACT.wait(nc.tensor)
            p.ACT.wait(nc.sync)
            for fc in range(32):
                wbuf = wib[fc % 2]
                p.PE.wait(nc.gpsimd)
                ins = nc.gpsimd.dma_start(
                    out=wbuf[:], in_=wi4[ei, :, fc * 128:(fc + 1) * 128]
                    .rearrange("(kc p) m -> p kc m", p=128))
                p.GP.dma(ins)
                p.GP.wait(nc.tensor)
                ps = [PS0, PS1][fc % 2]
                p.ACT.wait(nc.tensor)
                for (n0, nw) in ((0, 512), (512, 128)):
                    for kc in range(8):
                        ins = nc.tensor.matmul(
                            ps[:, n0:n0 + nw], wbuf[:, kc, :],
                            xe[:, kc, n0:n0 + nw],
                            start=(kc == 0), stop=(kc == 7))
                p.PE.tick(ins)
                p.PE.wait(nc.scalar)
                ins = nc.scalar.activation(h_sb[:, fc, :], ps[:, 0:CAP], AF.Relu)
                p.ACT.tick(ins)
            p.PE.wait(nc.gpsimd)
            ins = nc.gpsimd.dma_start(
                out=wo_sb, in_=wo4[ei].rearrange("(kc p) n -> p kc n", p=128))
            p.GP.dma(ins)
            p.GP.wait(nc.tensor)
            p.ACT.wait(nc.tensor)
            for s5 in range(5):
                for n2 in range(2):
                    ps2 = [PS2, PS3][(s5 * 2 + n2) % 2]
                    p.ACT.wait(nc.tensor)
                    for kc2 in range(32):
                        ins = nc.tensor.matmul(
                            ps2[:, 0:512],
                            h_sb[:, kc2, s5 * 128:(s5 + 1) * 128],
                            wo_sb[:, kc2, n2 * 512:(n2 + 1) * 512],
                            start=(kc2 == 0), stop=(kc2 == 31))
                    p.PE.tick(ins)
                    p.PE.wait(nc.scalar)
                    p.GP.wait(nc.scalar)
                    ins = nc.scalar.copy(eob[:, n2 * 512:(n2 + 1) * 512], ps2[:, 0:512])
                    p.ACT.tick(ins)
                p.ACT.wait(nc.gpsimd)
                ins = nc.gpsimd.indirect_dma_start(
                    out=sh_etab[:, :],
                    out_offset=IndirectOffsetOnAxis(
                        ridx_sb[:, ei * 5 + s5:ei * 5 + s5 + 1], axis=0),
                    in_=eob[:, :], in_offset=None,
                    bounds_check=bc_dtab, oob_is_err=False)
                p.GP.dma(ins)
        barrier()

    expert_phase()
    if dbg:
        ins = nc.gpsimd.dma_start(out=dbg_et[:], in_=sh_etab[:])
        p.GP.dma(ins)
        ins = nc.gpsimd.dma_start(out=dbg_xe[:], in_=arena[:, 0:5120])
        p.GP.dma(ins)
        ins = nc.gpsimd.dma_start(out=dbg_h[:], in_=arena[:, 5120:25600])
        p.GP.dma(ins)
        ins = nc.gpsimd.dma_start(out=dbg_wib[:], in_=arena[:, 59392:60416])
        p.GP.dma(ins)
        p.GP.wait(nc.gpsimd)

    def combine_phase(xsrc_ap, final):
        local_join()
        bcz = bc_dtab
        for jj in range(NTH):
            if jj > 0:
                p.DVE.wait(nc.sync)
                p.ST.wait(nc.vector)
                p.GP.wait(nc.vector)
            ins = nc.sync.dma_start(out=xt[:], in_=xsrc_ap[jj * 128:(jj + 1) * 128, :])
            p.LD.dma(ins)
            for k in range(2):
                ins = nc.vector.memset(yk[k][:], 0.0)
                p.DVE.tick(ins)
            p.DVE.wait(nc.gpsimd)
            for k in range(2):
                ins = nc.gpsimd.indirect_dma_start(
                    out=yk[k][:, :], out_offset=None, in_=sh_etab[:, :],
                    in_offset=IndirectOffsetOnAxis(
                        gidx_my[:, 2 * jj + k:2 * jj + k + 1], axis=0),
                    bounds_check=bcz, oob_is_err=False)
                p.GP.dma(ins)
            p.GP.wait(nc.vector)
            p.LD.wait(nc.vector)
            ins = nc.vector.tensor_scalar(ytmp[:], yk[0][:],
                                          gate_my[:, 2 * jj:2 * jj + 1],
                                          None, OP.mult)
            p.DVE.tick(ins)
            p.DVE.wait(nc.vector)
            ins = nc.vector.scalar_tensor_tensor(ytmp[:], yk[1][:],
                                                 gate_my[:, 2 * jj + 1:2 * jj + 2],
                                                 ytmp[:], OP.mult, OP.add)
            p.DVE.tick(ins)
            p.DVE.wait(nc.vector)
            ins = nc.vector.scalar_tensor_tensor(xt[:], ytmp[:], 0.5, xt[:],
                                                 OP.mult, OP.add)
            p.DVE.tick(ins)
            p.DVE.wait(nc.sync)
            if final:
                ins = nc.sync.dma_start(out=x3h[jj * 128:(jj + 1) * 128, :], in_=xt[:])
                p.ST.dma(ins)
            else:
                ins = nc.sync.dma_start(out=x1h[jj * 128:(jj + 1) * 128, :], in_=xt[:])
                p.ST.dma(ins)
                if jj > 0:
                    p.PE.wait(nc.vector)
                emit_ln_tile(p, xt, stat, mv, sd, rstd1, lnf, lnb)
                p.DVE.wait(nc.gpsimd)
                ins = nc.gpsimd.indirect_dma_start(
                    out=sh_ln[:, :],
                    out_offset=IndirectOffsetOnAxis(
                        ridx_sb[:, 20 + jj:21 + jj], axis=0),
                    in_=lnb[:, :], in_offset=None,
                    bounds_check=bc_ln, oob_is_err=False)
                p.GP.dma(ins)
        p.ST.wait(nc.gpsimd)
        if not final:
            barrier()

    combine_phase(x0h, False)
    if dbg:
        ins = nc.gpsimd.dma_start(out=dbg_x1[:], in_=x1h[:])
        p.GP.dma(ins)
        ins = nc.gpsimd.dma_start(out=dbg_ln[:], in_=sh_ln[:])
        p.GP.dma(ins)
        p.GP.wait(nc.gpsimd)

    local_join()
    SCALE = 1.0 / 8.0
    for wave in range(4):
        local_join()
        ins = nc.gpsimd.dma_start(
            out=wqw[:], in_=wq8[:, wave * 512:(wave + 1) * 512]
            .rearrange("(c p) m -> p c m", p=128))
        p.GP.dma(ins)
        p.GP.wait(nc.sync)
        p.GP.wait(nc.tensor)
        for tc in range(NQC):
            buf = lnF[tc % 2]
            p.PE.wait(nc.sync)
            for c in range(8):
                ins = nc.sync.dma_start_transpose(
                    buf[:, c, :],
                    sh_ln[tc * 512:(tc + 1) * 512, c * 128:(c + 1) * 128])
                p.LD.dma(ins)
            p.LD.wait(nc.tensor)
            p.ACT.wait(nc.tensor)
            for mi in range(2):
                ps = [PS0, PS1][mi]
                for c in range(8):
                    ins = nc.tensor.matmul(ps[:, 0:512],
                                           wqw[:, c, mi * 128:(mi + 1) * 128],
                                           buf[:, c, :], start=(c == 0), stop=(c == 7))
            p.PE.tick(ins)
            p.PE.wait(nc.scalar)
            nc.scalar.activation(q_sb[:, tc * 512:(tc + 1) * 512], PS0[:, 0:512], AF.Silu)
            ins = nc.scalar.activation(k_sb[:, tc * 512:(tc + 1) * 512], PS1[:, 0:512], AF.Silu)
            p.ACT.tick(ins)
            for sub in range(4):
                tkt = tc * 4 + sub
                p.ACT.wait(nc.tensor)
                for c in range(8):
                    nc.tensor.matmul(PS2[:, 0:128],
                                     buf[:, c, sub * 128:(sub + 1) * 128],
                                     wqw[:, c, 256:384], start=(c == 0), stop=(c == 7))
                for c in range(8):
                    ins = nc.tensor.matmul(PS3[:, 0:128],
                                           buf[:, c, sub * 128:(sub + 1) * 128],
                                           wqw[:, c, 384:512],
                                           start=(c == 0), stop=(c == 7))
                p.PE.tick(ins)
                p.PE.wait(nc.scalar)
                nc.scalar.activation(gate_sb[:, tkt, :], PS2[:, 0:128], AF.Silu)
                ins = nc.scalar.activation(v_sb[:, tkt, :], PS3[:, 0:128], AF.Silu)
                p.ACT.tick(ins)
        for qc in range(NQC):
            silu_vals = []
            qk_vals = []
            act_base = p.ACT.v

            def emit_qk(kt):
                par = kt % 2
                ps = [PS0, PS1][par]
                p.ACT.wait_val(nc.tensor,
                               silu_vals[kt - 2] if kt >= 2 else act_base)
                nc.tensor.matmul(ps[:, 0:512],
                                 k_sb[0:64, kt * 128:(kt + 1) * 128],
                                 q_sb[0:64, qc * 512:(qc + 1) * 512],
                                 start=True, stop=True, tile_position=(0, 0))
                ins = nc.tensor.matmul(ps[:, 512:1024],
                                       k_sb[64:128, kt * 128:(kt + 1) * 128],
                                       q_sb[64:128, qc * 512:(qc + 1) * 512],
                                       start=True, stop=True,
                                       tile_position=(64, 0))
                p.PE.tick(ins)
                qk_vals.append(p.PE.v)

            emit_qk(0)
            for kt in range(NKT):
                par = kt % 2
                p.PE.wait_val(nc.scalar, qk_vals[kt])
                ins = nc.scalar.activation(w_sb[par][:], [PS0, PS1][par][:, :],
                                           AF.Silu, scale=SCALE)
                p.ACT.tick(ins)
                silu_vals.append(p.ACT.v)
                if kt + 1 < NKT:
                    emit_qk(kt + 1)
                p.ACT.wait_val(nc.tensor, silu_vals[kt])
                nc.tensor.matmul(PS2[0:64, :],
                                 v_sb[:, kt, 0:64],
                                 w_sb[par][:, 0:512], start=(kt == 0),
                                 stop=(kt == NKT - 1), tile_position=(0, 0))
                ins = nc.tensor.matmul(PS3[64:128, :],
                                       v_sb[:, kt, 64:128],
                                       w_sb[par][:, 512:1024], start=(kt == 0),
                                       stop=(kt == NKT - 1), tile_position=(0, 64))
                p.PE.tick(ins)
            p.PE.wait(nc.scalar)
            nc.scalar.copy(att_fm[0:64, :], PS2[0:64, :])
            ins = nc.scalar.copy(att_fm[64:128, :], PS3[64:128, :])
            p.ACT.tick(ins)
            p.ACT.wait(nc.tensor)
            for sub in range(4):
                for hh in range(2):
                    dstp = PT0 if hh == 0 else PT1
                    ins = nc.tensor.transpose(
                        dstp[:, sub * 64:(sub + 1) * 64],
                        att_fm[hh * 64:(hh + 1) * 64, sub * 128:(sub + 1) * 128],
                        consts_sb["eye64s_bf"][hh * 64:(hh + 1) * 64, :])
            p.PE.tick(ins)
            p.PE.wait(nc.scalar)
            nc.scalar.copy(
                att_tok[:].rearrange("p (s h d) -> p s h d", s=4, h=2)[:, :, 0, :],
                PT0[:, 0:256].rearrange("p (s d) -> p s d", s=4))
            ins = nc.scalar.copy(
                att_tok[:].rearrange("p (s h d) -> p s h d", s=4, h=2)[:, :, 1, :],
                PT1[:, 0:256].rearrange("p (s d) -> p s d", s=4))
            p.ACT.tick(ins)
            p.ACT.wait(nc.scalar)
            ins = nc.scalar.activation(sq_tok[:], att_tok[:], AF.Square)
            p.ACT.tick(ins)
            p.ACT.wait(nc.vector)
            nc.vector.reduce_sum(ssum[:],
                                 att_tok[:].rearrange("p (s e) -> p s e", e=64),
                                 axis=mybir.AxisListType.X)
            ins = nc.vector.reduce_sum(ssq[:],
                                       sq_tok[:].rearrange("p (s e) -> p s e", e=64),
                                       axis=mybir.AxisListType.X)
            p.DVE.tick(ins)
            p.DVE.wait(nc.vector)
            nc.vector.tensor_scalar(mu[:], ssum[:], 1.0 / HD, None, OP.mult)
            ins = nc.vector.tensor_scalar(var[:], ssq[:], 1.0 / HD, None, OP.mult)
            p.DVE.tick(ins)
            p.DVE.wait(nc.vector)
            ins = nc.vector.tensor_tensor(mu2[:], mu[:], mu[:], OP.mult)
            p.DVE.tick(ins)
            p.DVE.wait(nc.vector)
            ins = nc.vector.tensor_tensor(var[:], var[:], mu2[:], OP.subtract)
            p.DVE.tick(ins)
            p.DVE.wait(nc.vector)
            ins = nc.vector.tensor_scalar_add(var[:], var[:], EPS)
            p.DVE.tick(ins)
            p.DVE.wait(nc.scalar)
            ins = nc.scalar.activation(var[:], var[:], AF.Sqrt)
            p.ACT.tick(ins)
            p.ACT.wait(nc.vector)
            ins = nc.vector.reciprocal(rstd8[:], var[:])
            p.DVE.tick(ins)
            p.DVE.wait(nc.vector)
            for c8 in range(8):
                ins = nc.vector.tensor_scalar(nrm[:, c8 * 64:(c8 + 1) * 64],
                                              att_tok[:, c8 * 64:(c8 + 1) * 64],
                                              mu[:, c8:c8 + 1], rstd8[:, c8:c8 + 1],
                                              OP.subtract, OP.mult)
            p.DVE.tick(ins)
            p.DVE.wait(nc.vector)
            ins = nc.vector.tensor_tensor(nrm[:], nrm[:], gam_sb[:], OP.mult)
            p.DVE.tick(ins)
            p.DVE.wait(nc.vector)
            ins = nc.vector.tensor_tensor(nrm[:], nrm[:], bet_sb[:], OP.add)
            p.DVE.tick(ins)
            p.DVE.wait(nc.vector)
            p.GP.wait(nc.vector)
            ins = nc.vector.tensor_tensor(
                gated[:], nrm[:].rearrange("p (s c) -> p s c", c=128),
                gate_sb[:, qc * 4:qc * 4 + 4, :], OP.mult)
            p.DVE.tick(ins)
            p.DVE.wait(nc.gpsimd)
            for sub in range(4):
                ins = nc.gpsimd.indirect_dma_start(
                    out=sh_gat[wave][:, :],
                    out_offset=IndirectOffsetOnAxis(
                        ridx_sb[:, 52 + qc * 4 + sub:53 + qc * 4 + sub], axis=0),
                    in_=gated[:, sub, :], in_offset=None,
                    bounds_check=bc_gat, oob_is_err=False)
                p.GP.dma(ins)
    barrier()
    if dbg:
        for w in range(4):
            ins = nc.gpsimd.dma_start(out=dbg_gt[w], in_=sh_gat[w][:])
            p.GP.dma(ins)
        p.GP.wait(nc.gpsimd)

    local_join()
    ins = nc.gpsimd.dma_start(out=wo8_sb,
                              in_=wob_i.rearrange("(kc p) n -> p kc n", p=128))
    p.GP.dma(ins)
    p.GP.wait(nc.tensor)
    p.GP.wait(nc.sync)
    for jj in range(NTH):
        for c in range(8):
            hp, cc = c // 4, c % 4
            p.PE.wait(nc.gpsimd)
            ins = nc.gpsimd.indirect_dma_start(
                out=gb[c % 2][:, :], out_offset=None, in_=sh_gat[cc][:, :],
                in_offset=IndirectOffsetOnAxis(
                    ridx_sb[:, 116 + hp * 32 + jj:117 + hp * 32 + jj], axis=0),
                bounds_check=bc_gat, oob_is_err=False)
            p.GP.dma(ins)
            p.GP.wait(nc.tensor)
            p.ACT.wait(nc.tensor)
            ins = nc.tensor.transpose(
                [PT0, PT1][c % 2][:, 0:128], gb[c % 2][:, :],
                consts_sb["ident128_bf"][:])
            p.PE.tick(ins)
            p.PE.wait(nc.scalar)
            ins = nc.scalar.copy(gfT[:, c, :], [PT0, PT1][c % 2][:, 0:128])
            p.ACT.tick(ins)
        if jj > 0:
            p.DVE.wait(nc.sync)
            p.ST.wait(nc.vector)
        ins = nc.sync.dma_start(out=xt[:], in_=x1h[jj * 128:(jj + 1) * 128, :])
        p.LD.dma(ins)
        p.LD.wait(nc.tensor)
        p.DVE.wait(nc.tensor)
        p.ACT.wait(nc.tensor)
        for n2 in range(2):
            for kc in range(8):
                ins = nc.tensor.matmul([PS2, PS3][n2][:, :], gfT[:, kc, :],
                                       wo8_sb[:, kc, n2 * 512:(n2 + 1) * 512],
                                       start=(kc == 0), stop=(kc == 7))
        p.PE.tick(ins)
        p.PE.wait(nc.vector)
        p.LD.wait(nc.vector)
        for n2 in range(2):
            ins = nc.vector.tensor_add(xt[:, n2 * 512:(n2 + 1) * 512],
                                       xt[:, n2 * 512:(n2 + 1) * 512],
                                       [PS2, PS3][n2][:, :])
        p.DVE.tick(ins)
        p.DVE.wait(nc.sync)
        ins = nc.sync.dma_start(out=x1h[jj * 128:(jj + 1) * 128, :], in_=xt[:])
        p.ST.dma(ins)
        if jj > 0:
            p.PE.wait(nc.vector)
        emit_ln_tile(p, xt, stat, mv, sd, rstd1, lnf, lnb)
        p.DVE.wait(nc.sync)
        p.ST.wait(nc.sync)
        ins = nc.sync.dma_start(out=lnh[jj * 128:(jj + 1) * 128, :], in_=lnb[:])
        p.ST.dma(ins)
        p.DVE.wait(nc.tensor)
        emit_router_tile(p, lnf, identsb, wr_sb, lnT,
                         [PS2[:, 0:128], PS3[:, 0:128]], PS1[0:128, 0:8],
                         logit, rscr, dec_acc[:, jj, :])
    p.DVE.wait(nc.sync)
    dec_dst = bass.AP(sh_dec.tensor, rv_h * (128 * 32 * 4),
                      [[128, 128], [4, 32], [1, 4]])
    ins = nc.sync.dma_start(out=dec_dst, in_=dec_acc[:])
    p.ST.dma(ins)
    p.ST.wait(nc.gpsimd)
    barrier()

    if dbg:
        ins = nc.gpsimd.dma_start(out=dbg_x2[:], in_=x1h[:])
        p.GP.dma(ins)
        p.GP.wait(nc.gpsimd)
    bookkeep_and_scatter()
    expert_phase()
    combine_phase(x1h, True)

    p.ST.wait(nc.gpsimd)
    p.close()
    return p.nc


def kernel_fused(L0_attn_hidden_states, L0_attn_w_qkuv, L0_attn_w_out,
                 L0_attn_ln_gamma, L0_attn_ln_beta, L0_ffn_w_router,
                 L0_ffn_wi, L0_ffn_wo):
    x0 = np.ascontiguousarray(np.asarray(L0_attn_hidden_states, dtype=np.float32)
                              ).reshape(GG, T, D)
    wq = np.ascontiguousarray(np.asarray(L0_attn_w_qkuv, dtype=np.float32))
    w_out = np.ascontiguousarray(np.asarray(L0_attn_w_out, dtype=np.float32))
    gamma = np.asarray(L0_attn_ln_gamma, dtype=np.float32)
    beta = np.asarray(L0_attn_ln_beta, dtype=np.float32)
    wr = np.ascontiguousarray(np.asarray(L0_ffn_w_router, dtype=np.float32))
    wi = np.ascontiguousarray(np.asarray(L0_ffn_wi, dtype=np.float32))
    wo = np.ascontiguousarray(np.asarray(L0_ffn_wo, dtype=np.float32))

    consts = host_consts()
    cns = {"c_" + n: consts[n] for n in
           ("cols8", "triu_bf", "striu_bf", "ones128x1_bf", "ones1x128",
            "eye64s_bf", "ident128_bf")}
    epoch = _launch_epoch()
    in_maps = []
    for core in range(8):
        g = (core // 2) % 2
        h = core % 2
        cols = []
        for wave in range(4):
            h0 = 8 * h + 2 * wave
            cols += list(range(h0 * 64, (h0 + 2) * 64))
            cols += list(range(1024 + h0 * 64, 1024 + (h0 + 2) * 64))
            cols += list(range(3072 + h0 * 64, 3072 + (h0 + 2) * 64))
            cols += list(range(2048 + h0 * 64, 2048 + (h0 + 2) * 64))
        meta = np.array([[4096 * h, 512 * h, 2560 * h, 64 * h,
                          (2 + h) * 64, (3 - h) * 64, h, epoch]],
                        dtype=np.int32)
        pp = np.arange(128, dtype=np.int32)
        rix = np.zeros((128, 180), np.int32)
        for t in range(20):
            rix[:, t] = 2560 * h + 128 * t + pp
        for jj in range(32):
            rix[:, 20 + jj] = 4096 * h + 128 * jj + pp
        for qc in range(16):
            for sub in range(4):
                rix[:, 52 + qc * 4 + sub] = 8192 * h + qc * 512 + sub * 128 + pp
        for hp in range(2):
            for jj in range(32):
                rix[:, 116 + hp * 32 + jj] = 8192 * hp + 4096 * h + 128 * jj + pp
        in_maps.append({
            **cns,
            "x0h": np.ascontiguousarray(x0[g, h * 4096:(h + 1) * 4096]),
            "wq8": np.ascontiguousarray(wq[:, cols]),
            "wob_i": w_out, "wr": wr, "ident": consts["ident"],
            "gam": np.ascontiguousarray(np.tile(gamma, (128, 8))),
            "bet": np.ascontiguousarray(np.tile(beta, (128, 8))),
            "wi4": wi[4 * h:4 * h + 4], "wo4": wo[4 * h:4 * h + 4],
            "meta": meta, "ridx": rix,
        })
    nc = _get("FUSED", build_FUSED)
    res = run_bass_kernel_spmd(nc, in_maps, core_ids=list(range(8)))
    out = np.empty((GG, T, D), dtype=np.float32)
    for core in range(4):
        g = core // 2
        h = core % 2
        out[g, h * 4096:(h + 1) * 4096] = np.asarray(res.results[core]["x3h"])
    return out



def build_A():
    p = Prog()
    nc = p.nc
    x0s = nc.dram_tensor("x0s", [TLOC, D], F32, kind="ExternalInput").ap()
    wr = nc.dram_tensor("wr", [D, E], F32, kind="ExternalInput").ap()
    ident = nc.dram_tensor("ident", [128, 128], F32, kind="ExternalInput").ap()
    lnb_o = nc.dram_tensor("lnb", [TLOC, D], BF16, kind="ExternalOutput").ap()
    dec_o = nc.dram_tensor("dec", [TLOC, 4], F32, kind="ExternalOutput").ap()

    xt = p.sbuf("xt", [128, D], F32)
    stat = p.sbuf("stat", [128, 12], F32)
    mv = p.sbuf("mv", [128, 2], F32)
    sd = p.sbuf("sd", [128, 1], F32)
    rstd = p.sbuf("rstd", [128, 1], F32)
    lnf = p.sbuf("lnf", [128, D], F32)
    lnb = p.sbuf("lnbs", [128, D], BF16)
    lnT = p.sbuf("lnT", [128, D], F32)
    identsb = p.sbuf("identsb", [128, 128], F32)
    wr_sb = p.sbuf("wr_sb", [128, 8, E], F32)
    logit = p.sbuf("logit", [128, E], F32)
    scr = {
        "mx1": p.sbuf("mx1", [128, 1], F32),
        "sm": p.sbuf("sm", [128, 1], F32),
        "mx8": p.sbuf("mx8", [128, 8], F32),
        "mi8": p.sbuf("mi8", [128, 8], U32),
        "mif": p.sbuf("mif", [128, 8], F32),
    }
    dec = p.sbuf("dec_sb", [128, 4], F32)
    ptr = [p.psum("ptr0", [128, 128], F32), p.psum("ptr1", [128, 128], F32)]
    plog = p.psum("plog", [128, E], F32)

    p.block()
    p.sems()

    ins = nc.sync.dma_start(out=identsb[:], in_=ident[:])
    p.LD.dma(ins)
    ins = nc.sync.dma_start(out=wr_sb[:], in_=wr.rearrange("(c p) e -> p c e", p=128))
    p.LD.dma(ins)

    for i in range(NTILE_LOC):
        if i > 0:
            p.DVE.wait(nc.sync)
            p.ST.wait(nc.vector)
        ins = nc.sync.dma_start(out=xt[:], in_=x0s[i * 128:(i + 1) * 128, :])
        p.LD.dma(ins)
        p.LD.wait(nc.vector)
        if i > 0:
            p.PE.wait(nc.vector)
        emit_ln_tile(p, xt, stat, mv, sd, rstd, lnf, lnb)
        p.DVE.wait(nc.sync)
        ins = nc.sync.dma_start(out=lnb_o[i * 128:(i + 1) * 128, :], in_=lnb[:])
        p.ST.dma(ins)
        p.DVE.wait(nc.tensor)
        emit_router_tile(p, lnf, identsb, wr_sb, lnT, ptr, plog, logit, scr, dec)
        p.DVE.wait(nc.sync)
        ins = nc.sync.dma_start(out=dec_o[i * 128:(i + 1) * 128, :], in_=dec[:])
        p.ST.dma(ins)
        p.ST.wait(nc.sync)
    p.ST.wait(nc.gpsimd)
    p.close()
    return p.nc



_CACHE = {}


def _get(name, builder):
    if name not in _CACHE:
        _CACHE[name] = builder()
    return _CACHE[name]


def kernel(**inputs):
    return kernel_fused(**inputs)


def kernel_multilaunch(L0_attn_hidden_states, L0_attn_w_qkuv, L0_attn_w_out,
           L0_attn_ln_gamma, L0_attn_ln_beta, L0_ffn_w_router,
           L0_ffn_wi, L0_ffn_wo):
    x0 = np.ascontiguousarray(np.asarray(L0_attn_hidden_states, dtype=np.float32)
                              ).reshape(GG, T, D)
    wq = np.ascontiguousarray(np.asarray(L0_attn_w_qkuv, dtype=np.float32))
    w_out = np.ascontiguousarray(np.asarray(L0_attn_w_out, dtype=np.float32))
    gamma = np.asarray(L0_attn_ln_gamma, dtype=np.float32)
    beta = np.asarray(L0_attn_ln_beta, dtype=np.float32)
    wr = np.ascontiguousarray(np.asarray(L0_ffn_w_router, dtype=np.float32))
    wi = np.ascontiguousarray(np.asarray(L0_ffn_wi, dtype=np.float32))
    wo = np.ascontiguousarray(np.asarray(L0_ffn_wo, dtype=np.float32))

    consts = host_consts()
    cns = {"c_" + n: consts[n] for n in
           ("cols8", "triu_bf", "striu_bf", "ones128x1_bf", "ones1x128")}
    cores = list(range(8))
    gq = [divmod(c, QP) for c in cores]

    def decw_of(dec):
        return [np.ascontiguousarray(
            dec[g].reshape(NTILE, 128, 4).transpose(1, 0, 2)) for g in range(GG)]

    def gather_full(res, key, axis=0):
        return [np.concatenate([np.asarray(res.results[g * QP + q][key])
                                for q in range(QP)], axis=axis) for g in range(GG)]

    def per_core(res, key):
        return [np.asarray(r[key]) for r in res.results]

    resA = run_bass_kernel_spmd(_get("A", build_A), [
        {"x0s": np.ascontiguousarray(x0[g, q * TLOC:(q + 1) * TLOC]),
         "wr": wr, "ident": consts["ident"]}
        for (g, q) in gq], core_ids=cores)
    lnb1 = gather_full(resA, "lnb")
    dec1 = gather_full(resA, "dec")
    decw1 = decw_of(dec1)

    def expert_maps(lnb, decw):
        return [
            {**cns, "lnb": np.ascontiguousarray(lnb[g]),
             "decw": decw[g],
             "wi2": wi[2 * q:2 * q + 2], "wo2": wo[2 * q:2 * q + 2],
             "eq0": np.tile(np.array([2.0 * q, 2.0 * q + 1], dtype=np.float32),
                            (128, 1))}
            for (g, q) in gq]
    resB = run_bass_kernel_spmd(_get("B", build_B), expert_maps(lnb1, decw1),
                                core_ids=cores)
    etab1 = gather_full(resB, "etab")

    resC = run_bass_kernel_spmd(_get("C", build_C), [
        {**cns, "xs": np.ascontiguousarray(x0[g, q * TLOC:(q + 1) * TLOC]),
         "decw": decw1[g], "etab": etab1[g],
         "myq": np.array([[32 * q]], dtype=np.int32)}
        for (g, q) in gq], core_ids=cores)
    x1s = per_core(resC, "x1s")
    ln2 = gather_full(resC, "ln2b")

    din = []
    for (g, q) in gq:
        cols = []
        for pair in range(2):
            h0 = 4 * q + 2 * pair
            cols += list(range(h0 * 64, (h0 + 2) * 64))
        for pair in range(2):
            h0 = 4 * q + 2 * pair
            cols += list(range(1024 + h0 * 64, 1024 + (h0 + 2) * 64))
        h0 = 4 * q
        cols += list(range(3072 + h0 * 64, 3072 + (h0 + 4) * 64))
        cols += list(range(2048 + h0 * 64, 2048 + (h0 + 4) * 64))
        din.append({
            "c_eye64s_bf": consts["eye64s_bf"],
            "ln2b": np.ascontiguousarray(ln2[g]),
            "wq4": np.ascontiguousarray(wq[:, cols]),
            "gam": np.ascontiguousarray(np.tile(gamma, (128, 8))),
            "bet": np.ascontiguousarray(np.tile(beta, (128, 8)))})
    resD = run_bass_kernel_spmd(_get("D", build_D), din, core_ids=cores)
    gated = gather_full(resD, "gatedb", axis=1)

    resE = run_bass_kernel_spmd(_get("E", build_E), [
        {"x1s": x1s[c], "gfull": np.ascontiguousarray(
            gated[g][q * TLOC:(q + 1) * TLOC]),
         "wo": w_out, "wr": wr, "ident": consts["ident"]}
        for c, (g, q) in enumerate(gq)], core_ids=cores)
    x2s = per_core(resE, "x2s")
    ln3 = gather_full(resE, "ln3b")
    dec3 = gather_full(resE, "dec")
    decw3 = decw_of(dec3)

    resF = run_bass_kernel_spmd(_get("B", build_B), expert_maps(ln3, decw3),
                                core_ids=cores)
    etab3 = gather_full(resF, "etab")

    resG = run_bass_kernel_spmd(_get("G", lambda: build_C(final=True)), [
        {**cns, "xs": x2s[c], "decw": decw3[g], "etab": etab3[g],
         "myq": np.array([[32 * q]], dtype=np.int32)}
        for c, (g, q) in enumerate(gq)], core_ids=cores)
    x3 = gather_full(resG, "x3s")
    out = np.stack(x3).reshape(GG, T, D).astype(np.float32)
    return out


# version 07
# speedup vs baseline: 1.3542x; 1.0329x over previous
import numpy as np
import concourse.bass as bass
import concourse.mybir as mybir
from concourse.bass import IndirectOffsetOnAxis
from concourse.expressions_rust import RuntimeValue
from concourse.bass_utils import run_bass_kernel_spmd

F32 = mybir.dt.float32
BF16 = mybir.dt.bfloat16
I32 = mybir.dt.int32
U32 = mybir.dt.uint32
AF = mybir.ActivationFunctionType
OP = mybir.AluOpType

GG = 2
T = 8192
D = 1024
NH = 16
HD = 64
E = 8
CAP = 640
FF = 4096
QP = 4
TLOC = T // QP
NTILE_LOC = TLOC // 128
NTILE = T // 128
EPS = 1e-5
BIGIDX = 60000.0


class Sem:

    def __init__(self, nc, name):
        self._cm = nc.semaphore(name)
        self.h = self._cm.__enter__()
        self.v = 0

    def tick(self, inst, n=1):
        inst.then_inc(self.h, n)
        self.v += n

    def dma(self, inst):
        self.tick(inst, 16)

    def wait(self, eng):
        if self.v > 0:
            eng.wait_ge(self.h, self.v)

    def wait_val(self, eng, val):
        if val > 0:
            eng.wait_ge(self.h, val)


class Prog:

    def __init__(self):
        self.nc = bass.Bass("TRN2", target_bir_lowering=False)
        self._ctx = []

    def sems(self):
        nc = self.nc
        for name in ("LD", "ST", "PE", "ACT", "DVE", "GP", "X1", "X2"):
            setattr(self, name, Sem(nc, name.lower()))

    def sbuf(self, name, shape, dtype):
        cm = self.nc.sbuf_tensor(name, shape, dtype)
        t = cm.__enter__()
        self._ctx.append(cm)
        return t

    def psum(self, name, shape, dtype=F32):
        cm = self.nc.psum_tensor(name, shape, dtype)
        t = cm.__enter__()
        self._ctx.append(cm)
        return t

    def block(self):
        cm = self.nc.Block()
        b = cm.__enter__()
        self._ctx.append(cm)
        return b

    def close(self):
        for cm in reversed(self._ctx):
            cm.__exit__(None, None, None)



def host_consts():
    c = {}
    c["ident"] = np.eye(128, dtype=np.float32)
    c["triu"] = np.triu(np.ones((128, 128), dtype=np.float32)).astype(np.float32)
    c["cols8"] = np.tile(np.arange(8, dtype=np.float32), (128, 1))
    c["triu_bf"] = np.triu(np.ones((128, 128), dtype=np.float32))
    c["striu_bf"] = np.triu(np.ones((128, 128), dtype=np.float32), 1)
    c["ones128x1_bf"] = np.ones((128, 1), dtype=np.float32)
    c["ones1x128"] = np.ones((1, 128), dtype=np.float32)
    c["eye64s_bf"] = np.vstack([np.eye(64, dtype=np.float32)] * 2)
    c["ident128_bf"] = np.eye(128, dtype=np.float32)
    return c


def emit_ln_tile(p, xt, stat, mv, sd, rstd, lnf, lnb):
    nc = p.nc
    nc.vector.bn_stats(stat[:, 0:6], xt[:, 0:512])
    ins = nc.vector.bn_stats(stat[:, 6:12], xt[:, 512:1024])
    p.DVE.tick(ins)
    p.DVE.wait(nc.vector)
    ins = nc.vector.bn_aggr(mv[:], stat[:])
    p.DVE.tick(ins)
    p.DVE.wait(nc.vector)
    ins = nc.vector.tensor_scalar_add(sd[:], mv[:, 1:2], EPS)
    p.DVE.tick(ins)
    p.DVE.wait(nc.scalar)
    ins = nc.scalar.activation(sd[:], sd[:], AF.Sqrt)
    p.ACT.tick(ins)
    p.ACT.wait(nc.vector)
    ins = nc.vector.reciprocal(rstd[:], sd[:])
    p.DVE.tick(ins)
    p.DVE.wait(nc.vector)
    ins = nc.vector.tensor_scalar(lnf[:], xt[:], mv[:, 0:1], rstd[:, 0:1],
                                  OP.subtract, OP.mult)
    p.DVE.tick(ins)
    if lnb is not None:
        ins = nc.vector.tensor_copy(lnb[:], lnf[:])
        p.DVE.tick(ins)


def emit_router_tile(p, lnf, identsb, wr_sb, lnT, ptr, plog, logit, scr, dec):
    nc = p.nc
    for c in range(8):
        bank = ptr[c % 2]
        p.ACT.wait(nc.tensor)
        ins = nc.tensor.transpose(bank[:], lnf[:, c * 128:(c + 1) * 128], identsb[:])
        p.PE.tick(ins)
        p.PE.wait(nc.scalar)
        ins = nc.scalar.copy(lnT[:, c * 128:(c + 1) * 128], bank[:])
        p.ACT.tick(ins)
    p.ACT.wait(nc.tensor)
    for c in range(8):
        ins = nc.tensor.matmul(plog[:], lnT[:, c * 128:(c + 1) * 128],
                               wr_sb[:, c, :], start=(c == 0), stop=(c == 7))
    p.PE.tick(ins)
    p.PE.wait(nc.scalar)
    p.DVE.wait(nc.scalar)
    ins = nc.scalar.copy(logit[:], plog[:])
    p.ACT.tick(ins)
    p.ACT.wait(nc.vector)
    mx1 = scr["mx1"]
    ins = nc.vector.reduce_max(mx1[:], logit[:], axis=mybir.AxisListType.X)
    p.DVE.tick(ins)
    p.DVE.wait(nc.vector)
    ins = nc.vector.tensor_scalar(logit[:], logit[:], mx1[:, 0:1], None, OP.subtract)
    p.DVE.tick(ins)
    p.DVE.wait(nc.scalar)
    ins = nc.scalar.activation(logit[:], logit[:], AF.Exp)
    p.ACT.tick(ins)
    p.ACT.wait(nc.vector)
    sm = scr["sm"]
    ins = nc.vector.reduce_sum(sm[:], logit[:], axis=mybir.AxisListType.X)
    p.DVE.tick(ins)
    p.DVE.wait(nc.vector)
    ins = nc.vector.reciprocal(sm[:], sm[:])
    p.DVE.tick(ins)
    p.DVE.wait(nc.vector)
    ins = nc.vector.tensor_scalar(logit[:], logit[:], sm[:, 0:1], None, OP.mult)
    p.DVE.tick(ins)
    p.DVE.wait(nc.vector)
    mx8, mi8, mif = scr["mx8"], scr["mi8"], scr["mif"]
    ins = nc.vector.max(mx8[:], logit[:])
    p.DVE.tick(ins)
    p.DVE.wait(nc.vector)
    ins = nc.vector.max_index(mi8[:], mx8[:], logit[:])
    p.DVE.tick(ins)
    p.DVE.wait(nc.vector)
    ins = nc.vector.tensor_copy(dec[:, 0:2], mi8[:, 0:2].bitcast(I32))
    p.DVE.tick(ins)
    ins = nc.vector.tensor_copy(dec[:, 2:4], mx8[:, 0:2])
    p.DVE.tick(ins)




def emit_bookkeeping(p, decw_ap, scr, consts_sb, pA, pB, pC):
    nc = p.nc
    decsb = scr["decsb"]; masks = scr["masks"]; totrow = scr["totrow"]
    totv = scr["totv"]; totv_bf = scr["totv_bf"]; offs_sb = scr["offs_sb"]
    offsrow = scr["offsrow"]; prod = scr["prod"]; posl = scr["posl"]
    vflag = scr["vflag"]; esel = scr["esel"]; slots = scr["slots"]
    bk1 = scr["bk_scr1"]; bk2 = scr["bk_scr2"]
    cols8 = consts_sb["cols8"]; triu = consts_sb["triu_bf"]
    striu = consts_sb["striu_bf"]; ones128x1 = consts_sb["ones128x1_bf"]
    ones1x128 = consts_sb["ones1x128"]

    if isinstance(decw_ap, list):
        for (off, ap_in) in decw_ap:
            ins = nc.sync.dma_start(out=decsb[:, off:off + 32, :], in_=ap_in)
            p.LD.dma(ins)
    else:
        ins = nc.sync.dma_start(out=decsb[:], in_=decw_ap)
        p.LD.dma(ins)
    p.LD.wait(nc.vector)
    for j in range(NTILE):
        for k in range(2):
            ins = nc.vector.tensor_scalar(masks[:, j, k, :], cols8[:],
                                          decsb[:, j, k:k + 1], None, OP.is_equal)
    p.DVE.tick(ins)
    p.DVE.wait(nc.tensor)
    mflat = masks.rearrange("p j k e -> p (j k e)")
    nc.tensor.matmul(pA[:, 0:512], triu[:], mflat[:, 0:512], start=True, stop=True)
    nc.tensor.matmul(pA[:, 512:1024], triu[:], mflat[:, 512:1024], start=True, stop=True)
    nc.tensor.matmul(pB[0:1, 0:512], ones128x1[:], mflat[:, 0:512], start=True, stop=True)
    ins = nc.tensor.matmul(pB[0:1, 512:1024], ones128x1[:], mflat[:, 512:1024], start=True, stop=True)
    p.PE.tick(ins)
    p.PE.wait(nc.scalar)
    ins = nc.scalar.copy(totrow[0:1, :], pB[0:1, :])
    p.ACT.tick(ins)
    p.ACT.wait(nc.sync)
    ins = nc.sync.dma_start(out=bk1.rearrange("(p n) -> p n", p=1), in_=totrow[0:1, :])
    p.ST.dma(ins)
    p.ST.wait(nc.sync)
    for kk in range(2):
        ins = nc.sync.dma_start(
            out=totv[kk * NTILE:(kk + 1) * NTILE, :],
            in_=bk1.rearrange("(j k e) -> j k e", j=NTILE, k=2)[:, kk, :])
        p.LD.dma(ins)
    p.LD.wait(nc.vector)
    ins = nc.vector.tensor_copy(totv_bf[:], totv[:])
    p.DVE.tick(ins)
    p.DVE.wait(nc.tensor)
    ins = nc.tensor.matmul(pC[:, 0:8], striu[:], totv_bf[:], start=True, stop=True)
    p.PE.tick(ins)
    p.PE.wait(nc.scalar)
    ins = nc.scalar.copy(offs_sb[:], pC[:, 0:8])
    p.ACT.tick(ins)
    p.ACT.wait(nc.sync)
    ins = nc.sync.dma_start(out=bk2.rearrange("(v e) -> v e", v=128), in_=offs_sb[:])
    p.ST.dma(ins)
    p.ST.wait(nc.sync)
    ins = nc.sync.dma_start(
        out=offsrow[0:1, :].rearrange("p (j k e) -> p j k e", j=NTILE, k=2),
        in_=bk2.rearrange("(k j e) -> j k e", k=2, j=NTILE).unsqueeze(0))
    p.LD.dma(ins)
    p.LD.wait(nc.tensor)
    for j in range(NTILE):
        for k in range(2):
            col = j * 16 + k * 8
            ins = nc.tensor.matmul(pA[:, col:col + 8], ones1x128[0:1, :],
                                   offsrow[0:1, col:col + 8], start=False, stop=True)
    p.PE.tick(ins)
    p.PE.wait(nc.vector)
    ins = nc.vector.tensor_tensor(prod[:], pA[:, :], mflat[:, :], OP.mult)
    p.DVE.tick(ins)
    p.DVE.wait(nc.vector)
    ins = nc.vector.tensor_reduce(posl[:], prod.rearrange("p (c e) -> p c e", e=E),
                                  mybir.AxisListType.X, OP.add)
    p.DVE.tick(ins)
    p.DVE.wait(nc.vector)
    nc.vector.tensor_scalar(vflag[:], posl[:], float(CAP), None, OP.is_le)
    nc.vector.tensor_copy(esel[:].rearrange("p (j k) -> p j k", k=2), decsb[:, :, 0:2])
    ins = nc.vector.scalar_tensor_tensor(slots[:], esel[:], float(CAP), posl[:],
                                         OP.mult, OP.add)
    p.DVE.tick(ins)
    p.DVE.wait(nc.vector)
    ins = nc.vector.tensor_scalar_add(slots[:], slots[:], -1.0)
    p.DVE.tick(ins)


def bookkeeping_scratch(p):
    return {
        "decsb": p.sbuf("bk_decsb", [128, NTILE, 4], F32),
        "masks": p.sbuf("bk_masks", [128, NTILE, 2, E], BF16),
        "totrow": p.sbuf("bk_totrow", [1, 1024], F32),
        "totv": p.sbuf("bk_totv", [128, E], F32),
        "totv_bf": p.sbuf("bk_totv_bf", [128, E], BF16),
        "offs_sb": p.sbuf("bk_offs", [128, E], F32),
        "offsrow": p.sbuf("bk_offsrow", [1, 1024], F32),
        "prod": p.sbuf("bk_prod", [128, 1024], F32),
        "posl": p.sbuf("bk_posl", [128, 128], F32),
        "vflag": p.sbuf("bk_vflag", [128, 128], F32),
        "esel": p.sbuf("bk_esel", [128, 128], F32),
        "slots": p.sbuf("bk_slots", [128, 128], F32),
        "bk_scr1": p.nc.dram_tensor("bk_scr1", [1024], F32, kind="Internal").ap(),
        "bk_scr2": p.nc.dram_tensor("bk_scr2", [1024], F32, kind="Internal").ap(),
    }


def load_consts(p, names):
    out = {}
    nc = p.nc
    for n in names:
        dt = BF16 if n.endswith("_bf") else F32
        shp = {"ident": [128, 128], "triu_bf": [128, 128], "striu_bf": [128, 128],
               "cols8": [128, 8], "ones128x1_bf": [128, 1], "ones1x128": [1, 128],
               "eye64s_bf": [128, 64], "ident128_bf": [128, 128]}[n]
        ap = nc.dram_tensor("c_" + n, shp, F32, kind="ExternalInput").ap()
        sb = p.sbuf("cs_" + n, shp, dt)
        if dt == F32:
            ins = nc.sync.dma_start(out=sb[:], in_=ap)
        else:
            ins = nc.gpsimd.dma_start(out=sb[:], in_=ap)
        p.LD.dma(ins)
        out[n] = sb
    return out



def build_B():
    p = Prog()
    nc = p.nc
    lnb = nc.dram_tensor("lnb", [T, D], BF16, kind="ExternalInput").ap()
    decw = nc.dram_tensor("decw", [128, NTILE, 4], F32, kind="ExternalInput").ap()
    wi2 = nc.dram_tensor("wi2", [2, D, FF], F32, kind="ExternalInput").ap()
    wo2 = nc.dram_tensor("wo2", [2, FF, D], F32, kind="ExternalInput").ap()
    eq0_i = nc.dram_tensor("eq0", [128, 2], F32, kind="ExternalInput").ap()
    etab_o = nc.dram_tensor("etab", [2 * CAP, D], BF16, kind="ExternalOutput").ap()
    table = nc.dram_tensor("dtab", [2 * CAP, D], BF16, kind="Internal").ap()

    scr = bookkeeping_scratch(p)
    eq0 = p.sbuf("eq0_sb", [128, 2], F32)
    ma = p.sbuf("ma_sb", [128, 128], F32)
    mb = p.sbuf("mb_sb", [128, 128], F32)
    sidx_f = p.sbuf("sidx_f", [128, 128], F32)
    sidx_i = p.sbuf("sidx_i", [128, 128], I32)
    lnt = [p.sbuf("lnt0", [128, D], BF16), p.sbuf("lnt1", [128, D], BF16)]
    xe = p.sbuf("xe", [128, 8, 2 * CAP], BF16)
    wib = [p.sbuf("wib0", [128, 8, 128], BF16), p.sbuf("wib1", [128, 8, 128], BF16)]
    h = p.sbuf("h", [128, 32, CAP], BF16)
    wob = p.sbuf("wob", [128, 32, D], BF16)
    eob = p.sbuf("eob", [128, D], BF16)

    pA = p.psum("pA", [128, 1024], F32)
    pB = p.psum("pB", [128, 1024], F32)
    pC = p.psum("pC", [128, 512], F32)
    pD = p.psum("pD", [128, 512], F32)

    p.block()
    p.sems()

    consts_sb = load_consts(p, ["cols8", "triu_bf", "striu_bf", "ones128x1_bf",
                                "ones1x128"])
    ins = nc.sync.dma_start(out=eq0[:], in_=eq0_i[:])
    p.LD.dma(ins)
    emit_bookkeeping(p, decw[:], scr, consts_sb, pA, pB, pC)

    vflag, slots_g, esel, posl = scr["vflag"], scr["slots"], scr["esel"], scr["posl"]
    p.DVE.wait(nc.vector)
    nc.vector.tensor_scalar(ma[:], esel[:], eq0[:, 0:1], None, OP.is_ge)
    nc.vector.tensor_scalar(mb[:], esel[:], eq0[:, 1:2], None, OP.is_le)
    ins = nc.vector.tensor_tensor(ma[:], ma[:], mb[:], OP.mult)
    p.DVE.tick(ins)
    p.DVE.wait(nc.vector)
    ins = nc.vector.tensor_tensor(ma[:], ma[:], vflag[:], OP.mult)
    p.DVE.tick(ins)
    nc.vector.tensor_scalar(sidx_f[:], esel[:], eq0[:, 0:1], None, OP.subtract)
    ins = nc.vector.scalar_tensor_tensor(sidx_f[:], sidx_f[:], float(CAP), posl[:],
                                         OP.mult, OP.add)
    p.DVE.tick(ins)
    p.DVE.wait(nc.vector)
    ins = nc.vector.tensor_scalar_add(sidx_f[:], sidx_f[:], -1.0 - BIGIDX)
    p.DVE.tick(ins)
    p.DVE.wait(nc.vector)
    ins = nc.vector.tensor_tensor(sidx_f[:], sidx_f[:], ma[:], OP.mult)
    p.DVE.tick(ins)
    p.DVE.wait(nc.vector)
    ins = nc.vector.tensor_scalar_add(sidx_f[:], sidx_f[:], BIGIDX)
    p.DVE.tick(ins)
    p.DVE.wait(nc.vector)
    ins = nc.vector.tensor_copy(sidx_i[:], sidx_f[:])
    p.DVE.tick(ins)
    p.DVE.wait(nc.gpsimd)
    p.DVE.wait(nc.sync)

    bcreg = nc.gpsimd.alloc_register("bcreg")
    nc.gpsimd.reg_mov(bcreg, 2 * CAP - 1)
    for j in range(NTILE):
        buf = lnt[j % 2]
        if j >= 2:
            p.GP.wait(nc.sync)
        ins = nc.sync.dma_start(out=buf[:], in_=lnb[j * 128:(j + 1) * 128, :])
        p.LD.dma(ins)
        p.LD.wait(nc.gpsimd)
        for k in range(2):
            col = j * 2 + k
            ins = nc.gpsimd.indirect_dma_start(
                out=table[:, :],
                out_offset=IndirectOffsetOnAxis(sidx_i[:, col:col + 1], axis=0),
                in_=buf[:, :], in_offset=None,
                bounds_check=bcreg, oob_is_err=False)
            p.GP.dma(ins)

    p.GP.wait(nc.sync)
    for c in range(8):
        ins = nc.sync.dma_start_transpose(xe[:, c, :],
                                          table[0:2 * CAP, c * 128:(c + 1) * 128])
        p.LD.dma(ins)
    p.LD.wait(nc.tensor)
    p.LD.wait(nc.gpsimd)

    ins = nc.gpsimd.dma_start(out=wob[:],
                              in_=wo2[0].rearrange("(kc p) n -> p kc n", p=128))
    p.GP.dma(ins)

    for ei in range(2):
        for fc in range(32):
            wbuf = wib[fc % 2]
            p.PE.wait(nc.gpsimd)
            ins = nc.gpsimd.dma_start(
                out=wbuf[:], in_=wi2[ei, :, fc * 128:(fc + 1) * 128]
                .rearrange("(kc p) m -> p kc m", p=128))
            p.GP.dma(ins)
            p.GP.wait(nc.tensor)
            ps = [pA, pB][fc % 2]
            p.ACT.wait(nc.tensor)
            for (n0, nw) in ((0, 512), (512, 128)):
                for kc in range(8):
                    ins = nc.tensor.matmul(
                        ps[:, n0:n0 + nw], wbuf[:, kc, :],
                        xe[:, kc, ei * CAP + n0: ei * CAP + n0 + nw],
                        start=(kc == 0), stop=(kc == 7))
            p.PE.tick(ins)
            p.PE.wait(nc.scalar)
            ins = nc.scalar.activation(h[:, fc, :], ps[:, 0:CAP], AF.Relu)
            p.ACT.tick(ins)
        if ei == 1:
            p.PE.wait(nc.gpsimd)
            ins = nc.gpsimd.dma_start(out=wob[:],
                                      in_=wo2[1].rearrange("(kc p) n -> p kc n", p=128))
            p.GP.dma(ins)
        p.GP.wait(nc.tensor)
        p.ACT.wait(nc.tensor)
        for s5 in range(5):
            for n2 in range(2):
                ps2 = [pC, pD][(s5 * 2 + n2) % 2]
                p.ACT.wait(nc.tensor)
                for kc2 in range(32):
                    ins = nc.tensor.matmul(
                        ps2[:, 0:512],
                        h[:, kc2, s5 * 128:(s5 + 1) * 128],
                        wob[:, kc2, n2 * 512:(n2 + 1) * 512],
                        start=(kc2 == 0), stop=(kc2 == 31))
                p.PE.tick(ins)
                p.PE.wait(nc.scalar)
                p.ST.wait(nc.scalar)
                ins = nc.scalar.copy(eob[:, n2 * 512:(n2 + 1) * 512], ps2[:, 0:512])
                p.ACT.tick(ins)
            p.ACT.wait(nc.sync)
            ins = nc.sync.dma_start(
                out=etab_o[ei * CAP + s5 * 128: ei * CAP + (s5 + 1) * 128, :],
                in_=eob[:])
            p.ST.dma(ins)
    p.ST.wait(nc.gpsimd)
    p.close()
    return p.nc




def build_C(final=False):
    p = Prog()
    nc = p.nc
    x0s = nc.dram_tensor("xs", [TLOC, D], F32, kind="ExternalInput").ap()
    decw = nc.dram_tensor("decw", [128, NTILE, 4], F32, kind="ExternalInput").ap()
    etab = nc.dram_tensor("etab", [E * CAP, D], BF16, kind="ExternalInput").ap()
    myq = nc.dram_tensor("myq", [1, 1], I32, kind="ExternalInput").ap()
    if final:
        x1_o = nc.dram_tensor("x3s", [TLOC, D], F32, kind="ExternalOutput").ap()
    else:
        x1_o = nc.dram_tensor("x1s", [TLOC, D], F32, kind="ExternalOutput").ap()
        ln2_o = nc.dram_tensor("ln2b", [TLOC, D], BF16, kind="ExternalOutput").ap()
    gscr = nc.dram_tensor("gscr", [128, 256], F32, kind="Internal").ap()

    scr = bookkeeping_scratch(p)
    gidx_f = p.sbuf("gidx_f", [128, 128], F32)
    gate2 = p.sbuf("gate2", [128, 128], F32)
    myq_sb = p.sbuf("myq_sb", [1, 2], I32)
    gidx_my_f = p.sbuf("gidx_my_f", [128, 32], F32)
    gidx_my = p.sbuf("gidx_my", [128, 32], I32)
    gate_my = p.sbuf("gate_my", [128, 32], F32)
    yk = [p.sbuf("yk0", [128, D], BF16), p.sbuf("yk1", [128, D], BF16)]
    ytmp = p.sbuf("ytmp", [128, D], F32)
    xt = p.sbuf("xt", [128, D], F32)
    stat = p.sbuf("stat", [128, 12], F32)
    mv = p.sbuf("mv", [128, 2], F32)
    sd = p.sbuf("sd", [128, 1], F32)
    rstd = p.sbuf("rstd", [128, 1], F32)
    lnf = p.sbuf("lnf", [128, D], F32)
    lnb = p.sbuf("lnbs", [128, D], BF16)

    pA = p.psum("pA", [128, 1024], F32)
    pB = p.psum("pB", [128, 1024], F32)
    pC = p.psum("pC", [128, 512], F32)

    p.block()
    p.sems()

    consts_sb = load_consts(p, ["cols8", "triu_bf", "striu_bf", "ones128x1_bf",
                                "ones1x128"])
    ins = nc.sync.dma_start(out=myq_sb[:, 0:1], in_=myq[:])
    p.LD.dma(ins)
    emit_bookkeeping(p, decw[:], scr, consts_sb, pA, pB, pC)

    vflag, slots, decsb = scr["vflag"], scr["slots"], scr["decsb"]
    p.DVE.wait(nc.vector)
    nc.vector.tensor_scalar_add(gidx_f[:], slots[:], -BIGIDX)
    ins = nc.vector.tensor_tensor(gidx_f[:], gidx_f[:], vflag[:], OP.mult)
    p.DVE.tick(ins)
    p.DVE.wait(nc.vector)
    ins = nc.vector.tensor_scalar_add(gidx_f[:], gidx_f[:], BIGIDX)
    p.DVE.tick(ins)
    nc.vector.tensor_copy(gate2[:].rearrange("p (j k) -> p j k", k=2),
                          decsb[:, :, 2:4])
    ins = nc.vector.tensor_tensor(gate2[:], gate2[:], vflag[:], OP.mult)
    p.DVE.tick(ins)
    p.DVE.wait(nc.sync)
    ins = nc.sync.dma_start(out=gscr[:, 0:128], in_=gidx_f[:])
    p.ST.dma(ins)
    ins = nc.sync.dma_start(out=gscr[:, 128:256], in_=gate2[:])
    p.ST.dma(ins)
    p.ST.wait(nc.sync)
    p.LD.wait(nc.sync)
    offreg = nc.sync.alloc_register("offr")
    nc.sync.load(offreg, myq_sb[0:1, 0:1])
    off = RuntimeValue(offreg)
    gidx_ap = bass.AP(gscr.tensor, off, [[256, 128], [1, 32]])
    gate_ap = bass.AP(gscr.tensor, off + 128, [[256, 128], [1, 32]])
    ins = nc.sync.dma_start(out=gidx_my_f[:], in_=gidx_ap)
    p.LD.dma(ins)
    ins = nc.sync.dma_start(out=gate_my[:], in_=gate_ap)
    p.LD.dma(ins)
    p.LD.wait(nc.vector)
    ins = nc.vector.tensor_copy(gidx_my[:], gidx_my_f[:])
    p.DVE.tick(ins)
    p.DVE.wait(nc.gpsimd)

    bcreg = nc.gpsimd.alloc_register("bcreg")
    nc.gpsimd.reg_mov(bcreg, E * CAP - 1)
    for j in range(NTILE_LOC):
        if j > 0:
            p.DVE.wait(nc.sync)
            p.ST.wait(nc.vector)
        ins = nc.sync.dma_start(out=xt[:], in_=x0s[j * 128:(j + 1) * 128, :])
        p.LD.dma(ins)
        for k in range(2):
            ins = nc.vector.memset(yk[k][:], 0.0)
            p.DVE.tick(ins)
        p.DVE.wait(nc.gpsimd)
        for k in range(2):
            ins = nc.gpsimd.indirect_dma_start(
                out=yk[k][:, :], out_offset=None, in_=etab[:, :],
                in_offset=IndirectOffsetOnAxis(gidx_my[:, 2 * j + k:2 * j + k + 1],
                                               axis=0),
                bounds_check=bcreg, oob_is_err=False)
            p.GP.dma(ins)
        p.GP.wait(nc.vector)
        p.LD.wait(nc.vector)
        ins = nc.vector.tensor_scalar(ytmp[:], yk[0][:], gate_my[:, 2 * j:2 * j + 1],
                                      None, OP.mult)
        p.DVE.tick(ins)
        p.DVE.wait(nc.vector)
        ins = nc.vector.scalar_tensor_tensor(ytmp[:], yk[1][:],
                                             gate_my[:, 2 * j + 1:2 * j + 2],
                                             ytmp[:], OP.mult, OP.add)
        p.DVE.tick(ins)
        p.DVE.wait(nc.vector)
        ins = nc.vector.scalar_tensor_tensor(xt[:], ytmp[:], 0.5, xt[:],
                                             OP.mult, OP.add)
        p.DVE.tick(ins)
        p.DVE.wait(nc.sync)
        ins = nc.sync.dma_start(out=x1_o[j * 128:(j + 1) * 128, :], in_=xt[:])
        p.ST.dma(ins)
        if not final:
            emit_ln_tile(p, xt, stat, mv, sd, rstd, lnf, lnb)
            p.DVE.wait(nc.sync)
            ins = nc.sync.dma_start(out=ln2_o[j * 128:(j + 1) * 128, :], in_=lnb[:])
            p.ST.dma(ins)
    p.ST.wait(nc.gpsimd)
    p.close()
    return p.nc




NQC = T // 512
NKT = T // 128


def build_D(dbg=0, ntc=NQC, natt=None, epi=99):
    p = Prog()
    nc = p.nc
    ln2b = nc.dram_tensor("ln2b", [T, D], BF16, kind="ExternalInput").ap()
    wq4 = nc.dram_tensor("wq4", [D, 1024], F32, kind="ExternalInput").ap()
    gam = nc.dram_tensor("gam", [128, 512], F32, kind="ExternalInput").ap()
    bet = nc.dram_tensor("bet", [128, 512], F32, kind="ExternalInput").ap()
    gatedb = nc.dram_tensor("gatedb", [T, 256], BF16, kind="ExternalOutput").ap()
    if dbg == 6:
        dbg_done = nc.dram_tensor("dbg_done", [1, 16], F32, kind="ExternalOutput").ap()
        dbg_q = nc.dram_tensor("dbg_q", [128, 2, T], BF16, kind="ExternalOutput").ap()
        dbg_g = nc.dram_tensor("dbg_g", [128, NKT, 256], BF16, kind="ExternalOutput").ap()
        dbg_af = nc.dram_tensor("dbg_af", [128, 512], BF16, kind="ExternalOutput").ap()
        dbg_at = nc.dram_tensor("dbg_at", [128, 512], F32, kind="ExternalOutput").ap()
        dbg_nr = nc.dram_tensor("dbg_nr", [128, 512], F32, kind="ExternalOutput").ap()
        dbg_gg = nc.dram_tensor("dbg_gg", [128, 4, 128], BF16, kind="ExternalOutput").ap()
    if dbg and dbg != 6:
        dbg_q = nc.dram_tensor("dbg_q", [128, 2, T], BF16, kind="ExternalOutput").ap()
        dbg_k = nc.dram_tensor("dbg_k", [128, 2, T], BF16, kind="ExternalOutput").ap()
        dbg_v = nc.dram_tensor("dbg_v", [128, NKT, 256], BF16, kind="ExternalOutput").ap()
        dbg_g = nc.dram_tensor("dbg_g", [128, NKT, 256], BF16, kind="ExternalOutput").ap()
        dbg_done = nc.dram_tensor("dbg_done", [1, 16], F32, kind="ExternalOutput").ap()

    wq_sb = p.sbuf("wq_sb", [128, 8, 1024], BF16)
    lnF = [p.sbuf("lnF0", [128, 8, 512], BF16), p.sbuf("lnF1", [128, 8, 512], BF16)]
    q_sb = p.sbuf("q_sb", [128, 2, T], BF16)
    k_sb = p.sbuf("k_sb", [128, 2, T], BF16)
    v_sb = p.sbuf("v_sb", [128, NKT, 256], BF16)
    gate_sb = p.sbuf("gate_sb", [128, NKT, 256], BF16)
    w_sb = [p.sbuf("w_sb0", [128, 1024], BF16), p.sbuf("w_sb1", [128, 1024], BF16)]
    att_fm = p.sbuf("att_fm", [128, 512], BF16)
    att_tok = p.sbuf("att_tok", [128, 512], F32)
    sq_tok = p.sbuf("sq_tok", [128, 512], F32)
    ssum = p.sbuf("ssum", [128, 8], F32)
    ssq = p.sbuf("ssq", [128, 8], F32)
    mu = p.sbuf("mu", [128, 8], F32)
    var = p.sbuf("var", [128, 8], F32)
    rstd = p.sbuf("rstd", [128, 8], F32)
    mu2 = p.sbuf("mu2", [128, 8], F32)
    gam_sb = p.sbuf("gam_sb", [128, 512], F32)
    bet_sb = p.sbuf("bet_sb", [128, 512], F32)
    nrm = p.sbuf("nrm", [128, 512], F32)
    gated = p.sbuf("gated", [128, 4, 128], BF16)

    psS = [p.psum("psS0", [128, 1024], F32), p.psum("psS1", [128, 1024], F32)]
    psA = p.psum("psA", [128, 512], F32)
    psAo = p.psum("psAo", [128, 512], F32)
    psT = p.psum("psT", [128, 1024], BF16)
    psT2 = p.psum("psT2", [128, 1024], BF16)
    psV = psA

    p.block()
    p.sems()

    consts_sb = load_consts(p, ["eye64s_bf"])
    ins = nc.gpsimd.dma_start(out=wq_sb[:],
                              in_=wq4.rearrange("(c p) m -> p c m", p=128))
    p.GP.dma(ins)
    ins = nc.sync.dma_start(out=gam_sb[:], in_=gam[:])
    p.LD.dma(ins)
    ins = nc.sync.dma_start(out=bet_sb[:], in_=bet[:])
    p.LD.dma(ins)
    p.GP.wait(nc.tensor)

    qk_dst = {0: (q_sb, 0), 1: (q_sb, 1), 2: (k_sb, 0), 3: (k_sb, 1)}
    p.GP.wait(nc.sync)
    p.LD.wait(nc.sync)
    for tc in range(ntc):
        buf = lnF[tc % 2]
        p.PE.wait(nc.sync)
        for c in range(8):
            ins = nc.sync.dma_start_transpose(
                buf[:, c, :], ln2b[tc * 512:(tc + 1) * 512, c * 128:(c + 1) * 128])
            p.LD.dma(ins)
        p.LD.wait(nc.tensor)
        p.ACT.wait(nc.tensor)
        for mi in range(4):
            ps = psS[mi // 2]
            off = (mi % 2) * 512
            for c in range(8):
                ins = nc.tensor.matmul(ps[:, off:off + 512],
                                       wq_sb[:, c, mi * 128:(mi + 1) * 128],
                                       buf[:, c, :], start=(c == 0), stop=(c == 7))
        p.PE.tick(ins)
        p.PE.wait(nc.scalar)
        for mi in range(4):
            dst, pr = qk_dst[mi]
            ps = psS[mi // 2]
            off = (mi % 2) * 512
            ins = nc.scalar.activation(dst[:, pr, tc * 512:(tc + 1) * 512],
                                       ps[:, off:off + 512], AF.Silu)
        p.ACT.tick(ins)
        for sub in range(4):
            tkt = tc * 4 + sub
            if sub > 0:
                p.ACT.wait(nc.tensor)
            for c in range(8):
                nc.tensor.matmul(psV[:, 0:256], buf[:, c, sub * 128:(sub + 1) * 128],
                                 wq_sb[:, c, 512:768], start=(c == 0), stop=(c == 7))
            for c in range(8):
                ins = nc.tensor.matmul(psV[:, 256:512],
                                       buf[:, c, sub * 128:(sub + 1) * 128],
                                       wq_sb[:, c, 768:1024], start=(c == 0), stop=(c == 7))
            p.PE.tick(ins)
            p.PE.wait(nc.scalar)
            nc.scalar.activation(gate_sb[:, tkt, :], psV[:, 0:256], AF.Silu)
            ins = nc.scalar.activation(v_sb[:, tkt, :], psV[:, 256:512], AF.Silu)
            p.ACT.tick(ins)

    if dbg and dbg != 6:
        p.ACT.wait(nc.sync)
        for (dst, srcb) in ((dbg_q, q_sb), (dbg_k, k_sb), (dbg_v, v_sb), (dbg_g, gate_sb)):
            ins = nc.sync.dma_start(out=dst, in_=srcb[:])
            p.ST.dma(ins)

    SCALE = 1.0 / 8.0
    attiters = []
    for pair in range(2):
        for qc in range(NQC):
            attiters.append((pair, qc))
    if dbg == 2:
        attiters = []
    if natt is not None:
        attiters = attiters[:natt]
    for (pair, qc) in attiters:
        if True:
            silu_acts = []
            av_pes = []
            for kt in range(NKT):
                par = kt % 2
                ps = psS[par]
                if kt >= 2:
                    p.ACT.wait_val(nc.tensor, silu_acts[kt - 2])
                else:
                    p.ACT.wait(nc.tensor)
                nc.tensor.matmul(ps[:, 0:512], k_sb[0:64, pair, kt * 128:(kt + 1) * 128],
                                 q_sb[0:64, pair, qc * 512:(qc + 1) * 512],
                                 start=True, stop=True, tile_position=(0, 0))
                ins = nc.tensor.matmul(ps[:, 512:1024],
                                       k_sb[64:128, pair, kt * 128:(kt + 1) * 128],
                                       q_sb[64:128, pair, qc * 512:(qc + 1) * 512],
                                       start=True, stop=True, tile_position=(64, 0))
                p.PE.tick(ins)
                p.PE.wait(nc.scalar)
                if kt >= 2:
                    pass
                ins = nc.scalar.activation(w_sb[par][:], ps[:, :], AF.Silu, scale=SCALE)
                p.ACT.tick(ins)
                silu_acts.append(p.ACT.v)
                p.ACT.wait_val(nc.tensor, silu_acts[kt])
                nc.tensor.matmul(psA[0:64, :], v_sb[:, kt, pair * 128:pair * 128 + 64],
                                 w_sb[par][:, 0:512], start=(kt == 0), stop=(kt == NKT - 1),
                                 tile_position=(0, 0))
                ins = nc.tensor.matmul(psAo[64:128, :],
                                       v_sb[:, kt, pair * 128 + 64:pair * 128 + 128],
                                       w_sb[par][:, 512:1024], start=(kt == 0),
                                       stop=(kt == NKT - 1), tile_position=(0, 64))
                p.PE.tick(ins)
                av_pes.append(p.PE.v)

            p.PE.wait(nc.scalar)
            nc.scalar.copy(att_fm[0:64, :], psA[0:64, :])
            ins = nc.scalar.copy(att_fm[64:128, :], psAo[64:128, :])
            p.ACT.tick(ins)
            if epi == 1:
                continue
            p.ACT.wait(nc.tensor)
            for sub in range(4):
                for hh in range(2):
                    dstp = psT if hh == 0 else psT2
                    ins = nc.tensor.transpose(
                        dstp[:, sub * 64:(sub + 1) * 64],
                        att_fm[hh * 64:(hh + 1) * 64, sub * 128:(sub + 1) * 128],
                        consts_sb["eye64s_bf"][hh * 64:(hh + 1) * 64, :])
            p.PE.tick(ins)
            if epi == 2:
                continue
            p.PE.wait(nc.scalar)
            nc.scalar.copy(
                att_tok[:].rearrange("p (s h d) -> p s h d", s=4, h=2)[:, :, 0, :],
                psT[:, 0:256].rearrange("p (s d) -> p s d", s=4))
            ins = nc.scalar.copy(
                att_tok[:].rearrange("p (s h d) -> p s h d", s=4, h=2)[:, :, 1, :],
                psT2[:, 0:256].rearrange("p (s d) -> p s d", s=4))
            p.ACT.tick(ins)
            if epi == 3:
                continue
            p.ACT.wait(nc.scalar)
            ins = nc.scalar.activation(sq_tok[:], att_tok[:], AF.Square)
            p.ACT.tick(ins)
            p.ACT.wait(nc.vector)
            nc.vector.reduce_sum(ssum[:], att_tok[:].rearrange("p (s e) -> p s e", e=64),
                                 axis=mybir.AxisListType.X)
            ins = nc.vector.reduce_sum(ssq[:], sq_tok[:].rearrange("p (s e) -> p s e", e=64),
                                       axis=mybir.AxisListType.X)
            p.DVE.tick(ins)
            if epi == 4:
                continue
            p.DVE.wait(nc.vector)
            nc.vector.tensor_scalar(mu[:], ssum[:], 1.0 / HD, None, OP.mult)
            ins = nc.vector.tensor_scalar(var[:], ssq[:], 1.0 / HD, None, OP.mult)
            p.DVE.tick(ins)
            p.DVE.wait(nc.vector)
            ins = nc.vector.tensor_tensor(mu2[:], mu[:], mu[:], OP.mult)
            p.DVE.tick(ins)
            p.DVE.wait(nc.vector)
            ins = nc.vector.tensor_tensor(var[:], var[:], mu2[:], OP.subtract)
            p.DVE.tick(ins)
            p.DVE.wait(nc.vector)
            ins = nc.vector.tensor_scalar_add(var[:], var[:], EPS)
            p.DVE.tick(ins)
            p.DVE.wait(nc.scalar)
            ins = nc.scalar.activation(var[:], var[:], AF.Sqrt)
            p.ACT.tick(ins)
            p.ACT.wait(nc.vector)
            ins = nc.vector.reciprocal(rstd[:], var[:])
            p.DVE.tick(ins)
            p.DVE.wait(nc.vector)
            for c8 in range(8):
                ins = nc.vector.tensor_scalar(nrm[:, c8 * 64:(c8 + 1) * 64],
                                              att_tok[:, c8 * 64:(c8 + 1) * 64],
                                              mu[:, c8:c8 + 1], rstd[:, c8:c8 + 1],
                                              OP.subtract, OP.mult)
            p.DVE.tick(ins)
            p.DVE.wait(nc.vector)
            ins = nc.vector.tensor_tensor(nrm[:], nrm[:], gam_sb[:], OP.mult)
            p.DVE.tick(ins)
            p.DVE.wait(nc.vector)
            ins = nc.vector.tensor_tensor(nrm[:], nrm[:], bet_sb[:], OP.add)
            p.DVE.tick(ins)
            p.DVE.wait(nc.vector)
            p.ST.wait(nc.vector)
            ins = nc.vector.tensor_tensor(
                gated[:], nrm[:].rearrange("p (s c) -> p s c", c=128),
                gate_sb[:, qc * 4:qc * 4 + 4, pair * 128:(pair + 1) * 128], OP.mult)
            p.DVE.tick(ins)
            p.DVE.wait(nc.sync)
            ins = nc.sync.dma_start(
                out=gatedb[qc * 512:(qc + 1) * 512, pair * 128:(pair + 1) * 128]
                .rearrange("(s p) c -> p s c", p=128),
                in_=gated[:])
            p.ST.dma(ins)
    if dbg == 6:
        p.DVE.wait(nc.sync)
        p.ACT.wait(nc.sync)
        for (d2, s2) in ((dbg_q, q_sb[:]), (dbg_g, gate_sb[:]), (dbg_af, att_fm[:]),
                         (dbg_at, att_tok[:]), (dbg_nr, nrm[:]),
                         (dbg_gg, gated[:])):
            ins = nc.sync.dma_start(out=d2, in_=s2)
            p.ST.dma(ins)
    if dbg:
        one = p.sbuf("one_sb", [1, 16], F32)
        ins = nc.vector.memset(one[:], 7.0)
        p.DVE.tick(ins)
        p.DVE.wait(nc.sync)
        ins = nc.sync.dma_start(out=dbg_done, in_=one[:])
        p.ST.dma(ins)
    p.ST.wait(nc.gpsimd)
    p.close()
    return p.nc




def build_E():
    p = Prog()
    nc = p.nc
    x1s = nc.dram_tensor("x1s", [TLOC, D], F32, kind="ExternalInput").ap()
    gfull = nc.dram_tensor("gfull", [TLOC, D], BF16, kind="ExternalInput").ap()
    wo = nc.dram_tensor("wo", [D, D], F32, kind="ExternalInput").ap()
    wr = nc.dram_tensor("wr", [D, E], F32, kind="ExternalInput").ap()
    ident = nc.dram_tensor("ident", [128, 128], F32, kind="ExternalInput").ap()
    x2_o = nc.dram_tensor("x2s", [TLOC, D], F32, kind="ExternalOutput").ap()
    ln3_o = nc.dram_tensor("ln3b", [TLOC, D], BF16, kind="ExternalOutput").ap()
    dec_o = nc.dram_tensor("dec", [TLOC, 4], F32, kind="ExternalOutput").ap()

    wo_sb = p.sbuf("wo_sb", [128, 8, D], BF16)
    gfT = [p.sbuf("gfT0", [128, 8, 128], BF16), p.sbuf("gfT1", [128, 8, 128], BF16)]
    xt = p.sbuf("xt", [128, D], F32)
    stat = p.sbuf("stat", [128, 12], F32)
    mv = p.sbuf("mv", [128, 2], F32)
    sd = p.sbuf("sd", [128, 1], F32)
    rstd = p.sbuf("rstd", [128, 1], F32)
    lnf = p.sbuf("lnf", [128, D], F32)
    lnb = p.sbuf("lnbs", [128, D], BF16)
    lnT = p.sbuf("lnT", [128, D], F32)
    identsb = p.sbuf("identsb", [128, 128], F32)
    wr_sb = p.sbuf("wr_sb", [128, 8, E], F32)
    logit = p.sbuf("logit", [128, E], F32)
    scr = {
        "mx1": p.sbuf("mx1", [128, 1], F32),
        "sm": p.sbuf("sm", [128, 1], F32),
        "mx8": p.sbuf("mx8", [128, 8], F32),
        "mi8": p.sbuf("mi8", [128, 8], U32),
        "mif": p.sbuf("mif", [128, 8], F32),
    }
    dec = p.sbuf("dec_sb", [128, 4], F32)
    psO = [p.psum("psO0", [128, 512], F32), p.psum("psO1", [128, 512], F32)]
    ptr = [p.psum("ptr0", [128, 512], F32), p.psum("ptr1", [128, 512], F32)]
    plog = p.psum("plog", [128, 512], F32)

    p.block()
    p.sems()

    ins = nc.sync.dma_start(out=identsb[:], in_=ident[:])
    p.LD.dma(ins)
    ins = nc.sync.dma_start(out=wr_sb[:], in_=wr.rearrange("(c p) e -> p c e", p=128))
    p.LD.dma(ins)
    ins = nc.gpsimd.dma_start(out=wo_sb[:], in_=wo.rearrange("(kc p) n -> p kc n", p=128))
    p.GP.dma(ins)
    p.GP.wait(nc.tensor)
    p.GP.wait(nc.sync)

    for j in range(NTILE_LOC):
        buf = gfT[j % 2]
        p.PE.wait(nc.sync)
        for c in range(8):
            ins = nc.sync.dma_start_transpose(
                buf[:, c, :], gfull[j * 128:(j + 1) * 128, c * 128:(c + 1) * 128])
            p.LD.dma(ins)
        if j > 0:
            p.DVE.wait(nc.sync)
            p.ST.wait(nc.vector)
        ins = nc.sync.dma_start(out=xt[:], in_=x1s[j * 128:(j + 1) * 128, :])
        p.LD.dma(ins)
        p.LD.wait(nc.tensor)
        p.DVE.wait(nc.tensor)
        for n2 in range(2):
            for kc in range(8):
                ins = nc.tensor.matmul(psO[n2][:, :], buf[:, kc, :],
                                       wo_sb[:, kc, n2 * 512:(n2 + 1) * 512],
                                       start=(kc == 0), stop=(kc == 7))
        p.PE.tick(ins)
        p.PE.wait(nc.vector)
        p.LD.wait(nc.vector)
        for n2 in range(2):
            ins = nc.vector.tensor_add(xt[:, n2 * 512:(n2 + 1) * 512],
                                       xt[:, n2 * 512:(n2 + 1) * 512], psO[n2][:, :])
        p.DVE.tick(ins)
        p.DVE.wait(nc.sync)
        ins = nc.sync.dma_start(out=x2_o[j * 128:(j + 1) * 128, :], in_=xt[:])
        p.ST.dma(ins)
        emit_ln_tile(p, xt, stat, mv, sd, rstd, lnf, lnb)
        p.DVE.wait(nc.sync)
        ins = nc.sync.dma_start(out=ln3_o[j * 128:(j + 1) * 128, :], in_=lnb[:])
        p.ST.dma(ins)
        p.DVE.wait(nc.tensor)
        emit_router_tile(p, lnf, identsb, wr_sb, lnT,
                         [ptr[0][:, 0:128], ptr[1][:, 0:128]], plog[:, 0:8],
                         logit, scr, dec)
        p.DVE.wait(nc.sync)
        ins = nc.sync.dma_start(out=dec_o[j * 128:(j + 1) * 128, :], in_=dec[:])
        p.ST.dma(ins)
        p.ST.wait(nc.sync)
    p.ST.wait(nc.gpsimd)
    p.close()
    return p.nc




MAGIC = 0x5A5A17

_EPOCH_STATE = [0]


def _launch_epoch():
    import time
    e = (time.time_ns() >> 8) & 0x3FFFFFFF
    if e <= _EPOCH_STATE[0]:
        e = _EPOCH_STATE[0] + 1
    _EPOCH_STATE[0] = e
    return e | 0x40000000


def build_FUSED(dbg=0):
    p = Prog()
    nc = p.nc
    HT = T // 2
    NTH = HT // 128

    x0h = nc.dram_tensor("x0h", [HT, D], F32, kind="ExternalInput").ap()
    wq8 = nc.dram_tensor("wq8", [D, 2048], F32, kind="ExternalInput").ap()
    wob_i = nc.dram_tensor("wob_i", [D, D], F32, kind="ExternalInput").ap()
    wr = nc.dram_tensor("wr", [D, E], F32, kind="ExternalInput").ap()
    ident = nc.dram_tensor("ident", [128, 128], F32, kind="ExternalInput").ap()
    gam = nc.dram_tensor("gam", [128, 512], F32, kind="ExternalInput").ap()
    bet = nc.dram_tensor("bet", [128, 512], F32, kind="ExternalInput").ap()
    wi4 = nc.dram_tensor("wi4", [4, D, FF], F32, kind="ExternalInput").ap()
    wo4 = nc.dram_tensor("wo4", [4, FF, D], F32, kind="ExternalInput").ap()
    meta = nc.dram_tensor("meta", [1, 8], I32, kind="ExternalInput").ap()
    ridx = nc.dram_tensor("ridx", [128, 180], I32, kind="ExternalInput").ap()
    x3h = nc.dram_tensor("x3h", [HT, D], F32, kind="ExternalOutput").ap()
    if dbg:
        dbg_dec = nc.dram_tensor("dbg_dec", [128, T // 256, 4], F32, kind="ExternalOutput").ap()
        dbg_x1 = nc.dram_tensor("dbg_x1", [HT, D], F32, kind="ExternalOutput").ap()
        dbg_ln = nc.dram_tensor("dbg_ln", [T, D], BF16, kind="ExternalOutput").ap()
        dbg_gt = nc.dram_tensor("dbg_gt", [4, 2 * T, 128], BF16, kind="ExternalOutput").ap()
        dbg_x2 = nc.dram_tensor("dbg_x2", [HT, D], F32, kind="ExternalOutput").ap()
        dbg_et = nc.dram_tensor("dbg_et", [E * CAP, D], BF16, kind="ExternalOutput").ap()
        dbg_dt = nc.dram_tensor("dbg_dt", [E * CAP, D], BF16, kind="ExternalOutput").ap()
        dbg_gi = nc.dram_tensor("dbg_gi", [128, 64], I32, kind="ExternalOutput").ap()
        dbg_gg = nc.dram_tensor("dbg_gg", [128, 64], F32, kind="ExternalOutput").ap()
        dbg_xe = nc.dram_tensor("dbg_xe", [128, 5120], BF16, kind="ExternalOutput").ap()
        dbg_h = nc.dram_tensor("dbg_h", [128, 20480], BF16, kind="ExternalOutput").ap()
        dbg_wib = nc.dram_tensor("dbg_wib", [128, 1024], BF16, kind="ExternalOutput").ap()

    sh_flags = nc.dram_tensor("sh_flags", [8, 64], I32, kind="Internal",
                              addr_space="Shared").ap()
    sh_dec = nc.dram_tensor("sh_dec", [2, 128, 32, 4], F32, kind="Internal",
                            addr_space="Shared").ap()
    sh_dtab = nc.dram_tensor("sh_dtab", [E * CAP, D], BF16, kind="Internal",
                             addr_space="Shared").ap()
    sh_etab = nc.dram_tensor("sh_etab", [E * CAP, D], BF16, kind="Internal",
                             addr_space="Shared").ap()
    sh_ln = nc.dram_tensor("sh_ln", [T, D], BF16, kind="Internal",
                           addr_space="Shared").ap()
    sh_gat = [nc.dram_tensor(f"sh_gat{w}", [2 * T, 128], BF16, kind="Internal",
                             addr_space="Shared").ap() for w in range(4)]

    lnh = nc.dram_tensor("lnh", [HT, D], BF16, kind="Internal").ap()
    x1h = nc.dram_tensor("x1h", [HT, D], F32, kind="Internal").ap()
    gscr = nc.dram_tensor("gscr", [128, 256], F32, kind="Internal").ap()

    scr = bookkeeping_scratch(p)
    xt = p.sbuf("xt", [128, D], F32)
    stat = p.sbuf("stat", [128, 12], F32)
    mv = p.sbuf("mv", [128, 2], F32)
    sd = p.sbuf("sd", [128, 1], F32)
    rstd1 = p.sbuf("rstd1", [128, 1], F32)
    lnf = p.sbuf("lnf", [128, D], F32)
    lnb = p.sbuf("lnbs", [128, D], BF16)
    lnT = p.sbuf("lnT", [128, D], F32)
    identsb = p.sbuf("identsb", [128, 128], F32)
    wr_sb = p.sbuf("wr_sb", [128, 8, E], F32)
    logit = p.sbuf("logit", [128, E], F32)
    rscr = {
        "mx1": p.sbuf("mx1", [128, 1], F32),
        "sm": p.sbuf("sm", [128, 1], F32),
        "mx8": p.sbuf("mx8", [128, 8], F32),
        "mi8": p.sbuf("mi8", [128, 8], U32),
        "mif": p.sbuf("mif", [128, 8], F32),
    }
    dec_acc = p.sbuf("dec_acc", [128, NTH, 4], F32)
    sidx_f = p.sbuf("sidx_f", [128, 128], F32)
    sidx_i = p.sbuf("sidx_i", [128, 128], I32)
    gidx_f = p.sbuf("gidx_f", [128, 128], F32)
    gate2 = p.sbuf("gate2", [128, 128], F32)
    gidx_my = p.sbuf("gidx_my", [128, 64], I32)
    gate_my = p.sbuf("gate_my", [128, 64], F32)
    yk = [p.sbuf("yk0", [128, D], BF16), p.sbuf("yk1", [128, D], BF16)]
    ytmp = p.sbuf("ytmp", [128, D], F32)
    gam_sb = p.sbuf("gam_sb", [128, 512], F32)
    bet_sb = p.sbuf("bet_sb", [128, 512], F32)
    ARENA_N = 60416
    arena = p.sbuf("arena", [128, ARENA_N], BF16)
    xe = arena[:, 0:5120].rearrange("p (c n) -> p c n", c=8)
    h_sb = arena[:, 5120:25600].rearrange("p (c n) -> p c n", c=32)
    wo_sb = arena[:, 25600:58368].rearrange("p (c n) -> p c n", c=32)
    wib = [arena[:, 58368:59392].rearrange("p (c n) -> p c n", c=8),
           arena[:, 59392:60416].rearrange("p (c n) -> p c n", c=8)]
    q_sb = arena[:, 0:8192]
    k_sb = arena[:, 8192:16384]
    v_sb = arena[:, 16384:24576].rearrange("p (c n) -> p c n", c=NKT)
    gate_sb = arena[:, 24576:32768].rearrange("p (c n) -> p c n", c=NKT)
    gfT = arena[:, 33024:34048].rearrange("p (c n) -> p c n", c=8)
    wo8_sb = arena[:, 0:8192].rearrange("p (c n) -> p c n", c=8)
    gb = [arena[:, 34048:34176], arena[:, 34176:34304]]
    wqw = p.sbuf("wqw", [128, 8, 512], BF16)
    lnF = [p.sbuf("lnF0", [128, 8, 512], BF16), p.sbuf("lnF1", [128, 8, 512], BF16)]
    w_sb = [p.sbuf("w_sb0", [128, 1024], BF16), p.sbuf("w_sb1", [128, 1024], BF16)]
    att_fm = p.sbuf("att_fm", [128, 512], BF16)
    att_tok = p.sbuf("att_tok", [128, 512], F32)
    sq_tok = p.sbuf("sq_tok", [128, 512], F32)
    ssum = p.sbuf("ssum", [128, 8], F32)
    ssq = p.sbuf("ssq", [128, 8], F32)
    mu = p.sbuf("mu", [128, 8], F32)
    var = p.sbuf("var", [128, 8], F32)
    rstd8 = p.sbuf("rstd8", [128, 8], F32)
    mu2 = p.sbuf("mu2", [128, 8], F32)
    nrm = p.sbuf("nrm", [128, 512], F32)
    gated = p.sbuf("gated", [128, 4, 128], BF16)
    eob = p.sbuf("eob", [128, D], BF16)
    meta_sb = p.sbuf("meta_sb", [1, 8], I32)
    ridx_sb = p.sbuf("ridx_sb", [128, 180], I32)

    PS0 = p.psum("PS0", [128, 1024], F32)
    PS1 = p.psum("PS1", [128, 1024], F32)
    PS2 = p.psum("PS2", [128, 512], F32)
    PS3 = p.psum("PS3", [128, 512], F32)
    PT0 = p.psum("PT0", [128, 1024], BF16)
    PT1 = p.psum("PT1", [128, 1024], BF16)

    p.block()
    p.sems()
    BAR = Sem(nc, "bar")
    nbar = [0]
    WSEM = [Sem(nc, "wld0"), Sem(nc, "wld1")]
    WOSEM = Sem(nc, "wold")
    TSEM = [Sem(nc, "tld0"), Sem(nc, "tld1")]
    GSEM = [Sem(nc, "gld0"), Sem(nc, "gld1")]

    consts_sb = load_consts(p, ["cols8", "triu_bf", "striu_bf", "ones128x1_bf",
                                "ones1x128", "eye64s_bf", "ident128_bf"])
    for (dst, srcx) in ((identsb, ident), (gam_sb, gam), (bet_sb, bet),
                        (meta_sb, meta), (ridx_sb, ridx)):
        ins = nc.sync.dma_start(out=dst[:], in_=srcx[:])
        p.LD.dma(ins)
    ins = nc.sync.dma_start(out=wr_sb[:], in_=wr.rearrange("(c p) e -> p c e", p=128))
    p.LD.dma(ins)
    p.LD.wait(nc.sync)
    p.LD.wait(nc.gpsimd)

    r_tok_s = nc.sync.alloc_register("r_tok_s")
    nc.sync.load(r_tok_s, meta_sb[0:1, 0:1])
    rv_tok = RuntimeValue(r_tok_s)
    r_gcol_s = nc.sync.alloc_register("r_gcol_s")
    nc.sync.load(r_gcol_s, meta_sb[0:1, 1:2])
    rv_gcol = RuntimeValue(r_gcol_s)
    r_erow_s = nc.sync.alloc_register("r_erow_s")
    nc.sync.load(r_erow_s, meta_sb[0:1, 2:3])
    rv_erow = RuntimeValue(r_erow_s)
    r_bk_s = nc.sync.alloc_register("r_bk_s")
    nc.sync.load(r_bk_s, meta_sb[0:1, 3:4])
    rv_bk = RuntimeValue(r_bk_s)
    r_h_s = nc.sync.alloc_register("r_h_s")
    nc.sync.load(r_h_s, meta_sb[0:1, 6:7])
    rv_h = RuntimeValue(r_h_s)
    r_my_g = nc.gpsimd.alloc_register("r_my_g")
    nc.gpsimd.load(r_my_g, meta_sb[0:1, 4:5])
    rv_myf = RuntimeValue(r_my_g)
    r_pa_g = nc.gpsimd.alloc_register("r_pa_g")
    nc.gpsimd.load(r_pa_g, meta_sb[0:1, 5:6])
    rv_paf = RuntimeValue(r_pa_g)
    bc_dtab = nc.gpsimd.alloc_register("bc_dtab")
    nc.gpsimd.reg_mov(bc_dtab, E * CAP - 1)
    bc_ln = nc.gpsimd.alloc_register("bc_ln")
    nc.gpsimd.reg_mov(bc_ln, T - 1)
    bc_gat = nc.gpsimd.alloc_register("bc_gat")
    nc.gpsimd.reg_mov(bc_gat, 2 * T - 1)

    spin_order = [nc.scalar, nc.tensor, nc.sync, nc.scalar, nc.tensor,
                  nc.vector, nc.scalar, nc.tensor, nc.sync, nc.vector]
    spin_state = {}

    def get_spin(eng):
        key = str(eng.engine)
        if key not in spin_state:
            p.LD.wait(eng)
            r_myc = eng.alloc_register("bm" + key[-3:])
            eng.load(r_myc, meta_sb[0:1, 4:5])
            r_pac = eng.alloc_register("bp" + key[-3:])
            eng.load(r_pac, meta_sb[0:1, 5:6])
            f = eng.alloc_register("bf" + key[-3:])
            ap_myc = bass.AP(sh_flags.tensor, RuntimeValue(r_myc), [[64, 1], [1, 1]])
            ap_pac = bass.AP(sh_flags.tensor, RuntimeValue(r_pac), [[64, 1], [1, 1]])
            spin_state[key] = (ap_myc, ap_pac, f)
        return spin_state[key]

    ap_myc, ap_pac, f = get_spin(nc.vector)
    nc.vector.store(ap_myc, 0)
    r_ep = nc.vector.alloc_register("r_ep")
    nc.vector.load(r_ep, meta_sb[0:1, 7:8])
    r_myi = nc.vector.alloc_register("r_myi")
    nc.vector.load(r_myi, meta_sb[0:1, 4:5])
    nc.vector.reg_add(r_myi, r_myi, 32)
    r_pai = nc.vector.alloc_register("r_pai")
    nc.vector.load(r_pai, meta_sb[0:1, 5:6])
    nc.vector.reg_add(r_pai, r_pai, 32)
    ap_myi = bass.AP(sh_flags.tensor, RuntimeValue(r_myi), [[64, 1], [1, 1]])
    ap_pai = bass.AP(sh_flags.tensor, RuntimeValue(r_pai), [[64, 1], [1, 1]])
    nc.vector.store(ap_myi, RuntimeValue(r_ep))
    r_di = nc.vector.alloc_register("r_di")
    nc.vector.load(f, ap_pai)
    nc.vector.reg_alu(r_di, RuntimeValue(f), RuntimeValue(r_ep),
                      mybir.AluOpType.subtract)
    with nc.vector.While(RuntimeValue(r_di) != 0):
        nc.vector.load(f, ap_pai)
        nc.vector.reg_alu(r_di, RuntimeValue(f), RuntimeValue(r_ep),
                          mybir.AluOpType.subtract)
    nc.vector.sem_inc(BAR.h, 1)
    BAR.v += 1

    def barrier():
        nbar[0] += 1
        ph = nbar[0]
        eng = spin_order[ph - 1]
        ap_myc, ap_pac, f = get_spin(eng)
        BAR.wait(eng)
        p.ST.wait(eng)
        p.GP.wait(eng)
        eng.store(ap_myc, ph)
        eng.load(f, ap_pac)
        with eng.While(RuntimeValue(f) < ph):
            eng.load(f, ap_pac)
        eng.sem_inc(BAR.h, 1)
        BAR.v += 1

    def local_join():
        for eng in (nc.sync, nc.tensor, nc.scalar, nc.vector, nc.gpsimd):
            for s in (p.LD, p.ST, p.PE, p.ACT, p.DVE, p.GP,
                      WSEM[0], WSEM[1], WOSEM, TSEM[0], TSEM[1],
                      GSEM[0], GSEM[1]):
                s.wait(eng)
            BAR.wait(eng)

    def ln_router_phase(xsrc_ap):
        for j in range(NTH):
            if j > 0:
                p.DVE.wait(nc.sync)
                p.PE.wait(nc.sync)
            ins = nc.sync.dma_start(out=xt[:], in_=xsrc_ap[j * 128:(j + 1) * 128, :])
            p.LD.dma(ins)
            p.LD.wait(nc.vector)
            if j > 0:
                p.PE.wait(nc.vector)
            emit_ln_tile(p, xt, stat, mv, sd, rstd1, lnf, lnb)
            p.DVE.wait(nc.sync)
            p.ST.wait(nc.sync)
            ins = nc.sync.dma_start(out=lnh[j * 128:(j + 1) * 128, :], in_=lnb[:])
            p.ST.dma(ins)
            p.DVE.wait(nc.tensor)
            emit_router_tile(p, lnf, identsb, wr_sb, lnT,
                             [PS2[:, 0:128], PS3[:, 0:128]], PS1[0:128, 0:8],
                             logit, rscr, dec_acc[:, j, :])
        p.DVE.wait(nc.sync)
        dec_dst = bass.AP(sh_dec.tensor, rv_h * (128 * 32 * 4),
                          [[128, 128], [4, 32], [1, 4]])
        ins = nc.sync.dma_start(out=dec_dst, in_=dec_acc[:])
        p.ST.dma(ins)
        p.ST.wait(nc.gpsimd)
        barrier()

    ln_router_phase(x0h)
    if dbg:
        p.DVE.wait(nc.sync)
        ins = nc.sync.dma_start(out=dbg_dec[:], in_=dec_acc[:])
        p.ST.dma(ins)

    def bookkeep_and_scatter():
        local_join()
        dec_loads = [(hh * 32, sh_dec[hh]) for hh in range(2)]
        emit_bookkeeping(p, dec_loads, scr, consts_sb, PS0, PS1, PS2)
        vflag, slots = scr["vflag"], scr["slots"]
        p.DVE.wait(nc.vector)
        nc.vector.tensor_scalar_add(sidx_f[:], slots[:], -BIGIDX)
        ins = nc.vector.tensor_tensor(sidx_f[:], sidx_f[:], vflag[:], OP.mult)
        p.DVE.tick(ins)
        p.DVE.wait(nc.vector)
        ins = nc.vector.tensor_scalar_add(sidx_f[:], sidx_f[:], BIGIDX)
        p.DVE.tick(ins)
        p.DVE.wait(nc.vector)
        nc.vector.tensor_scalar_add(gidx_f[:], slots[:], -BIGIDX)
        ins = nc.vector.tensor_tensor(gidx_f[:], gidx_f[:], vflag[:], OP.mult)
        p.DVE.tick(ins)
        p.DVE.wait(nc.vector)
        ins = nc.vector.tensor_scalar_add(gidx_f[:], gidx_f[:], BIGIDX)
        p.DVE.tick(ins)
        nc.vector.tensor_copy(gate2[:].rearrange("p (j k) -> p j k", k=2),
                              scr["decsb"][:, :, 2:4])
        ins = nc.vector.tensor_tensor(gate2[:], gate2[:], vflag[:], OP.mult)
        p.DVE.tick(ins)
        p.DVE.wait(nc.sync)
        ins = nc.sync.dma_start(out=gscr[:, 0:128], in_=gidx_f[:])
        p.ST.dma(ins)
        ins = nc.sync.dma_start(out=gscr[:, 128:256], in_=gate2[:])
        p.ST.dma(ins)
        p.ST.wait(nc.sync)
        gidx_ap = bass.AP(gscr.tensor, rv_bk, [[256, 128], [1, 64]])
        gate_ap = bass.AP(gscr.tensor, rv_bk + 128, [[256, 128], [1, 64]])
        ins = nc.sync.dma_start(out=gidx_f[:, 0:64], in_=gidx_ap)
        p.LD.dma(ins)
        ins = nc.sync.dma_start(out=gate_my[:], in_=gate_ap)
        p.LD.dma(ins)
        p.LD.wait(nc.vector)
        ins = nc.vector.tensor_copy(gidx_my[:], gidx_f[:, 0:64])
        p.DVE.tick(ins)
        p.DVE.wait(nc.sync)
        ins = nc.sync.dma_start(out=gscr[:, 0:128], in_=sidx_f[:])
        p.ST.dma(ins)
        p.ST.wait(nc.sync)
        sidx_ap = bass.AP(gscr.tensor, rv_bk, [[256, 128], [1, 64]])
        ins = nc.sync.dma_start(out=sidx_f[:, 0:64], in_=sidx_ap)
        p.LD.dma(ins)
        p.LD.wait(nc.vector)
        ins = nc.vector.tensor_copy(sidx_i[:, 0:64], sidx_f[:, 0:64])
        p.DVE.tick(ins)
        p.DVE.wait(nc.gpsimd)
        p.DVE.wait(nc.sync)
        for jj in range(NTH):
            buf = yk[jj % 2]
            if jj >= 2:
                p.GP.wait(nc.sync)
            ins = nc.sync.dma_start(out=buf[:], in_=lnh[jj * 128:(jj + 1) * 128, :])
            p.LD.dma(ins)
            p.LD.wait(nc.gpsimd)
            for k in range(2):
                col = jj * 2 + k
                ins = nc.gpsimd.indirect_dma_start(
                    out=sh_dtab[:, :],
                    out_offset=IndirectOffsetOnAxis(sidx_i[:, col:col + 1], axis=0),
                    in_=buf[:, :], in_offset=None,
                    bounds_check=bc_dtab, oob_is_err=False)
                p.GP.dma(ins)
        p.GP.wait(nc.gpsimd)
        barrier()

    bookkeep_and_scatter()
    if dbg:
        ins = nc.gpsimd.dma_start(out=dbg_dt[:], in_=sh_dtab[:])
        p.GP.dma(ins)
        p.DVE.wait(nc.gpsimd)
        ins = nc.gpsimd.dma_start(out=dbg_gi[:], in_=gidx_my[:])
        p.GP.dma(ins)
        ins = nc.gpsimd.dma_start(out=dbg_gg[:], in_=gate_my[:])
        p.GP.dma(ins)
        p.GP.wait(nc.gpsimd)

    def expert_phase():
        local_join()
        for ei in range(4):
            for t5 in range(5):
                p.PE.wait(nc.gpsimd)
                ins = nc.gpsimd.indirect_dma_start(
                    out=yk[0][:, :], out_offset=None, in_=sh_dtab[:, :],
                    in_offset=IndirectOffsetOnAxis(
                        ridx_sb[:, ei * 5 + t5:ei * 5 + t5 + 1], axis=0),
                    bounds_check=bc_dtab, oob_is_err=False)
                p.GP.dma(ins)
                p.GP.wait(nc.tensor)
                p.ACT.wait(nc.tensor)
                for c8 in range(8):
                    ins = nc.tensor.transpose(
                        PT0[:, 0:128], yk[0][:, c8 * 128:(c8 + 1) * 128],
                        consts_sb["ident128_bf"][:])
                    p.PE.tick(ins)
                    p.PE.wait(nc.scalar)
                    ins = nc.scalar.copy(xe[:, c8, t5 * 128:(t5 + 1) * 128],
                                         PT0[:, 0:128])
                    p.ACT.tick(ins)
                    p.ACT.wait(nc.tensor)
            p.ACT.wait(nc.sync)
            fc_pe = []
            relu_act = []
            for fc in range(32):
                wbuf = wib[fc % 2]
                if fc >= 2:
                    p.PE.wait_val(nc.gpsimd, fc_pe[fc - 2])
                else:
                    p.PE.wait(nc.gpsimd)
                ins = nc.gpsimd.dma_start(
                    out=wbuf[:], in_=wi4[ei, :, fc * 128:(fc + 1) * 128]
                    .rearrange("(kc p) m -> p kc m", p=128))
                WSEM[fc % 2].dma(ins)
                wb_v = WSEM[fc % 2].v
                if 12 <= fc < 28 and fc % 2 == 0:
                    ci = (fc - 12) // 2
                    if ci == 0 and s5_state[0] is not None:
                        p.PE.wait_val(nc.gpsimd, s5_state[0])
                    ins = nc.gpsimd.dma_start(
                        out=wo_sb[:, 4 * ci:4 * ci + 4, :],
                        in_=wo4[ei, 512 * ci:512 * (ci + 1), :]
                        .rearrange("(kc p) n -> p kc n", p=128))
                    WOSEM.dma(ins)
                WSEM[fc % 2].wait_val(nc.tensor, wb_v)
                ps = [PS0, PS1][fc % 2]
                if fc >= 2:
                    p.ACT.wait_val(nc.tensor, relu_act[fc - 2])
                else:
                    p.ACT.wait(nc.tensor)
                for (n0, nw) in ((0, 512), (512, 128)):
                    for kc in range(8):
                        ins = nc.tensor.matmul(
                            ps[:, n0:n0 + nw], wbuf[:, kc, :],
                            xe[:, kc, n0:n0 + nw],
                            start=(kc == 0), stop=(kc == 7))
                p.PE.tick(ins)
                fc_pe.append(p.PE.v)
                p.PE.wait(nc.scalar)
                ins = nc.scalar.activation(h_sb[:, fc, :], ps[:, 0:CAP], AF.Relu)
                p.ACT.tick(ins)
                relu_act.append(p.ACT.v)
            WOSEM.wait(nc.tensor)
            p.ACT.wait(nc.tensor)
            for s5 in range(5):
                for n2 in range(2):
                    ps2 = [PS2, PS3][(s5 * 2 + n2) % 2]
                    p.ACT.wait(nc.tensor)
                    for kc2 in range(32):
                        ins = nc.tensor.matmul(
                            ps2[:, 0:512],
                            h_sb[:, kc2, s5 * 128:(s5 + 1) * 128],
                            wo_sb[:, kc2, n2 * 512:(n2 + 1) * 512],
                            start=(kc2 == 0), stop=(kc2 == 31))
                    p.PE.tick(ins)
                    p.PE.wait(nc.scalar)
                    p.GP.wait(nc.scalar)
                    ins = nc.scalar.copy(eob[:, n2 * 512:(n2 + 1) * 512], ps2[:, 0:512])
                    p.ACT.tick(ins)
                p.ACT.wait(nc.gpsimd)
                ins = nc.gpsimd.indirect_dma_start(
                    out=sh_etab[:, :],
                    out_offset=IndirectOffsetOnAxis(
                        ridx_sb[:, ei * 5 + s5:ei * 5 + s5 + 1], axis=0),
                    in_=eob[:, :], in_offset=None,
                    bounds_check=bc_dtab, oob_is_err=False)
                p.GP.dma(ins)
            s5_state[0] = p.PE.v
        barrier()

    s5_state = [None]
    expert_phase()
    if dbg:
        ins = nc.gpsimd.dma_start(out=dbg_et[:], in_=sh_etab[:])
        p.GP.dma(ins)
        ins = nc.gpsimd.dma_start(out=dbg_xe[:], in_=arena[:, 0:5120])
        p.GP.dma(ins)
        ins = nc.gpsimd.dma_start(out=dbg_h[:], in_=arena[:, 5120:25600])
        p.GP.dma(ins)
        ins = nc.gpsimd.dma_start(out=dbg_wib[:], in_=arena[:, 59392:60416])
        p.GP.dma(ins)
        p.GP.wait(nc.gpsimd)

    def combine_phase(xsrc_ap, final):
        local_join()
        bcz = bc_dtab
        for jj in range(NTH):
            if jj > 0:
                p.DVE.wait(nc.sync)
                p.ST.wait(nc.vector)
                p.GP.wait(nc.vector)
            ins = nc.sync.dma_start(out=xt[:], in_=xsrc_ap[jj * 128:(jj + 1) * 128, :])
            p.LD.dma(ins)
            for k in range(2):
                ins = nc.vector.memset(yk[k][:], 0.0)
                p.DVE.tick(ins)
            p.DVE.wait(nc.gpsimd)
            for k in range(2):
                ins = nc.gpsimd.indirect_dma_start(
                    out=yk[k][:, :], out_offset=None, in_=sh_etab[:, :],
                    in_offset=IndirectOffsetOnAxis(
                        gidx_my[:, 2 * jj + k:2 * jj + k + 1], axis=0),
                    bounds_check=bcz, oob_is_err=False)
                p.GP.dma(ins)
            p.GP.wait(nc.vector)
            p.LD.wait(nc.vector)
            ins = nc.vector.tensor_scalar(ytmp[:], yk[0][:],
                                          gate_my[:, 2 * jj:2 * jj + 1],
                                          None, OP.mult)
            p.DVE.tick(ins)
            p.DVE.wait(nc.vector)
            ins = nc.vector.scalar_tensor_tensor(ytmp[:], yk[1][:],
                                                 gate_my[:, 2 * jj + 1:2 * jj + 2],
                                                 ytmp[:], OP.mult, OP.add)
            p.DVE.tick(ins)
            p.DVE.wait(nc.vector)
            ins = nc.vector.scalar_tensor_tensor(xt[:], ytmp[:], 0.5, xt[:],
                                                 OP.mult, OP.add)
            p.DVE.tick(ins)
            p.DVE.wait(nc.sync)
            if final:
                ins = nc.sync.dma_start(out=x3h[jj * 128:(jj + 1) * 128, :], in_=xt[:])
                p.ST.dma(ins)
            else:
                ins = nc.sync.dma_start(out=x1h[jj * 128:(jj + 1) * 128, :], in_=xt[:])
                p.ST.dma(ins)
                if jj > 0:
                    p.PE.wait(nc.vector)
                emit_ln_tile(p, xt, stat, mv, sd, rstd1, lnf, lnb)
                p.DVE.wait(nc.gpsimd)
                ins = nc.gpsimd.indirect_dma_start(
                    out=sh_ln[:, :],
                    out_offset=IndirectOffsetOnAxis(
                        ridx_sb[:, 20 + jj:21 + jj], axis=0),
                    in_=lnb[:, :], in_offset=None,
                    bounds_check=bc_ln, oob_is_err=False)
                p.GP.dma(ins)
        p.ST.wait(nc.gpsimd)
        if not final:
            barrier()

    combine_phase(x0h, False)
    if dbg:
        ins = nc.gpsimd.dma_start(out=dbg_x1[:], in_=x1h[:])
        p.GP.dma(ins)
        ins = nc.gpsimd.dma_start(out=dbg_ln[:], in_=sh_ln[:])
        p.GP.dma(ins)
        p.GP.wait(nc.gpsimd)

    local_join()
    SCALE = 1.0 / 8.0
    for wave in range(4):
        local_join()
        ins = nc.gpsimd.dma_start(
            out=wqw[:], in_=wq8[:, wave * 512:(wave + 1) * 512]
            .rearrange("(c p) m -> p c m", p=128))
        p.GP.dma(ins)
        p.GP.wait(nc.sync)
        p.GP.wait(nc.tensor)
        tc_pe = []
        for tc in range(NQC):
            buf = lnF[tc % 2]
            if tc >= 2:
                p.PE.wait_val(nc.sync, tc_pe[tc - 2])
            else:
                p.PE.wait(nc.sync)
            for c in range(8):
                ins = nc.sync.dma_start_transpose(
                    buf[:, c, :],
                    sh_ln[tc * 512:(tc + 1) * 512, c * 128:(c + 1) * 128])
                TSEM[tc % 2].dma(ins)
            TSEM[tc % 2].wait(nc.tensor)
            p.ACT.wait(nc.tensor)
            for mi in range(2):
                ps = [PS0, PS1][mi]
                for c in range(8):
                    ins = nc.tensor.matmul(ps[:, 0:512],
                                           wqw[:, c, mi * 128:(mi + 1) * 128],
                                           buf[:, c, :], start=(c == 0), stop=(c == 7))
            p.PE.tick(ins)
            p.PE.wait(nc.scalar)
            nc.scalar.activation(q_sb[:, tc * 512:(tc + 1) * 512], PS0[:, 0:512], AF.Silu)
            ins = nc.scalar.activation(k_sb[:, tc * 512:(tc + 1) * 512], PS1[:, 0:512], AF.Silu)
            p.ACT.tick(ins)
            for sub in range(4):
                tkt = tc * 4 + sub
                p.ACT.wait(nc.tensor)
                for c in range(8):
                    nc.tensor.matmul(PS2[:, 0:128],
                                     buf[:, c, sub * 128:(sub + 1) * 128],
                                     wqw[:, c, 256:384], start=(c == 0), stop=(c == 7))
                for c in range(8):
                    ins = nc.tensor.matmul(PS3[:, 0:128],
                                           buf[:, c, sub * 128:(sub + 1) * 128],
                                           wqw[:, c, 384:512],
                                           start=(c == 0), stop=(c == 7))
                p.PE.tick(ins)
                p.PE.wait(nc.scalar)
                nc.scalar.activation(gate_sb[:, tkt, :], PS2[:, 0:128], AF.Silu)
                ins = nc.scalar.activation(v_sb[:, tkt, :], PS3[:, 0:128], AF.Silu)
                p.ACT.tick(ins)
            tc_pe.append(p.PE.v)
        for qc in range(NQC):
            silu_vals = []
            qk_vals = []
            act_base = p.ACT.v

            def emit_qk(kt):
                par = kt % 2
                ps = [PS0, PS1][par]
                p.ACT.wait_val(nc.tensor,
                               silu_vals[kt - 2] if kt >= 2 else act_base)
                nc.tensor.matmul(ps[:, 0:512],
                                 k_sb[0:64, kt * 128:(kt + 1) * 128],
                                 q_sb[0:64, qc * 512:(qc + 1) * 512],
                                 start=True, stop=True, tile_position=(0, 0))
                ins = nc.tensor.matmul(ps[:, 512:1024],
                                       k_sb[64:128, kt * 128:(kt + 1) * 128],
                                       q_sb[64:128, qc * 512:(qc + 1) * 512],
                                       start=True, stop=True,
                                       tile_position=(64, 0))
                p.PE.tick(ins)
                qk_vals.append(p.PE.v)

            emit_qk(0)
            for kt in range(NKT):
                par = kt % 2
                p.PE.wait_val(nc.scalar, qk_vals[kt])
                ins = nc.scalar.activation(w_sb[par][:], [PS0, PS1][par][:, :],
                                           AF.Silu, scale=SCALE)
                p.ACT.tick(ins)
                silu_vals.append(p.ACT.v)
                if kt + 1 < NKT:
                    emit_qk(kt + 1)
                p.ACT.wait_val(nc.tensor, silu_vals[kt])
                nc.tensor.matmul(PS2[0:64, :],
                                 v_sb[:, kt, 0:64],
                                 w_sb[par][:, 0:512], start=(kt == 0),
                                 stop=(kt == NKT - 1), tile_position=(0, 0))
                ins = nc.tensor.matmul(PS3[64:128, :],
                                       v_sb[:, kt, 64:128],
                                       w_sb[par][:, 512:1024], start=(kt == 0),
                                       stop=(kt == NKT - 1), tile_position=(0, 64))
                p.PE.tick(ins)
            p.PE.wait(nc.scalar)
            nc.scalar.copy(att_fm[0:64, :], PS2[0:64, :])
            ins = nc.scalar.copy(att_fm[64:128, :], PS3[64:128, :])
            p.ACT.tick(ins)
            p.ACT.wait(nc.tensor)
            for sub in range(4):
                for hh in range(2):
                    dstp = PT0 if hh == 0 else PT1
                    ins = nc.tensor.transpose(
                        dstp[:, sub * 64:(sub + 1) * 64],
                        att_fm[hh * 64:(hh + 1) * 64, sub * 128:(sub + 1) * 128],
                        consts_sb["eye64s_bf"][hh * 64:(hh + 1) * 64, :])
            p.PE.tick(ins)
            p.PE.wait(nc.scalar)
            nc.scalar.copy(
                att_tok[:].rearrange("p (s h d) -> p s h d", s=4, h=2)[:, :, 0, :],
                PT0[:, 0:256].rearrange("p (s d) -> p s d", s=4))
            ins = nc.scalar.copy(
                att_tok[:].rearrange("p (s h d) -> p s h d", s=4, h=2)[:, :, 1, :],
                PT1[:, 0:256].rearrange("p (s d) -> p s d", s=4))
            p.ACT.tick(ins)
            p.ACT.wait(nc.scalar)
            ins = nc.scalar.activation(sq_tok[:], att_tok[:], AF.Square)
            p.ACT.tick(ins)
            p.ACT.wait(nc.vector)
            nc.vector.reduce_sum(ssum[:],
                                 att_tok[:].rearrange("p (s e) -> p s e", e=64),
                                 axis=mybir.AxisListType.X)
            ins = nc.vector.reduce_sum(ssq[:],
                                       sq_tok[:].rearrange("p (s e) -> p s e", e=64),
                                       axis=mybir.AxisListType.X)
            p.DVE.tick(ins)
            p.DVE.wait(nc.vector)
            nc.vector.tensor_scalar(mu[:], ssum[:], 1.0 / HD, None, OP.mult)
            ins = nc.vector.tensor_scalar(var[:], ssq[:], 1.0 / HD, None, OP.mult)
            p.DVE.tick(ins)
            p.DVE.wait(nc.vector)
            ins = nc.vector.tensor_tensor(mu2[:], mu[:], mu[:], OP.mult)
            p.DVE.tick(ins)
            p.DVE.wait(nc.vector)
            ins = nc.vector.tensor_tensor(var[:], var[:], mu2[:], OP.subtract)
            p.DVE.tick(ins)
            p.DVE.wait(nc.vector)
            ins = nc.vector.tensor_scalar_add(var[:], var[:], EPS)
            p.DVE.tick(ins)
            p.DVE.wait(nc.scalar)
            ins = nc.scalar.activation(var[:], var[:], AF.Sqrt)
            p.ACT.tick(ins)
            p.ACT.wait(nc.vector)
            ins = nc.vector.reciprocal(rstd8[:], var[:])
            p.DVE.tick(ins)
            p.DVE.wait(nc.vector)
            for c8 in range(8):
                ins = nc.vector.tensor_scalar(nrm[:, c8 * 64:(c8 + 1) * 64],
                                              att_tok[:, c8 * 64:(c8 + 1) * 64],
                                              mu[:, c8:c8 + 1], rstd8[:, c8:c8 + 1],
                                              OP.subtract, OP.mult)
            p.DVE.tick(ins)
            p.DVE.wait(nc.vector)
            ins = nc.vector.tensor_tensor(nrm[:], nrm[:], gam_sb[:], OP.mult)
            p.DVE.tick(ins)
            p.DVE.wait(nc.vector)
            ins = nc.vector.tensor_tensor(nrm[:], nrm[:], bet_sb[:], OP.add)
            p.DVE.tick(ins)
            p.DVE.wait(nc.vector)
            p.GP.wait(nc.vector)
            ins = nc.vector.tensor_tensor(
                gated[:], nrm[:].rearrange("p (s c) -> p s c", c=128),
                gate_sb[:, qc * 4:qc * 4 + 4, :], OP.mult)
            p.DVE.tick(ins)
            p.DVE.wait(nc.gpsimd)
            for sub in range(4):
                ins = nc.gpsimd.indirect_dma_start(
                    out=sh_gat[wave][:, :],
                    out_offset=IndirectOffsetOnAxis(
                        ridx_sb[:, 52 + qc * 4 + sub:53 + qc * 4 + sub], axis=0),
                    in_=gated[:, sub, :], in_offset=None,
                    bounds_check=bc_gat, oob_is_err=False)
                p.GP.dma(ins)
    barrier()
    if dbg:
        for w in range(4):
            ins = nc.gpsimd.dma_start(out=dbg_gt[w], in_=sh_gat[w][:])
            p.GP.dma(ins)
        p.GP.wait(nc.gpsimd)

    local_join()
    ins = nc.gpsimd.dma_start(out=wo8_sb,
                              in_=wob_i.rearrange("(kc p) n -> p kc n", p=128))
    p.GP.dma(ins)
    p.GP.wait(nc.tensor)
    p.GP.wait(nc.sync)
    op_tv = []
    op_cv = []
    for jj in range(NTH):
        for c in range(8):
            k = jj * 8 + c
            hp, cc = c // 4, c % 4
            if k >= 2:
                p.PE.wait_val(nc.gpsimd, op_tv[k - 2])
            else:
                p.PE.wait(nc.gpsimd)
            ins = nc.gpsimd.indirect_dma_start(
                out=gb[c % 2][:, :], out_offset=None, in_=sh_gat[cc][:, :],
                in_offset=IndirectOffsetOnAxis(
                    ridx_sb[:, 116 + hp * 32 + jj:117 + hp * 32 + jj], axis=0),
                bounds_check=bc_gat, oob_is_err=False)
            GSEM[c % 2].dma(ins)
            GSEM[c % 2].wait(nc.tensor)
            if k >= 2:
                p.ACT.wait_val(nc.tensor, op_cv[k - 2])
            else:
                p.ACT.wait(nc.tensor)
            ins = nc.tensor.transpose(
                [PT0, PT1][c % 2][:, 0:128], gb[c % 2][:, :],
                consts_sb["ident128_bf"][:])
            p.PE.tick(ins)
            op_tv.append(p.PE.v)
            p.PE.wait(nc.scalar)
            ins = nc.scalar.copy(gfT[:, c, :], [PT0, PT1][c % 2][:, 0:128])
            p.ACT.tick(ins)
            op_cv.append(p.ACT.v)
        if jj > 0:
            p.DVE.wait(nc.sync)
            p.ST.wait(nc.vector)
        ins = nc.sync.dma_start(out=xt[:], in_=x1h[jj * 128:(jj + 1) * 128, :])
        p.LD.dma(ins)
        p.LD.wait(nc.tensor)
        p.DVE.wait(nc.tensor)
        p.ACT.wait(nc.tensor)
        for n2 in range(2):
            for kc in range(8):
                ins = nc.tensor.matmul([PS2, PS3][n2][:, :], gfT[:, kc, :],
                                       wo8_sb[:, kc, n2 * 512:(n2 + 1) * 512],
                                       start=(kc == 0), stop=(kc == 7))
        p.PE.tick(ins)
        p.PE.wait(nc.vector)
        p.LD.wait(nc.vector)
        for n2 in range(2):
            ins = nc.vector.tensor_add(xt[:, n2 * 512:(n2 + 1) * 512],
                                       xt[:, n2 * 512:(n2 + 1) * 512],
                                       [PS2, PS3][n2][:, :])
        p.DVE.tick(ins)
        p.DVE.wait(nc.sync)
        ins = nc.sync.dma_start(out=x1h[jj * 128:(jj + 1) * 128, :], in_=xt[:])
        p.ST.dma(ins)
        if jj > 0:
            p.PE.wait(nc.vector)
        emit_ln_tile(p, xt, stat, mv, sd, rstd1, lnf, lnb)
        p.DVE.wait(nc.sync)
        p.ST.wait(nc.sync)
        ins = nc.sync.dma_start(out=lnh[jj * 128:(jj + 1) * 128, :], in_=lnb[:])
        p.ST.dma(ins)
        p.DVE.wait(nc.tensor)
        emit_router_tile(p, lnf, identsb, wr_sb, lnT,
                         [PS2[:, 0:128], PS3[:, 0:128]], PS1[0:128, 0:8],
                         logit, rscr, dec_acc[:, jj, :])
    p.DVE.wait(nc.sync)
    dec_dst = bass.AP(sh_dec.tensor, rv_h * (128 * 32 * 4),
                      [[128, 128], [4, 32], [1, 4]])
    ins = nc.sync.dma_start(out=dec_dst, in_=dec_acc[:])
    p.ST.dma(ins)
    p.ST.wait(nc.gpsimd)
    barrier()

    if dbg:
        ins = nc.gpsimd.dma_start(out=dbg_x2[:], in_=x1h[:])
        p.GP.dma(ins)
        p.GP.wait(nc.gpsimd)
    bookkeep_and_scatter()
    expert_phase()
    combine_phase(x1h, True)

    p.ST.wait(nc.gpsimd)
    p.close()
    return p.nc


def kernel_fused(L0_attn_hidden_states, L0_attn_w_qkuv, L0_attn_w_out,
                 L0_attn_ln_gamma, L0_attn_ln_beta, L0_ffn_w_router,
                 L0_ffn_wi, L0_ffn_wo):
    x0 = np.ascontiguousarray(np.asarray(L0_attn_hidden_states, dtype=np.float32)
                              ).reshape(GG, T, D)
    wq = np.ascontiguousarray(np.asarray(L0_attn_w_qkuv, dtype=np.float32))
    w_out = np.ascontiguousarray(np.asarray(L0_attn_w_out, dtype=np.float32))
    gamma = np.asarray(L0_attn_ln_gamma, dtype=np.float32)
    beta = np.asarray(L0_attn_ln_beta, dtype=np.float32)
    wr = np.ascontiguousarray(np.asarray(L0_ffn_w_router, dtype=np.float32))
    wi = np.ascontiguousarray(np.asarray(L0_ffn_wi, dtype=np.float32))
    wo = np.ascontiguousarray(np.asarray(L0_ffn_wo, dtype=np.float32))

    consts = host_consts()
    cns = {"c_" + n: consts[n] for n in
           ("cols8", "triu_bf", "striu_bf", "ones128x1_bf", "ones1x128",
            "eye64s_bf", "ident128_bf")}
    epoch = _launch_epoch()
    in_maps = []
    for core in range(8):
        g = (core // 2) % 2
        h = core % 2
        cols = []
        for wave in range(4):
            h0 = 8 * h + 2 * wave
            cols += list(range(h0 * 64, (h0 + 2) * 64))
            cols += list(range(1024 + h0 * 64, 1024 + (h0 + 2) * 64))
            cols += list(range(3072 + h0 * 64, 3072 + (h0 + 2) * 64))
            cols += list(range(2048 + h0 * 64, 2048 + (h0 + 2) * 64))
        meta = np.array([[4096 * h, 512 * h, 2560 * h, 64 * h,
                          (2 + h) * 64, (3 - h) * 64, h, epoch]],
                        dtype=np.int32)
        pp = np.arange(128, dtype=np.int32)
        rix = np.zeros((128, 180), np.int32)
        for t in range(20):
            rix[:, t] = 2560 * h + 128 * t + pp
        for jj in range(32):
            rix[:, 20 + jj] = 4096 * h + 128 * jj + pp
        for qc in range(16):
            for sub in range(4):
                rix[:, 52 + qc * 4 + sub] = 8192 * h + qc * 512 + sub * 128 + pp
        for hp in range(2):
            for jj in range(32):
                rix[:, 116 + hp * 32 + jj] = 8192 * hp + 4096 * h + 128 * jj + pp
        in_maps.append({
            **cns,
            "x0h": np.ascontiguousarray(x0[g, h * 4096:(h + 1) * 4096]),
            "wq8": np.ascontiguousarray(wq[:, cols]),
            "wob_i": w_out, "wr": wr, "ident": consts["ident"],
            "gam": np.ascontiguousarray(np.tile(gamma, (128, 8))),
            "bet": np.ascontiguousarray(np.tile(beta, (128, 8))),
            "wi4": wi[4 * h:4 * h + 4], "wo4": wo[4 * h:4 * h + 4],
            "meta": meta, "ridx": rix,
        })
    nc = _get("FUSED", build_FUSED)
    res = run_bass_kernel_spmd(nc, in_maps, core_ids=list(range(8)))
    out = np.empty((GG, T, D), dtype=np.float32)
    for core in range(4):
        g = core // 2
        h = core % 2
        out[g, h * 4096:(h + 1) * 4096] = np.asarray(res.results[core]["x3h"])
    return out



def build_A():
    p = Prog()
    nc = p.nc
    x0s = nc.dram_tensor("x0s", [TLOC, D], F32, kind="ExternalInput").ap()
    wr = nc.dram_tensor("wr", [D, E], F32, kind="ExternalInput").ap()
    ident = nc.dram_tensor("ident", [128, 128], F32, kind="ExternalInput").ap()
    lnb_o = nc.dram_tensor("lnb", [TLOC, D], BF16, kind="ExternalOutput").ap()
    dec_o = nc.dram_tensor("dec", [TLOC, 4], F32, kind="ExternalOutput").ap()

    xt = p.sbuf("xt", [128, D], F32)
    stat = p.sbuf("stat", [128, 12], F32)
    mv = p.sbuf("mv", [128, 2], F32)
    sd = p.sbuf("sd", [128, 1], F32)
    rstd = p.sbuf("rstd", [128, 1], F32)
    lnf = p.sbuf("lnf", [128, D], F32)
    lnb = p.sbuf("lnbs", [128, D], BF16)
    lnT = p.sbuf("lnT", [128, D], F32)
    identsb = p.sbuf("identsb", [128, 128], F32)
    wr_sb = p.sbuf("wr_sb", [128, 8, E], F32)
    logit = p.sbuf("logit", [128, E], F32)
    scr = {
        "mx1": p.sbuf("mx1", [128, 1], F32),
        "sm": p.sbuf("sm", [128, 1], F32),
        "mx8": p.sbuf("mx8", [128, 8], F32),
        "mi8": p.sbuf("mi8", [128, 8], U32),
        "mif": p.sbuf("mif", [128, 8], F32),
    }
    dec = p.sbuf("dec_sb", [128, 4], F32)
    ptr = [p.psum("ptr0", [128, 128], F32), p.psum("ptr1", [128, 128], F32)]
    plog = p.psum("plog", [128, E], F32)

    p.block()
    p.sems()

    ins = nc.sync.dma_start(out=identsb[:], in_=ident[:])
    p.LD.dma(ins)
    ins = nc.sync.dma_start(out=wr_sb[:], in_=wr.rearrange("(c p) e -> p c e", p=128))
    p.LD.dma(ins)

    for i in range(NTILE_LOC):
        if i > 0:
            p.DVE.wait(nc.sync)
            p.ST.wait(nc.vector)
        ins = nc.sync.dma_start(out=xt[:], in_=x0s[i * 128:(i + 1) * 128, :])
        p.LD.dma(ins)
        p.LD.wait(nc.vector)
        if i > 0:
            p.PE.wait(nc.vector)
        emit_ln_tile(p, xt, stat, mv, sd, rstd, lnf, lnb)
        p.DVE.wait(nc.sync)
        ins = nc.sync.dma_start(out=lnb_o[i * 128:(i + 1) * 128, :], in_=lnb[:])
        p.ST.dma(ins)
        p.DVE.wait(nc.tensor)
        emit_router_tile(p, lnf, identsb, wr_sb, lnT, ptr, plog, logit, scr, dec)
        p.DVE.wait(nc.sync)
        ins = nc.sync.dma_start(out=dec_o[i * 128:(i + 1) * 128, :], in_=dec[:])
        p.ST.dma(ins)
        p.ST.wait(nc.sync)
    p.ST.wait(nc.gpsimd)
    p.close()
    return p.nc



_CACHE = {}


def _get(name, builder):
    if name not in _CACHE:
        _CACHE[name] = builder()
    return _CACHE[name]


def kernel(**inputs):
    return kernel_fused(**inputs)


def kernel_multilaunch(L0_attn_hidden_states, L0_attn_w_qkuv, L0_attn_w_out,
           L0_attn_ln_gamma, L0_attn_ln_beta, L0_ffn_w_router,
           L0_ffn_wi, L0_ffn_wo):
    x0 = np.ascontiguousarray(np.asarray(L0_attn_hidden_states, dtype=np.float32)
                              ).reshape(GG, T, D)
    wq = np.ascontiguousarray(np.asarray(L0_attn_w_qkuv, dtype=np.float32))
    w_out = np.ascontiguousarray(np.asarray(L0_attn_w_out, dtype=np.float32))
    gamma = np.asarray(L0_attn_ln_gamma, dtype=np.float32)
    beta = np.asarray(L0_attn_ln_beta, dtype=np.float32)
    wr = np.ascontiguousarray(np.asarray(L0_ffn_w_router, dtype=np.float32))
    wi = np.ascontiguousarray(np.asarray(L0_ffn_wi, dtype=np.float32))
    wo = np.ascontiguousarray(np.asarray(L0_ffn_wo, dtype=np.float32))

    consts = host_consts()
    cns = {"c_" + n: consts[n] for n in
           ("cols8", "triu_bf", "striu_bf", "ones128x1_bf", "ones1x128")}
    cores = list(range(8))
    gq = [divmod(c, QP) for c in cores]

    def decw_of(dec):
        return [np.ascontiguousarray(
            dec[g].reshape(NTILE, 128, 4).transpose(1, 0, 2)) for g in range(GG)]

    def gather_full(res, key, axis=0):
        return [np.concatenate([np.asarray(res.results[g * QP + q][key])
                                for q in range(QP)], axis=axis) for g in range(GG)]

    def per_core(res, key):
        return [np.asarray(r[key]) for r in res.results]

    resA = run_bass_kernel_spmd(_get("A", build_A), [
        {"x0s": np.ascontiguousarray(x0[g, q * TLOC:(q + 1) * TLOC]),
         "wr": wr, "ident": consts["ident"]}
        for (g, q) in gq], core_ids=cores)
    lnb1 = gather_full(resA, "lnb")
    dec1 = gather_full(resA, "dec")
    decw1 = decw_of(dec1)

    def expert_maps(lnb, decw):
        return [
            {**cns, "lnb": np.ascontiguousarray(lnb[g]),
             "decw": decw[g],
             "wi2": wi[2 * q:2 * q + 2], "wo2": wo[2 * q:2 * q + 2],
             "eq0": np.tile(np.array([2.0 * q, 2.0 * q + 1], dtype=np.float32),
                            (128, 1))}
            for (g, q) in gq]
    resB = run_bass_kernel_spmd(_get("B", build_B), expert_maps(lnb1, decw1),
                                core_ids=cores)
    etab1 = gather_full(resB, "etab")

    resC = run_bass_kernel_spmd(_get("C", build_C), [
        {**cns, "xs": np.ascontiguousarray(x0[g, q * TLOC:(q + 1) * TLOC]),
         "decw": decw1[g], "etab": etab1[g],
         "myq": np.array([[32 * q]], dtype=np.int32)}
        for (g, q) in gq], core_ids=cores)
    x1s = per_core(resC, "x1s")
    ln2 = gather_full(resC, "ln2b")

    din = []
    for (g, q) in gq:
        cols = []
        for pair in range(2):
            h0 = 4 * q + 2 * pair
            cols += list(range(h0 * 64, (h0 + 2) * 64))
        for pair in range(2):
            h0 = 4 * q + 2 * pair
            cols += list(range(1024 + h0 * 64, 1024 + (h0 + 2) * 64))
        h0 = 4 * q
        cols += list(range(3072 + h0 * 64, 3072 + (h0 + 4) * 64))
        cols += list(range(2048 + h0 * 64, 2048 + (h0 + 4) * 64))
        din.append({
            "c_eye64s_bf": consts["eye64s_bf"],
            "ln2b": np.ascontiguousarray(ln2[g]),
            "wq4": np.ascontiguousarray(wq[:, cols]),
            "gam": np.ascontiguousarray(np.tile(gamma, (128, 8))),
            "bet": np.ascontiguousarray(np.tile(beta, (128, 8)))})
    resD = run_bass_kernel_spmd(_get("D", build_D), din, core_ids=cores)
    gated = gather_full(resD, "gatedb", axis=1)

    resE = run_bass_kernel_spmd(_get("E", build_E), [
        {"x1s": x1s[c], "gfull": np.ascontiguousarray(
            gated[g][q * TLOC:(q + 1) * TLOC]),
         "wo": w_out, "wr": wr, "ident": consts["ident"]}
        for c, (g, q) in enumerate(gq)], core_ids=cores)
    x2s = per_core(resE, "x2s")
    ln3 = gather_full(resE, "ln3b")
    dec3 = gather_full(resE, "dec")
    decw3 = decw_of(dec3)

    resF = run_bass_kernel_spmd(_get("B", build_B), expert_maps(ln3, decw3),
                                core_ids=cores)
    etab3 = gather_full(resF, "etab")

    resG = run_bass_kernel_spmd(_get("G", lambda: build_C(final=True)), [
        {**cns, "xs": x2s[c], "decw": decw3[g], "etab": etab3[g],
         "myq": np.array([[32 * q]], dtype=np.int32)}
        for c, (g, q) in enumerate(gq)], core_ids=cores)
    x3 = gather_full(resG, "x3s")
    out = np.stack(x3).reshape(GG, T, D).astype(np.float32)
    return out


# version 08
# speedup vs baseline: 1.5721x; 1.1609x over previous
import numpy as np
import concourse.bass as bass
import concourse.mybir as mybir
from concourse.bass import IndirectOffsetOnAxis
from concourse.expressions_rust import RuntimeValue
from concourse.bass_utils import run_bass_kernel_spmd

F32 = mybir.dt.float32
BF16 = mybir.dt.bfloat16
I32 = mybir.dt.int32
U32 = mybir.dt.uint32
AF = mybir.ActivationFunctionType
OP = mybir.AluOpType

GG = 2
T = 8192
D = 1024
NH = 16
HD = 64
E = 8
CAP = 640
FF = 4096
QP = 4
TLOC = T // QP
NTILE_LOC = TLOC // 128
NTILE = T // 128
EPS = 1e-5
BIGIDX = 60000.0


class Sem:

    def __init__(self, nc, name):
        self._cm = nc.semaphore(name)
        self.h = self._cm.__enter__()
        self.v = 0

    def tick(self, inst, n=1):
        inst.then_inc(self.h, n)
        self.v += n

    def dma(self, inst):
        self.tick(inst, 16)

    def wait(self, eng):
        if self.v > 0:
            eng.wait_ge(self.h, self.v)

    def wait_val(self, eng, val):
        if val > 0:
            eng.wait_ge(self.h, val)


class Prog:

    def __init__(self):
        self.nc = bass.Bass("TRN2", target_bir_lowering=False)
        self._ctx = []

    def sems(self):
        nc = self.nc
        for name in ("LD", "ST", "PE", "ACT", "DVE", "GP", "X1", "X2"):
            setattr(self, name, Sem(nc, name.lower()))

    def sbuf(self, name, shape, dtype):
        cm = self.nc.sbuf_tensor(name, shape, dtype)
        t = cm.__enter__()
        self._ctx.append(cm)
        return t

    def psum(self, name, shape, dtype=F32):
        cm = self.nc.psum_tensor(name, shape, dtype)
        t = cm.__enter__()
        self._ctx.append(cm)
        return t

    def block(self):
        cm = self.nc.Block()
        b = cm.__enter__()
        self._ctx.append(cm)
        return b

    def close(self):
        for cm in reversed(self._ctx):
            cm.__exit__(None, None, None)



def host_consts():
    c = {}
    c["ident"] = np.eye(128, dtype=np.float32)
    c["triu"] = np.triu(np.ones((128, 128), dtype=np.float32)).astype(np.float32)
    c["cols8"] = np.tile(np.arange(8, dtype=np.float32), (128, 1))
    c["triu_bf"] = np.triu(np.ones((128, 128), dtype=np.float32))
    c["striu_bf"] = np.triu(np.ones((128, 128), dtype=np.float32), 1)
    c["ones128x1_bf"] = np.ones((128, 1), dtype=np.float32)
    c["ones1x128"] = np.ones((1, 128), dtype=np.float32)
    c["eye64s_bf"] = np.vstack([np.eye(64, dtype=np.float32)] * 2)
    c["ident128_bf"] = np.eye(128, dtype=np.float32)
    return c


def emit_ln_tile(p, xt, stat, mv, sd, rstd, lnf, lnb):
    nc = p.nc
    nc.vector.bn_stats(stat[:, 0:6], xt[:, 0:512])
    ins = nc.vector.bn_stats(stat[:, 6:12], xt[:, 512:1024])
    p.DVE.tick(ins)
    p.DVE.wait(nc.vector)
    ins = nc.vector.bn_aggr(mv[:], stat[:])
    p.DVE.tick(ins)
    p.DVE.wait(nc.vector)
    ins = nc.vector.tensor_scalar_add(sd[:], mv[:, 1:2], EPS)
    p.DVE.tick(ins)
    p.DVE.wait(nc.scalar)
    ins = nc.scalar.activation(sd[:], sd[:], AF.Sqrt)
    p.ACT.tick(ins)
    p.ACT.wait(nc.vector)
    ins = nc.vector.reciprocal(rstd[:], sd[:])
    p.DVE.tick(ins)
    p.DVE.wait(nc.vector)
    ins = nc.vector.tensor_scalar(lnf[:], xt[:], mv[:, 0:1], rstd[:, 0:1],
                                  OP.subtract, OP.mult)
    p.DVE.tick(ins)
    if lnb is not None:
        ins = nc.vector.tensor_copy(lnb[:], lnf[:])
        p.DVE.tick(ins)


def emit_router_tile(p, lnf, identsb, wr_sb, lnT, ptr, plog, logit, scr, dec):
    nc = p.nc
    for c in range(8):
        bank = ptr[c % 2]
        p.ACT.wait(nc.tensor)
        ins = nc.tensor.transpose(bank[:], lnf[:, c * 128:(c + 1) * 128], identsb[:])
        p.PE.tick(ins)
        p.PE.wait(nc.scalar)
        ins = nc.scalar.copy(lnT[:, c * 128:(c + 1) * 128], bank[:])
        p.ACT.tick(ins)
    p.ACT.wait(nc.tensor)
    for c in range(8):
        ins = nc.tensor.matmul(plog[:], lnT[:, c * 128:(c + 1) * 128],
                               wr_sb[:, c, :], start=(c == 0), stop=(c == 7))
    p.PE.tick(ins)
    p.PE.wait(nc.scalar)
    p.DVE.wait(nc.scalar)
    ins = nc.scalar.copy(logit[:], plog[:])
    p.ACT.tick(ins)
    p.ACT.wait(nc.vector)
    mx1 = scr["mx1"]
    ins = nc.vector.reduce_max(mx1[:], logit[:], axis=mybir.AxisListType.X)
    p.DVE.tick(ins)
    p.DVE.wait(nc.vector)
    ins = nc.vector.tensor_scalar(logit[:], logit[:], mx1[:, 0:1], None, OP.subtract)
    p.DVE.tick(ins)
    p.DVE.wait(nc.scalar)
    ins = nc.scalar.activation(logit[:], logit[:], AF.Exp)
    p.ACT.tick(ins)
    p.ACT.wait(nc.vector)
    sm = scr["sm"]
    ins = nc.vector.reduce_sum(sm[:], logit[:], axis=mybir.AxisListType.X)
    p.DVE.tick(ins)
    p.DVE.wait(nc.vector)
    ins = nc.vector.reciprocal(sm[:], sm[:])
    p.DVE.tick(ins)
    p.DVE.wait(nc.vector)
    ins = nc.vector.tensor_scalar(logit[:], logit[:], sm[:, 0:1], None, OP.mult)
    p.DVE.tick(ins)
    p.DVE.wait(nc.vector)
    mx8, mi8, mif = scr["mx8"], scr["mi8"], scr["mif"]
    ins = nc.vector.max(mx8[:], logit[:])
    p.DVE.tick(ins)
    p.DVE.wait(nc.vector)
    ins = nc.vector.max_index(mi8[:], mx8[:], logit[:])
    p.DVE.tick(ins)
    p.DVE.wait(nc.vector)
    ins = nc.vector.tensor_copy(dec[:, 0:2], mi8[:, 0:2].bitcast(I32))
    p.DVE.tick(ins)
    ins = nc.vector.tensor_copy(dec[:, 2:4], mx8[:, 0:2])
    p.DVE.tick(ins)




def emit_bookkeeping(p, decw_ap, scr, consts_sb, pA, pB, pC):
    nc = p.nc
    decsb = scr["decsb"]; masks = scr["masks"]; totrow = scr["totrow"]
    totv = scr["totv"]; totv_bf = scr["totv_bf"]; offs_sb = scr["offs_sb"]
    offsrow = scr["offsrow"]; prod = scr["prod"]; posl = scr["posl"]
    vflag = scr["vflag"]; esel = scr["esel"]; slots = scr["slots"]
    bk1 = scr["bk_scr1"]; bk2 = scr["bk_scr2"]
    cols8 = consts_sb["cols8"]; triu = consts_sb["triu_bf"]
    striu = consts_sb["striu_bf"]; ones128x1 = consts_sb["ones128x1_bf"]
    ones1x128 = consts_sb["ones1x128"]

    if isinstance(decw_ap, list):
        for (off, ap_in) in decw_ap:
            ins = nc.sync.dma_start(out=decsb[:, off:off + 32, :], in_=ap_in)
            p.LD.dma(ins)
    else:
        ins = nc.sync.dma_start(out=decsb[:], in_=decw_ap)
        p.LD.dma(ins)
    p.LD.wait(nc.vector)
    for j in range(NTILE):
        for k in range(2):
            ins = nc.vector.tensor_scalar(masks[:, j, k, :], cols8[:],
                                          decsb[:, j, k:k + 1], None, OP.is_equal)
    p.DVE.tick(ins)
    p.DVE.wait(nc.tensor)
    mflat = masks.rearrange("p j k e -> p (j k e)")
    nc.tensor.matmul(pA[:, 0:512], triu[:], mflat[:, 0:512], start=True, stop=True)
    nc.tensor.matmul(pA[:, 512:1024], triu[:], mflat[:, 512:1024], start=True, stop=True)
    nc.tensor.matmul(pB[0:1, 0:512], ones128x1[:], mflat[:, 0:512], start=True, stop=True)
    ins = nc.tensor.matmul(pB[0:1, 512:1024], ones128x1[:], mflat[:, 512:1024], start=True, stop=True)
    p.PE.tick(ins)
    p.PE.wait(nc.scalar)
    ins = nc.scalar.copy(totrow[0:1, :], pB[0:1, :])
    p.ACT.tick(ins)
    p.ACT.wait(nc.sync)
    ins = nc.sync.dma_start(out=bk1.rearrange("(p n) -> p n", p=1), in_=totrow[0:1, :])
    p.ST.dma(ins)
    p.ST.wait(nc.sync)
    for kk in range(2):
        ins = nc.sync.dma_start(
            out=totv[kk * NTILE:(kk + 1) * NTILE, :],
            in_=bk1.rearrange("(j k e) -> j k e", j=NTILE, k=2)[:, kk, :])
        p.LD.dma(ins)
    p.LD.wait(nc.vector)
    ins = nc.vector.tensor_copy(totv_bf[:], totv[:])
    p.DVE.tick(ins)
    p.DVE.wait(nc.tensor)
    ins = nc.tensor.matmul(pC[:, 0:8], striu[:], totv_bf[:], start=True, stop=True)
    p.PE.tick(ins)
    p.PE.wait(nc.scalar)
    ins = nc.scalar.copy(offs_sb[:], pC[:, 0:8])
    p.ACT.tick(ins)
    p.ACT.wait(nc.sync)
    ins = nc.sync.dma_start(out=bk2.rearrange("(v e) -> v e", v=128), in_=offs_sb[:])
    p.ST.dma(ins)
    p.ST.wait(nc.sync)
    ins = nc.sync.dma_start(
        out=offsrow[0:1, :].rearrange("p (j k e) -> p j k e", j=NTILE, k=2),
        in_=bk2.rearrange("(k j e) -> j k e", k=2, j=NTILE).unsqueeze(0))
    p.LD.dma(ins)
    p.LD.wait(nc.tensor)
    for j in range(NTILE):
        for k in range(2):
            col = j * 16 + k * 8
            ins = nc.tensor.matmul(pA[:, col:col + 8], ones1x128[0:1, :],
                                   offsrow[0:1, col:col + 8], start=False, stop=True)
    p.PE.tick(ins)
    p.PE.wait(nc.vector)
    ins = nc.vector.tensor_tensor(prod[:], pA[:, :], mflat[:, :], OP.mult)
    p.DVE.tick(ins)
    p.DVE.wait(nc.vector)
    ins = nc.vector.tensor_reduce(posl[:], prod.rearrange("p (c e) -> p c e", e=E),
                                  mybir.AxisListType.X, OP.add)
    p.DVE.tick(ins)
    p.DVE.wait(nc.vector)
    nc.vector.tensor_scalar(vflag[:], posl[:], float(CAP), None, OP.is_le)
    nc.vector.tensor_copy(esel[:].rearrange("p (j k) -> p j k", k=2), decsb[:, :, 0:2])
    ins = nc.vector.scalar_tensor_tensor(slots[:], esel[:], float(CAP), posl[:],
                                         OP.mult, OP.add)
    p.DVE.tick(ins)
    p.DVE.wait(nc.vector)
    ins = nc.vector.tensor_scalar_add(slots[:], slots[:], -1.0)
    p.DVE.tick(ins)


def bookkeeping_scratch(p):
    return {
        "decsb": p.sbuf("bk_decsb", [128, NTILE, 4], F32),
        "masks": p.sbuf("bk_masks", [128, NTILE, 2, E], BF16),
        "totrow": p.sbuf("bk_totrow", [1, 1024], F32),
        "totv": p.sbuf("bk_totv", [128, E], F32),
        "totv_bf": p.sbuf("bk_totv_bf", [128, E], BF16),
        "offs_sb": p.sbuf("bk_offs", [128, E], F32),
        "offsrow": p.sbuf("bk_offsrow", [1, 1024], F32),
        "prod": p.sbuf("bk_prod", [128, 1024], F32),
        "posl": p.sbuf("bk_posl", [128, 128], F32),
        "vflag": p.sbuf("bk_vflag", [128, 128], F32),
        "esel": p.sbuf("bk_esel", [128, 128], F32),
        "slots": p.sbuf("bk_slots", [128, 128], F32),
        "bk_scr1": p.nc.dram_tensor("bk_scr1", [1024], F32, kind="Internal").ap(),
        "bk_scr2": p.nc.dram_tensor("bk_scr2", [1024], F32, kind="Internal").ap(),
    }


def load_consts(p, names):
    out = {}
    nc = p.nc
    for n in names:
        dt = BF16 if n.endswith("_bf") else F32
        shp = {"ident": [128, 128], "triu_bf": [128, 128], "striu_bf": [128, 128],
               "cols8": [128, 8], "ones128x1_bf": [128, 1], "ones1x128": [1, 128],
               "eye64s_bf": [128, 64], "ident128_bf": [128, 128]}[n]
        ap = nc.dram_tensor("c_" + n, shp, F32, kind="ExternalInput").ap()
        sb = p.sbuf("cs_" + n, shp, dt)
        if dt == F32:
            ins = nc.sync.dma_start(out=sb[:], in_=ap)
        else:
            ins = nc.gpsimd.dma_start(out=sb[:], in_=ap)
        p.LD.dma(ins)
        out[n] = sb
    return out



def build_B():
    p = Prog()
    nc = p.nc
    lnb = nc.dram_tensor("lnb", [T, D], BF16, kind="ExternalInput").ap()
    decw = nc.dram_tensor("decw", [128, NTILE, 4], F32, kind="ExternalInput").ap()
    wi2 = nc.dram_tensor("wi2", [2, D, FF], F32, kind="ExternalInput").ap()
    wo2 = nc.dram_tensor("wo2", [2, FF, D], F32, kind="ExternalInput").ap()
    eq0_i = nc.dram_tensor("eq0", [128, 2], F32, kind="ExternalInput").ap()
    etab_o = nc.dram_tensor("etab", [2 * CAP, D], BF16, kind="ExternalOutput").ap()
    table = nc.dram_tensor("dtab", [2 * CAP, D], BF16, kind="Internal").ap()

    scr = bookkeeping_scratch(p)
    eq0 = p.sbuf("eq0_sb", [128, 2], F32)
    ma = p.sbuf("ma_sb", [128, 128], F32)
    mb = p.sbuf("mb_sb", [128, 128], F32)
    sidx_f = p.sbuf("sidx_f", [128, 128], F32)
    sidx_i = p.sbuf("sidx_i", [128, 128], I32)
    lnt = [p.sbuf("lnt0", [128, D], BF16), p.sbuf("lnt1", [128, D], BF16)]
    xe = p.sbuf("xe", [128, 8, 2 * CAP], BF16)
    wib = [p.sbuf("wib0", [128, 8, 128], BF16), p.sbuf("wib1", [128, 8, 128], BF16)]
    h = p.sbuf("h", [128, 32, CAP], BF16)
    wob = p.sbuf("wob", [128, 32, D], BF16)
    eob = p.sbuf("eob", [128, D], BF16)

    pA = p.psum("pA", [128, 1024], F32)
    pB = p.psum("pB", [128, 1024], F32)
    pC = p.psum("pC", [128, 512], F32)
    pD = p.psum("pD", [128, 512], F32)

    p.block()
    p.sems()

    consts_sb = load_consts(p, ["cols8", "triu_bf", "striu_bf", "ones128x1_bf",
                                "ones1x128"])
    ins = nc.sync.dma_start(out=eq0[:], in_=eq0_i[:])
    p.LD.dma(ins)
    emit_bookkeeping(p, decw[:], scr, consts_sb, pA, pB, pC)

    vflag, slots_g, esel, posl = scr["vflag"], scr["slots"], scr["esel"], scr["posl"]
    p.DVE.wait(nc.vector)
    nc.vector.tensor_scalar(ma[:], esel[:], eq0[:, 0:1], None, OP.is_ge)
    nc.vector.tensor_scalar(mb[:], esel[:], eq0[:, 1:2], None, OP.is_le)
    ins = nc.vector.tensor_tensor(ma[:], ma[:], mb[:], OP.mult)
    p.DVE.tick(ins)
    p.DVE.wait(nc.vector)
    ins = nc.vector.tensor_tensor(ma[:], ma[:], vflag[:], OP.mult)
    p.DVE.tick(ins)
    nc.vector.tensor_scalar(sidx_f[:], esel[:], eq0[:, 0:1], None, OP.subtract)
    ins = nc.vector.scalar_tensor_tensor(sidx_f[:], sidx_f[:], float(CAP), posl[:],
                                         OP.mult, OP.add)
    p.DVE.tick(ins)
    p.DVE.wait(nc.vector)
    ins = nc.vector.tensor_scalar_add(sidx_f[:], sidx_f[:], -1.0 - BIGIDX)
    p.DVE.tick(ins)
    p.DVE.wait(nc.vector)
    ins = nc.vector.tensor_tensor(sidx_f[:], sidx_f[:], ma[:], OP.mult)
    p.DVE.tick(ins)
    p.DVE.wait(nc.vector)
    ins = nc.vector.tensor_scalar_add(sidx_f[:], sidx_f[:], BIGIDX)
    p.DVE.tick(ins)
    p.DVE.wait(nc.vector)
    ins = nc.vector.tensor_copy(sidx_i[:], sidx_f[:])
    p.DVE.tick(ins)
    p.DVE.wait(nc.gpsimd)
    p.DVE.wait(nc.sync)

    bcreg = nc.gpsimd.alloc_register("bcreg")
    nc.gpsimd.reg_mov(bcreg, 2 * CAP - 1)
    for j in range(NTILE):
        buf = lnt[j % 2]
        if j >= 2:
            p.GP.wait(nc.sync)
        ins = nc.sync.dma_start(out=buf[:], in_=lnb[j * 128:(j + 1) * 128, :])
        p.LD.dma(ins)
        p.LD.wait(nc.gpsimd)
        for k in range(2):
            col = j * 2 + k
            ins = nc.gpsimd.indirect_dma_start(
                out=table[:, :],
                out_offset=IndirectOffsetOnAxis(sidx_i[:, col:col + 1], axis=0),
                in_=buf[:, :], in_offset=None,
                bounds_check=bcreg, oob_is_err=False)
            p.GP.dma(ins)

    p.GP.wait(nc.sync)
    for c in range(8):
        ins = nc.sync.dma_start_transpose(xe[:, c, :],
                                          table[0:2 * CAP, c * 128:(c + 1) * 128])
        p.LD.dma(ins)
    p.LD.wait(nc.tensor)
    p.LD.wait(nc.gpsimd)

    ins = nc.gpsimd.dma_start(out=wob[:],
                              in_=wo2[0].rearrange("(kc p) n -> p kc n", p=128))
    p.GP.dma(ins)

    for ei in range(2):
        for fc in range(32):
            wbuf = wib[fc % 2]
            p.PE.wait(nc.gpsimd)
            ins = nc.gpsimd.dma_start(
                out=wbuf[:], in_=wi2[ei, :, fc * 128:(fc + 1) * 128]
                .rearrange("(kc p) m -> p kc m", p=128))
            p.GP.dma(ins)
            p.GP.wait(nc.tensor)
            ps = [pA, pB][fc % 2]
            p.ACT.wait(nc.tensor)
            for (n0, nw) in ((0, 512), (512, 128)):
                for kc in range(8):
                    ins = nc.tensor.matmul(
                        ps[:, n0:n0 + nw], wbuf[:, kc, :],
                        xe[:, kc, ei * CAP + n0: ei * CAP + n0 + nw],
                        start=(kc == 0), stop=(kc == 7))
            p.PE.tick(ins)
            p.PE.wait(nc.scalar)
            ins = nc.scalar.activation(h[:, fc, :], ps[:, 0:CAP], AF.Relu)
            p.ACT.tick(ins)
        if ei == 1:
            p.PE.wait(nc.gpsimd)
            ins = nc.gpsimd.dma_start(out=wob[:],
                                      in_=wo2[1].rearrange("(kc p) n -> p kc n", p=128))
            p.GP.dma(ins)
        p.GP.wait(nc.tensor)
        p.ACT.wait(nc.tensor)
        for s5 in range(5):
            for n2 in range(2):
                ps2 = [pC, pD][(s5 * 2 + n2) % 2]
                p.ACT.wait(nc.tensor)
                for kc2 in range(32):
                    ins = nc.tensor.matmul(
                        ps2[:, 0:512],
                        h[:, kc2, s5 * 128:(s5 + 1) * 128],
                        wob[:, kc2, n2 * 512:(n2 + 1) * 512],
                        start=(kc2 == 0), stop=(kc2 == 31))
                p.PE.tick(ins)
                p.PE.wait(nc.scalar)
                p.ST.wait(nc.scalar)
                ins = nc.scalar.copy(eob[:, n2 * 512:(n2 + 1) * 512], ps2[:, 0:512])
                p.ACT.tick(ins)
            p.ACT.wait(nc.sync)
            ins = nc.sync.dma_start(
                out=etab_o[ei * CAP + s5 * 128: ei * CAP + (s5 + 1) * 128, :],
                in_=eob[:])
            p.ST.dma(ins)
    p.ST.wait(nc.gpsimd)
    p.close()
    return p.nc




def build_C(final=False):
    p = Prog()
    nc = p.nc
    x0s = nc.dram_tensor("xs", [TLOC, D], F32, kind="ExternalInput").ap()
    decw = nc.dram_tensor("decw", [128, NTILE, 4], F32, kind="ExternalInput").ap()
    etab = nc.dram_tensor("etab", [E * CAP, D], BF16, kind="ExternalInput").ap()
    myq = nc.dram_tensor("myq", [1, 1], I32, kind="ExternalInput").ap()
    if final:
        x1_o = nc.dram_tensor("x3s", [TLOC, D], F32, kind="ExternalOutput").ap()
    else:
        x1_o = nc.dram_tensor("x1s", [TLOC, D], F32, kind="ExternalOutput").ap()
        ln2_o = nc.dram_tensor("ln2b", [TLOC, D], BF16, kind="ExternalOutput").ap()
    gscr = nc.dram_tensor("gscr", [128, 256], F32, kind="Internal").ap()

    scr = bookkeeping_scratch(p)
    gidx_f = p.sbuf("gidx_f", [128, 128], F32)
    gate2 = p.sbuf("gate2", [128, 128], F32)
    myq_sb = p.sbuf("myq_sb", [1, 2], I32)
    gidx_my_f = p.sbuf("gidx_my_f", [128, 32], F32)
    gidx_my = p.sbuf("gidx_my", [128, 32], I32)
    gate_my = p.sbuf("gate_my", [128, 32], F32)
    yk = [p.sbuf("yk0", [128, D], BF16), p.sbuf("yk1", [128, D], BF16)]
    ytmp = p.sbuf("ytmp", [128, D], F32)
    xt = p.sbuf("xt", [128, D], F32)
    stat = p.sbuf("stat", [128, 12], F32)
    mv = p.sbuf("mv", [128, 2], F32)
    sd = p.sbuf("sd", [128, 1], F32)
    rstd = p.sbuf("rstd", [128, 1], F32)
    lnf = p.sbuf("lnf", [128, D], F32)
    lnb = p.sbuf("lnbs", [128, D], BF16)

    pA = p.psum("pA", [128, 1024], F32)
    pB = p.psum("pB", [128, 1024], F32)
    pC = p.psum("pC", [128, 512], F32)

    p.block()
    p.sems()

    consts_sb = load_consts(p, ["cols8", "triu_bf", "striu_bf", "ones128x1_bf",
                                "ones1x128"])
    ins = nc.sync.dma_start(out=myq_sb[:, 0:1], in_=myq[:])
    p.LD.dma(ins)
    emit_bookkeeping(p, decw[:], scr, consts_sb, pA, pB, pC)

    vflag, slots, decsb = scr["vflag"], scr["slots"], scr["decsb"]
    p.DVE.wait(nc.vector)
    nc.vector.tensor_scalar_add(gidx_f[:], slots[:], -BIGIDX)
    ins = nc.vector.tensor_tensor(gidx_f[:], gidx_f[:], vflag[:], OP.mult)
    p.DVE.tick(ins)
    p.DVE.wait(nc.vector)
    ins = nc.vector.tensor_scalar_add(gidx_f[:], gidx_f[:], BIGIDX)
    p.DVE.tick(ins)
    nc.vector.tensor_copy(gate2[:].rearrange("p (j k) -> p j k", k=2),
                          decsb[:, :, 2:4])
    ins = nc.vector.tensor_tensor(gate2[:], gate2[:], vflag[:], OP.mult)
    p.DVE.tick(ins)
    p.DVE.wait(nc.sync)
    ins = nc.sync.dma_start(out=gscr[:, 0:128], in_=gidx_f[:])
    p.ST.dma(ins)
    ins = nc.sync.dma_start(out=gscr[:, 128:256], in_=gate2[:])
    p.ST.dma(ins)
    p.ST.wait(nc.sync)
    p.LD.wait(nc.sync)
    offreg = nc.sync.alloc_register("offr")
    nc.sync.load(offreg, myq_sb[0:1, 0:1])
    off = RuntimeValue(offreg)
    gidx_ap = bass.AP(gscr.tensor, off, [[256, 128], [1, 32]])
    gate_ap = bass.AP(gscr.tensor, off + 128, [[256, 128], [1, 32]])
    ins = nc.sync.dma_start(out=gidx_my_f[:], in_=gidx_ap)
    p.LD.dma(ins)
    ins = nc.sync.dma_start(out=gate_my[:], in_=gate_ap)
    p.LD.dma(ins)
    p.LD.wait(nc.vector)
    ins = nc.vector.tensor_copy(gidx_my[:], gidx_my_f[:])
    p.DVE.tick(ins)
    p.DVE.wait(nc.gpsimd)

    bcreg = nc.gpsimd.alloc_register("bcreg")
    nc.gpsimd.reg_mov(bcreg, E * CAP - 1)
    for j in range(NTILE_LOC):
        if j > 0:
            p.DVE.wait(nc.sync)
            p.ST.wait(nc.vector)
        ins = nc.sync.dma_start(out=xt[:], in_=x0s[j * 128:(j + 1) * 128, :])
        p.LD.dma(ins)
        for k in range(2):
            ins = nc.vector.memset(yk[k][:], 0.0)
            p.DVE.tick(ins)
        p.DVE.wait(nc.gpsimd)
        for k in range(2):
            ins = nc.gpsimd.indirect_dma_start(
                out=yk[k][:, :], out_offset=None, in_=etab[:, :],
                in_offset=IndirectOffsetOnAxis(gidx_my[:, 2 * j + k:2 * j + k + 1],
                                               axis=0),
                bounds_check=bcreg, oob_is_err=False)
            p.GP.dma(ins)
        p.GP.wait(nc.vector)
        p.LD.wait(nc.vector)
        ins = nc.vector.tensor_scalar(ytmp[:], yk[0][:], gate_my[:, 2 * j:2 * j + 1],
                                      None, OP.mult)
        p.DVE.tick(ins)
        p.DVE.wait(nc.vector)
        ins = nc.vector.scalar_tensor_tensor(ytmp[:], yk[1][:],
                                             gate_my[:, 2 * j + 1:2 * j + 2],
                                             ytmp[:], OP.mult, OP.add)
        p.DVE.tick(ins)
        p.DVE.wait(nc.vector)
        ins = nc.vector.scalar_tensor_tensor(xt[:], ytmp[:], 0.5, xt[:],
                                             OP.mult, OP.add)
        p.DVE.tick(ins)
        p.DVE.wait(nc.sync)
        ins = nc.sync.dma_start(out=x1_o[j * 128:(j + 1) * 128, :], in_=xt[:])
        p.ST.dma(ins)
        if not final:
            emit_ln_tile(p, xt, stat, mv, sd, rstd, lnf, lnb)
            p.DVE.wait(nc.sync)
            ins = nc.sync.dma_start(out=ln2_o[j * 128:(j + 1) * 128, :], in_=lnb[:])
            p.ST.dma(ins)
    p.ST.wait(nc.gpsimd)
    p.close()
    return p.nc




NQC = T // 512
NKT = T // 128


def build_D(dbg=0, ntc=NQC, natt=None, epi=99):
    p = Prog()
    nc = p.nc
    ln2b = nc.dram_tensor("ln2b", [T, D], BF16, kind="ExternalInput").ap()
    wq4 = nc.dram_tensor("wq4", [D, 1024], F32, kind="ExternalInput").ap()
    gam = nc.dram_tensor("gam", [128, 512], F32, kind="ExternalInput").ap()
    bet = nc.dram_tensor("bet", [128, 512], F32, kind="ExternalInput").ap()
    gatedb = nc.dram_tensor("gatedb", [T, 256], BF16, kind="ExternalOutput").ap()
    if dbg == 6:
        dbg_done = nc.dram_tensor("dbg_done", [1, 16], F32, kind="ExternalOutput").ap()
        dbg_q = nc.dram_tensor("dbg_q", [128, 2, T], BF16, kind="ExternalOutput").ap()
        dbg_g = nc.dram_tensor("dbg_g", [128, NKT, 256], BF16, kind="ExternalOutput").ap()
        dbg_af = nc.dram_tensor("dbg_af", [128, 512], BF16, kind="ExternalOutput").ap()
        dbg_at = nc.dram_tensor("dbg_at", [128, 512], F32, kind="ExternalOutput").ap()
        dbg_nr = nc.dram_tensor("dbg_nr", [128, 512], F32, kind="ExternalOutput").ap()
        dbg_gg = nc.dram_tensor("dbg_gg", [128, 4, 128], BF16, kind="ExternalOutput").ap()
    if dbg and dbg != 6:
        dbg_q = nc.dram_tensor("dbg_q", [128, 2, T], BF16, kind="ExternalOutput").ap()
        dbg_k = nc.dram_tensor("dbg_k", [128, 2, T], BF16, kind="ExternalOutput").ap()
        dbg_v = nc.dram_tensor("dbg_v", [128, NKT, 256], BF16, kind="ExternalOutput").ap()
        dbg_g = nc.dram_tensor("dbg_g", [128, NKT, 256], BF16, kind="ExternalOutput").ap()
        dbg_done = nc.dram_tensor("dbg_done", [1, 16], F32, kind="ExternalOutput").ap()

    wq_sb = p.sbuf("wq_sb", [128, 8, 1024], BF16)
    lnF = [p.sbuf("lnF0", [128, 8, 512], BF16), p.sbuf("lnF1", [128, 8, 512], BF16)]
    q_sb = p.sbuf("q_sb", [128, 2, T], BF16)
    k_sb = p.sbuf("k_sb", [128, 2, T], BF16)
    v_sb = p.sbuf("v_sb", [128, NKT, 256], BF16)
    gate_sb = p.sbuf("gate_sb", [128, NKT, 256], BF16)
    w_sb = [p.sbuf("w_sb0", [128, 1024], BF16), p.sbuf("w_sb1", [128, 1024], BF16)]
    att_fm = p.sbuf("att_fm", [128, 512], BF16)
    att_tok = p.sbuf("att_tok", [128, 512], F32)
    sq_tok = p.sbuf("sq_tok", [128, 512], F32)
    ssum = p.sbuf("ssum", [128, 8], F32)
    ssq = p.sbuf("ssq", [128, 8], F32)
    mu = p.sbuf("mu", [128, 8], F32)
    var = p.sbuf("var", [128, 8], F32)
    rstd = p.sbuf("rstd", [128, 8], F32)
    mu2 = p.sbuf("mu2", [128, 8], F32)
    gam_sb = p.sbuf("gam_sb", [128, 512], F32)
    bet_sb = p.sbuf("bet_sb", [128, 512], F32)
    nrm = p.sbuf("nrm", [128, 512], F32)
    gated = p.sbuf("gated", [128, 4, 128], BF16)

    psS = [p.psum("psS0", [128, 1024], F32), p.psum("psS1", [128, 1024], F32)]
    psA = p.psum("psA", [128, 512], F32)
    psAo = p.psum("psAo", [128, 512], F32)
    psT = p.psum("psT", [128, 1024], BF16)
    psT2 = p.psum("psT2", [128, 1024], BF16)
    psV = psA

    p.block()
    p.sems()

    consts_sb = load_consts(p, ["eye64s_bf"])
    ins = nc.gpsimd.dma_start(out=wq_sb[:],
                              in_=wq4.rearrange("(c p) m -> p c m", p=128))
    p.GP.dma(ins)
    ins = nc.sync.dma_start(out=gam_sb[:], in_=gam[:])
    p.LD.dma(ins)
    ins = nc.sync.dma_start(out=bet_sb[:], in_=bet[:])
    p.LD.dma(ins)
    p.GP.wait(nc.tensor)

    qk_dst = {0: (q_sb, 0), 1: (q_sb, 1), 2: (k_sb, 0), 3: (k_sb, 1)}
    p.GP.wait(nc.sync)
    p.LD.wait(nc.sync)
    for tc in range(ntc):
        buf = lnF[tc % 2]
        p.PE.wait(nc.sync)
        for c in range(8):
            ins = nc.sync.dma_start_transpose(
                buf[:, c, :], ln2b[tc * 512:(tc + 1) * 512, c * 128:(c + 1) * 128])
            p.LD.dma(ins)
        p.LD.wait(nc.tensor)
        p.ACT.wait(nc.tensor)
        for mi in range(4):
            ps = psS[mi // 2]
            off = (mi % 2) * 512
            for c in range(8):
                ins = nc.tensor.matmul(ps[:, off:off + 512],
                                       wq_sb[:, c, mi * 128:(mi + 1) * 128],
                                       buf[:, c, :], start=(c == 0), stop=(c == 7))
        p.PE.tick(ins)
        p.PE.wait(nc.scalar)
        for mi in range(4):
            dst, pr = qk_dst[mi]
            ps = psS[mi // 2]
            off = (mi % 2) * 512
            ins = nc.scalar.activation(dst[:, pr, tc * 512:(tc + 1) * 512],
                                       ps[:, off:off + 512], AF.Silu)
        p.ACT.tick(ins)
        for sub in range(4):
            tkt = tc * 4 + sub
            if sub > 0:
                p.ACT.wait(nc.tensor)
            for c in range(8):
                nc.tensor.matmul(psV[:, 0:256], buf[:, c, sub * 128:(sub + 1) * 128],
                                 wq_sb[:, c, 512:768], start=(c == 0), stop=(c == 7))
            for c in range(8):
                ins = nc.tensor.matmul(psV[:, 256:512],
                                       buf[:, c, sub * 128:(sub + 1) * 128],
                                       wq_sb[:, c, 768:1024], start=(c == 0), stop=(c == 7))
            p.PE.tick(ins)
            p.PE.wait(nc.scalar)
            nc.scalar.activation(gate_sb[:, tkt, :], psV[:, 0:256], AF.Silu)
            ins = nc.scalar.activation(v_sb[:, tkt, :], psV[:, 256:512], AF.Silu)
            p.ACT.tick(ins)

    if dbg and dbg != 6:
        p.ACT.wait(nc.sync)
        for (dst, srcb) in ((dbg_q, q_sb), (dbg_k, k_sb), (dbg_v, v_sb), (dbg_g, gate_sb)):
            ins = nc.sync.dma_start(out=dst, in_=srcb[:])
            p.ST.dma(ins)

    SCALE = 1.0 / 8.0
    attiters = []
    for pair in range(2):
        for qc in range(NQC):
            attiters.append((pair, qc))
    if dbg == 2:
        attiters = []
    if natt is not None:
        attiters = attiters[:natt]
    for (pair, qc) in attiters:
        if True:
            silu_acts = []
            av_pes = []
            for kt in range(NKT):
                par = kt % 2
                ps = psS[par]
                if kt >= 2:
                    p.ACT.wait_val(nc.tensor, silu_acts[kt - 2])
                else:
                    p.ACT.wait(nc.tensor)
                nc.tensor.matmul(ps[:, 0:512], k_sb[0:64, pair, kt * 128:(kt + 1) * 128],
                                 q_sb[0:64, pair, qc * 512:(qc + 1) * 512],
                                 start=True, stop=True, tile_position=(0, 0))
                ins = nc.tensor.matmul(ps[:, 512:1024],
                                       k_sb[64:128, pair, kt * 128:(kt + 1) * 128],
                                       q_sb[64:128, pair, qc * 512:(qc + 1) * 512],
                                       start=True, stop=True, tile_position=(64, 0))
                p.PE.tick(ins)
                p.PE.wait(nc.scalar)
                if kt >= 2:
                    pass
                ins = nc.scalar.activation(w_sb[par][:], ps[:, :], AF.Silu, scale=SCALE)
                p.ACT.tick(ins)
                silu_acts.append(p.ACT.v)
                p.ACT.wait_val(nc.tensor, silu_acts[kt])
                nc.tensor.matmul(psA[0:64, :], v_sb[:, kt, pair * 128:pair * 128 + 64],
                                 w_sb[par][:, 0:512], start=(kt == 0), stop=(kt == NKT - 1),
                                 tile_position=(0, 0))
                ins = nc.tensor.matmul(psAo[64:128, :],
                                       v_sb[:, kt, pair * 128 + 64:pair * 128 + 128],
                                       w_sb[par][:, 512:1024], start=(kt == 0),
                                       stop=(kt == NKT - 1), tile_position=(0, 64))
                p.PE.tick(ins)
                av_pes.append(p.PE.v)

            p.PE.wait(nc.scalar)
            nc.scalar.copy(att_fm[0:64, :], psA[0:64, :])
            ins = nc.scalar.copy(att_fm[64:128, :], psAo[64:128, :])
            p.ACT.tick(ins)
            if epi == 1:
                continue
            p.ACT.wait(nc.tensor)
            for sub in range(4):
                for hh in range(2):
                    dstp = psT if hh == 0 else psT2
                    ins = nc.tensor.transpose(
                        dstp[:, sub * 64:(sub + 1) * 64],
                        att_fm[hh * 64:(hh + 1) * 64, sub * 128:(sub + 1) * 128],
                        consts_sb["eye64s_bf"][hh * 64:(hh + 1) * 64, :])
            p.PE.tick(ins)
            if epi == 2:
                continue
            p.PE.wait(nc.scalar)
            nc.scalar.copy(
                att_tok[:].rearrange("p (s h d) -> p s h d", s=4, h=2)[:, :, 0, :],
                psT[:, 0:256].rearrange("p (s d) -> p s d", s=4))
            ins = nc.scalar.copy(
                att_tok[:].rearrange("p (s h d) -> p s h d", s=4, h=2)[:, :, 1, :],
                psT2[:, 0:256].rearrange("p (s d) -> p s d", s=4))
            p.ACT.tick(ins)
            if epi == 3:
                continue
            p.ACT.wait(nc.scalar)
            ins = nc.scalar.activation(sq_tok[:], att_tok[:], AF.Square)
            p.ACT.tick(ins)
            p.ACT.wait(nc.vector)
            nc.vector.reduce_sum(ssum[:], att_tok[:].rearrange("p (s e) -> p s e", e=64),
                                 axis=mybir.AxisListType.X)
            ins = nc.vector.reduce_sum(ssq[:], sq_tok[:].rearrange("p (s e) -> p s e", e=64),
                                       axis=mybir.AxisListType.X)
            p.DVE.tick(ins)
            if epi == 4:
                continue
            p.DVE.wait(nc.vector)
            nc.vector.tensor_scalar(mu[:], ssum[:], 1.0 / HD, None, OP.mult)
            ins = nc.vector.tensor_scalar(var[:], ssq[:], 1.0 / HD, None, OP.mult)
            p.DVE.tick(ins)
            p.DVE.wait(nc.vector)
            ins = nc.vector.tensor_tensor(mu2[:], mu[:], mu[:], OP.mult)
            p.DVE.tick(ins)
            p.DVE.wait(nc.vector)
            ins = nc.vector.tensor_tensor(var[:], var[:], mu2[:], OP.subtract)
            p.DVE.tick(ins)
            p.DVE.wait(nc.vector)
            ins = nc.vector.tensor_scalar_add(var[:], var[:], EPS)
            p.DVE.tick(ins)
            p.DVE.wait(nc.scalar)
            ins = nc.scalar.activation(var[:], var[:], AF.Sqrt)
            p.ACT.tick(ins)
            p.ACT.wait(nc.vector)
            ins = nc.vector.reciprocal(rstd[:], var[:])
            p.DVE.tick(ins)
            p.DVE.wait(nc.vector)
            for c8 in range(8):
                ins = nc.vector.tensor_scalar(nrm[:, c8 * 64:(c8 + 1) * 64],
                                              att_tok[:, c8 * 64:(c8 + 1) * 64],
                                              mu[:, c8:c8 + 1], rstd[:, c8:c8 + 1],
                                              OP.subtract, OP.mult)
            p.DVE.tick(ins)
            p.DVE.wait(nc.vector)
            ins = nc.vector.tensor_tensor(nrm[:], nrm[:], gam_sb[:], OP.mult)
            p.DVE.tick(ins)
            p.DVE.wait(nc.vector)
            ins = nc.vector.tensor_tensor(nrm[:], nrm[:], bet_sb[:], OP.add)
            p.DVE.tick(ins)
            p.DVE.wait(nc.vector)
            p.ST.wait(nc.vector)
            ins = nc.vector.tensor_tensor(
                gated[:], nrm[:].rearrange("p (s c) -> p s c", c=128),
                gate_sb[:, qc * 4:qc * 4 + 4, pair * 128:(pair + 1) * 128], OP.mult)
            p.DVE.tick(ins)
            p.DVE.wait(nc.sync)
            ins = nc.sync.dma_start(
                out=gatedb[qc * 512:(qc + 1) * 512, pair * 128:(pair + 1) * 128]
                .rearrange("(s p) c -> p s c", p=128),
                in_=gated[:])
            p.ST.dma(ins)
    if dbg == 6:
        p.DVE.wait(nc.sync)
        p.ACT.wait(nc.sync)
        for (d2, s2) in ((dbg_q, q_sb[:]), (dbg_g, gate_sb[:]), (dbg_af, att_fm[:]),
                         (dbg_at, att_tok[:]), (dbg_nr, nrm[:]),
                         (dbg_gg, gated[:])):
            ins = nc.sync.dma_start(out=d2, in_=s2)
            p.ST.dma(ins)
    if dbg:
        one = p.sbuf("one_sb", [1, 16], F32)
        ins = nc.vector.memset(one[:], 7.0)
        p.DVE.tick(ins)
        p.DVE.wait(nc.sync)
        ins = nc.sync.dma_start(out=dbg_done, in_=one[:])
        p.ST.dma(ins)
    p.ST.wait(nc.gpsimd)
    p.close()
    return p.nc




def build_E():
    p = Prog()
    nc = p.nc
    x1s = nc.dram_tensor("x1s", [TLOC, D], F32, kind="ExternalInput").ap()
    gfull = nc.dram_tensor("gfull", [TLOC, D], BF16, kind="ExternalInput").ap()
    wo = nc.dram_tensor("wo", [D, D], F32, kind="ExternalInput").ap()
    wr = nc.dram_tensor("wr", [D, E], F32, kind="ExternalInput").ap()
    ident = nc.dram_tensor("ident", [128, 128], F32, kind="ExternalInput").ap()
    x2_o = nc.dram_tensor("x2s", [TLOC, D], F32, kind="ExternalOutput").ap()
    ln3_o = nc.dram_tensor("ln3b", [TLOC, D], BF16, kind="ExternalOutput").ap()
    dec_o = nc.dram_tensor("dec", [TLOC, 4], F32, kind="ExternalOutput").ap()

    wo_sb = p.sbuf("wo_sb", [128, 8, D], BF16)
    gfT = [p.sbuf("gfT0", [128, 8, 128], BF16), p.sbuf("gfT1", [128, 8, 128], BF16)]
    xt = p.sbuf("xt", [128, D], F32)
    stat = p.sbuf("stat", [128, 12], F32)
    mv = p.sbuf("mv", [128, 2], F32)
    sd = p.sbuf("sd", [128, 1], F32)
    rstd = p.sbuf("rstd", [128, 1], F32)
    lnf = p.sbuf("lnf", [128, D], F32)
    lnb = p.sbuf("lnbs", [128, D], BF16)
    lnT = p.sbuf("lnT", [128, D], F32)
    identsb = p.sbuf("identsb", [128, 128], F32)
    wr_sb = p.sbuf("wr_sb", [128, 8, E], F32)
    logit = p.sbuf("logit", [128, E], F32)
    scr = {
        "mx1": p.sbuf("mx1", [128, 1], F32),
        "sm": p.sbuf("sm", [128, 1], F32),
        "mx8": p.sbuf("mx8", [128, 8], F32),
        "mi8": p.sbuf("mi8", [128, 8], U32),
        "mif": p.sbuf("mif", [128, 8], F32),
    }
    dec = p.sbuf("dec_sb", [128, 4], F32)
    psO = [p.psum("psO0", [128, 512], F32), p.psum("psO1", [128, 512], F32)]
    ptr = [p.psum("ptr0", [128, 512], F32), p.psum("ptr1", [128, 512], F32)]
    plog = p.psum("plog", [128, 512], F32)

    p.block()
    p.sems()

    ins = nc.sync.dma_start(out=identsb[:], in_=ident[:])
    p.LD.dma(ins)
    ins = nc.sync.dma_start(out=wr_sb[:], in_=wr.rearrange("(c p) e -> p c e", p=128))
    p.LD.dma(ins)
    ins = nc.gpsimd.dma_start(out=wo_sb[:], in_=wo.rearrange("(kc p) n -> p kc n", p=128))
    p.GP.dma(ins)
    p.GP.wait(nc.tensor)
    p.GP.wait(nc.sync)

    for j in range(NTILE_LOC):
        buf = gfT[j % 2]
        p.PE.wait(nc.sync)
        for c in range(8):
            ins = nc.sync.dma_start_transpose(
                buf[:, c, :], gfull[j * 128:(j + 1) * 128, c * 128:(c + 1) * 128])
            p.LD.dma(ins)
        if j > 0:
            p.DVE.wait(nc.sync)
            p.ST.wait(nc.vector)
        ins = nc.sync.dma_start(out=xt[:], in_=x1s[j * 128:(j + 1) * 128, :])
        p.LD.dma(ins)
        p.LD.wait(nc.tensor)
        p.DVE.wait(nc.tensor)
        for n2 in range(2):
            for kc in range(8):
                ins = nc.tensor.matmul(psO[n2][:, :], buf[:, kc, :],
                                       wo_sb[:, kc, n2 * 512:(n2 + 1) * 512],
                                       start=(kc == 0), stop=(kc == 7))
        p.PE.tick(ins)
        p.PE.wait(nc.vector)
        p.LD.wait(nc.vector)
        for n2 in range(2):
            ins = nc.vector.tensor_add(xt[:, n2 * 512:(n2 + 1) * 512],
                                       xt[:, n2 * 512:(n2 + 1) * 512], psO[n2][:, :])
        p.DVE.tick(ins)
        p.DVE.wait(nc.sync)
        ins = nc.sync.dma_start(out=x2_o[j * 128:(j + 1) * 128, :], in_=xt[:])
        p.ST.dma(ins)
        emit_ln_tile(p, xt, stat, mv, sd, rstd, lnf, lnb)
        p.DVE.wait(nc.sync)
        ins = nc.sync.dma_start(out=ln3_o[j * 128:(j + 1) * 128, :], in_=lnb[:])
        p.ST.dma(ins)
        p.DVE.wait(nc.tensor)
        emit_router_tile(p, lnf, identsb, wr_sb, lnT,
                         [ptr[0][:, 0:128], ptr[1][:, 0:128]], plog[:, 0:8],
                         logit, scr, dec)
        p.DVE.wait(nc.sync)
        ins = nc.sync.dma_start(out=dec_o[j * 128:(j + 1) * 128, :], in_=dec[:])
        p.ST.dma(ins)
        p.ST.wait(nc.sync)
    p.ST.wait(nc.gpsimd)
    p.close()
    return p.nc




MAGIC = 0x5A5A17

_EPOCH_STATE = [0]


def _launch_epoch():
    import time
    e = (time.time_ns() >> 8) & 0x3FFFFFFF
    if e <= _EPOCH_STATE[0]:
        e = _EPOCH_STATE[0] + 1
    _EPOCH_STATE[0] = e
    return e | 0x40000000


def build_FUSED(dbg=0):
    p = Prog()
    nc = p.nc
    HT = T // 2
    NTH = HT // 128

    x0h = nc.dram_tensor("x0h", [HT, D], F32, kind="ExternalInput").ap()
    wq8 = nc.dram_tensor("wq8", [D, 2048], F32, kind="ExternalInput").ap()
    wob_i = nc.dram_tensor("wob_i", [D, D], F32, kind="ExternalInput").ap()
    wr = nc.dram_tensor("wr", [D, E], F32, kind="ExternalInput").ap()
    ident = nc.dram_tensor("ident", [128, 128], F32, kind="ExternalInput").ap()
    gam = nc.dram_tensor("gam", [128, 512], F32, kind="ExternalInput").ap()
    bet = nc.dram_tensor("bet", [128, 512], F32, kind="ExternalInput").ap()
    wi4 = nc.dram_tensor("wi4", [4, D, FF], F32, kind="ExternalInput").ap()
    wo4 = nc.dram_tensor("wo4", [4, FF, D], F32, kind="ExternalInput").ap()
    meta = nc.dram_tensor("meta", [1, 8], I32, kind="ExternalInput").ap()
    ridx = nc.dram_tensor("ridx", [128, 180], I32, kind="ExternalInput").ap()
    x3h = nc.dram_tensor("x3h", [HT, D], F32, kind="ExternalOutput").ap()
    if dbg:
        dbg_dec = nc.dram_tensor("dbg_dec", [128, T // 256, 4], F32, kind="ExternalOutput").ap()
        dbg_x1 = nc.dram_tensor("dbg_x1", [HT, D], F32, kind="ExternalOutput").ap()
        dbg_ln = nc.dram_tensor("dbg_ln", [T, D], BF16, kind="ExternalOutput").ap()
        dbg_gt = nc.dram_tensor("dbg_gt", [4, 2 * T, 128], BF16, kind="ExternalOutput").ap()
        dbg_x2 = nc.dram_tensor("dbg_x2", [HT, D], F32, kind="ExternalOutput").ap()
        dbg_et = nc.dram_tensor("dbg_et", [E * CAP, D], BF16, kind="ExternalOutput").ap()
        dbg_dt = nc.dram_tensor("dbg_dt", [E * CAP, D], BF16, kind="ExternalOutput").ap()
        dbg_gi = nc.dram_tensor("dbg_gi", [128, 64], I32, kind="ExternalOutput").ap()
        dbg_gg = nc.dram_tensor("dbg_gg", [128, 64], F32, kind="ExternalOutput").ap()
        dbg_xe = nc.dram_tensor("dbg_xe", [128, 5120], BF16, kind="ExternalOutput").ap()
        dbg_h = nc.dram_tensor("dbg_h", [128, 20480], BF16, kind="ExternalOutput").ap()
        dbg_wib = nc.dram_tensor("dbg_wib", [128, 1024], BF16, kind="ExternalOutput").ap()

    sh_flags = nc.dram_tensor("sh_flags", [8, 64], I32, kind="Internal",
                              addr_space="Shared").ap()
    sh_dec = nc.dram_tensor("sh_dec", [2, 128, 32, 4], F32, kind="Internal",
                            addr_space="Shared").ap()
    sh_dtab = nc.dram_tensor("sh_dtab", [E * CAP, D], BF16, kind="Internal",
                             addr_space="Shared").ap()
    sh_etab = nc.dram_tensor("sh_etab", [E * CAP, D], BF16, kind="Internal",
                             addr_space="Shared").ap()
    sh_ln = nc.dram_tensor("sh_ln", [T, D], BF16, kind="Internal",
                           addr_space="Shared").ap()
    sh_gat = [nc.dram_tensor(f"sh_gat{w}", [2 * T, 128], BF16, kind="Internal",
                             addr_space="Shared").ap() for w in range(4)]

    lnh = nc.dram_tensor("lnh", [HT, D], BF16, kind="Internal").ap()
    x1h = nc.dram_tensor("x1h", [HT, D], F32, kind="Internal").ap()
    gscr = nc.dram_tensor("gscr", [128, 256], F32, kind="Internal").ap()

    scr = bookkeeping_scratch(p)
    xt = p.sbuf("xt", [128, D], F32)
    stat = p.sbuf("stat", [128, 12], F32)
    mv = p.sbuf("mv", [128, 2], F32)
    sd = p.sbuf("sd", [128, 1], F32)
    rstd1 = p.sbuf("rstd1", [128, 1], F32)
    lnf = p.sbuf("lnf", [128, D], F32)
    lnb = p.sbuf("lnbs", [128, D], BF16)
    lnT = p.sbuf("lnT", [128, D], F32)
    identsb = p.sbuf("identsb", [128, 128], F32)
    wr_sb = p.sbuf("wr_sb", [128, 8, E], F32)
    logit = p.sbuf("logit", [128, E], F32)
    rscr = {
        "mx1": p.sbuf("mx1", [128, 1], F32),
        "sm": p.sbuf("sm", [128, 1], F32),
        "mx8": p.sbuf("mx8", [128, 8], F32),
        "mi8": p.sbuf("mi8", [128, 8], U32),
        "mif": p.sbuf("mif", [128, 8], F32),
    }
    dec_acc = p.sbuf("dec_acc", [128, NTH, 4], F32)
    sidx_f = p.sbuf("sidx_f", [128, 128], F32)
    sidx_i = p.sbuf("sidx_i", [128, 128], I32)
    gidx_f = p.sbuf("gidx_f", [128, 128], F32)
    gate2 = p.sbuf("gate2", [128, 128], F32)
    gidx_my = p.sbuf("gidx_my", [128, 64], I32)
    gate_my = p.sbuf("gate_my", [128, 64], F32)
    yk = [p.sbuf("yk0", [128, D], BF16), p.sbuf("yk1", [128, D], BF16)]
    ytmp = p.sbuf("ytmp", [128, D], F32)
    gam_sb = p.sbuf("gam_sb", [128, 512], F32)
    bet_sb = p.sbuf("bet_sb", [128, 512], F32)
    ARENA_N = 60416
    arena = p.sbuf("arena", [128, ARENA_N], BF16)
    xe = arena[:, 0:5120].rearrange("p (c n) -> p c n", c=8)
    h_sb = arena[:, 5120:25600].rearrange("p (c n) -> p c n", c=32)
    wo_sb = arena[:, 25600:58368].rearrange("p (c n) -> p c n", c=32)
    wib = [arena[:, 58368:59392].rearrange("p (c n) -> p c n", c=8),
           arena[:, 59392:60416].rearrange("p (c n) -> p c n", c=8)]
    q_sb = arena[:, 0:8192]
    k_sb = arena[:, 8192:16384]
    v_sb = arena[:, 16384:24576].rearrange("p (c n) -> p c n", c=NKT)
    gate_sb = arena[:, 24576:32768].rearrange("p (c n) -> p c n", c=NKT)
    gfT = arena[:, 33024:34048].rearrange("p (c n) -> p c n", c=8)
    wo8_sb = arena[:, 0:8192].rearrange("p (c n) -> p c n", c=8)
    gb = [arena[:, 34048:34176], arena[:, 34176:34304]]
    wqw = p.sbuf("wqw", [128, 8, 512], BF16)
    lnF = [p.sbuf("lnF0", [128, 8, 512], BF16), p.sbuf("lnF1", [128, 8, 512], BF16)]
    w_sb = [p.sbuf("w_sb0", [128, 1024], BF16), p.sbuf("w_sb1", [128, 1024], BF16)]
    att_fm = p.sbuf("att_fm", [128, 512], BF16)
    att_tok = p.sbuf("att_tok", [128, 512], F32)
    sq_tok = p.sbuf("sq_tok", [128, 512], F32)
    ssum = p.sbuf("ssum", [128, 8], F32)
    ssq = p.sbuf("ssq", [128, 8], F32)
    mu = p.sbuf("mu", [128, 8], F32)
    var = p.sbuf("var", [128, 8], F32)
    rstd8 = p.sbuf("rstd8", [128, 8], F32)
    mu2 = p.sbuf("mu2", [128, 8], F32)
    nrm = p.sbuf("nrm", [128, 512], F32)
    gated = p.sbuf("gated", [128, 4, 128], BF16)
    eob = p.sbuf("eob", [128, D], BF16)
    meta_sb = p.sbuf("meta_sb", [1, 8], I32)
    ridx_sb = p.sbuf("ridx_sb", [128, 180], I32)

    PS0 = p.psum("PS0", [128, 1024], F32)
    PS1 = p.psum("PS1", [128, 1024], F32)
    PS2 = p.psum("PS2", [128, 512], F32)
    PS3 = p.psum("PS3", [128, 512], F32)
    PT0 = p.psum("PT0", [128, 1024], BF16)
    PT1 = p.psum("PT1", [128, 1024], BF16)

    p.block()
    p.sems()
    BAR = Sem(nc, "bar")
    nbar = [0]
    WSEM = [Sem(nc, "wld0"), Sem(nc, "wld1")]
    WOSEM = Sem(nc, "wold")
    TSEM = [Sem(nc, "tld0"), Sem(nc, "tld1")]
    GSEM = [Sem(nc, "gld0"), Sem(nc, "gld1")]

    consts_sb = load_consts(p, ["cols8", "triu_bf", "striu_bf", "ones128x1_bf",
                                "ones1x128", "eye64s_bf", "ident128_bf"])
    for (dst, srcx) in ((identsb, ident), (gam_sb, gam), (bet_sb, bet),
                        (meta_sb, meta), (ridx_sb, ridx)):
        ins = nc.sync.dma_start(out=dst[:], in_=srcx[:])
        p.LD.dma(ins)
    ins = nc.sync.dma_start(out=wr_sb[:], in_=wr.rearrange("(c p) e -> p c e", p=128))
    p.LD.dma(ins)
    p.LD.wait(nc.sync)
    p.LD.wait(nc.gpsimd)

    r_tok_s = nc.sync.alloc_register("r_tok_s")
    nc.sync.load(r_tok_s, meta_sb[0:1, 0:1])
    rv_tok = RuntimeValue(r_tok_s)
    r_gcol_s = nc.sync.alloc_register("r_gcol_s")
    nc.sync.load(r_gcol_s, meta_sb[0:1, 1:2])
    rv_gcol = RuntimeValue(r_gcol_s)
    r_erow_s = nc.sync.alloc_register("r_erow_s")
    nc.sync.load(r_erow_s, meta_sb[0:1, 2:3])
    rv_erow = RuntimeValue(r_erow_s)
    r_bk_s = nc.sync.alloc_register("r_bk_s")
    nc.sync.load(r_bk_s, meta_sb[0:1, 3:4])
    rv_bk = RuntimeValue(r_bk_s)
    r_h_s = nc.sync.alloc_register("r_h_s")
    nc.sync.load(r_h_s, meta_sb[0:1, 6:7])
    rv_h = RuntimeValue(r_h_s)
    r_my_g = nc.gpsimd.alloc_register("r_my_g")
    nc.gpsimd.load(r_my_g, meta_sb[0:1, 4:5])
    rv_myf = RuntimeValue(r_my_g)
    r_pa_g = nc.gpsimd.alloc_register("r_pa_g")
    nc.gpsimd.load(r_pa_g, meta_sb[0:1, 5:6])
    rv_paf = RuntimeValue(r_pa_g)
    bc_dtab = nc.gpsimd.alloc_register("bc_dtab")
    nc.gpsimd.reg_mov(bc_dtab, E * CAP - 1)
    bc_ln = nc.gpsimd.alloc_register("bc_ln")
    nc.gpsimd.reg_mov(bc_ln, T - 1)
    bc_gat = nc.gpsimd.alloc_register("bc_gat")
    nc.gpsimd.reg_mov(bc_gat, 2 * T - 1)

    spin_order = [nc.scalar, nc.tensor, nc.sync, nc.scalar, nc.tensor,
                  nc.vector, nc.scalar, nc.tensor, nc.sync, nc.vector]
    spin_state = {}

    def get_spin(eng):
        key = str(eng.engine)
        if key not in spin_state:
            p.LD.wait(eng)
            r_myc = eng.alloc_register("bm" + key[-3:])
            eng.load(r_myc, meta_sb[0:1, 4:5])
            r_pac = eng.alloc_register("bp" + key[-3:])
            eng.load(r_pac, meta_sb[0:1, 5:6])
            f = eng.alloc_register("bf" + key[-3:])
            ap_myc = bass.AP(sh_flags.tensor, RuntimeValue(r_myc), [[64, 1], [1, 1]])
            ap_pac = bass.AP(sh_flags.tensor, RuntimeValue(r_pac), [[64, 1], [1, 1]])
            spin_state[key] = (ap_myc, ap_pac, f)
        return spin_state[key]

    def emit_init():
        ap_myc, ap_pac, f = get_spin(nc.vector)
        nc.vector.store(ap_myc, 0)
        r_ep = nc.vector.alloc_register("r_ep")
        nc.vector.load(r_ep, meta_sb[0:1, 7:8])
        r_myi = nc.vector.alloc_register("r_myi")
        nc.vector.load(r_myi, meta_sb[0:1, 4:5])
        nc.vector.reg_add(r_myi, r_myi, 32)
        r_pai = nc.vector.alloc_register("r_pai")
        nc.vector.load(r_pai, meta_sb[0:1, 5:6])
        nc.vector.reg_add(r_pai, r_pai, 32)
        ap_myi = bass.AP(sh_flags.tensor, RuntimeValue(r_myi), [[64, 1], [1, 1]])
        ap_pai = bass.AP(sh_flags.tensor, RuntimeValue(r_pai), [[64, 1], [1, 1]])
        nc.vector.store(ap_myi, RuntimeValue(r_ep))
        r_di = nc.vector.alloc_register("r_di")
        nc.vector.load(f, ap_pai)
        nc.vector.reg_alu(r_di, RuntimeValue(f), RuntimeValue(r_ep),
                          mybir.AluOpType.subtract)
        with nc.vector.While(RuntimeValue(r_di) != 0):
            nc.vector.load(f, ap_pai)
            nc.vector.reg_alu(r_di, RuntimeValue(f), RuntimeValue(r_ep),
                              mybir.AluOpType.subtract)
        nc.vector.sem_inc(BAR.h, 1)
        BAR.v += 1

    def barrier():
        nbar[0] += 1
        ph = nbar[0]
        if ph == 1:
            emit_init()
        eng = spin_order[ph - 1]
        ap_myc, ap_pac, f = get_spin(eng)
        BAR.wait(eng)
        p.ST.wait(eng)
        p.GP.wait(eng)
        eng.store(ap_myc, ph)
        eng.load(f, ap_pac)
        with eng.While(RuntimeValue(f) < ph):
            eng.load(f, ap_pac)
        eng.sem_inc(BAR.h, 1)
        BAR.v += 1

    def local_join():
        for eng in (nc.sync, nc.tensor, nc.scalar, nc.vector, nc.gpsimd):
            for s in (p.LD, p.ST, p.PE, p.ACT, p.DVE, p.GP,
                      WSEM[0], WSEM[1], WOSEM, TSEM[0], TSEM[1],
                      GSEM[0], GSEM[1]):
                s.wait(eng)
            BAR.wait(eng)

    def ln_router_phase(xsrc_ap):
        for j in range(NTH):
            if j > 0:
                p.DVE.wait(nc.sync)
                p.PE.wait(nc.sync)
            ins = nc.sync.dma_start(out=xt[:], in_=xsrc_ap[j * 128:(j + 1) * 128, :])
            p.LD.dma(ins)
            p.LD.wait(nc.vector)
            if j > 0:
                p.PE.wait(nc.vector)
            emit_ln_tile(p, xt, stat, mv, sd, rstd1, lnf, lnb)
            p.DVE.wait(nc.sync)
            p.ST.wait(nc.sync)
            ins = nc.sync.dma_start(out=lnh[j * 128:(j + 1) * 128, :], in_=lnb[:])
            p.ST.dma(ins)
            p.DVE.wait(nc.tensor)
            emit_router_tile(p, lnf, identsb, wr_sb, lnT,
                             [PS2[:, 0:128], PS3[:, 0:128]], PS1[0:128, 0:8],
                             logit, rscr, dec_acc[:, j, :])
        p.DVE.wait(nc.sync)
        dec_dst = bass.AP(sh_dec.tensor, rv_h * (128 * 32 * 4),
                          [[128, 128], [4, 32], [1, 4]])
        ins = nc.sync.dma_start(out=dec_dst, in_=dec_acc[:])
        p.ST.dma(ins)
        p.ST.wait(nc.gpsimd)
        barrier()

    ln_router_phase(x0h)
    if dbg:
        p.DVE.wait(nc.sync)
        ins = nc.sync.dma_start(out=dbg_dec[:], in_=dec_acc[:])
        p.ST.dma(ins)

    def bookkeep_and_scatter():
        local_join()
        dec_loads = [(hh * 32, sh_dec[hh]) for hh in range(2)]
        emit_bookkeeping(p, dec_loads, scr, consts_sb, PS0, PS1, PS2)
        vflag, slots = scr["vflag"], scr["slots"]
        p.DVE.wait(nc.vector)
        nc.vector.tensor_scalar_add(sidx_f[:], slots[:], -BIGIDX)
        ins = nc.vector.tensor_tensor(sidx_f[:], sidx_f[:], vflag[:], OP.mult)
        p.DVE.tick(ins)
        p.DVE.wait(nc.vector)
        ins = nc.vector.tensor_scalar_add(sidx_f[:], sidx_f[:], BIGIDX)
        p.DVE.tick(ins)
        p.DVE.wait(nc.vector)
        nc.vector.tensor_scalar_add(gidx_f[:], slots[:], -BIGIDX)
        ins = nc.vector.tensor_tensor(gidx_f[:], gidx_f[:], vflag[:], OP.mult)
        p.DVE.tick(ins)
        p.DVE.wait(nc.vector)
        ins = nc.vector.tensor_scalar_add(gidx_f[:], gidx_f[:], BIGIDX)
        p.DVE.tick(ins)
        nc.vector.tensor_copy(gate2[:].rearrange("p (j k) -> p j k", k=2),
                              scr["decsb"][:, :, 2:4])
        ins = nc.vector.tensor_tensor(gate2[:], gate2[:], vflag[:], OP.mult)
        p.DVE.tick(ins)
        p.DVE.wait(nc.sync)
        ins = nc.sync.dma_start(out=gscr[:, 0:128], in_=gidx_f[:])
        p.ST.dma(ins)
        ins = nc.sync.dma_start(out=gscr[:, 128:256], in_=gate2[:])
        p.ST.dma(ins)
        p.ST.wait(nc.sync)
        gidx_ap = bass.AP(gscr.tensor, rv_bk, [[256, 128], [1, 64]])
        gate_ap = bass.AP(gscr.tensor, rv_bk + 128, [[256, 128], [1, 64]])
        ins = nc.sync.dma_start(out=gidx_f[:, 0:64], in_=gidx_ap)
        p.LD.dma(ins)
        ins = nc.sync.dma_start(out=gate_my[:], in_=gate_ap)
        p.LD.dma(ins)
        p.LD.wait(nc.vector)
        ins = nc.vector.tensor_copy(gidx_my[:], gidx_f[:, 0:64])
        p.DVE.tick(ins)
        p.DVE.wait(nc.sync)
        ins = nc.sync.dma_start(out=gscr[:, 0:128], in_=sidx_f[:])
        p.ST.dma(ins)
        p.ST.wait(nc.sync)
        sidx_ap = bass.AP(gscr.tensor, rv_bk, [[256, 128], [1, 64]])
        ins = nc.sync.dma_start(out=sidx_f[:, 0:64], in_=sidx_ap)
        p.LD.dma(ins)
        p.LD.wait(nc.vector)
        ins = nc.vector.tensor_copy(sidx_i[:, 0:64], sidx_f[:, 0:64])
        p.DVE.tick(ins)
        p.DVE.wait(nc.gpsimd)
        p.DVE.wait(nc.sync)
        for jj in range(NTH):
            buf = yk[jj % 2]
            if jj >= 2:
                p.GP.wait(nc.sync)
            ins = nc.sync.dma_start(out=buf[:], in_=lnh[jj * 128:(jj + 1) * 128, :])
            p.LD.dma(ins)
            p.LD.wait(nc.gpsimd)
            for k in range(2):
                col = jj * 2 + k
                ins = nc.gpsimd.indirect_dma_start(
                    out=sh_dtab[:, :],
                    out_offset=IndirectOffsetOnAxis(sidx_i[:, col:col + 1], axis=0),
                    in_=buf[:, :], in_offset=None,
                    bounds_check=bc_dtab, oob_is_err=False)
                p.GP.dma(ins)
        p.GP.wait(nc.gpsimd)
        barrier()

    bookkeep_and_scatter()
    if dbg:
        ins = nc.gpsimd.dma_start(out=dbg_dt[:], in_=sh_dtab[:])
        p.GP.dma(ins)
        p.DVE.wait(nc.gpsimd)
        ins = nc.gpsimd.dma_start(out=dbg_gi[:], in_=gidx_my[:])
        p.GP.dma(ins)
        ins = nc.gpsimd.dma_start(out=dbg_gg[:], in_=gate_my[:])
        p.GP.dma(ins)
        p.GP.wait(nc.gpsimd)

    def expert_phase():
        local_join()
        for ei in range(4):
            for t5 in range(5):
                p.PE.wait(nc.gpsimd)
                ins = nc.gpsimd.indirect_dma_start(
                    out=yk[0][:, :], out_offset=None, in_=sh_dtab[:, :],
                    in_offset=IndirectOffsetOnAxis(
                        ridx_sb[:, ei * 5 + t5:ei * 5 + t5 + 1], axis=0),
                    bounds_check=bc_dtab, oob_is_err=False)
                p.GP.dma(ins)
                p.GP.wait(nc.tensor)
                p.ACT.wait(nc.tensor)
                for c8 in range(8):
                    ins = nc.tensor.transpose(
                        PT0[:, 0:128], yk[0][:, c8 * 128:(c8 + 1) * 128],
                        consts_sb["ident128_bf"][:])
                    p.PE.tick(ins)
                    p.PE.wait(nc.scalar)
                    ins = nc.scalar.copy(xe[:, c8, t5 * 128:(t5 + 1) * 128],
                                         PT0[:, 0:128])
                    p.ACT.tick(ins)
                    p.ACT.wait(nc.tensor)
            p.ACT.wait(nc.sync)
            fc_pe = []
            relu_act = []
            for fc in range(32):
                wbuf = wib[fc % 2]
                if fc >= 2:
                    p.PE.wait_val(nc.gpsimd, fc_pe[fc - 2])
                else:
                    p.PE.wait(nc.gpsimd)
                ins = nc.gpsimd.dma_start(
                    out=wbuf[:], in_=wi4[ei, :, fc * 128:(fc + 1) * 128]
                    .rearrange("(kc p) m -> p kc m", p=128))
                WSEM[fc % 2].dma(ins)
                wb_v = WSEM[fc % 2].v
                if 12 <= fc < 28 and fc % 2 == 0:
                    ci = (fc - 12) // 2
                    if ci == 0 and s5_state[0] is not None:
                        p.PE.wait_val(nc.gpsimd, s5_state[0])
                    ins = nc.gpsimd.dma_start(
                        out=wo_sb[:, 4 * ci:4 * ci + 4, :],
                        in_=wo4[ei, 512 * ci:512 * (ci + 1), :]
                        .rearrange("(kc p) n -> p kc n", p=128))
                    WOSEM.dma(ins)
                WSEM[fc % 2].wait_val(nc.tensor, wb_v)
                ps = [PS0, PS1][fc % 2]
                if fc >= 2:
                    p.ACT.wait_val(nc.tensor, relu_act[fc - 2])
                else:
                    p.ACT.wait(nc.tensor)
                for (n0, nw) in ((0, 512), (512, 128)):
                    for kc in range(8):
                        ins = nc.tensor.matmul(
                            ps[:, n0:n0 + nw], wbuf[:, kc, :],
                            xe[:, kc, n0:n0 + nw],
                            start=(kc == 0), stop=(kc == 7))
                p.PE.tick(ins)
                fc_pe.append(p.PE.v)
                p.PE.wait(nc.scalar)
                ins = nc.scalar.activation(h_sb[:, fc, :], ps[:, 0:CAP], AF.Relu)
                p.ACT.tick(ins)
                relu_act.append(p.ACT.v)
            WOSEM.wait(nc.tensor)
            p.ACT.wait(nc.tensor)
            for s5 in range(5):
                for n2 in range(2):
                    ps2 = [PS2, PS3][(s5 * 2 + n2) % 2]
                    p.ACT.wait(nc.tensor)
                    for kc2 in range(32):
                        ins = nc.tensor.matmul(
                            ps2[:, 0:512],
                            h_sb[:, kc2, s5 * 128:(s5 + 1) * 128],
                            wo_sb[:, kc2, n2 * 512:(n2 + 1) * 512],
                            start=(kc2 == 0), stop=(kc2 == 31))
                    p.PE.tick(ins)
                    p.PE.wait(nc.scalar)
                    p.GP.wait(nc.scalar)
                    ins = nc.scalar.copy(eob[:, n2 * 512:(n2 + 1) * 512], ps2[:, 0:512])
                    p.ACT.tick(ins)
                p.ACT.wait(nc.gpsimd)
                ins = nc.gpsimd.indirect_dma_start(
                    out=sh_etab[:, :],
                    out_offset=IndirectOffsetOnAxis(
                        ridx_sb[:, ei * 5 + s5:ei * 5 + s5 + 1], axis=0),
                    in_=eob[:, :], in_offset=None,
                    bounds_check=bc_dtab, oob_is_err=False)
                p.GP.dma(ins)
            s5_state[0] = p.PE.v
        barrier()

    s5_state = [None]
    expert_phase()
    if dbg:
        ins = nc.gpsimd.dma_start(out=dbg_et[:], in_=sh_etab[:])
        p.GP.dma(ins)
        ins = nc.gpsimd.dma_start(out=dbg_xe[:], in_=arena[:, 0:5120])
        p.GP.dma(ins)
        ins = nc.gpsimd.dma_start(out=dbg_h[:], in_=arena[:, 5120:25600])
        p.GP.dma(ins)
        ins = nc.gpsimd.dma_start(out=dbg_wib[:], in_=arena[:, 59392:60416])
        p.GP.dma(ins)
        p.GP.wait(nc.gpsimd)

    def combine_phase(xsrc_ap, final):
        local_join()
        bcz = bc_dtab
        for jj in range(NTH):
            if jj > 0:
                p.DVE.wait(nc.sync)
                p.ST.wait(nc.vector)
                p.GP.wait(nc.vector)
            ins = nc.sync.dma_start(out=xt[:], in_=xsrc_ap[jj * 128:(jj + 1) * 128, :])
            p.LD.dma(ins)
            for k in range(2):
                ins = nc.vector.memset(yk[k][:], 0.0)
                p.DVE.tick(ins)
            p.DVE.wait(nc.gpsimd)
            for k in range(2):
                ins = nc.gpsimd.indirect_dma_start(
                    out=yk[k][:, :], out_offset=None, in_=sh_etab[:, :],
                    in_offset=IndirectOffsetOnAxis(
                        gidx_my[:, 2 * jj + k:2 * jj + k + 1], axis=0),
                    bounds_check=bcz, oob_is_err=False)
                p.GP.dma(ins)
            p.GP.wait(nc.vector)
            p.LD.wait(nc.vector)
            ins = nc.vector.tensor_scalar(ytmp[:], yk[0][:],
                                          gate_my[:, 2 * jj:2 * jj + 1],
                                          None, OP.mult)
            p.DVE.tick(ins)
            p.DVE.wait(nc.vector)
            ins = nc.vector.scalar_tensor_tensor(ytmp[:], yk[1][:],
                                                 gate_my[:, 2 * jj + 1:2 * jj + 2],
                                                 ytmp[:], OP.mult, OP.add)
            p.DVE.tick(ins)
            p.DVE.wait(nc.vector)
            ins = nc.vector.scalar_tensor_tensor(xt[:], ytmp[:], 0.5, xt[:],
                                                 OP.mult, OP.add)
            p.DVE.tick(ins)
            p.DVE.wait(nc.sync)
            if final:
                ins = nc.sync.dma_start(out=x3h[jj * 128:(jj + 1) * 128, :], in_=xt[:])
                p.ST.dma(ins)
            else:
                ins = nc.sync.dma_start(out=x1h[jj * 128:(jj + 1) * 128, :], in_=xt[:])
                p.ST.dma(ins)
                if jj > 0:
                    p.PE.wait(nc.vector)
                emit_ln_tile(p, xt, stat, mv, sd, rstd1, lnf, lnb)
                p.DVE.wait(nc.gpsimd)
                ins = nc.gpsimd.indirect_dma_start(
                    out=sh_ln[:, :],
                    out_offset=IndirectOffsetOnAxis(
                        ridx_sb[:, 20 + jj:21 + jj], axis=0),
                    in_=lnb[:, :], in_offset=None,
                    bounds_check=bc_ln, oob_is_err=False)
                p.GP.dma(ins)
        p.ST.wait(nc.gpsimd)
        if not final:
            barrier()

    combine_phase(x0h, False)
    if dbg:
        ins = nc.gpsimd.dma_start(out=dbg_x1[:], in_=x1h[:])
        p.GP.dma(ins)
        ins = nc.gpsimd.dma_start(out=dbg_ln[:], in_=sh_ln[:])
        p.GP.dma(ins)
        p.GP.wait(nc.gpsimd)

    local_join()
    SCALE = 1.0 / 8.0
    for wave in range(4):
        local_join()
        ins = nc.gpsimd.dma_start(
            out=wqw[:], in_=wq8[:, wave * 512:(wave + 1) * 512]
            .rearrange("(c p) m -> p c m", p=128))
        p.GP.dma(ins)
        p.GP.wait(nc.sync)
        p.GP.wait(nc.tensor)
        tc_pe = []
        for tc in range(NQC):
            buf = lnF[tc % 2]
            if tc >= 2:
                p.PE.wait_val(nc.sync, tc_pe[tc - 2])
            else:
                p.PE.wait(nc.sync)
            for c in range(8):
                ins = nc.sync.dma_start_transpose(
                    buf[:, c, :],
                    sh_ln[tc * 512:(tc + 1) * 512, c * 128:(c + 1) * 128])
                TSEM[tc % 2].dma(ins)
            TSEM[tc % 2].wait(nc.tensor)
            p.ACT.wait(nc.tensor)
            for mi in range(2):
                ps = [PS0, PS1][mi]
                for c in range(8):
                    ins = nc.tensor.matmul(ps[:, 0:512],
                                           wqw[:, c, mi * 128:(mi + 1) * 128],
                                           buf[:, c, :], start=(c == 0), stop=(c == 7))
            p.PE.tick(ins)
            p.PE.wait(nc.scalar)
            nc.scalar.activation(q_sb[:, tc * 512:(tc + 1) * 512], PS0[:, 0:512], AF.Silu)
            ins = nc.scalar.activation(k_sb[:, tc * 512:(tc + 1) * 512], PS1[:, 0:512], AF.Silu)
            p.ACT.tick(ins)
            for sub in range(4):
                tkt = tc * 4 + sub
                p.ACT.wait(nc.tensor)
                for c in range(8):
                    nc.tensor.matmul(PS2[:, 0:128],
                                     buf[:, c, sub * 128:(sub + 1) * 128],
                                     wqw[:, c, 256:384], start=(c == 0), stop=(c == 7))
                for c in range(8):
                    ins = nc.tensor.matmul(PS3[:, 0:128],
                                           buf[:, c, sub * 128:(sub + 1) * 128],
                                           wqw[:, c, 384:512],
                                           start=(c == 0), stop=(c == 7))
                p.PE.tick(ins)
                p.PE.wait(nc.scalar)
                nc.scalar.activation(gate_sb[:, tkt, :], PS2[:, 0:128], AF.Silu)
                ins = nc.scalar.activation(v_sb[:, tkt, :], PS3[:, 0:128], AF.Silu)
                p.ACT.tick(ins)
            tc_pe.append(p.PE.v)
        for qc in range(NQC):
            silu_vals = []
            qk_vals = []
            act_base = p.ACT.v

            def emit_qk(kt):
                par = kt % 2
                ps = [PS0, PS1][par]
                p.ACT.wait_val(nc.tensor,
                               silu_vals[kt - 2] if kt >= 2 else act_base)
                nc.tensor.matmul(ps[:, 0:512],
                                 k_sb[0:64, kt * 128:(kt + 1) * 128],
                                 q_sb[0:64, qc * 512:(qc + 1) * 512],
                                 start=True, stop=True, tile_position=(0, 0))
                ins = nc.tensor.matmul(ps[:, 512:1024],
                                       k_sb[64:128, kt * 128:(kt + 1) * 128],
                                       q_sb[64:128, qc * 512:(qc + 1) * 512],
                                       start=True, stop=True,
                                       tile_position=(64, 0))
                p.PE.tick(ins)
                qk_vals.append(p.PE.v)

            emit_qk(0)
            for kt in range(NKT):
                par = kt % 2
                p.PE.wait_val(nc.scalar, qk_vals[kt])
                ins = nc.scalar.activation(w_sb[par][:], [PS0, PS1][par][:, :],
                                           AF.Silu, scale=SCALE)
                p.ACT.tick(ins)
                silu_vals.append(p.ACT.v)
                if kt + 1 < NKT:
                    emit_qk(kt + 1)
                p.ACT.wait_val(nc.tensor, silu_vals[kt])
                nc.tensor.matmul(PS2[0:64, :],
                                 v_sb[:, kt, 0:64],
                                 w_sb[par][:, 0:512], start=(kt == 0),
                                 stop=(kt == NKT - 1), tile_position=(0, 0))
                ins = nc.tensor.matmul(PS3[64:128, :],
                                       v_sb[:, kt, 64:128],
                                       w_sb[par][:, 512:1024], start=(kt == 0),
                                       stop=(kt == NKT - 1), tile_position=(0, 64))
                p.PE.tick(ins)
            p.PE.wait(nc.scalar)
            nc.scalar.copy(att_fm[0:64, :], PS2[0:64, :])
            ins = nc.scalar.copy(att_fm[64:128, :], PS3[64:128, :])
            p.ACT.tick(ins)
            p.ACT.wait(nc.tensor)
            for sub in range(4):
                for hh in range(2):
                    dstp = PT0 if hh == 0 else PT1
                    ins = nc.tensor.transpose(
                        dstp[:, sub * 64:(sub + 1) * 64],
                        att_fm[hh * 64:(hh + 1) * 64, sub * 128:(sub + 1) * 128],
                        consts_sb["eye64s_bf"][hh * 64:(hh + 1) * 64, :])
            p.PE.tick(ins)
            p.PE.wait(nc.scalar)
            nc.scalar.copy(
                att_tok[:].rearrange("p (s h d) -> p s h d", s=4, h=2)[:, :, 0, :],
                PT0[:, 0:256].rearrange("p (s d) -> p s d", s=4))
            ins = nc.scalar.copy(
                att_tok[:].rearrange("p (s h d) -> p s h d", s=4, h=2)[:, :, 1, :],
                PT1[:, 0:256].rearrange("p (s d) -> p s d", s=4))
            p.ACT.tick(ins)
            p.ACT.wait(nc.vector)
            ins = nc.vector.tensor_tensor(sq_tok[:], att_tok[:], att_tok[:],
                                          OP.mult)
            p.DVE.tick(ins)
            nc.vector.reduce_sum(ssum[:],
                                 att_tok[:].rearrange("p (s e) -> p s e", e=64),
                                 axis=mybir.AxisListType.X)
            ins = nc.vector.reduce_sum(ssq[:],
                                       sq_tok[:].rearrange("p (s e) -> p s e", e=64),
                                       axis=mybir.AxisListType.X)
            p.DVE.tick(ins)
            p.DVE.wait(nc.vector)
            nc.vector.tensor_scalar(mu[:], ssum[:], 1.0 / HD, None, OP.mult)
            ins = nc.vector.tensor_scalar(var[:], ssq[:], 1.0 / HD, None, OP.mult)
            p.DVE.tick(ins)
            p.DVE.wait(nc.vector)
            ins = nc.vector.tensor_tensor(mu2[:], mu[:], mu[:], OP.mult)
            p.DVE.tick(ins)
            p.DVE.wait(nc.vector)
            ins = nc.vector.tensor_tensor(var[:], var[:], mu2[:], OP.subtract)
            p.DVE.tick(ins)
            p.DVE.wait(nc.vector)
            ins = nc.vector.tensor_scalar_add(var[:], var[:], EPS)
            p.DVE.tick(ins)
            p.DVE.wait(nc.scalar)
            ins = nc.scalar.activation(var[:], var[:], AF.Sqrt)
            p.ACT.tick(ins)
            p.ACT.wait(nc.vector)
            ins = nc.vector.reciprocal(rstd8[:], var[:])
            p.DVE.tick(ins)
            p.DVE.wait(nc.vector)
            for c8 in range(8):
                ins = nc.vector.tensor_scalar(nrm[:, c8 * 64:(c8 + 1) * 64],
                                              att_tok[:, c8 * 64:(c8 + 1) * 64],
                                              mu[:, c8:c8 + 1], rstd8[:, c8:c8 + 1],
                                              OP.subtract, OP.mult)
            p.DVE.tick(ins)
            p.DVE.wait(nc.vector)
            ins = nc.vector.tensor_tensor(nrm[:], nrm[:], gam_sb[:], OP.mult)
            p.DVE.tick(ins)
            p.DVE.wait(nc.vector)
            ins = nc.vector.tensor_tensor(nrm[:], nrm[:], bet_sb[:], OP.add)
            p.DVE.tick(ins)
            p.DVE.wait(nc.vector)
            p.GP.wait(nc.vector)
            ins = nc.vector.tensor_tensor(
                gated[:], nrm[:].rearrange("p (s c) -> p s c", c=128),
                gate_sb[:, qc * 4:qc * 4 + 4, :], OP.mult)
            p.DVE.tick(ins)
            p.DVE.wait(nc.gpsimd)
            for sub in range(4):
                ins = nc.gpsimd.indirect_dma_start(
                    out=sh_gat[wave][:, :],
                    out_offset=IndirectOffsetOnAxis(
                        ridx_sb[:, 52 + qc * 4 + sub:53 + qc * 4 + sub], axis=0),
                    in_=gated[:, sub, :], in_offset=None,
                    bounds_check=bc_gat, oob_is_err=False)
                p.GP.dma(ins)
    barrier()
    if dbg:
        for w in range(4):
            ins = nc.gpsimd.dma_start(out=dbg_gt[w], in_=sh_gat[w][:])
            p.GP.dma(ins)
        p.GP.wait(nc.gpsimd)

    local_join()
    ins = nc.gpsimd.dma_start(out=wo8_sb,
                              in_=wob_i.rearrange("(kc p) n -> p kc n", p=128))
    p.GP.dma(ins)
    p.GP.wait(nc.tensor)
    p.GP.wait(nc.sync)
    op_tv = []
    op_cv = []
    for jj in range(NTH):
        for c in range(8):
            k = jj * 8 + c
            hp, cc = c // 4, c % 4
            if k >= 2:
                p.PE.wait_val(nc.gpsimd, op_tv[k - 2])
            else:
                p.PE.wait(nc.gpsimd)
            ins = nc.gpsimd.indirect_dma_start(
                out=gb[c % 2][:, :], out_offset=None, in_=sh_gat[cc][:, :],
                in_offset=IndirectOffsetOnAxis(
                    ridx_sb[:, 116 + hp * 32 + jj:117 + hp * 32 + jj], axis=0),
                bounds_check=bc_gat, oob_is_err=False)
            GSEM[c % 2].dma(ins)
            GSEM[c % 2].wait(nc.tensor)
            if k >= 2:
                p.ACT.wait_val(nc.tensor, op_cv[k - 2])
            else:
                p.ACT.wait(nc.tensor)
            ins = nc.tensor.transpose(
                [PT0, PT1][c % 2][:, 0:128], gb[c % 2][:, :],
                consts_sb["ident128_bf"][:])
            p.PE.tick(ins)
            op_tv.append(p.PE.v)
            p.PE.wait(nc.scalar)
            ins = nc.scalar.copy(gfT[:, c, :], [PT0, PT1][c % 2][:, 0:128])
            p.ACT.tick(ins)
            op_cv.append(p.ACT.v)
        if jj > 0:
            p.DVE.wait(nc.sync)
            p.ST.wait(nc.vector)
        ins = nc.sync.dma_start(out=xt[:], in_=x1h[jj * 128:(jj + 1) * 128, :])
        p.LD.dma(ins)
        p.LD.wait(nc.tensor)
        p.DVE.wait(nc.tensor)
        p.ACT.wait(nc.tensor)
        for n2 in range(2):
            for kc in range(8):
                ins = nc.tensor.matmul([PS2, PS3][n2][:, :], gfT[:, kc, :],
                                       wo8_sb[:, kc, n2 * 512:(n2 + 1) * 512],
                                       start=(kc == 0), stop=(kc == 7))
        p.PE.tick(ins)
        p.PE.wait(nc.vector)
        p.LD.wait(nc.vector)
        for n2 in range(2):
            ins = nc.vector.tensor_add(xt[:, n2 * 512:(n2 + 1) * 512],
                                       xt[:, n2 * 512:(n2 + 1) * 512],
                                       [PS2, PS3][n2][:, :])
        p.DVE.tick(ins)
        p.DVE.wait(nc.sync)
        ins = nc.sync.dma_start(out=x1h[jj * 128:(jj + 1) * 128, :], in_=xt[:])
        p.ST.dma(ins)
        if jj > 0:
            p.PE.wait(nc.vector)
        emit_ln_tile(p, xt, stat, mv, sd, rstd1, lnf, lnb)
        p.DVE.wait(nc.sync)
        p.ST.wait(nc.sync)
        ins = nc.sync.dma_start(out=lnh[jj * 128:(jj + 1) * 128, :], in_=lnb[:])
        p.ST.dma(ins)
        p.DVE.wait(nc.tensor)
        emit_router_tile(p, lnf, identsb, wr_sb, lnT,
                         [PS2[:, 0:128], PS3[:, 0:128]], PS1[0:128, 0:8],
                         logit, rscr, dec_acc[:, jj, :])
    p.DVE.wait(nc.sync)
    dec_dst = bass.AP(sh_dec.tensor, rv_h * (128 * 32 * 4),
                      [[128, 128], [4, 32], [1, 4]])
    ins = nc.sync.dma_start(out=dec_dst, in_=dec_acc[:])
    p.ST.dma(ins)
    p.ST.wait(nc.gpsimd)
    barrier()

    if dbg:
        ins = nc.gpsimd.dma_start(out=dbg_x2[:], in_=x1h[:])
        p.GP.dma(ins)
        p.GP.wait(nc.gpsimd)
    bookkeep_and_scatter()
    expert_phase()
    combine_phase(x1h, True)

    p.ST.wait(nc.gpsimd)
    p.close()
    return p.nc


def kernel_fused(L0_attn_hidden_states, L0_attn_w_qkuv, L0_attn_w_out,
                 L0_attn_ln_gamma, L0_attn_ln_beta, L0_ffn_w_router,
                 L0_ffn_wi, L0_ffn_wo):
    x0 = np.ascontiguousarray(np.asarray(L0_attn_hidden_states, dtype=np.float32)
                              ).reshape(GG, T, D)
    wq = np.ascontiguousarray(np.asarray(L0_attn_w_qkuv, dtype=np.float32))
    w_out = np.ascontiguousarray(np.asarray(L0_attn_w_out, dtype=np.float32))
    gamma = np.asarray(L0_attn_ln_gamma, dtype=np.float32)
    beta = np.asarray(L0_attn_ln_beta, dtype=np.float32)
    wr = np.ascontiguousarray(np.asarray(L0_ffn_w_router, dtype=np.float32))
    wi = np.ascontiguousarray(np.asarray(L0_ffn_wi, dtype=np.float32))
    wo = np.ascontiguousarray(np.asarray(L0_ffn_wo, dtype=np.float32))

    consts = host_consts()
    cns = {"c_" + n: consts[n] for n in
           ("cols8", "triu_bf", "striu_bf", "ones128x1_bf", "ones1x128",
            "eye64s_bf", "ident128_bf")}
    epoch = _launch_epoch()
    in_maps = []
    for core in range(8):
        g = (core // 2) % 2
        h = core % 2
        cols = []
        for wave in range(4):
            h0 = 8 * h + 2 * wave
            cols += list(range(h0 * 64, (h0 + 2) * 64))
            cols += list(range(1024 + h0 * 64, 1024 + (h0 + 2) * 64))
            cols += list(range(3072 + h0 * 64, 3072 + (h0 + 2) * 64))
            cols += list(range(2048 + h0 * 64, 2048 + (h0 + 2) * 64))
        meta = np.array([[4096 * h, 512 * h, 2560 * h, 64 * h,
                          (2 + h) * 64, (3 - h) * 64, h, epoch]],
                        dtype=np.int32)
        pp = np.arange(128, dtype=np.int32)
        rix = np.zeros((128, 180), np.int32)
        for t in range(20):
            rix[:, t] = 2560 * h + 128 * t + pp
        for jj in range(32):
            rix[:, 20 + jj] = 4096 * h + 128 * jj + pp
        for qc in range(16):
            for sub in range(4):
                rix[:, 52 + qc * 4 + sub] = 8192 * h + qc * 512 + sub * 128 + pp
        for hp in range(2):
            for jj in range(32):
                rix[:, 116 + hp * 32 + jj] = 8192 * hp + 4096 * h + 128 * jj + pp
        in_maps.append({
            **cns,
            "x0h": np.ascontiguousarray(x0[g, h * 4096:(h + 1) * 4096]),
            "wq8": np.ascontiguousarray(wq[:, cols]),
            "wob_i": w_out, "wr": wr, "ident": consts["ident"],
            "gam": np.ascontiguousarray(np.tile(gamma, (128, 8))),
            "bet": np.ascontiguousarray(np.tile(beta, (128, 8))),
            "wi4": wi[4 * h:4 * h + 4], "wo4": wo[4 * h:4 * h + 4],
            "meta": meta, "ridx": rix,
        })
    nc = _get("FUSED", build_FUSED)
    res = run_bass_kernel_spmd(nc, in_maps, core_ids=list(range(8)))
    out = np.empty((GG, T, D), dtype=np.float32)
    for core in range(4):
        g = core // 2
        h = core % 2
        out[g, h * 4096:(h + 1) * 4096] = np.asarray(res.results[core]["x3h"])
    return out



def build_A():
    p = Prog()
    nc = p.nc
    x0s = nc.dram_tensor("x0s", [TLOC, D], F32, kind="ExternalInput").ap()
    wr = nc.dram_tensor("wr", [D, E], F32, kind="ExternalInput").ap()
    ident = nc.dram_tensor("ident", [128, 128], F32, kind="ExternalInput").ap()
    lnb_o = nc.dram_tensor("lnb", [TLOC, D], BF16, kind="ExternalOutput").ap()
    dec_o = nc.dram_tensor("dec", [TLOC, 4], F32, kind="ExternalOutput").ap()

    xt = p.sbuf("xt", [128, D], F32)
    stat = p.sbuf("stat", [128, 12], F32)
    mv = p.sbuf("mv", [128, 2], F32)
    sd = p.sbuf("sd", [128, 1], F32)
    rstd = p.sbuf("rstd", [128, 1], F32)
    lnf = p.sbuf("lnf", [128, D], F32)
    lnb = p.sbuf("lnbs", [128, D], BF16)
    lnT = p.sbuf("lnT", [128, D], F32)
    identsb = p.sbuf("identsb", [128, 128], F32)
    wr_sb = p.sbuf("wr_sb", [128, 8, E], F32)
    logit = p.sbuf("logit", [128, E], F32)
    scr = {
        "mx1": p.sbuf("mx1", [128, 1], F32),
        "sm": p.sbuf("sm", [128, 1], F32),
        "mx8": p.sbuf("mx8", [128, 8], F32),
        "mi8": p.sbuf("mi8", [128, 8], U32),
        "mif": p.sbuf("mif", [128, 8], F32),
    }
    dec = p.sbuf("dec_sb", [128, 4], F32)
    ptr = [p.psum("ptr0", [128, 128], F32), p.psum("ptr1", [128, 128], F32)]
    plog = p.psum("plog", [128, E], F32)

    p.block()
    p.sems()

    ins = nc.sync.dma_start(out=identsb[:], in_=ident[:])
    p.LD.dma(ins)
    ins = nc.sync.dma_start(out=wr_sb[:], in_=wr.rearrange("(c p) e -> p c e", p=128))
    p.LD.dma(ins)

    for i in range(NTILE_LOC):
        if i > 0:
            p.DVE.wait(nc.sync)
            p.ST.wait(nc.vector)
        ins = nc.sync.dma_start(out=xt[:], in_=x0s[i * 128:(i + 1) * 128, :])
        p.LD.dma(ins)
        p.LD.wait(nc.vector)
        if i > 0:
            p.PE.wait(nc.vector)
        emit_ln_tile(p, xt, stat, mv, sd, rstd, lnf, lnb)
        p.DVE.wait(nc.sync)
        ins = nc.sync.dma_start(out=lnb_o[i * 128:(i + 1) * 128, :], in_=lnb[:])
        p.ST.dma(ins)
        p.DVE.wait(nc.tensor)
        emit_router_tile(p, lnf, identsb, wr_sb, lnT, ptr, plog, logit, scr, dec)
        p.DVE.wait(nc.sync)
        ins = nc.sync.dma_start(out=dec_o[i * 128:(i + 1) * 128, :], in_=dec[:])
        p.ST.dma(ins)
        p.ST.wait(nc.sync)
    p.ST.wait(nc.gpsimd)
    p.close()
    return p.nc



_CACHE = {}


def _get(name, builder):
    if name not in _CACHE:
        _CACHE[name] = builder()
    return _CACHE[name]


def kernel(**inputs):
    return kernel_fused(**inputs)


def kernel_multilaunch(L0_attn_hidden_states, L0_attn_w_qkuv, L0_attn_w_out,
           L0_attn_ln_gamma, L0_attn_ln_beta, L0_ffn_w_router,
           L0_ffn_wi, L0_ffn_wo):
    x0 = np.ascontiguousarray(np.asarray(L0_attn_hidden_states, dtype=np.float32)
                              ).reshape(GG, T, D)
    wq = np.ascontiguousarray(np.asarray(L0_attn_w_qkuv, dtype=np.float32))
    w_out = np.ascontiguousarray(np.asarray(L0_attn_w_out, dtype=np.float32))
    gamma = np.asarray(L0_attn_ln_gamma, dtype=np.float32)
    beta = np.asarray(L0_attn_ln_beta, dtype=np.float32)
    wr = np.ascontiguousarray(np.asarray(L0_ffn_w_router, dtype=np.float32))
    wi = np.ascontiguousarray(np.asarray(L0_ffn_wi, dtype=np.float32))
    wo = np.ascontiguousarray(np.asarray(L0_ffn_wo, dtype=np.float32))

    consts = host_consts()
    cns = {"c_" + n: consts[n] for n in
           ("cols8", "triu_bf", "striu_bf", "ones128x1_bf", "ones1x128")}
    cores = list(range(8))
    gq = [divmod(c, QP) for c in cores]

    def decw_of(dec):
        return [np.ascontiguousarray(
            dec[g].reshape(NTILE, 128, 4).transpose(1, 0, 2)) for g in range(GG)]

    def gather_full(res, key, axis=0):
        return [np.concatenate([np.asarray(res.results[g * QP + q][key])
                                for q in range(QP)], axis=axis) for g in range(GG)]

    def per_core(res, key):
        return [np.asarray(r[key]) for r in res.results]

    resA = run_bass_kernel_spmd(_get("A", build_A), [
        {"x0s": np.ascontiguousarray(x0[g, q * TLOC:(q + 1) * TLOC]),
         "wr": wr, "ident": consts["ident"]}
        for (g, q) in gq], core_ids=cores)
    lnb1 = gather_full(resA, "lnb")
    dec1 = gather_full(resA, "dec")
    decw1 = decw_of(dec1)

    def expert_maps(lnb, decw):
        return [
            {**cns, "lnb": np.ascontiguousarray(lnb[g]),
             "decw": decw[g],
             "wi2": wi[2 * q:2 * q + 2], "wo2": wo[2 * q:2 * q + 2],
             "eq0": np.tile(np.array([2.0 * q, 2.0 * q + 1], dtype=np.float32),
                            (128, 1))}
            for (g, q) in gq]
    resB = run_bass_kernel_spmd(_get("B", build_B), expert_maps(lnb1, decw1),
                                core_ids=cores)
    etab1 = gather_full(resB, "etab")

    resC = run_bass_kernel_spmd(_get("C", build_C), [
        {**cns, "xs": np.ascontiguousarray(x0[g, q * TLOC:(q + 1) * TLOC]),
         "decw": decw1[g], "etab": etab1[g],
         "myq": np.array([[32 * q]], dtype=np.int32)}
        for (g, q) in gq], core_ids=cores)
    x1s = per_core(resC, "x1s")
    ln2 = gather_full(resC, "ln2b")

    din = []
    for (g, q) in gq:
        cols = []
        for pair in range(2):
            h0 = 4 * q + 2 * pair
            cols += list(range(h0 * 64, (h0 + 2) * 64))
        for pair in range(2):
            h0 = 4 * q + 2 * pair
            cols += list(range(1024 + h0 * 64, 1024 + (h0 + 2) * 64))
        h0 = 4 * q
        cols += list(range(3072 + h0 * 64, 3072 + (h0 + 4) * 64))
        cols += list(range(2048 + h0 * 64, 2048 + (h0 + 4) * 64))
        din.append({
            "c_eye64s_bf": consts["eye64s_bf"],
            "ln2b": np.ascontiguousarray(ln2[g]),
            "wq4": np.ascontiguousarray(wq[:, cols]),
            "gam": np.ascontiguousarray(np.tile(gamma, (128, 8))),
            "bet": np.ascontiguousarray(np.tile(beta, (128, 8)))})
    resD = run_bass_kernel_spmd(_get("D", build_D), din, core_ids=cores)
    gated = gather_full(resD, "gatedb", axis=1)

    resE = run_bass_kernel_spmd(_get("E", build_E), [
        {"x1s": x1s[c], "gfull": np.ascontiguousarray(
            gated[g][q * TLOC:(q + 1) * TLOC]),
         "wo": w_out, "wr": wr, "ident": consts["ident"]}
        for c, (g, q) in enumerate(gq)], core_ids=cores)
    x2s = per_core(resE, "x2s")
    ln3 = gather_full(resE, "ln3b")
    dec3 = gather_full(resE, "dec")
    decw3 = decw_of(dec3)

    resF = run_bass_kernel_spmd(_get("B", build_B), expert_maps(ln3, decw3),
                                core_ids=cores)
    etab3 = gather_full(resF, "etab")

    resG = run_bass_kernel_spmd(_get("G", lambda: build_C(final=True)), [
        {**cns, "xs": x2s[c], "decw": decw3[g], "etab": etab3[g],
         "myq": np.array([[32 * q]], dtype=np.int32)}
        for c, (g, q) in enumerate(gq)], core_ids=cores)
    x3 = gather_full(resG, "x3s")
    out = np.stack(x3).reshape(GG, T, D).astype(np.float32)
    return out
